# version 12
# baseline (speedup 1.0000x reference)
import math
import resource as _resource
import numpy as np
import concourse.bass as bass
import concourse.mybir as mybir
from concourse.bass_utils import run_bass_kernel_spmd

try:
    _resource.setrlimit(_resource.RLIMIT_STACK,
                        (1 << 30, _resource.getrlimit(_resource.RLIMIT_STACK)[1]))
except (ValueError, OSError):
    pass

F32 = mybir.dt.float32
BF16 = mybir.dt.bfloat16
F32R = mybir.dt.float32r
I32 = mybir.dt.int32
AF = mybir.ActivationFunctionType
AL = mybir.AluOpType
AX = mybir.AxisListType

NCORES = 8
D = 2048
T = 8192
TL = T // NCORES
NDT = D // 128
RG = [list(range(NCORES))]
LN_EPS = 1e-5
RMS_EPS = 1.1920929e-07


def bf16dt():
    import ml_dtypes
    return ml_dtypes.bfloat16


class Sem:
    __slots__ = ("h", "n")

    def __init__(self, nc, name):
        self.h = nc.alloc_semaphore(name)
        self.n = 0

    def inc(self, n=1):
        self.n += n
        return self.n


_P_BY_NC = {}


class _BlockProxy:
    __slots__ = ("blk",)

    def __init__(self, blk):
        self.blk = blk

    def __enter__(self):
        return self.blk

    def __exit__(self, *a):
        return False


_FAST = [False]
FAST_BLOCKS = frozenset({2})


def _blk(nc):
    P_ = _P_BY_NC[id(nc)]
    if not _FAST[0]:
        if P_._blk is not None:
            P_.close_block()
        return nc.Block()
    if P_._blk is None:
        P_.open_block()
    return _BlockProxy(P_._blk)


def set_fast(on):
    _FAST[0] = on


class P:

    def __init__(self):
        self.nc = bass.Bass("TRN2", target_bir_lowering=False)
        self._consts = {}
        self.ld = Sem(self.nc, "ld")
        self.gd = Sem(self.nc, "gd")
        self.sd = Sem(self.nc, "sdq")
        self.pe = Sem(self.nc, "pe")
        self.ve = Sem(self.nc, "ve")
        self.sc = Sem(self.nc, "sc")
        self.cc = Sem(self.nc, "cc")
        self.pool = None
        self._blk = None
        _P_BY_NC[id(self.nc)] = self

    def open_block(self):
        assert self._blk is None
        self._blk = self.nc.Block()
        self._blk.__enter__()

    def close_block(self):
        if self._blk is not None:
            self._blk.__exit__(None, None, None)
            self._blk = None

    def make_pool(self):
        if self.pool is None:
            self._psum_base0 = self.nc.psum_base
            self.pool = [self.ps(f"pool{i}", [128, 512], F32) for i in range(8)]
        return self.pool

    def new_pool(self, tag):
        self.make_pool()
        self.nc.psum_base = self._psum_base0
        self.pool = [self.ps(f"pool{i}_{tag}", [128, 512], F32) for i in range(8)]
        return self.pool

    def const(self, name, arr):
        if name not in self._consts:
            self._consts[name] = self.nc.inline_tensor(np.ascontiguousarray(arr), name=name)
        return self._consts[name].ap()

    def sb(self, name, shape, dtype):
        return self.nc.alloc_sbuf_tensor(name, shape, dtype)

    def ps(self, name, shape, dtype=F32):
        return self.nc.alloc_psum_tensor(name, shape, dtype)

    def sconst(self, name, arr):
        arr = np.ascontiguousarray(arr)
        if not hasattr(self, "_sconsts"):
            self._sconsts, self._sconst_pend, self.v_consts = {}, [], 0
        if name not in self._sconsts:
            dt = {np.dtype(np.float32): F32, np.dtype(bf16dt()): BF16,
                  np.dtype(np.int32): I32}[arr.dtype]
            t = self.sb("sc_" + name, list(arr.shape), dt)
            d = self.const("cd_" + name, arr)
            self._sconsts[name] = t
            self._sconst_pend.append((t, d))
        return self._sconsts[name]

    def flush_consts(self):
        pend = getattr(self, "_sconst_pend", [])
        if not pend:
            return self.ld.n
        with _blk(self.nc) as b:
            @b.sync
            def _(s):
                for (t, d) in pend:
                    s.dma_start(out=t[:, :], in_=d[:, :]).then_inc(self.ld.h, 16)
        self.v_consts = self.ld.inc(16 * len(pend))
        self._sconst_pend = []
        return self.v_consts

    def ones_col_b(self):
        return self.sconst("ones_col_b", np.ones((128, 1), bf16dt()))

    def ones_row_f(self):
        return self.sconst("ones_row_f", np.ones((1, 128), np.float32))

    def ident_b(self):
        return self.sconst("ident_b", np.eye(128, dtype=bf16dt()))



def load_residual(P_, xr, xin_ap):
    with _blk(P_.nc) as b:
        @b.sync
        def _(s):
            for i in range(NDT):
                s.dma_start(out=xr[i][:, :], in_=xin_ap[128 * i:128 * (i + 1), :]
                            ).then_inc(P_.ld.h, 16)
    return P_.ld.inc(16 * NDT)


def store_residual(P_, xr, xout_ap, waits):
    with _blk(P_.nc) as b:
        @b.sync
        def _(s):
            for (sm, v) in waits:
                s.wait_ge(sm.h, v)
            for i in range(NDT):
                s.dma_start(out=xout_ap[128 * i:128 * (i + 1), :], in_=xr[i][:, :]
                            ).then_inc(P_.ld.h, 16)
            s.wait_ge(P_.ld.h, P_.ld.inc(16 * NDT))



def emit_norm(P_, ctx, xr, xnb, y_tiles=None, y_ready=None, subtract_mean=True,
              eps=LN_EPS, pre_waits=(), write_back=True):
    nc = P_.nc
    sq = ctx["sq"]
    atmp = ctx["atmp"]
    r_msq, r_mean, r_tmp, r_a, r_nb = (ctx["r_msq"], ctx["r_mean"], ctx["r_tmp"],
                                       ctx["r_a"], ctx["r_nb"])
    pstat, pstat2, pbc, pbc2 = ctx["pstat"], ctx["pstat2"], ctx["pbc"], ctx["pbc2"]
    ones_b = P_.ones_col_b()[:, :]
    ones_r = P_.ones_row_f()[:, :]
    v_consts = P_.v_consts

    v_add0 = P_.ve.n
    with _blk(nc) as b:
        @b.vector
        def _(v):
            for (sm, w) in pre_waits:
                v.wait_ge(sm.h, w)
            for i in range(NDT):
                if y_tiles is not None:
                    if y_ready is not None and y_ready[i] is not None:
                        sm, w = y_ready[i]
                        v.wait_ge(sm.h, w)
                    v.tensor_add(xr[i][:, :], xr[i][:, :], y_tiles[i]).then_inc(P_.ve.h, 1)
                else:
                    v.engine_nop().then_inc(P_.ve.h, 1)
    P_.ve.inc(NDT)

    with _blk(nc) as b:
        @b.scalar
        def _(s):
            for i in range(NDT):
                s.wait_ge(P_.ve.h, v_add0 + i + 1)
                s.copy(xnb[i][:, :], xr[i][:, :]).then_inc(P_.sc.h, 1)
    P_.sc.inc(NDT)

    for tc in range(TL // 512):
        sl = slice(512 * tc, 512 * (tc + 1))
        sc0 = P_.sc.n
        with _blk(nc) as b:
            @b.scalar
            def _(s):
                if tc > 0:
                    s.wait_ge(P_.pe.h, ctx["_norm_pelast"])
                for i in range(NDT):
                    s.activation(sq[i][:, :], xr[i][:, sl], AF.Square
                                 ).then_inc(P_.sc.h, 1)
        P_.sc.inc(NDT)
        with _blk(nc) as b:
            @b.tensor
            def _(t):
                if tc > 0:
                    t.wait_ge(P_.ve.h, ctx["_norm_vlast"])
                t.wait_ge(P_.ld.h, v_consts)
                for i in range(NDT):
                    t.wait_ge(P_.sc.h, sc0 + i + 1)
                    t.matmul(pstat[0:1, :], ones_b, sq[i][:, :],
                             start=(i == 0), stop=(i == NDT - 1))
                    ins = t.matmul(pstat2[0:1, :], ones_b, xnb[i][:, sl],
                                   start=(i == 0), stop=(i == NDT - 1))
                    if i == NDT - 1:
                        ins.then_inc(P_.pe.h, 1)
        v_sums = P_.pe.inc(1)
        ctx["_norm_pelast"] = v_sums

        with _blk(nc) as b:
            @b.vector
            def _(v):
                v.wait_ge(P_.pe.h, v_sums)
                msq, mean, tmp = r_msq[0:1, :], r_mean[0:1, :], r_tmp[0:1, :]
                v.tensor_scalar(msq, pstat[0:1, :], 1.0 / D, None, AL.mult)
                if subtract_mean:
                    v.tensor_scalar(mean, pstat2[0:1, :], 1.0 / D, None, AL.mult)
                    v.tensor_tensor(tmp, mean, mean, AL.mult)
                    v.tensor_sub(msq, msq, tmp)
                v.tensor_scalar(msq, msq, float(eps), None, AL.add).then_inc(P_.ve.h, 1)
        v_var = P_.ve.inc(1)

        with _blk(nc) as b:
            @b.scalar
            def _(s):
                s.wait_ge(P_.ve.h, v_var)
                s.activation(r_a[0:1, :], r_msq[0:1, :], AF.Ln)
                s.activation(r_a[0:1, :], r_a[0:1, :], AF.Exp, scale=-0.5
                             ).then_inc(P_.sc.h, 1)
        v_a = P_.sc.inc(1)

        if subtract_mean:
            with _blk(nc) as b:
                @b.vector
                def _(v):
                    v.wait_ge(P_.sc.h, v_a)
                    v.tensor_tensor(r_nb[0:1, :], r_mean[0:1, :], r_a[0:1, :],
                                    AL.mult).then_inc(P_.ve.h, 1)
            P_.ve.inc(1)
        v_rows_v, v_rows_s = P_.ve.n, P_.sc.n

        with _blk(nc) as b:
            @b.tensor
            def _(t):
                t.wait_ge(P_.sc.h, v_rows_s)
                t.wait_ge(P_.ve.h, v_rows_v)
                t.matmul(pbc[:, :], ones_r, r_a[0:1, :], start=True, stop=True
                         ).then_inc(P_.pe.h, 1)
                if subtract_mean:
                    t.matmul(pbc2[:, :], ones_r, r_nb[0:1, :], start=True, stop=True
                             ).then_inc(P_.pe.h, 1)
        v_bc = P_.pe.inc(2 if subtract_mean else 1)

        with _blk(nc) as b:
            @b.vector
            def _(v):
                v.wait_ge(P_.pe.h, v_bc)
                for i in range(NDT):
                    if write_back:
                        v.tensor_tensor(atmp[:, :], xr[i][:, sl], pbc[:, :], AL.mult)
                        if subtract_mean:
                            v.tensor_sub(xr[i][:, sl], atmp[:, :], pbc2[:, :])
                        else:
                            v.tensor_copy(xr[i][:, sl], atmp[:, :])
                        v.tensor_copy(xnb[i][:, sl], xr[i][:, sl])
                    else:
                        v.tensor_tensor(xnb[i][:, sl], xr[i][:, sl], pbc[:, :],
                                        AL.mult)
                v.engine_nop().then_inc(P_.ve.h, 1)
        ctx["_norm_vlast"] = P_.ve.inc(1)
    return (P_.ve, P_.ve.n)


def make_norm_ctx(P_, tag="", sq_alias=None, atmp_alias=None, rows_alias=None):
    d = {
        "sq": (sq_alias if sq_alias is not None else
               [P_.sb(f"nsq{tag}_{i}", [128, 512], BF16) for i in range(NDT)]),
        "atmp": (atmp_alias if atmp_alias is not None else
                 P_.sb(f"natmp{tag}", [128, 512], F32)),
        "r_msq": (rows_alias[0] if rows_alias else P_.sb(f"nrmsq{tag}", [1, 512], F32)),
        "r_mean": (rows_alias[1] if rows_alias else P_.sb(f"nrmean{tag}", [1, 512], F32)),
        "r_tmp": (rows_alias[2] if rows_alias else P_.sb(f"nrtmp{tag}", [1, 512], F32)),
        "r_a": (rows_alias[3] if rows_alias else P_.sb(f"nra{tag}", [1, 512], F32)),
        "r_nb": ((rows_alias[4] if rows_alias[4] is not None
                  else P_.sb(f"nrnb{tag}", [1, 512], F32)) if rows_alias
                 else P_.sb(f"nrnb{tag}", [1, 512], F32)),
        "_norm_pelast": 0,
        "pstat": P_.make_pool()[0],
        "pstat2": P_.make_pool()[1],
        "pbc": P_.make_pool()[2],
        "pbc2": P_.make_pool()[3],
    }
    return d



def emit_proj_dp(P_, wT_ap, o_size, xnb, xnb_ready, evac, wbuf, psums, tag,
                 t_len=TL, tch=512):
    nc = P_.nc
    n_oc = (o_size + 127) // 128
    ntc = t_len // tch
    ld_done = []
    for i in range(NDT):
        buf = wbuf[i % 2]
        waits = []
        if i >= 2:
            waits.append(("pe", ld_done[i - 2][1]))
        with _blk(nc) as b:
            @b.gpsimd
            def _(g, i=i, buf=buf, waits=waits):
                for (kind, w) in waits:
                    g.wait_ge(P_.pe.h, w)
                g.dma_start(out=buf[:, :], in_=wT_ap[128 * i:128 * (i + 1), :]
                            ).then_inc(P_.gd.h, 16)
        v_ld = P_.gd.inc(16)
        with _blk(nc) as b:
            @b.tensor
            def _(t, i=i, buf=buf, v_ld=v_ld):
                t.wait_ge(P_.gd.h, v_ld)
                if i == 0 and xnb_ready is not None:
                    t.wait_ge(xnb_ready[0].h, xnb_ready[1])
                for oc in range(n_oc):
                    osz = min(128, o_size - 128 * oc)
                    for tc in range(ntc):
                        pa = psums[(oc * ntc + tc) % len(psums)]
                        ins = t.matmul(pa[0:osz, :],
                                       buf[:, 128 * oc:128 * oc + osz],
                                       xnb_sl(xnb, i, tc, tch),
                                       start=(i == 0), stop=(i == NDT - 1))
                        if i == NDT - 1:
                            ins.then_inc(P_.pe.h, 1)
        if i == NDT - 1:
            P_.pe.inc(n_oc * ntc)
        ld_done.append(("pe", P_.pe.n))
    base = P_.pe.n - n_oc * ntc
    for oc in range(n_oc):
        for tc in range(ntc):
            k = oc * ntc + tc
            evac(oc, psums[k % len(psums)], tc, (P_.pe, base + k + 1))
    return


def xnb_sl(xnb, i, tc, tch):
    return xnb[i][:, tch * tc:tch * (tc + 1)]



def emit_l2_ffn(P_, ctx, xr, xnb, xnb_ready, wT_ap, w_ap, bias_ap):
    nc = P_.nc
    sgnT = ctx["sgnT"]
    wstage = ctx["wstage"]
    alphab = ctx["alphab"]
    psA, psB = ctx["psA"], ctx["psB"]
    v_x = ctx["v_x"]

    ctx["_a0"] = P_.ve.n
    for j in range(NDT):
        with _blk(nc) as b:
            @b.gpsimd
            def _(g, j=j):
                if j >= 1:
                    g.wait_ge(P_.ve.h, ctx["_a0"] + j)
                g.dma_start(out=wstage[:, :], in_=w_ap[128 * j:128 * (j + 1), :]
                            ).then_inc(P_.gd.h, 16)
        v_ld = P_.gd.inc(16)
        with _blk(nc) as b:
            @b.vector
            def _(v, j=j, v_ld=v_ld):
                v.wait_ge(P_.gd.h, v_ld)
                v.tensor_reduce(alphab[:, j:j + 1], wstage[:, :], AX.X, AL.add,
                                apply_absolute_value=True).then_inc(P_.ve.h, 1)
    v_alpha0 = P_.ve.n
    with _blk(nc) as b:
        @b.vector
        def _(v):
            v.wait_ge(P_.ve.h, v_alpha0)
            v.tensor_scalar(alphab[:, 0:NDT], alphab[:, 0:NDT], 1.0 / D, None,
                            AL.mult).then_inc(P_.ve.h, 1)
    v_alpha = P_.ve.inc(1)

    v_bias = P_.ld.inc(16)
    with _blk(nc) as b:
        @b.sync
        def _(s):
            s.dma_start(out=alphab[:, 17:33], in_=bias_ap[:, :]).then_inc(P_.ld.h, 16)

    ctx["_s0"] = P_.sc.n
    for i in range(NDT):
        with _blk(nc) as b:
            @b.gpsimd
            def _(g, i=i):
                if i == 0:
                    g.wait_ge(P_.ve.h, ctx["_a0"] + NDT)
                else:
                    g.wait_ge(P_.sc.h, ctx["_s0"] + i)
                g.dma_start(out=wstage[:, :], in_=wT_ap[128 * i:128 * (i + 1), :]
                            ).then_inc(P_.gd.h, 16)
        v_ld = P_.gd.inc(16)
        with _blk(nc) as b:
            @b.scalar
            def _(s, i=i, v_ld=v_ld):
                s.wait_ge(P_.gd.h, v_ld)
                s.sign(sgnT[i][:, :], wstage[:, :]).then_inc(P_.sc.h, 1)
    v_sgn = P_.sc.n

    vb0 = P_.ve.n
    with _blk(nc) as b:
        @b.vector
        def _(v):
            v.wait_ge(P_.ld.h, v_bias)
            v.wait_ge(v_x[0].h, v_x[1])
            for oc in range(NDT):
                v.tensor_scalar(xr[oc][:, :], xr[oc][:, :],
                                alphab[:, 17 + oc:18 + oc], None, AL.add
                                ).then_inc(P_.ve.h, 1)
    P_.ve.inc(NDT)

    ev_waits = {}
    for oc in range(NDT):
        for tc in range(2):
            pa = psA if (oc * 2 + tc) % 2 == 0 else psB
            with _blk(nc) as b:
                @b.tensor
                def _(t, oc=oc, tc=tc, pa=pa):
                    t.wait_ge(P_.sc.h, v_sgn)
                    t.wait_ge(xnb_ready[0].h, xnb_ready[1])
                    k = oc * 2 + tc
                    if k >= 2:
                        t.wait_ge(P_.ve.h, ev_waits[k - 2])
                    for i in range(NDT):
                        ins = t.matmul(pa[:, :], sgnT[i][:, 128 * oc:128 * (oc + 1)],
                                       xnb[i][:, 512 * tc:512 * (tc + 1)],
                                       start=(i == 0), stop=(i == NDT - 1))
                    ins.then_inc(P_.pe.h, 1)
            v_mm = P_.pe.inc(1)
            with _blk(nc) as b:
                @b.vector
                def _(v, oc=oc, tc=tc, pa=pa, v_mm=v_mm):
                    v.wait_ge(P_.pe.h, v_mm)
                    v.wait_ge(P_.ve.h, v_alpha)
                    sl = slice(512 * tc, 512 * (tc + 1))
                    v.scalar_tensor_tensor(xr[oc][:, sl], pa[:, :],
                                           alphab[:, oc:oc + 1], xr[oc][:, sl],
                                           AL.mult, AL.add).then_inc(P_.ve.h, 1)
            ev_waits[oc * 2 + tc] = P_.ve.inc(1)

    return None, [(P_.ve, P_.ve.n)]



def build_l2_ffn_prog(skip_norm=False):
    Pr = P()
    nc = Pr.nc
    xin = nc.dram_tensor("xin", [D, TL], F32, kind="ExternalInput").ap()
    wT = nc.dram_tensor("l2f_wT", [D, D], F32, kind="ExternalInput").ap()
    w = nc.dram_tensor("l2f_w", [D, D], F32, kind="ExternalInput").ap()
    bias = nc.dram_tensor("l2f_bias", [128, 16], F32, kind="ExternalInput").ap()
    xout = nc.dram_tensor("xout", [D, TL], F32, kind="ExternalOutput").ap()

    xr = [Pr.sb(f"xr{i}", [128, TL], F32) for i in range(NDT)]
    xnb = [Pr.sb(f"xnb{i}", [128, TL], BF16) for i in range(NDT)]
    nctx = make_norm_ctx(Pr)
    ctx = {
        "sgnT": [Pr.sb(f"sgnT{i}", [128, D], BF16) for i in range(NDT)],
        "wstage": Pr.sb("wstage", [128, D], BF16),
        "alphab": Pr.sb("alphab", [128, 33], F32),
        "psA": Pr.make_pool()[4],
        "psB": Pr.make_pool()[5],
    }
    Pr.ones_col_b(); Pr.ones_row_f()
    Pr.flush_consts()
    v_x = load_residual(Pr, xr, xin)

    with _blk(nc) as b:
        @b.scalar
        def _(s):
            s.wait_ge(Pr.ld.h, v_x)
            for i in range(NDT):
                s.copy(xnb[i][:, :], xr[i][:, :]).then_inc(Pr.sc.h, 1)
    xnb_ready = (Pr.sc, Pr.sc.inc(NDT))

    ctx["v_x"] = (Pr.ld, v_x)
    _, y_ready = emit_l2_ffn(Pr, ctx, xr, xnb, xnb_ready, wT, w, bias)
    if skip_norm:
        store_residual(Pr, xr, xout, y_ready)
        Pr.close_block()
        return Pr
    fin = emit_norm(Pr, nctx, xr, xnb, None, None, subtract_mean=True,
                    pre_waits=y_ready)
    store_residual(Pr, xr, xout, [fin])
    Pr.close_block()
    return Pr


def build_l3_ffn_prog():
    Pr = P()
    nc = Pr.nc
    xin = nc.dram_tensor("xin", [D, TL], F32, kind="ExternalInput").ap()
    w1s = nc.dram_tensor("l3f_w1s", [D, 1024], F32, kind="ExternalInput").ap()
    w3s = nc.dram_tensor("l3f_w3s", [D, 1024], F32, kind="ExternalInput").ap()
    w2s = nc.dram_tensor("l3f_w2s", [H_L3, 256], F32, kind="ExternalInput").ap()
    xout = nc.dram_tensor("xout", [D, TL], F32, kind="ExternalOutput").ap()
    Pr.make_pool()
    xr = [Pr.sb(f"xr{i}", [128, TL], F32) for i in range(NDT)]
    ht = [Pr.sb(f"l3ht{i}", [128, 512], BF16) for i in range(NHT)]
    xnb = [Pr.sb(f"xnb{i}", [128, TL], BF16) for i in range(NDT)]
    ctx = {
        "ht": ht,
        "w1st": [Pr.sb(f"w1st{i}", [128, 2048], BF16) for i in range(2)],
        "w3st": [Pr.sb(f"w3st{i}", [128, 2048], BF16) for i in range(2)],
        "w2st": [Pr.sb(f"w2st{i}", [128, 1024], BF16) for i in range(2)],
        "silt": Pr.sb("silt", [128, 2048], BF16),
        "wt_alias": ht,
    }
    nctx = make_norm_ctx(Pr, sq_alias=[ht[i] for i in range(NDT)])
    Pr.ones_col_b(); Pr.ones_row_f()
    Pr.sconst("h128_bf", _hadamard128_bf())
    Pr.flush_consts()
    Pr._l3_wt = [ht[i][:, 0:256] for i in range(NHT)]
    Pr._l3_wtmp = ctx["silt"][:, 0:256]
    w1b, w3b, w2b, wready = emit_l3_ffn_prep(Pr, w1s, w3s, w2s)
    v_x = load_residual(Pr, xr, xin)
    with _blk(nc) as b:
        @b.scalar
        def _(s):
            s.wait_ge(Pr.ld.h, v_x)
            for i in range(NDT):
                s.copy(xnb[i][:, :], xr[i][:, :]).then_inc(Pr.sc.h, 1)
    xnb_ready = (Pr.sc, Pr.sc.inc(NDT))
    fin = emit_l3_ffn(Pr, ctx, xr, xnb, xnb_ready, w1b, w3b, w2b, wready,
                      (Pr.ld, v_x))
    fin2 = emit_norm(Pr, nctx, xr, xnb, None, None, subtract_mean=True,
                     pre_waits=fin)
    store_residual(Pr, xr, xout, [fin2])
    Pr.close_block()
    return Pr


def run_l3_ffn(x, W1, W2, W3):
    Pr = build_l3_ffn_prog()
    xs = _shard_x(x)
    w1T = np.ascontiguousarray(W1.T)
    w3T = np.ascontiguousarray(W3.T)
    w2T = np.ascontiguousarray(W2.T)
    in_maps = [{"xin": xs[c],
                "l3f_w1s": np.ascontiguousarray(w1T[:, 1024 * c:1024 * (c + 1)]),
                "l3f_w3s": np.ascontiguousarray(w3T[:, 1024 * c:1024 * (c + 1)]),
                "l3f_w2s": np.ascontiguousarray(w2T[:, 256 * c:256 * (c + 1)])}
               for c in range(NCORES)]
    res = run_bass_kernel_spmd(Pr.nc, in_maps, core_ids=list(range(NCORES)))
    return _unshard_x([res.results[c]["xout"] for c in range(NCORES)])


def build_norm_prog():
    Pr = P()
    nc = Pr.nc
    xin = nc.dram_tensor("xin", [D, TL], F32, kind="ExternalInput").ap()
    xout = nc.dram_tensor("xout", [D, TL], F32, kind="ExternalOutput").ap()
    nout = nc.dram_tensor("nout", [D, TL], F32, kind="ExternalOutput").ap()
    xr = [Pr.sb(f"xr{i}", [128, TL], F32) for i in range(NDT)]
    xnb = [Pr.sb(f"xnb{i}", [128, TL], BF16) for i in range(NDT)]
    nb32 = [Pr.sb(f"nb32{i}", [128, TL], F32) for i in range(NDT)]
    nctx = make_norm_ctx(Pr)
    Pr.ones_col_b(); Pr.ones_row_f()
    Pr.flush_consts()
    v_x = load_residual(Pr, xr, xin)
    fin = emit_norm(Pr, nctx, xr, xnb, None, None, subtract_mean=True,
                    pre_waits=[(Pr.ld, v_x)])
    with _blk(nc) as b:
        @b.vector
        def _(v):
            v.wait_ge(fin[0].h, fin[1])
            for i in range(NDT):
                v.tensor_copy(nb32[i][:, :], xnb[i][:, :]).then_inc(Pr.ve.h, 1)
    vc = Pr.ve.inc(NDT)
    store_residual(Pr, xr, xout, [(Pr.ve, vc)])
    with _blk(nc) as b:
        @b.sync
        def _(s):
            for i in range(NDT):
                s.dma_start(out=nout[128 * i:128 * (i + 1), :], in_=nb32[i][:, :]
                            ).then_inc(Pr.ld.h, 16)
            s.wait_ge(Pr.ld.h, Pr.ld.inc(16 * NDT))
    Pr.close_block()
    return Pr


def run_norm(x):
    Pr = build_norm_prog()
    xs = _shard_x(x)
    res = run_bass_kernel_spmd(Pr.nc, [{"xin": xs[c]} for c in range(NCORES)],
                               core_ids=list(range(NCORES)))
    return (_unshard_x([res.results[c]["xout"] for c in range(NCORES)]),
            _unshard_x([res.results[c]["nout"] for c in range(NCORES)]))





H_L3 = 8192
NHT = H_L3 // 128


def emit_l3_ffn_prep(P_, w1s_ap, w3s_ap, w2s_ap):
    nc = P_.nc
    w1l = nc.dram_tensor("l3f_w1l", [D, 1024], BF16)
    w3l = nc.dram_tensor("l3f_w3l", [D, 1024], BF16)
    w2l = nc.dram_tensor("l3f_w2l", [H_L3, 256], BF16)
    w1b = nc.dram_tensor("l3f_w1b", [NCORES * D, 1024], BF16, addr_space="Shared")
    w3b = nc.dram_tensor("l3f_w3b", [NCORES * D, 1024], BF16, addr_space="Shared")
    w2b = nc.dram_tensor("l3f_w2b", [NCORES * H_L3, 256], BF16, addr_space="Shared")
    with _blk(nc) as b:
        @b.gpsimd
        def _(g):
            for j in range(2):
                g.dma_start(out=w1l.ap()[1024 * j:1024 * (j + 1), :],
                            in_=w1s_ap[1024 * j:1024 * (j + 1), :]).then_inc(P_.gd.h, 16)
                g.dma_start(out=w3l.ap()[1024 * j:1024 * (j + 1), :],
                            in_=w3s_ap[1024 * j:1024 * (j + 1), :]).then_inc(P_.gd.h, 16)
    v_cast13 = P_.gd.inc(64)
    wt = [w[:, 0:256] if hasattr(w, "shape") else w for w in P_._l3_wt]
    wtmp = P_._l3_wtmp
    h128 = P_.sconst("h128_bf", _hadamard128_bf())
    pw = P_.pool[6][:, 0:256]
    with _blk(nc) as b:
        @b.gpsimd
        def _(g):
            for i in range(NHT):
                g.dma_start(out=wt[i], in_=w2s_ap[128 * i:128 * (i + 1), :]
                            ).then_inc(P_.gd.h, 16)
    v_w2ld = P_.gd.inc(16 * NHT)
    v0 = P_.ve.n
    with _blk(nc) as b:
        @b.vector
        def _(v):
            v.wait_ge(P_.gd.h, v_w2ld)
            for s in range(6):
                step = 1 << s
                for a in range(NHT):
                    if a & step:
                        continue
                    A, B = wt[a], wt[a ^ step]
                    v.tensor_copy(wtmp, A)
                    v.tensor_add(A, wtmp, B)
                    v.tensor_sub(B, wtmp, B)
            v.engine_nop().then_inc(P_.ve.h, 1)
    v_bfly = P_.ve.inc(1)
    for i in range(NHT):
        v_prev_ev = P_.sc.n
        with _blk(nc) as b:
            @b.tensor
            def _(t, i=i, v_prev_ev=v_prev_ev):
                t.wait_ge(P_.ve.h, v_bfly)
                t.wait_ge(P_.ld.h, P_.v_consts)
                if i >= 1:
                    t.wait_ge(P_.sc.h, v_prev_ev)
                t.matmul(pw, h128[:, :], wt[i], start=True, stop=True
                         ).then_inc(P_.pe.h, 1)
        v_mm = P_.pe.inc(1)
        with _blk(nc) as b:
            @b.scalar
            def _(s, i=i, v_mm=v_mm):
                s.wait_ge(P_.pe.h, v_mm)
                s.copy(wt[i], pw).then_inc(P_.sc.h, 1)
        v_ev = P_.sc.inc(1)
        with _blk(nc) as b:
            @b.gpsimd
            def _(g, i=i, v_ev=v_ev):
                g.wait_ge(P_.sc.h, v_ev)
                g.dma_start(out=w2l.ap()[128 * i:128 * (i + 1), :], in_=wt[i]
                            ).then_inc(P_.gd.h, 16)
    v_w2st = P_.gd.n
    with _blk(nc) as b:
        @b.gpsimd
        def _(g):
            g.wait_ge(P_.gd.h, v_cast13)
            g.collective_compute("AllGather", AL.bypass, replica_groups=RG,
                                 ins=[w1l.ap()], outs=[w1b.ap()]).then_inc(P_.cc.h)
            g.collective_compute("AllGather", AL.bypass, replica_groups=RG,
                                 ins=[w3l.ap()], outs=[w3b.ap()]).then_inc(P_.cc.h)
            g.wait_ge(P_.gd.h, v_w2st)
            g.collective_compute("AllGather", AL.bypass, replica_groups=RG,
                                 ins=[w2l.ap()], outs=[w2b.ap()]).then_inc(P_.cc.h)
    ready = (P_.cc, P_.cc.inc(3))
    return w1b, w3b, w2b, ready


def _hadamard128_bf():
    h = np.array([[1.0]], np.float32)
    while h.shape[0] < 128:
        h = np.block([[h, h], [h, -h]])
    return (h * (2.0 ** -7)).astype(bf16dt())


def _w1b_tile_ap(wb, i, hc, hw):
    r = hc // 1024
    off = hc - 1024 * r
    assert off + hw <= 1024
    return wb.ap()[r * D + 128 * i: r * D + 128 * (i + 1), off:off + hw]


def emit_l3_ffn(P_, ctx, xr, xnb, xnb_ready, w1b, w3b, w2b, wready, v_x):
    nc = P_.nc
    ht = ctx["ht"]
    w1st = ctx["w1st"]
    w3st = ctx["w3st"]
    w2st = ctx["w2st"]
    silt = ctx["silt"]
    pool = P_.pool
    SQ2 = float(np.sqrt(2.0))

    for tc in range(2):
        for g_ in range(16):
            hc = 512 * g_
            for qt in range(4):
                hb = 4 * g_ + qt
                bufi = hb % 2
                v_war = ctx.get(("l3mm", tc, hb - 2))
                with _blk(nc) as b:
                    @b.sync
                    def _(s, hc=hc, qt=qt, bufi=bufi, v_war=v_war):
                        if v_war is None:
                            s.wait_ge(P_.cc.h, wready[1])
                        else:
                            s.wait_ge(P_.pe.h, v_war)
                        r, off = hc // 1024, hc % 1024
                        d0 = 512 * qt
                        src1 = w1b.ap()[r * D + d0: r * D + d0 + 512, off:off + 512]
                        src3 = w3b.ap()[r * D + d0: r * D + d0 + 512, off:off + 512]
                        s.dma_start(out=w1st[bufi][:, :].rearrange("p (i f) -> p i f", i=4),
                                    in_=src1.rearrange("(i p) f -> p i f", p=128)
                                    ).then_inc(P_.ld.h, 16)
                        s.dma_start(out=w3st[bufi][:, :].rearrange("p (i f) -> p i f", i=4),
                                    in_=src3.rearrange("(i p) f -> p i f", p=128)
                                    ).then_inc(P_.ld.h, 16)
                v_ld = P_.ld.inc(32)
                ev_g = ctx.get(("l3ev", tc, g_ - 1))
                with _blk(nc) as b:
                    @b.tensor
                    def _(t, g_=g_, qt=qt, bufi=bufi, v_ld=v_ld, tc=tc, ev_g=ev_g):
                        t.wait_ge(P_.ld.h, v_ld)
                        if qt == 0 and g_ == 0:
                            t.wait_ge(xnb_ready[0].h, xnb_ready[1])
                            if tc == 1 and ctx.get(("l3dn_war", 0)) is not None:
                                t.wait_ge(P_.ve.h, ctx[("l3dn_war", 0)])
                        if qt == 0 and ev_g is not None:
                            t.wait_ge(P_.ve.h, ev_g)
                        for di in range(4):
                            i = 4 * qt + di
                            for j in range(4):
                                t.matmul(pool[j][:, :],
                                         w1st[bufi][:, 512 * di + 128 * j:512 * di + 128 * (j + 1)],
                                         xnb[i][:, 512 * tc:512 * (tc + 1)],
                                         start=(i == 0), stop=(i == NDT - 1))
                                ins = t.matmul(pool[4 + j][:, :],
                                               w3st[bufi][:, 512 * di + 128 * j:512 * di + 128 * (j + 1)],
                                               xnb[i][:, 512 * tc:512 * (tc + 1)],
                                               start=(i == 0), stop=(i == NDT - 1))
                        ins.then_inc(P_.pe.h, 1)
                ctx[("l3mm", tc, hb)] = P_.pe.inc(1)
            v_mm = P_.pe.n
            sc0 = P_.sc.n
            ev_prev = ctx.get(("l3ev", tc, g_ - 1))
            with _blk(nc) as b:
                @b.scalar
                def _(s, v_mm=v_mm, ev_prev=ev_prev):
                    s.wait_ge(P_.pe.h, v_mm)
                    if ev_prev is not None:
                        s.wait_ge(P_.ve.h, ev_prev)
                    for j in range(4):
                        s.activation(silt[:, 512 * j:512 * (j + 1)],
                                     pool[j][:, :], AF.Silu).then_inc(P_.sc.h, 1)
            P_.sc.inc(4)
            with _blk(nc) as b:
                @b.vector
                def _(v, g_=g_, sc0=sc0):
                    for j in range(4):
                        v.wait_ge(P_.sc.h, sc0 + j + 1)
                        v.tensor_tensor(ht[4 * g_ + j][:, :],
                                        silt[:, 512 * j:512 * (j + 1)],
                                        pool[4 + j][:, :], AL.mult).then_inc(P_.ve.h, 1)
            ctx[("l3ev", tc, g_)] = P_.ve.inc(4)
        v_h_all = P_.ve.n

        for oh in range(2):
            for hi in range(NHT):
                bufi = hi % 2
                v_war = ctx.get(("l3dld", tc, oh, hi - 2))
                with _blk(nc) as b:
                    @b.sync
                    def _(s, oh=oh, hi=hi, bufi=bufi, v_war=v_war):
                        s.wait_ge(P_.cc.h, wready[1])
                        if v_war is not None:
                            s.wait_ge(P_.pe.h, v_war)
                        for rr in range(4):
                            r = 4 * oh + rr
                            s.dma_start(out=w2st[bufi][:, 256 * rr:256 * (rr + 1)],
                                        in_=w2b.ap()[r * H_L3 + 128 * hi:
                                                     r * H_L3 + 128 * (hi + 1), :]
                                        ).then_inc(P_.ld.h, 16)
                v_ld = P_.ld.inc(64)
                with _blk(nc) as b:
                    @b.tensor
                    def _(t, oh=oh, hi=hi, bufi=bufi, v_ld=v_ld, v_h_all=v_h_all, tc=tc):
                        t.wait_ge(P_.ld.h, v_ld)
                        if hi == 0:
                            t.wait_ge(P_.ve.h, v_h_all)
                            if oh == 1:
                                t.wait_ge(P_.ve.h, ctx[("l3dn_war", tc)])
                        for oc in range(8):
                            ins = t.matmul(pool[oc][:, :],
                                           w2st[bufi][:, 128 * oc:128 * (oc + 1)],
                                           ht[hi][:, :],
                                           start=(hi == 0), stop=(hi == NHT - 1))
                        ins.then_inc(P_.pe.h, 1)
                ctx[("l3dld", tc, oh, hi)] = P_.pe.inc(1)
            v_dn = P_.pe.n
            with _blk(nc) as b:
                @b.vector
                def _(v, oh=oh, tc=tc, v_dn=v_dn):
                    v.wait_ge(P_.pe.h, v_dn)
                    if oh == 0 and tc == 0:
                        v.wait_ge(v_x[0].h, v_x[1])
                    for oc in range(8):
                        i = 8 * oh + oc
                        v.scalar_tensor_tensor(xr[i][:, 512 * tc:512 * (tc + 1)],
                                               pool[oc][:, :], SQ2,
                                               xr[i][:, 512 * tc:512 * (tc + 1)],
                                               AL.mult, AL.add).then_inc(P_.ve.h, 1)
            ctx[("l3dn_war", tc)] = P_.ve.inc(8)
    return [(P_.ve, P_.ve.n)]





def emit_l0_ffn_prep(P_, w1s, w3s, w2s, ws1s, ws3s, ws2s):
    nc = P_.nc
    mk = lambda n, sh: nc.dram_tensor(n, sh, BF16)
    w1l, w3l = mk("l0f_w1l", [D, 1024]), mk("l0f_w3l", [D, 1024])
    w2l = mk("l0f_w2l", [1024, D])
    ws1l, ws3l = mk("l0f_ws1l", [256, 1024]), mk("l0f_ws3l", [256, 1024])
    ws2l = mk("l0f_ws2l", [128, D])
    sh = lambda n, sh_: nc.dram_tensor(n, sh_, BF16, addr_space="Shared")
    w1b, w3b = sh("l0f_w1b", [NCORES * D, 1024]), sh("l0f_w3b", [NCORES * D, 1024])
    w2b = sh("l0f_w2b", [NCORES * 1024, D])
    ws1b, ws3b = sh("l0f_ws1b", [D, 1024]), sh("l0f_ws3b", [D, 1024])
    ws2b = sh("l0f_ws2b", [1024, D])
    with _blk(nc) as b:
        @b.gpsimd
        def _(g):
            for (dst, srcp, n) in ((w1l, w1s, 2), (w3l, w3s, 2), (w2l, w2s, 1),
                                   (ws1l, ws1s, 1), (ws3l, ws3s, 1), (ws2l, ws2s, 1)):
                rows = dst.shape[0] // n
                for j in range(n):
                    g.dma_start(out=dst.ap()[rows * j:rows * (j + 1), :],
                                in_=srcp[rows * j:rows * (j + 1), :]).then_inc(P_.gd.h, 16)
    v_cast = P_.gd.inc(16 * 8)
    with _blk(nc) as b:
        @b.gpsimd
        def _(g):
            g.wait_ge(P_.gd.h, v_cast)
            for (i_, o_) in ((w1l, w1b), (w3l, w3b), (w2l, w2b), (ws1l, ws1b),
                             (ws3l, ws3b), (ws2l, ws2b)):
                g.collective_compute("AllGather", AL.bypass, replica_groups=RG,
                                     ins=[i_.ap()], outs=[o_.ap()]).then_inc(P_.cc.h)
    ready = (P_.cc, P_.cc.inc(6))
    return (w1b, w3b, w2b, ws1b, ws3b, ws2b), ready


def emit_l0_ffn(P_, ctx, xr, xnb, xnb_ready, gateT_ap, wbs, wready, v_x):
    nc = P_.nc
    (w1b, w3b, w2b, ws1b, ws3b, ws2b) = wbs
    he = ctx["he"]
    w13st = ctx["w13st"]
    w2st = ctx["w2st"]
    silt = ctx["silt"]
    silt2 = ctx["silt2"]
    gsb = ctx["gsb"]
    wE = ctx["wE"]
    wexp = ctx["wexp"]
    pool = P_.pool
    ident = P_.ident_b()

    gstage = ctx["gstage"]
    with _blk(nc) as b:
        @b.sync
        def _(s):
            s.dma_start(out=gstage[:, :].rearrange("p (i f) -> p i f", i=NDT),
                        in_=gateT_ap.rearrange("(i p) f -> p i f", p=128)
                        ).then_inc(P_.ld.h, 16)
    v_gld = P_.ld.inc(16)

    for tt in range(8):
        v_prev = ctx.get(("gts", tt - 1))
        with _blk(nc) as b:
            @b.tensor
            def _(t, tt=tt, v_prev=v_prev):
                t.wait_ge(P_.ld.h, v_gld)
                t.wait_ge(v_x[0].h, v_x[1])
                if v_prev is not None:
                    t.wait_ge(P_.ve.h, v_prev)
                if tt > 0:
                    t.wait_ge(P_.sc.h, ctx[("gtev", tt - 1)])
                for i in range(NDT):
                    ins = t.matmul(pool[6][:, 0:8], xr[i][:, 128 * tt:128 * (tt + 1)],
                                   gstage[:, 8 * i:8 * (i + 1)],
                                   start=(i == 0), stop=(i == NDT - 1))
                ins.then_inc(P_.pe.h, 1)
        v_sc = P_.pe.inc(1)
        with _blk(nc) as b:
            @b.scalar
            def _(s, tt=tt, v_sc=v_sc):
                s.wait_ge(P_.pe.h, v_sc)
                s.activation(gsb[:, 0:8], pool[6][:, 0:8], AF.Sigmoid).then_inc(P_.sc.h, 1)
        v_sg = P_.sc.inc(1)
        with _blk(nc) as b:
            @b.vector
            def _(v, tt=tt, v_sg=v_sg):
                b0 = P_.ve.n
                v.wait_ge(P_.sc.h, v_sg)
                v.reduce_max(gsb[:, 8:9], gsb[:, 0:8], AX.X).then_inc(P_.ve.h, 1)
                v.wait_ge(P_.ve.h, b0 + 1)
                v.tensor_scalar(gsb[:, 16:24], gsb[:, 0:8], gsb[:, 8:9], None,
                                AL.is_equal).then_inc(P_.ve.h, 1)
                v.wait_ge(P_.ve.h, b0 + 2)
                v.scalar_tensor_tensor(gsb[:, 16:24], gsb[:, 16:24], -1e30, gsb[:, 0:8],
                                       AL.mult, AL.add).then_inc(P_.ve.h, 1)
                v.wait_ge(P_.ve.h, b0 + 3)
                v.reduce_max(gsb[:, 9:10], gsb[:, 16:24], AX.X).then_inc(P_.ve.h, 1)
                v.wait_ge(P_.ve.h, b0 + 4)
                v.tensor_scalar(gsb[:, 16:24], gsb[:, 0:8], gsb[:, 9:10], None,
                                AL.is_ge).then_inc(P_.ve.h, 1)
                v.wait_ge(P_.ve.h, b0 + 5)
                v.tensor_tensor(gsb[:, 24:32], gsb[:, 0:8], gsb[:, 16:24], AL.mult
                                ).then_inc(P_.ve.h, 1)
        v_w = P_.ve.inc(6)
        ctx[("gts", tt)] = v_w
        with _blk(nc) as b:
            @b.tensor
            def _(t, tt=tt, v_w=v_w):
                t.wait_ge(P_.ve.h, v_w)
                if tt > 0:
                    t.wait_ge(P_.sc.h, ctx[("gtev", tt - 1)])
                t.transpose(pool[6][0:8, 0:128], gsb[:, 24:32],
                            ctx["identf"][:, :]).then_inc(P_.pe.h, 1)
        v_tr = P_.pe.inc(1)
        with _blk(nc) as b:
            @b.scalar
            def _(s, tt=tt, v_tr=v_tr):
                s.wait_ge(P_.pe.h, v_tr)
                s.copy(wE[0:8, 128 * tt:128 * (tt + 1)], pool[6][0:8, 0:128]
                       ).then_inc(P_.sc.h, 1)
        ctx[("gtev", tt)] = P_.sc.inc(1)
    v_wE = P_.sc.n

    selc = P_.sconst("l0f_sel", _sel8_const())

    fin_tc = []
    for tc in range(2):
        sl = slice(512 * tc, 512 * (tc + 1))
        for e in range(8):
            with _blk(nc) as b:
                @b.tensor
                def _(t, e=e, tc=tc):
                    t.wait_ge(P_.sc.h, v_wE)
                    t.wait_ge(P_.ld.h, P_.v_consts)
                    if ctx.get(("wexpv", tc, e - 1)) is not None:
                        t.wait_ge(P_.sc.h, ctx[("wexpv", tc, e - 1)])
                    if tc == 1 and e == 0:
                        t.wait_ge(P_.ve.h, ctx[("l0dn", 0)])
                    t.matmul(pool[7][:, 0:512], selc[:, 128 * e:128 * (e + 1)],
                             wE[0:8, sl], start=True, stop=True).then_inc(P_.pe.h, 1)
            v_m = P_.pe.inc(1)
            with _blk(nc) as b:
                @b.scalar
                def _(s, e=e, v_m=v_m):
                    s.wait_ge(P_.pe.h, v_m)
                    s.copy(wexp[e][:, :], pool[7][:, 0:512]).then_inc(P_.sc.h, 1)
            ctx[("wexpv", tc, e)] = P_.sc.inc(1)
        v_wexp = P_.sc.n

        for e in range(9):
            for hs in range(8):
                k = e * 8 + hs
                pg = pool[(k % 2) * 2]
                pu = pool[(k % 2) * 2 + 1]
                v_war = ctx.get(("l0ev", tc, k - 2))
                with _blk(nc) as b:
                    @b.sync
                    def _(s, e=e, hs=hs, v_war=v_war, tc=tc):
                        s.wait_ge(P_.cc.h, wready[1])
                        if ctx.get(("l0ld", tc, e * 8 + hs - 1)) is not None:
                            s.wait_ge(P_.pe.h, ctx[("l0ld", tc, e * 8 + hs - 1)])
                        if e < 8:
                            s1 = w1b.ap()[e * D: (e + 1) * D, 128 * hs:128 * (hs + 1)]
                            s3 = w3b.ap()[e * D: (e + 1) * D, 128 * hs:128 * (hs + 1)]
                        else:
                            s1 = ws1b.ap()[:, 128 * hs:128 * (hs + 1)]
                            s3 = ws3b.ap()[:, 128 * hs:128 * (hs + 1)]
                        s.dma_start(out=w13st[0][:, :].rearrange("p (i f) -> p i f", i=NDT),
                                    in_=s1.rearrange("(i p) f -> p i f", p=128)
                                    ).then_inc(P_.ld.h, 16)
                        s.dma_start(out=w13st[1][:, :].rearrange("p (i f) -> p i f", i=NDT),
                                    in_=s3.rearrange("(i p) f -> p i f", p=128)
                                    ).then_inc(P_.ld.h, 16)
                v_ld = P_.ld.inc(32)
                with _blk(nc) as b:
                    @b.tensor
                    def _(t, e=e, hs=hs, v_ld=v_ld, v_war=v_war, tc=tc, pg=pg, pu=pu):
                        t.wait_ge(P_.ld.h, v_ld)
                        if v_war is not None:
                            t.wait_ge(P_.ve.h, v_war)
                        if e == 0 and hs == 0:
                            t.wait_ge(xnb_ready[0].h, xnb_ready[1])
                        if tc == 1 and e * 8 + hs < 2:
                            t.wait_ge(P_.ve.h, ctx[("l0dn", 0)])
                        for i in range(NDT):
                            t.matmul(pg[:, :], w13st[0][:, 128 * i:128 * (i + 1)],
                                     xnb[i][:, sl], start=(i == 0), stop=(i == NDT - 1))
                            ins = t.matmul(pu[:, :], w13st[1][:, 128 * i:128 * (i + 1)],
                                           xnb[i][:, sl], start=(i == 0), stop=(i == NDT - 1))
                        ins.then_inc(P_.pe.h, 1)
                ctx[("l0ld", tc, e * 8 + hs)] = P_.pe.inc(1)
                v_mm = P_.pe.n
                sc_a = P_.sc.n
                with _blk(nc) as b:
                    @b.vector
                    def _(v, e=e, hs=hs, v_mm=v_mm, tc=tc, pg=pg, pu=pu, sc_a=sc_a):
                        v.wait_ge(P_.pe.h, v_mm)
                        if e < 8:
                            v.wait_ge(P_.sc.h, ctx[("wexpv", tc, e)])
                            v.tensor_tensor(silt2[:, :], pg[:, :], wexp[e][:, :], AL.mult
                                            ).then_inc(P_.ve.h, 1)
                        else:
                            v.engine_nop().then_inc(P_.ve.h, 1)
                va = P_.ve.inc(1)
                with _blk(nc) as b:
                    @b.scalar
                    def _(s, e=e, hs=hs, va=va, v_mm=v_mm, pg=pg):
                        s.wait_ge(P_.ve.h, va)
                        if e < 8:
                            s.activation(silt[:, :], silt2[:, :], AF.Silu).then_inc(P_.sc.h, 1)
                        else:
                            s.activation(silt[:, :], pg[:, :], AF.Silu).then_inc(P_.sc.h, 1)
                v_sil = P_.sc.inc(1)
                with _blk(nc) as b:
                    @b.vector
                    def _(v, e=e, hs=hs, v_sil=v_sil, tc=tc, pu=pu):
                        v.wait_ge(P_.sc.h, v_sil)
                        if e < 8:
                            v.tensor_tensor(silt2[:, :], pu[:, :], wexp[e][:, :], AL.mult)
                            v.tensor_tensor(he[e * 8 + hs][:, :], silt[:, :], silt2[:, :],
                                            AL.mult).then_inc(P_.ve.h, 2)
                        else:
                            v.tensor_tensor(he[64 + hs][:, :], silt[:, :], pu[:, :],
                                            AL.mult).then_inc(P_.ve.h, 2)
                ctx[("l0ev", tc, k)] = P_.ve.inc(2)
        v_he = P_.ve.n

        for oh in range(2):
            for e in range(9):
                for hs in range(8):
                    ki = e * 8 + hs
                    bufi = ki % 2
                    v_war = ctx.get(("l0dld", tc, oh, ki - 2))
                    with _blk(nc) as b:
                        @b.sync
                        def _(s, oh=oh, e=e, hs=hs, bufi=bufi, v_war=v_war):
                            if v_war is not None:
                                s.wait_ge(P_.pe.h, v_war)
                            if e < 8:
                                src = w2b.ap()[e * 1024 + 128 * hs: e * 1024 + 128 * (hs + 1),
                                               1024 * oh:1024 * (oh + 1)]
                            else:
                                src = ws2b.ap()[128 * hs:128 * (hs + 1),
                                                1024 * oh:1024 * (oh + 1)]
                            s.dma_start(out=w2st[bufi][:, :], in_=src).then_inc(P_.ld.h, 16)
                    v_ld = P_.ld.inc(16)
                    with _blk(nc) as b:
                        @b.tensor
                        def _(t, oh=oh, e=e, hs=hs, bufi=bufi, v_ld=v_ld, v_he=v_he, tc=tc):
                            t.wait_ge(P_.ld.h, v_ld)
                            ki = e * 8 + hs
                            if ki == 0:
                                t.wait_ge(P_.ve.h, v_he)
                                if oh == 1:
                                    t.wait_ge(P_.ve.h, ctx[("l0dn", tc)])
                            for oc in range(8):
                                ins = t.matmul(pool[oc][:, :],
                                               w2st[bufi][:, 128 * oc:128 * (oc + 1)],
                                               he[ki][:, :],
                                               start=(ki == 0), stop=(ki == 71))
                            ins.then_inc(P_.pe.h, 1)
                    ctx[("l0dld", tc, oh, ki)] = P_.pe.inc(1)
            v_dn = P_.pe.n
            with _blk(nc) as b:
                @b.vector
                def _(v, oh=oh, tc=tc, v_dn=v_dn):
                    v.wait_ge(P_.pe.h, v_dn)
                    for oc in range(8):
                        i = 8 * oh + oc
                        v.tensor_add(xr[i][:, sl], xr[i][:, sl], pool[oc][:, :]
                                     ).then_inc(P_.ve.h, 1)
            ctx[("l0dn", tc)] = P_.ve.inc(8)
        fin_tc.append(P_.ve.n)
    return [(P_.ve, P_.ve.n)]


def _sel8_const():
    m = np.zeros((8, 8 * 128), np.float32)
    for e in range(8):
        m[e, 128 * e:128 * (e + 1)] = 1.0
    return m.astype(bf16dt())




def build_l0_ffn_prog(dbg=False):
    Pr = P()
    nc = Pr.nc
    xin = nc.dram_tensor("xin", [D, TL], F32, kind="ExternalInput").ap()
    gateT = nc.dram_tensor("l0f_gateT", [D, 8], F32, kind="ExternalInput").ap()
    w1s = nc.dram_tensor("l0f_w1s", [D, 1024], F32, kind="ExternalInput").ap()
    w3s = nc.dram_tensor("l0f_w3s", [D, 1024], F32, kind="ExternalInput").ap()
    w2s = nc.dram_tensor("l0f_w2s", [1024, D], F32, kind="ExternalInput").ap()
    ws1s = nc.dram_tensor("l0f_ws1s", [256, 1024], F32, kind="ExternalInput").ap()
    ws3s = nc.dram_tensor("l0f_ws3s", [256, 1024], F32, kind="ExternalInput").ap()
    ws2s = nc.dram_tensor("l0f_ws2s", [128, D], F32, kind="ExternalInput").ap()
    xout = nc.dram_tensor("xout", [D, TL], F32, kind="ExternalOutput").ap()
    Pr.make_pool()
    xr = [Pr.sb(f"xr{i}", [128, TL], F32) for i in range(NDT)]
    he = [Pr.sb(f"he{i}", [128, 512], BF16) for i in range(72)]
    xnb = [Pr.sb(f"xnb{i}", [128, TL], BF16) for i in range(NDT)]
    ctx = {
        "he": he,
        "w13st": [Pr.sb(f"w13st{i}", [128, 2048], BF16) for i in range(2)],
        "w2st": [Pr.sb(f"w2st{i}", [128, 1024], BF16) for i in range(2)],
        "silt": Pr.sb("silt", [128, 512], BF16),
        "silt2": Pr.sb("silt2", [128, 512], F32),
        "gsb": Pr.sb("gsb", [128, 32], F32),
        "wE": Pr.sb("wE", [8, TL], BF16),
        "wexp": [Pr.sb(f"wexp{i}", [128, 512], BF16) for i in range(8)],
        "gstage": Pr.sb("gstage", [128, 128], F32),
    }
    nctx = make_norm_ctx(Pr, sq_alias=[he[i] for i in range(NDT)],
                         atmp_alias=ctx["silt2"])
    ctx["identf"] = Pr.sconst("ident_f", np.eye(128, dtype=np.float32))
    Pr.sconst("l0f_sel", _sel8_const())
    Pr.ones_col_b(); Pr.ones_row_f()
    Pr.flush_consts()
    wbs, wready = emit_l0_ffn_prep(Pr, w1s, w3s, w2s, ws1s, ws3s, ws2s)
    v_x = load_residual(Pr, xr, xin)
    with _blk(nc) as b:
        @b.scalar
        def _(s):
            s.wait_ge(Pr.ld.h, v_x)
            for i in range(NDT):
                s.copy(xnb[i][:, :], xr[i][:, :]).then_inc(Pr.sc.h, 1)
    xnb_ready = (Pr.sc, Pr.sc.inc(NDT))
    fin = emit_l0_ffn(Pr, ctx, xr, xnb, xnb_ready, gateT, wbs, wready,
                      (Pr.ld, v_x))
    if dbg:
        dwe = nc.dram_tensor("dbg_wE", [8, TL], F32, kind="ExternalOutput").ap()
        dhe = nc.dram_tensor("dbg_he", [128, 512 * 4], F32, kind="ExternalOutput").ap()
        srcs = [ctx["he"][0], ctx["he"][64], ctx["wexp"][0]]
        with _blk(nc) as b:
            @b.vector
            def _(v):
                v.wait_ge(fin[0][0].h, fin[0][1])
                v.engine_nop().then_inc(Pr.ve.h, 1)
        vd = Pr.ve.inc(1)
        with _blk(nc) as b:
            @b.gpsimd
            def _(g):
                g.wait_ge(Pr.ve.h, vd)
                g.dma_start(out=dwe, in_=ctx["wE"][:, :]).then_inc(Pr.gd.h, 16)
                for i in range(3):
                    g.dma_start(out=dhe[:, 512 * i:512 * (i + 1)], in_=srcs[i][:, :]
                                ).then_inc(Pr.gd.h, 16)
                g.dma_start(out=dhe[:, 1536:2048], in_=xr[0][:, 0:512]).then_inc(Pr.gd.h, 16)
                g.wait_ge(Pr.gd.h, Pr.gd.inc(16 * 5))
    fin2 = emit_norm(Pr, nctx, xr, xnb, None, None, subtract_mean=True,
                     pre_waits=fin)
    store_residual(Pr, xr, xout, [fin2])
    Pr.close_block()
    return Pr


def run_l0_ffn(x, gate_w, w1, w2, w3, ws1, ws2, ws3, dbg=False):
    Pr = build_l0_ffn_prog(dbg)
    xs = _shard_x(x)
    w1T = np.ascontiguousarray(w1.transpose(0, 2, 1))
    w3T = np.ascontiguousarray(w3.transpose(0, 2, 1))
    w2T = np.ascontiguousarray(w2.transpose(0, 2, 1))
    ws1T = np.ascontiguousarray(ws1.T); ws3T = np.ascontiguousarray(ws3.T)
    ws2T = np.ascontiguousarray(ws2.T)
    gT = np.ascontiguousarray(gate_w.T)
    in_maps = [{"xin": xs[c], "l0f_gateT": gT,
                "l0f_w1s": w1T[c], "l0f_w3s": w3T[c], "l0f_w2s": w2T[c],
                "l0f_ws1s": np.ascontiguousarray(ws1T[256 * c:256 * (c + 1)]),
                "l0f_ws3s": np.ascontiguousarray(ws3T[256 * c:256 * (c + 1)]),
                "l0f_ws2s": np.ascontiguousarray(ws2T[128 * c:128 * (c + 1)])}
               for c in range(NCORES)]
    res = run_bass_kernel_spmd(Pr.nc, in_maps, core_ids=list(range(NCORES)))
    out = _unshard_x([res.results[c]["xout"] for c in range(NCORES)])
    if dbg:
        return out, [res.results[c] for c in range(NCORES)]
    return out





def emit_l3_attn(P_, ctx, xr, v_x, wq_ap, wk_ap, wv_ap, wo_ap, bo_ap):
    nc = P_.nc
    pool = P_.pool
    qsb, ksb, vTsb = ctx["qsb"], ctx["ksb"], ctx["vTsb"]
    xstage = ctx["xstage"]
    wqkvt = ctx["wqkvt"]
    wost = ctx["wost"]
    es = ctx["es"]
    cbuf = ctx["cbuf"]
    vtmp = ctx["vtmp"]
    crow = ctx["crow"]
    cfix = ctx["cfix"]
    identf = ctx["identf"]
    ones_b = P_.ones_col_b()
    ones_r = P_.ones_row_f()
    QKSCALE = float(1.0 / np.sqrt(128.0))

    xl = nc.dram_tensor("l3a_xl", [D, TL], BF16)
    xall = nc.dram_tensor("l3a_xall", [NCORES * D, TL], BF16, addr_space="Shared")
    cl = nc.dram_tensor("l3a_cl", [256, T], BF16)
    call = nc.dram_tensor("l3a_call", [NCORES * 256, T], BF16, addr_space="Shared")

    with _blk(nc) as b:
        @b.gpsimd
        def _(g):
            g.dma_start(out=wqkvt[:, 0:4096].rearrange("p (i f) -> p i f", i=NDT),
                        in_=wq_ap.rearrange("(i p) f -> p i f", p=128)).then_inc(P_.gd.h, 16)
            g.dma_start(out=wqkvt[:, 4096:6144].rearrange("p (i f) -> p i f", i=NDT),
                        in_=wk_ap.rearrange("(i p) f -> p i f", p=128)).then_inc(P_.gd.h, 16)
            g.dma_start(out=wqkvt[:, 6144:8192].rearrange("p (i f) -> p i f", i=NDT),
                        in_=wv_ap.rearrange("(i p) f -> p i f", p=128)).then_inc(P_.gd.h, 16)
            g.wait_ge(v_x[0].h, v_x[1])
            for i in range(NDT):
                g.dma_start(out=xl.ap()[128 * i:128 * (i + 1), :], in_=xr[i][:, :]
                            ).then_inc(P_.gd.h, 16)
    v_w = P_.gd.inc(48 + 16 * NDT)
    with _blk(nc) as b:
        @b.gpsimd
        def _(g):
            g.wait_ge(P_.gd.h, v_w)
            g.collective_compute("AllGather", AL.bypass, replica_groups=RG,
                                 ins=[xl.ap()], outs=[xall.ap()]).then_inc(P_.cc.h)
    v_ag = P_.cc.inc(1)

    for tg in range(16):
        r, tl0 = tg // 2, 512 * (tg % 2)
        v_war = ctx.get(("l3a_pj", tg - 1))
        with _blk(nc) as b:
            @b.sync
            def _(s, r=r, tl0=tl0, v_war=v_war):
                s.wait_ge(P_.cc.h, v_ag)
                if v_war is not None:
                    s.wait_ge(P_.pe.h, v_war)
                s.dma_start(out=xstage[:, :].rearrange("p (i f) -> p i f", i=NDT),
                            in_=xall.ap()[r * D:(r + 1) * D, tl0:tl0 + 512]
                            .rearrange("(i p) f -> p i f", p=128)).then_inc(P_.ld.h, 16)
        v_ld = P_.ld.inc(16)
        with _blk(nc) as b:
            @b.tensor
            def _(t, tg=tg, v_ld=v_ld):
                t.wait_ge(P_.ld.h, v_ld)
                t.wait_ge(P_.gd.h, v_w)
                if tg >= 1:
                    t.wait_ge(P_.sc.h, ctx[("l3a_vT", tg - 1)])
                for i in range(NDT):
                    st, sp = (i == 0), (i == NDT - 1)
                    t.matmul(pool[0][:, :], wqkvt[:, 256 * i:256 * i + 128],
                             xstage[:, 512 * i:512 * (i + 1)], start=st, stop=sp)
                    t.matmul(pool[1][:, :], wqkvt[:, 256 * i + 128:256 * i + 256],
                             xstage[:, 512 * i:512 * (i + 1)], start=st, stop=sp)
                    t.matmul(pool[2][:, :], wqkvt[:, 4096 + 128 * i:4096 + 128 * (i + 1)],
                             xstage[:, 512 * i:512 * (i + 1)], start=st, stop=sp)
                    ins = t.matmul(pool[3][:, :], wqkvt[:, 6144 + 128 * i:6144 + 128 * (i + 1)],
                                   xstage[:, 512 * i:512 * (i + 1)], start=st, stop=sp)
                ins.then_inc(P_.pe.h, 1)
        ctx[("l3a_pj", tg)] = P_.pe.inc(1)
        v_mm = P_.pe.n
        with _blk(nc) as b:
            @b.scalar
            def _(s, tg=tg, v_mm=v_mm):
                s.wait_ge(P_.pe.h, v_mm)
                sl = slice(512 * tg, 512 * (tg + 1))
                s.copy(qsb[0][:, sl], pool[0][:, :]).then_inc(P_.sc.h, 1)
                s.copy(qsb[1][:, sl], pool[1][:, :]).then_inc(P_.sc.h, 1)
                s.copy(ksb[:, sl], pool[2][:, :]).then_inc(P_.sc.h, 1)
                s.copy(vtmp[:, :], pool[3][:, :]).then_inc(P_.sc.h, 1)
        v_ev = P_.sc.inc(4)
        with _blk(nc) as b:
            @b.tensor
            def _(t, tg=tg, v_ev=v_ev):
                t.wait_ge(P_.sc.h, v_ev)
                t.wait_ge(P_.ld.h, P_.v_consts)
                if tg >= 1:
                    t.wait_ge(P_.sc.h, ctx[("l3a_vT", tg - 1)])
                for j in range(4):
                    ins = t.transpose(pool[4][:, 128 * j:128 * (j + 1)],
                                      vtmp[:, 128 * j:128 * (j + 1)], identf[:, :])
                ins.then_inc(P_.pe.h, 1)
        v_tr = P_.pe.inc(1)
        with _blk(nc) as b:
            @b.scalar
            def _(s, tg=tg, v_tr=v_tr):
                s.wait_ge(P_.pe.h, v_tr)
                for j in range(4):
                    kt = 4 * tg + j
                    s.copy(vTsb[:, 128 * kt:128 * (kt + 1)],
                           pool[4][:, 128 * j:128 * (j + 1)]).then_inc(P_.sc.h, 1)
        ctx[("l3a_vT", tg)] = P_.sc.inc(4)
    v_proj_all = P_.sc.n

    for h in range(2):
        for qc in range(16):
            nkt = 4 * qc + 4
            prev_norm = ctx.get("_l3a_prev_norm")
            for kt in range(nkt):
                diag = kt - 4 * qc
                ps_s = pool[6 + (kt % 2)]
                esb = es[kt % 2]
                v_esprev = ctx.get(("l3a_es", h, qc, kt - 2))
                with _blk(nc) as b:
                    @b.tensor
                    def _(t, h=h, qc=qc, kt=kt, ps_s=ps_s, v_esprev=v_esprev,
                          prev_norm=prev_norm):
                        t.wait_ge(P_.sc.h, v_proj_all)
                        if v_esprev is not None:
                            t.wait_ge(P_.sc.h, v_esprev[1] if isinstance(v_esprev, tuple) else v_esprev)
                        if kt <= 1 and prev_norm is not None:
                            t.wait_ge(P_.sc.h, prev_norm[0])
                            t.wait_ge(P_.ve.h, prev_norm[1])
                        t.matmul(ps_s[:, :], ksb[:, 128 * kt:128 * (kt + 1)],
                                 qsb[h][:, 512 * qc:512 * (qc + 1)],
                                 start=True, stop=True).then_inc(P_.pe.h, 1)
                v_s = P_.pe.inc(1)
                with _blk(nc) as b:
                    @b.scalar
                    def _(s, v_s=v_s, ps_s=ps_s, esb=esb):
                        s.wait_ge(P_.pe.h, v_s)
                        s.activation(esb[:, :], ps_s[:, :], AF.Exp, scale=QKSCALE
                                     ).then_inc(P_.sc.h, 1)
                v_e = P_.sc.inc(1)
                ctx[("l3a_es", h, qc, kt)] = v_e
                if diag >= 0:
                    with _blk(nc) as b:
                        @b.vector
                        def _(v, diag=diag, v_e=v_e, esb=esb):
                            v.wait_ge(P_.sc.h, v_e)
                            v.tensor_tensor(esb[:, :], esb[:, :],
                                            ctx["dmask"][:, 512 * diag:512 * (diag + 1)],
                                            AL.mult).then_inc(P_.ve.h, 1)
                    gate = ("ve", P_.ve.inc(1))
                else:
                    gate = ("sc", v_e)
                with _blk(nc) as b:
                    @b.tensor
                    def _(t, kt=kt, gate=gate, esb=esb, nkt=nkt):
                        t.wait_ge((P_.ve if gate[0] == "ve" else P_.sc).h, gate[1])
                        t.matmul(pool[4][:, :], vTsb[:, 128 * kt:128 * (kt + 1)],
                                 esb[:, :], start=(kt == 0), stop=(kt == nkt - 1))
                        ins = t.matmul(pool[5][0:1, :], ones_b[:, :], esb[:, :],
                                       start=(kt == 0), stop=(kt == nkt - 1))
                        if kt == nkt - 1:
                            ins.then_inc(P_.pe.h, 1)
                if kt == nkt - 1:
                    P_.pe.inc(1)
            v_pv = P_.pe.n
            with _blk(nc) as b:
                @b.vector
                def _(v, v_pv=v_pv):
                    v.wait_ge(P_.pe.h, v_pv)
                    v.reciprocal(crow[0:1, :], pool[5][0:1, :]).then_inc(P_.ve.h, 1)
            v_r = P_.ve.inc(1)
            with _blk(nc) as b:
                @b.tensor
                def _(t, v_r=v_r):
                    t.wait_ge(P_.ve.h, v_r)
                    t.matmul(pool[6][:, :], ones_r[:, :], crow[0:1, :],
                             start=True, stop=True).then_inc(P_.pe.h, 1)
            v_bc = P_.pe.inc(1)
            with _blk(nc) as b:
                @b.scalar
                def _(s, v_bc=v_bc):
                    s.wait_ge(P_.pe.h, v_bc)
                    s.copy(cfix[:, :], pool[4][:, :]).then_inc(P_.sc.h, 1)
            v_cp = P_.sc.inc(1)
            cb = cbuf[(h * 16 + qc) % 2]
            v_cbw = ctx.get(("l3a_cb", h * 16 + qc - 2))
            with _blk(nc) as b:
                @b.vector
                def _(v, v_cp=v_cp, cb=cb, v_cbw=v_cbw):
                    v.wait_ge(P_.sc.h, v_cp)
                    if v_cbw is not None:
                        v.wait_ge(P_.gd.h, v_cbw)
                    v.tensor_tensor(cb[:, :], cfix[:, :], pool[6][:, :], AL.mult
                                    ).then_inc(P_.ve.h, 1)
            v_ct = P_.ve.inc(1)
            with _blk(nc) as b:
                @b.gpsimd
                def _(g, h=h, qc=qc, v_ct=v_ct, cb=cb):
                    g.wait_ge(P_.ve.h, v_ct)
                    g.dma_start(out=cl.ap()[128 * h:128 * (h + 1),
                                            512 * qc:512 * (qc + 1)],
                                in_=cb[:, :]).then_inc(P_.gd.h, 16)
            ctx[("l3a_cb", h * 16 + qc)] = P_.gd.inc(16)
            ctx["_l3a_prev_norm"] = (v_cp, v_ct)
    v_attn = P_.ve.n
    v_cl = P_.gd.n
    with _blk(nc) as b:
        @b.gpsimd
        def _(g):
            g.wait_ge(P_.gd.h, v_cl)
            g.collective_compute("AllGather", AL.bypass, replica_groups=RG,
                                 ins=[cl.ap()], outs=[call.ap()]).then_inc(P_.cc.h)
    v_cag = P_.cc.inc(1)

    v_bo = P_.ld.inc(16)
    with _blk(nc) as b:
        @b.sync
        def _(s):
            s.dma_start(out=cfix[:, 0:16], in_=bo_ap[:, :]).then_inc(P_.ld.h, 16)
    v_preadd0 = P_.ve.n
    with _blk(nc) as b:
        @b.vector
        def _(v):
            v.wait_ge(P_.ld.h, v_bo)
            v.wait_ge(v_x[0].h, v_x[1])
            v.wait_ge(P_.ve.h, v_attn)
            for i in range(NDT):
                v.tensor_scalar(xr[i][:, :], xr[i][:, :], cfix[:, i:i + 1], None,
                                AL.add).then_inc(P_.ve.h, 1)
    v_preadd = P_.ve.inc(NDT)

    for tc in range(2):
        off = ctx["pid_off"][tc]
        with _blk(nc) as b:
            @b.sync
            def _(s, tc=tc, off=off):
                s.wait_ge(P_.cc.h, v_cag)
                if tc == 1:
                    s.wait_ge(P_.pe.h, ctx[("l3a_wo", 0, NDT - 1)])
                s.dma_start(out=xstage[:, :].rearrange("p (i f) -> p i f", i=NDT),
                            in_=call.ap().rearrange("(i p) t -> p i t", p=128)
                            [:, :, bass.ts(off, 512)]).then_inc(P_.ld.h, 16)
        v_cs = P_.ld.inc(16)
        for oc in range(NDT):
            v_warw = ctx.get(("l3a_wol", tc, oc - 2))
            with _blk(nc) as b:
                @b.gpsimd
                def _(g, tc=tc, oc=oc, v_warw=v_warw):
                    if v_warw is not None:
                        g.wait_ge(P_.pe.h, v_warw)
                    g.dma_start(out=wost[oc % 2][:, :].rearrange("p (i f) -> p i f", i=NDT),
                                in_=wo_ap[:, 128 * oc:128 * (oc + 1)]
                                .rearrange("(i p) f -> p i f", p=128)).then_inc(P_.gd.h, 16)
            v_ld = P_.gd.inc(16)
            pa = pool[oc % 2]
            with _blk(nc) as b:
                @b.tensor
                def _(t, tc=tc, oc=oc, v_ld=v_ld, pa=pa, v_cs=v_cs):
                    t.wait_ge(P_.gd.h, v_ld)
                    t.wait_ge(P_.ld.h, v_cs)
                    if oc <= 1:
                        t.wait_ge(P_.ve.h, v_preadd)
                        if tc == 1:
                            t.wait_ge(P_.ve.h, ctx[("l3a_woe", 0, NDT - 1 - (1 - oc))])
                    else:
                        t.wait_ge(P_.ve.h, ctx[("l3a_woe", tc, oc - 2)])
                    for i in range(NDT):
                        ins = t.matmul(pa[:, :], wost[oc % 2][:, 128 * i:128 * (i + 1)],
                                       xstage[:, 512 * i:512 * (i + 1)],
                                       start=(i == 0), stop=(i == NDT - 1))
                    ins.then_inc(P_.pe.h, 1)
            ctx[("l3a_wo", tc, oc)] = P_.pe.inc(1)
            ctx[("l3a_wol", tc, oc)] = P_.pe.n
            v_mm = P_.pe.n
            with _blk(nc) as b:
                @b.vector
                def _(v, tc=tc, oc=oc, v_mm=v_mm, pa=pa):
                    v.wait_ge(P_.pe.h, v_mm)
                    v.tensor_add(xr[oc][:, 512 * tc:512 * (tc + 1)],
                                 xr[oc][:, 512 * tc:512 * (tc + 1)], pa[:, :]
                                 ).then_inc(P_.ve.h, 1)
            ctx[("l3a_woe", tc, oc)] = P_.ve.inc(1)
    return [(P_.ve, P_.ve.n)], call




def build_l3_attn_prog():
    Pr = P()
    nc = Pr.nc
    xin = nc.dram_tensor("xin", [D, TL], F32, kind="ExternalInput").ap()
    wq = nc.dram_tensor("l3a_wq", [D, 256], F32, kind="ExternalInput").ap()
    wk = nc.dram_tensor("l3a_wk", [D, 128], F32, kind="ExternalInput").ap()
    wv = nc.dram_tensor("l3a_wv", [D, 128], F32, kind="ExternalInput").ap()
    wo = nc.dram_tensor("l3a_wo", [D, D], F32, kind="ExternalInput").ap()
    bo = nc.dram_tensor("l3a_bo", [128, 16], F32, kind="ExternalInput").ap()
    xout = nc.dram_tensor("xout", [D, TL], F32, kind="ExternalOutput").ap()
    Pr.make_pool()
    xr = [Pr.sb(f"xr{i}", [128, TL], F32) for i in range(NDT)]
    qsb = [Pr.sb(f"qsb{i}", [128, T], BF16) for i in range(2)]
    ksb = Pr.sb("ksb", [128, T], BF16)
    vTsb = Pr.sb("vTsb", [128, T], BF16)
    ctx = {
        "qsb": qsb, "ksb": ksb, "vTsb": vTsb,
        "xstage": Pr.sb("xstage", [128, 8192], BF16),
        "wqkvt": Pr.sb("wqkvt", [128, 8192], BF16),
        "wost": [Pr.sb(f"wost{i}", [128, 2048], BF16) for i in range(2)],
        "es": [Pr.sb(f"es{i}", [128, 512], BF16) for i in range(2)],
        "cbuf": [Pr.sb(f"cbuf{i}", [128, 512], BF16) for i in range(2)],
        "vtmp": Pr.sb("vtmp", [128, 512], F32),
        "cfix": Pr.sb("cfix", [128, 512], F32),
    }
    ctx["dmask"] = Pr.sconst("l3a_dmask", _l3_dmask())
    ctx["identf"] = Pr.sconst("ident_f", np.eye(128, dtype=np.float32))
    Pr.ones_col_b(); Pr.ones_row_f()
    Pr.flush_consts()
    sq_alias = [vTsb[:, 512 * i:512 * (i + 1)] for i in range(NDT)]
    xnb_alias = [qsb[i // 8][:, 1024 * (i % 8):1024 * (i % 8 + 1)] for i in range(NDT)]
    nctx = make_norm_ctx(Pr, sq_alias=sq_alias, atmp_alias=ctx["cfix"])
    ctx["crow"] = nctx["r_msq"]
    with _blk(nc) as b:
        @b.sync
        def _(s):
            pid = nc.partition_id()
            ctx["pid_off"] = [nc.snap(pid * 2 + tc, min_val=0,
                                      max_val=2 * NCORES - 1) for tc in range(2)]
    v_x = load_residual(Pr, xr, xin)
    fin, _call = emit_l3_attn(Pr, ctx, xr, (Pr.ld, v_x), wq, wk, wv, wo, bo)
    fin2 = emit_norm(Pr, nctx, xr, xnb_alias, None, None, subtract_mean=True,
                     pre_waits=fin)
    store_residual(Pr, xr, xout, [fin2])
    Pr.close_block()
    return Pr


def _l3_dmask():
    m = np.zeros((128, 4 * 512), np.float32)
    for r in range(4):
        for kk in range(128):
            for qq in range(512):
                if 128 * r + kk <= qq:
                    m[kk, 512 * r + qq] = 1.0
    return m.astype(bf16dt())


def run_l3_attn(x, Wq, Wk, Wv, Wo, bo):
    Pr = build_l3_attn_prog()
    xs = _shard_x(x)
    bo_l = np.ascontiguousarray(bo.reshape(16, 128).T)
    in_maps = []
    for c in range(NCORES):
        g = c // 2
        in_maps.append({
            "xin": xs[c],
            "l3a_wq": np.ascontiguousarray(Wq[:, 256 * c:256 * (c + 1)]),
            "l3a_wk": np.ascontiguousarray(Wk[:, 128 * g:128 * (g + 1)]),
            "l3a_wv": np.ascontiguousarray(Wv[:, 128 * g:128 * (g + 1)]),
            "l3a_wo": Wo, "l3a_bo": bo_l,
        })
    res = run_bass_kernel_spmd(Pr.nc, in_maps, core_ids=list(range(NCORES)))
    return _unshard_x([res.results[c]["xout"] for c in range(NCORES)])





L2CAP = 50.0
L2SCALE = float(256 ** -0.5)


def _l2_rope_tables():
    ROT, BASE = 128, 10000.0
    theta = 1.0 / BASE ** (np.arange(0, ROT, 2, dtype=np.float64) / ROT)
    pos = np.arange(T, dtype=np.float64)
    ang = pos[:, None] * theta[None, :]
    cos = np.cos(ang); sin = np.sin(ang)
    C = np.empty((ROT, T), np.float32); S = np.empty((ROT, T), np.float32)
    C[0::2] = cos.T; C[1::2] = cos.T
    S[0::2] = -sin.T; S[1::2] = sin.T
    return C.astype(bf16dt()), S.astype(bf16dt())


def _pswap():
    m = np.zeros((128, 128), np.float32)
    for i in range(64):
        m[2 * i, 2 * i + 1] = 1.0
        m[2 * i + 1, 2 * i] = 1.0
    return m


def _l2_wmask():
    m = np.zeros((128, 4 * 512), np.float32)
    for d_ in range(4):
        delta = 5 + d_
        for kk in range(128):
            for qq in range(512):
                if 128 * delta + qq - kk < 1024:
                    m[kk, 512 * d_ + qq] = 1.0
    return m.astype(bf16dt())


def emit_l2_attn(P_, ctx, xr, v_x, wq_ap, wk_ap, wv_ap, wo_ap, qs_ap, ks_ap):
    nc = P_.nc
    pool = P_.pool
    qbuf = ctx["qbuf"]
    kring = ctx["kring"]
    vring = ctx["vring"]
    xstage = ctx["xstage"]
    wqkvt = ctx["wqkvt"]
    wost = ctx["wost"]
    es = ctx["es"]
    cbuf = ctx["cbuf"]
    qtmp = ctx["qtmp"]
    vtmp = ctx["vtmp"]
    cst = ctx["cst"]
    scl = ctx["scl"]
    crow = ctx["crow"]
    cfix = ctx["cfix"]
    identf = ctx["identf"]
    pswap = ctx["pswap"]
    dmask, wmask = ctx["dmask"], ctx["wmask"]
    ropeC, ropeS = ctx["ropeC"], ctx["ropeS"]
    ones_b = P_.ones_col_b()
    ones_r = P_.ones_row_f()

    xl = nc.dram_tensor("l2a_xl", [D, TL], BF16)
    xall = nc.dram_tensor("l2a_xall", [NCORES * D, TL], BF16, addr_space="Shared")
    cl = nc.dram_tensor("l2a_cl", [256, T], BF16)
    call = nc.dram_tensor("l2a_call", [NCORES * 256, T], BF16, addr_space="Shared")

    with _blk(nc) as b:
        @b.gpsimd
        def _(g):
            for (dst0, src_, w_) in ((0, wq_ap, 256), (4096, wk_ap, 256), (8192, wv_ap, 256)):
                g.dma_start(out=wqkvt[:, dst0:dst0 + 4096]
                            .rearrange("p (i f) -> p i f", i=NDT),
                            in_=src_.rearrange("(i p) f -> p i f", p=128)
                            ).then_inc(P_.gd.h, 16)
            g.dma_start(out=scl[:, 0:2], in_=qs_ap).then_inc(P_.gd.h, 16)
            g.dma_start(out=scl[:, 2:4], in_=ks_ap).then_inc(P_.gd.h, 16)
            g.wait_ge(v_x[0].h, v_x[1])
            for i in range(NDT):
                g.dma_start(out=xl.ap()[128 * i:128 * (i + 1), :], in_=xr[i][:, :]
                            ).then_inc(P_.gd.h, 16)
    v_w = P_.gd.inc(16 * (5 + NDT))
    with _blk(nc) as b:
        @b.gpsimd
        def _(g):
            g.wait_ge(P_.gd.h, v_w)
            g.collective_compute("AllGather", AL.bypass, replica_groups=RG,
                                 ins=[xl.ap()], outs=[xall.ap()]).then_inc(P_.cc.h)
    v_ag = P_.cc.inc(1)

    for tg in range(16):
        r, tl0 = tg // 2, 512 * (tg % 2)
        sl_glob = slice(512 * tg, 512 * (tg + 1))
        v_warx = ctx.get(("l2a_pj", tg - 1))
        with _blk(nc) as b:
            @b.sync
            def _(s, r=r, tl0=tl0, tg=tg, v_warx=v_warx):
                s.wait_ge(P_.cc.h, v_ag)
                if v_warx is not None:
                    s.wait_ge(P_.pe.h, v_warx)
                kk = ctx.get(("l2a_kv", tg - 1))
                if kk is not None:
                    s.wait_ge(P_.ve.h, kk)
                s.dma_start(out=xstage[:, :].rearrange("p (i f) -> p i f", i=NDT),
                            in_=xall.ap()[r * D:(r + 1) * D, tl0:tl0 + 512]
                            .rearrange("(i p) f -> p i f", p=128)).then_inc(P_.ld.h, 16)
                s.dma_start(out=cst[:, 0:512], in_=ropeC[:, sl_glob]).then_inc(P_.ld.h, 16)
                s.dma_start(out=cst[:, 512:1024], in_=ropeS[:, sl_glob]).then_inc(P_.ld.h, 16)
        v_ld = P_.ld.inc(48)
        with _blk(nc) as b:
            @b.tensor
            def _(t, tg=tg, v_ld=v_ld):
                t.wait_ge(P_.ld.h, v_ld)
                t.wait_ge(P_.gd.h, v_w)
                k = ctx.get(("l2a_pje", tg - 1))
                if k is not None:
                    t.wait_ge(P_.sc.h, k[0])
                    t.wait_ge(P_.ve.h, k[1])
                for i in range(NDT):
                    st, sp = (i == 0), (i == NDT - 1)
                    for oc in range(2):
                        t.matmul(pool[oc][:, :], wqkvt[:, 256 * i + 128 * oc:256 * i + 128 * (oc + 1)],
                                 xstage[:, 512 * i:512 * (i + 1)], start=st, stop=sp)
                        t.matmul(pool[2 + oc][:, :], wqkvt[:, 4096 + 256 * i + 128 * oc:4096 + 256 * i + 128 * (oc + 1)],
                                 xstage[:, 512 * i:512 * (i + 1)], start=st, stop=sp)
                        ins = t.matmul(pool[4 + oc][:, :], wqkvt[:, 8192 + 256 * i + 128 * oc:8192 + 256 * i + 128 * (oc + 1)],
                                       xstage[:, 512 * i:512 * (i + 1)], start=st, stop=sp)
                ins.then_inc(P_.pe.h, 1)
        ctx[("l2a_pj", tg)] = P_.pe.inc(1)
        v_pj = P_.pe.n

        with _blk(nc) as b:
            @b.scalar
            def _(s, v_pj=v_pj, tg=tg):
                s.wait_ge(P_.pe.h, v_pj)
                for oc in range(2):
                    s.copy(qtmp[:, 512 * oc:512 * (oc + 1)], pool[oc][:, :]).then_inc(P_.sc.h, 1)
                s.copy(ctx["vtmp0"][:, :], pool[4][:, :]).then_inc(P_.sc.h, 1)
        v_qev = P_.sc.inc(3)
        sq = ctx["sqn"]
        with _blk(nc) as b:
            @b.scalar
            def _(s, v_qev=v_qev):
                s.wait_ge(P_.sc.h, v_qev)
                s.activation(sq[:, 0:512], qtmp[:, 0:512], AF.Square).then_inc(P_.sc.h, 1)
                s.activation(sq[:, 512:1024], qtmp[:, 512:1024], AF.Square).then_inc(P_.sc.h, 1)
        v_sq = P_.sc.inc(2)
        with _blk(nc) as b:
            @b.tensor
            def _(t, v_sq=v_sq, tg=tg):
                t.wait_ge(P_.sc.h, v_sq)
                t.wait_ge(P_.ld.h, P_.v_consts)
                k = ctx.get(("l2a_aev", tg - 1))
                if k is not None:
                    t.wait_ge(P_.ve.h, k)
                t.matmul(pool[6][0:1, :], ones_b[:, :], sq[:, 0:512], start=True, stop=False)
                t.matmul(pool[6][0:1, :], ones_b[:, :], sq[:, 512:1024], start=False,
                         stop=True).then_inc(P_.pe.h, 1)
        v_ss = P_.pe.inc(1)
        with _blk(nc) as b:
            @b.vector
            def _(v, v_ss=v_ss):
                v.wait_ge(P_.pe.h, v_ss)
                v.tensor_scalar(crow[0:1, :], pool[6][0:1, :], 1.0 / 256, 1e-6,
                                AL.mult, AL.add).then_inc(P_.ve.h, 1)
        v_cr = P_.ve.inc(1)
        with _blk(nc) as b:
            @b.scalar
            def _(s, v_cr=v_cr):
                s.wait_ge(P_.ve.h, v_cr)
                s.activation(crow[0:1, :], crow[0:1, :], AF.Ln)
                s.activation(crow[0:1, :], crow[0:1, :], AF.Exp, scale=-0.5
                             ).then_inc(P_.sc.h, 1)
        v_a = P_.sc.inc(1)
        with _blk(nc) as b:
            @b.tensor
            def _(t, v_a=v_a):
                t.wait_ge(P_.sc.h, v_a)
                t.matmul(pool[7][:, :], ones_r[:, :], crow[0:1, :], start=True,
                         stop=True).then_inc(P_.pe.h, 1)
        v_bc = P_.pe.inc(1)
        with _blk(nc) as b:
            @b.vector
            def _(v, v_bc=v_bc):
                b0 = P_.ve.n
                v.wait_ge(P_.pe.h, v_bc)
                for oc in range(2):
                    v.tensor_tensor(qtmp[:, 512 * oc:512 * (oc + 1)],
                                    qtmp[:, 512 * oc:512 * (oc + 1)], pool[7][:, :],
                                    AL.mult).then_inc(P_.ve.h, 1)
                v.wait_ge(P_.ve.h, b0 + 2)
                for oc in range(2):
                    v.tensor_scalar(qtmp[:, 512 * oc:512 * (oc + 1)],
                                    qtmp[:, 512 * oc:512 * (oc + 1)], scl[:, oc:oc + 1],
                                    None, AL.mult).then_inc(P_.ve.h, 1)
        v_qn = P_.ve.inc(4)
        with _blk(nc) as b:
            @b.tensor
            def _(t, v_qn=v_qn):
                t.wait_ge(P_.ve.h, v_qn)
                t.matmul(pool[6][:, :], pswap[:, :], qtmp[:, 0:512], start=True,
                         stop=True).then_inc(P_.pe.h, 1)
        v_sw = P_.pe.inc(1)
        with _blk(nc) as b:
            @b.vector
            def _(v, v_sw=v_sw, tg=tg, v_qn=v_qn):
                b0 = P_.ve.n
                v.wait_ge(P_.pe.h, v_sw)
                v.tensor_tensor(sq[:, 0:512], qtmp[:, 0:512], cst[:, 0:512], AL.mult
                                ).then_inc(P_.ve.h, 1)
                v.tensor_tensor(sq[:, 512:1024], pool[6][:, :], cst[:, 512:1024],
                                AL.mult).then_inc(P_.ve.h, 1)
                v.wait_ge(P_.ve.h, b0 + 2)
                v.tensor_add(qbuf[:, 0:512], sq[:, 0:512], sq[:, 512:1024]
                             ).then_inc(P_.ve.h, 1)
                v.tensor_copy(qbuf[:, 512:1024], qtmp[:, 512:1024]).then_inc(P_.ve.h, 1)
        v_q = P_.ve.inc(4)

        with _blk(nc) as b:
            @b.scalar
            def _(s, v_q=v_q):
                s.wait_ge(P_.ve.h, v_q)
                for oc in range(2):
                    s.copy(qtmp[:, 512 * oc:512 * (oc + 1)], pool[2 + oc][:, :]
                           ).then_inc(P_.sc.h, 1)
        v_kev = P_.sc.inc(2)
        with _blk(nc) as b:
            @b.scalar
            def _(s, v_kev=v_kev):
                s.wait_ge(P_.sc.h, v_kev)
                s.activation(sq[:, 0:512], qtmp[:, 0:512], AF.Square).then_inc(P_.sc.h, 1)
                s.activation(sq[:, 512:1024], qtmp[:, 512:1024], AF.Square).then_inc(P_.sc.h, 1)
        v_sqk = P_.sc.inc(2)
        with _blk(nc) as b:
            @b.tensor
            def _(t, v_sqk=v_sqk, v_q=v_q):
                t.wait_ge(P_.sc.h, v_sqk)
                t.wait_ge(P_.ve.h, v_q)
                t.matmul(pool[6][0:1, :], ones_b[:, :], sq[:, 0:512], start=True, stop=False)
                t.matmul(pool[6][0:1, :], ones_b[:, :], sq[:, 512:1024], start=False,
                         stop=True).then_inc(P_.pe.h, 1)
        v_ssk = P_.pe.inc(1)
        with _blk(nc) as b:
            @b.vector
            def _(v, v_ssk=v_ssk):
                v.wait_ge(P_.pe.h, v_ssk)
                v.tensor_scalar(crow[0:1, :], pool[6][0:1, :], 1.0 / 256, 1e-6,
                                AL.mult, AL.add).then_inc(P_.ve.h, 1)
        v_crk = P_.ve.inc(1)
        with _blk(nc) as b:
            @b.scalar
            def _(s, v_crk=v_crk):
                s.wait_ge(P_.ve.h, v_crk)
                s.activation(crow[0:1, :], crow[0:1, :], AF.Ln)
                s.activation(crow[0:1, :], crow[0:1, :], AF.Exp, scale=-0.5
                             ).then_inc(P_.sc.h, 1)
        v_ak = P_.sc.inc(1)
        with _blk(nc) as b:
            @b.tensor
            def _(t, v_ak=v_ak):
                t.wait_ge(P_.sc.h, v_ak)
                t.matmul(pool[7][:, :], ones_r[:, :], crow[0:1, :], start=True,
                         stop=True).then_inc(P_.pe.h, 1)
        v_bck = P_.pe.inc(1)
        with _blk(nc) as b:
            @b.vector
            def _(v, v_bck=v_bck):
                b0 = P_.ve.n
                v.wait_ge(P_.pe.h, v_bck)
                for oc in range(2):
                    v.tensor_tensor(qtmp[:, 512 * oc:512 * (oc + 1)],
                                    qtmp[:, 512 * oc:512 * (oc + 1)], pool[7][:, :],
                                    AL.mult).then_inc(P_.ve.h, 1)
                v.wait_ge(P_.ve.h, b0 + 2)
                for oc in range(2):
                    v.tensor_scalar(qtmp[:, 512 * oc:512 * (oc + 1)],
                                    qtmp[:, 512 * oc:512 * (oc + 1)], scl[:, 2 + oc:3 + oc],
                                    None, AL.mult).then_inc(P_.ve.h, 1)
        v_kn = P_.ve.inc(4)
        with _blk(nc) as b:
            @b.tensor
            def _(t, v_kn=v_kn):
                t.wait_ge(P_.ve.h, v_kn)
                t.matmul(pool[6][:, :], pswap[:, :], qtmp[:, 0:512], start=True,
                         stop=True).then_inc(P_.pe.h, 1)
        v_swk = P_.pe.inc(1)
        slot = tg % 4
        with _blk(nc) as b:
            @b.vector
            def _(v, v_swk=v_swk, slot=slot):
                b0 = P_.ve.n
                v.wait_ge(P_.pe.h, v_swk)
                v.tensor_tensor(sq[:, 0:512], qtmp[:, 0:512], cst[:, 0:512], AL.mult
                                ).then_inc(P_.ve.h, 1)
                v.tensor_tensor(sq[:, 512:1024], pool[6][:, :], cst[:, 512:1024],
                                AL.mult).then_inc(P_.ve.h, 1)
                v.wait_ge(P_.ve.h, b0 + 2)
                v.tensor_add(kring[:, 4096 * 0 + 512 * slot:4096 * 0 + 512 * (slot + 1)],
                             sq[:, 0:512], sq[:, 512:1024]).then_inc(P_.ve.h, 1)
                v.tensor_copy(kring[:, 2048 + 512 * slot:2048 + 512 * (slot + 1)],
                              qtmp[:, 512:1024]).then_inc(P_.ve.h, 1)
        v_k = P_.ve.inc(4)

        with _blk(nc) as b:
            @b.scalar
            def _(s, v_pj=v_pj, v_k=v_k):
                s.wait_ge(P_.ve.h, v_k)
                s.copy(vtmp[:, :], pool[5][:, :]).then_inc(P_.sc.h, 1)
        v_vev = P_.sc.inc(1)
        with _blk(nc) as b:
            @b.tensor
            def _(t, v_vev=v_vev, slot=slot):
                t.wait_ge(P_.sc.h, v_vev)
                k = ctx.get(("l2a_vtr", tg - 1))
                for j in range(4):
                    t.transpose(pool[6][:, 128 * j:128 * (j + 1)],
                                ctx["vtmp0"][:, 128 * j:128 * (j + 1)], identf[:, :])
                ins = None
                for j in range(4):
                    ins = t.transpose(pool[7][:, 128 * j:128 * (j + 1)],
                                      vtmp[:, 128 * j:128 * (j + 1)], identf[:, :])
                ins.then_inc(P_.pe.h, 1)
        v_vt = P_.pe.inc(1)
        ctx[("l2a_vtr", tg)] = v_vt
        with _blk(nc) as b:
            @b.scalar
            def _(s, v_vt=v_vt, slot=slot):
                s.wait_ge(P_.pe.h, v_vt)
                for j in range(4):
                    s.copy(vring[:, 1024 * slot + 256 * j:1024 * slot + 256 * j + 128],
                           pool[6][:, 128 * j:128 * (j + 1)]).then_inc(P_.sc.h, 1)
                    s.copy(vring[:, 1024 * slot + 256 * j + 128:1024 * slot + 256 * (j + 1)],
                           pool[7][:, 128 * j:128 * (j + 1)]).then_inc(P_.sc.h, 1)
        v_vr = P_.sc.inc(8)

        qc = tg
        kts = [kt for kt in range(max(0, 4 * qc - 8), 4 * qc + 4)]
        nkt = len(kts)
        for idx, kt in enumerate(kts):
            diag = kt - 4 * qc
            delta = 4 * qc - kt
            ps_s = pool[0 if idx % 2 == 0 else 1]
            esb = es[idx % 2]
            v_esprev = ctx.get(("l2a_es", qc, idx - 2))
            kslot = (kt // 4) % 4
            koff = 512 * kslot + 128 * (kt % 4)
            with _blk(nc) as b:
                @b.tensor
                def _(t, qc=qc, kt=kt, idx=idx, ps_s=ps_s, v_esprev=v_esprev,
                      koff=koff, v_q=v_q, v_k=v_k):
                    t.wait_ge(P_.ve.h, v_q)
                    t.wait_ge(P_.ve.h, v_k)
                    if v_esprev is not None:
                        t.wait_ge(P_.sc.h, v_esprev)
                    t.matmul(ps_s[:, :], kring[:, koff:koff + 128],
                             qbuf[:, 0:512], start=True, stop=False)
                    t.matmul(ps_s[:, :], kring[:, 2048 + koff:2048 + koff + 128],
                             qbuf[:, 512:1024], start=False, stop=True).then_inc(P_.pe.h, 1)
            v_s = P_.pe.inc(1)
            with _blk(nc) as b:
                @b.scalar
                def _(s, v_s=v_s, ps_s=ps_s, esb=esb):
                    s.wait_ge(P_.pe.h, v_s)
                    s.activation(esb[:, :], ps_s[:, :], AF.Tanh,
                                 scale=L2SCALE / L2CAP).then_inc(P_.sc.h, 1)
                    s.activation(esb[:, :], esb[:, :], AF.Exp, scale=L2CAP
                                 ).then_inc(P_.sc.h, 1)
            v_e = P_.sc.inc(2)
            ctx[("l2a_es", qc, idx)] = v_e
            mask = None
            if diag >= 0:
                mask = dmask[:, 512 * diag:512 * (diag + 1)]
            elif delta >= 5:
                mask = wmask[:, 512 * (delta - 5):512 * (delta - 4)]
            if mask is not None:
                with _blk(nc) as b:
                    @b.vector
                    def _(v, v_e=v_e, esb=esb, mask=mask):
                        v.wait_ge(P_.sc.h, v_e)
                        v.tensor_tensor(esb[:, :], esb[:, :], mask, AL.mult
                                        ).then_inc(P_.ve.h, 1)
                gate = ("ve", P_.ve.inc(1))
            else:
                gate = ("sc", v_e)
            vs0 = 1024 * ((kt // 4) % 4) + 256 * (kt % 4)
            with _blk(nc) as b:
                @b.tensor
                def _(t, idx=idx, gate=gate, esb=esb, nkt=nkt, vs0=vs0):
                    t.wait_ge((P_.ve if gate[0] == "ve" else P_.sc).h, gate[1])
                    t.matmul(pool[2][:, :], vring[:, vs0:vs0 + 128], esb[:, :],
                             start=(idx == 0), stop=(idx == nkt - 1))
                    t.matmul(pool[3][:, :], vring[:, vs0 + 128:vs0 + 256], esb[:, :],
                             start=(idx == 0), stop=(idx == nkt - 1))
                    ins = t.matmul(pool[4][0:1, :], ones_b[:, :], esb[:, :],
                                   start=(idx == 0), stop=(idx == nkt - 1))
                    if idx == nkt - 1:
                        ins.then_inc(P_.pe.h, 1)
            if idx == nkt - 1:
                P_.pe.inc(1)
        v_pv = P_.pe.n
        with _blk(nc) as b:
            @b.vector
            def _(v, v_pv=v_pv):
                v.wait_ge(P_.pe.h, v_pv)
                v.reciprocal(crow[0:1, :], pool[4][0:1, :]).then_inc(P_.ve.h, 1)
        v_r = P_.ve.inc(1)
        with _blk(nc) as b:
            @b.tensor
            def _(t, v_r=v_r):
                t.wait_ge(P_.ve.h, v_r)
                t.matmul(pool[5][:, :], ones_r[:, :], crow[0:1, :], start=True,
                         stop=True).then_inc(P_.pe.h, 1)
        v_bc2 = P_.pe.inc(1)
        with _blk(nc) as b:
            @b.scalar
            def _(s, v_bc2=v_bc2):
                s.wait_ge(P_.pe.h, v_bc2)
                s.copy(cfix[:, :], pool[2][:, :]).then_inc(P_.sc.h, 1)
                s.copy(ctx["cfix2"][:, :], pool[3][:, :]).then_inc(P_.sc.h, 1)
        v_cp = P_.sc.inc(2)
        v_cbw = ctx.get(("l2a_cb", qc - 1))
        with _blk(nc) as b:
            @b.vector
            def _(v, v_cp=v_cp, v_cbw=v_cbw):
                v.wait_ge(P_.sc.h, v_cp)
                if v_cbw is not None:
                    v.wait_ge(P_.gd.h, v_cbw)
                v.tensor_tensor(cbuf[0][:, :], cfix[:, :], pool[5][:, :], AL.mult
                                ).then_inc(P_.ve.h, 1)
                v.tensor_tensor(cbuf[1][:, :], ctx["cfix2"][:, :], pool[5][:, :],
                                AL.mult).then_inc(P_.ve.h, 1)
        v_ct = P_.ve.inc(2)
        ctx[("l2a_aev", tg)] = v_ct
        ctx[("l2a_pje", tg)] = (P_.sc.n, v_ct)
        ctx[("l2a_kv", tg)] = v_k
        with _blk(nc) as b:
            @b.gpsimd
            def _(g, qc=qc, v_ct=v_ct):
                g.wait_ge(P_.ve.h, v_ct)
                g.dma_start(out=cl.ap()[0:128, 512 * qc:512 * (qc + 1)],
                            in_=cbuf[0][:, :]).then_inc(P_.gd.h, 16)
                g.dma_start(out=cl.ap()[128:256, 512 * qc:512 * (qc + 1)],
                            in_=cbuf[1][:, :]).then_inc(P_.gd.h, 16)
        ctx[("l2a_cb", qc)] = P_.gd.inc(32)
    v_cl = P_.gd.n

    with _blk(nc) as b:
        @b.gpsimd
        def _(g):
            g.wait_ge(P_.gd.h, v_cl)
            g.collective_compute("AllGather", AL.bypass, replica_groups=RG,
                                 ins=[cl.ap()], outs=[call.ap()]).then_inc(P_.cc.h)
    v_cag = P_.cc.inc(1)

    for tc in range(2):
        off = ctx["pid_off"][tc]
        with _blk(nc) as b:
            @b.sync
            def _(s, tc=tc, off=off):
                s.wait_ge(P_.cc.h, v_cag)
                if tc == 1:
                    s.wait_ge(P_.pe.h, ctx[("l2a_wo", 0, NDT - 1)])
                s.dma_start(out=xstage[:, :].rearrange("p (i f) -> p i f", i=NDT),
                            in_=call.ap().rearrange("(i p) t -> p i t", p=128)
                            [:, :, bass.ts(off, 512)]).then_inc(P_.ld.h, 16)
        v_cs = P_.ld.inc(16)
        for oc in range(NDT):
            v_warw = ctx.get(("l2a_wol", tc, oc - 2))
            with _blk(nc) as b:
                @b.gpsimd
                def _(g, oc=oc, v_warw=v_warw):
                    if v_warw is not None:
                        g.wait_ge(P_.pe.h, v_warw)
                    g.dma_start(out=wost[oc % 2][:, :].rearrange("p (i f) -> p i f", i=NDT),
                                in_=wo_ap[:, 128 * oc:128 * (oc + 1)]
                                .rearrange("(i p) f -> p i f", p=128)).then_inc(P_.gd.h, 16)
            v_ldw = P_.gd.inc(16)
            pa = pool[oc % 2]
            with _blk(nc) as b:
                @b.tensor
                def _(t, tc=tc, oc=oc, v_ldw=v_ldw, pa=pa, v_cs=v_cs):
                    t.wait_ge(P_.gd.h, v_ldw)
                    t.wait_ge(P_.ld.h, v_cs)
                    if oc <= 1:
                        if tc == 1:
                            t.wait_ge(P_.ve.h, ctx[("l2a_woe", 0, NDT - 2 + oc)])
                    else:
                        t.wait_ge(P_.ve.h, ctx[("l2a_woe", tc, oc - 2)])
                    for i in range(NDT):
                        ins = t.matmul(pa[:, :], wost[oc % 2][:, 128 * i:128 * (i + 1)],
                                       xstage[:, 512 * i:512 * (i + 1)],
                                       start=(i == 0), stop=(i == NDT - 1))
                    ins.then_inc(P_.pe.h, 1)
            ctx[("l2a_wo", tc, oc)] = P_.pe.inc(1)
            ctx[("l2a_wol", tc, oc)] = P_.pe.n
            v_mm = P_.pe.n
            with _blk(nc) as b:
                @b.vector
                def _(v, tc=tc, oc=oc, v_mm=v_mm, pa=pa):
                    v.wait_ge(P_.pe.h, v_mm)
                    v.wait_ge(v_x[0].h, v_x[1])
                    v.tensor_add(xr[oc][:, 512 * tc:512 * (tc + 1)],
                                 xr[oc][:, 512 * tc:512 * (tc + 1)], pa[:, :]
                                 ).then_inc(P_.ve.h, 1)
            ctx[("l2a_woe", tc, oc)] = P_.ve.inc(1)
    return [(P_.ve, P_.ve.n)]


def build_l2_attn_prog():
    Pr = P()
    nc = Pr.nc
    xin = nc.dram_tensor("xin", [D, TL], F32, kind="ExternalInput").ap()
    wq = nc.dram_tensor("l2a_wq", [D, 256], F32, kind="ExternalInput").ap()
    wk = nc.dram_tensor("l2a_wk", [D, 256], F32, kind="ExternalInput").ap()
    wv = nc.dram_tensor("l2a_wv", [D, 256], F32, kind="ExternalInput").ap()
    wo = nc.dram_tensor("l2a_wo", [D, D], F32, kind="ExternalInput").ap()
    qs = nc.dram_tensor("l2a_qs", [128, 2], F32, kind="ExternalInput").ap()
    ks = nc.dram_tensor("l2a_ks", [128, 2], F32, kind="ExternalInput").ap()
    xout = nc.dram_tensor("xout", [D, TL], F32, kind="ExternalOutput").ap()
    Pr.make_pool()
    xr = [Pr.sb(f"xr{i}", [128, TL], F32) for i in range(NDT)]
    xnb = [Pr.sb(f"xnb{i}", [128, TL], BF16) for i in range(NDT)]
    C, S = _l2_rope_tables()
    ctx = {
        "qbuf": Pr.sb("qbuf", [128, 1024], BF16),
        "kring": Pr.sb("kring", [128, 4096], BF16),
        "vring": Pr.sb("vring", [128, 4096], BF16),
        "xstage": Pr.sb("xstage", [128, 8192], BF16),
        "wqkvt": Pr.sb("wqkvt", [128, 12288], BF16),
        "wost": [Pr.sb(f"wost{i}", [128, 2048], BF16) for i in range(2)],
        "es": [Pr.sb(f"es{i}", [128, 512], BF16) for i in range(2)],
        "cbuf": [Pr.sb(f"cbuf{i}", [128, 512], BF16) for i in range(2)],
        "qtmp": Pr.sb("qtmp", [128, 1024], F32),
        "vtmp0": Pr.sb("vtmp0", [128, 512], F32),
        "vtmp": Pr.sb("vtmp", [128, 512], F32),
        "cst": Pr.sb("cst", [128, 1024], BF16),
        "scl": Pr.sb("scl", [128, 4], F32),
        "cfix": Pr.sb("cfix", [128, 512], F32),
        "cfix2": Pr.sb("cfix2", [128, 512], F32),
        "sqn": Pr.sb("sqn", [128, 1024], BF16),
    }
    ctx["ropeC"] = Pr.const("l2a_ropeC", C)
    ctx["ropeS"] = Pr.const("l2a_ropeS", S)
    ctx["identf"] = Pr.sconst("ident_f", np.eye(128, dtype=np.float32))
    ctx["pswap"] = Pr.sconst("l2a_pswap", _pswap())
    ctx["dmask"] = Pr.sconst("l3a_dmask", _l3_dmask())
    ctx["wmask"] = Pr.sconst("l2a_wmask", _l2_wmask())
    Pr.ones_col_b(); Pr.ones_row_f()
    Pr.flush_consts()
    nctx = make_norm_ctx(Pr, atmp_alias=ctx["cfix"],
                         sq_alias=[ctx["xstage"][:, 512 * i:512 * (i + 1)]
                                   for i in range(NDT)])
    ctx["crow"] = nctx["r_msq"]
    with _blk(nc) as b:
        @b.sync
        def _(s):
            pid = nc.partition_id()
            ctx["pid_off"] = [nc.snap(pid * 2 + tc, min_val=0,
                                      max_val=2 * NCORES - 1) for tc in range(2)]
    v_x = load_residual(Pr, xr, xin)
    fin = emit_l2_attn(Pr, ctx, xr, (Pr.ld, v_x), wq, wk, wv, wo, qs, ks)
    fin2 = emit_norm(Pr, nctx, xr, xnb, None, None, subtract_mean=True,
                     pre_waits=fin)
    store_residual(Pr, xr, xout, [fin2])
    Pr.close_block()
    return Pr


def run_l2_attn(x, Wq, Wk, Wv, Wo, q_scale, k_scale):
    Pr = build_l2_attn_prog()
    xs = _shard_x(x)
    qs = np.ascontiguousarray(q_scale.reshape(2, 128).T)
    ks = np.ascontiguousarray(k_scale.reshape(2, 128).T)
    in_maps = []
    for c in range(NCORES):
        g = c // 2
        in_maps.append({
            "xin": xs[c],
            "l2a_wq": np.ascontiguousarray(Wq[:, 256 * c:256 * (c + 1)]),
            "l2a_wk": np.ascontiguousarray(Wk[:, 256 * g:256 * (g + 1)]),
            "l2a_wv": np.ascontiguousarray(Wv[:, 256 * g:256 * (g + 1)]),
            "l2a_wo": Wo, "l2a_qs": qs, "l2a_ks": ks,
        })
    res = run_bass_kernel_spmd(Pr.nc, in_maps, core_ids=list(range(NCORES)))
    return _unshard_x([res.results[c]["xout"] for c in range(NCORES)])





HL1 = 5504
NHC1 = HL1 // 128


def emit_l1_ffn_build(P_, ctx, gsh, ush, dsh, lut_ap):
    nc = P_.nc
    pool = P_.pool
    idxst = ctx["idxst"]
    wtile = ctx["wtile"]
    ttmp = ctx["ttmp"]
    tacc = ctx["tacc"]
    lutsb = ctx["lutsb"]
    gb_l = nc.dram_tensor("l1f_gb_l", [256, HL1], BF16)
    ub_l = nc.dram_tensor("l1f_ub_l", [256, HL1], BF16)
    db_l = nc.dram_tensor("l1f_db_l", [HL1, 256], BF16)
    gb = nc.dram_tensor("l1f_gb", [NCORES * 256, HL1], BF16, addr_space="Shared")
    ub = nc.dram_tensor("l1f_ub", [NCORES * 256, HL1], BF16, addr_space="Shared")
    db = nc.dram_tensor("l1f_db", [NCORES * HL1, 256], BF16, addr_space="Shared")
    v_lut = P_.ld.inc(16)
    with _blk(nc) as b:
        @b.sync
        def _(s):
            s.dma_start(out=lutsb[:, :], in_=lut_ap).then_inc(P_.ld.h, 16)

    def build(src_ap, dst_dram, n_rows, n_cols, tag):
        nrt = (n_rows + 127) // 128
        for rt in range(nrt):
            rows = min(128, n_rows - 128 * rt)
            ncc = (n_cols + 1375) // 1376
            for cchunk in range(ncc):
                c0 = 1376 * cchunk
                cw = min(1376, n_cols - c0)
                v_war = ctx.get(("l1b", tag, rt, cchunk - 1)) or ctx.get(("l1b_last",))
                with _blk(nc) as b:
                    @b.sync
                    def _(s, rt=rt, c0=c0, cw=cw, rows=rows, v_war=v_war):
                        if v_war is not None:
                            s.wait_ge(P_.ve.h, v_war)
                        s.dma_start(out=idxst[0:rows, 0:cw],
                                    in_=src_ap[128 * rt:128 * rt + rows, c0:c0 + cw]
                                    ).then_inc(P_.ld.h, 16)
                v_ld = P_.ld.inc(16)
                with _blk(nc) as b:
                    @b.vector
                    def _(v, rt=rt, c0=c0, cw=cw, rows=rows, v_ld=v_ld):
                        b0 = P_.ve.n
                        v.wait_ge(P_.ld.h, v_ld)
                        v.wait_ge(P_.ld.h, v_lut)
                        v.tensor_copy(ttmp[0:rows, 0:cw], idxst[0:rows, 0:cw]
                                      ).then_inc(P_.ve.h, 1)
                        v.wait_ge(P_.ve.h, b0 + 1)
                        v.tensor_scalar(wtile[0:rows, c0:c0 + cw], ttmp[0:rows, 0:cw],
                                        0.0, lutsb[:, 0:1], AL.is_equal, AL.mult
                                        ).then_inc(P_.ve.h, 1)
                        for i in range(1, 16):
                            v.wait_ge(P_.ve.h, b0 + 2 * i)
                            v.tensor_scalar(tacc[0:rows, 0:cw], ttmp[0:rows, 0:cw],
                                            float(i), lutsb[:, i:i + 1],
                                            AL.is_equal, AL.mult).then_inc(P_.ve.h, 1)
                            v.wait_ge(P_.ve.h, b0 + 2 * i + 1)
                            v.tensor_add(wtile[0:rows, c0:c0 + cw],
                                         wtile[0:rows, c0:c0 + cw],
                                         tacc[0:rows, 0:cw]).then_inc(P_.ve.h, 1)
                ctx[("l1b", tag, rt, cchunk)] = P_.ve.inc(32)
            v_bt = P_.ve.n
            with _blk(nc) as b:
                @b.gpsimd
                def _(g, rt=rt, rows=rows, v_bt=v_bt):
                    g.wait_ge(P_.ve.h, v_bt)
                    g.dma_start(out=dst_dram.ap()[128 * rt:128 * rt + rows, :],
                                in_=wtile[0:rows, 0:n_cols]).then_inc(P_.gd.h, 16)
            ctx[("l1b_last",)] = None
            ctx[("l1bst", tag, rt)] = P_.gd.inc(16)

    build(gsh, gb_l, 256, HL1, "g")
    build(ush, ub_l, 256, HL1, "u")
    build(dsh, db_l, HL1, 256, "d")
    v_st = P_.gd.n
    with _blk(nc) as b:
        @b.gpsimd
        def _(g):
            g.wait_ge(P_.gd.h, v_st)
            for (i_, o_) in ((gb_l, gb), (ub_l, ub), (db_l, db)):
                g.collective_compute("AllGather", AL.bypass, replica_groups=RG,
                                     ins=[i_.ap()], outs=[o_.ap()]).then_inc(P_.cc.h)
    ready = (P_.cc, P_.cc.inc(3))
    return gb, ub, db, ready


def emit_l1_ffn(P_, ctx, xr, xnb, xnb_ready, gb, ub, db, wready):
    nc = P_.nc
    pool = P_.pool
    ht = ctx["ht1"]
    wst = ctx["wst1"]
    w2st = ctx["w2st1"]
    silt = ctx["silt1"]

    for tc in range(2):
        sl = slice(512 * tc, 512 * (tc + 1))
        for hc in range(NHC1):
            kb = 2 * NHC1 * tc + 2 * hc
            v_war = ctx.get(("l1mm", kb - 4))
            with _blk(nc) as b:
                @b.sync
                def _(s, hc=hc, kb=kb, v_war=v_war):
                    s.wait_ge(P_.cc.h, wready[1])
                    if v_war is not None:
                        s.wait_ge(P_.pe.h, v_war)
                    for (mat, col) in ((gb, 0), (ub, 1)):
                        s.dma_start(out=wst[(kb + col) % 4][:, :]
                                    .rearrange("p (i f) -> p i f", i=NDT),
                                    in_=mat.ap()[:, 128 * hc:128 * (hc + 1)]
                                    .rearrange("(i p) f -> p i f", p=128)
                                    ).then_inc(P_.ld.h, 16)
            v_ld = P_.ld.inc(32)
            ev_prev = ctx.get(("l1ev", tc, hc - 2))
            with _blk(nc) as b:
                @b.tensor
                def _(t, hc=hc, kb=kb, tc=tc, v_ld=v_ld, ev_prev=ev_prev):
                    t.wait_ge(P_.ld.h, v_ld)
                    if hc == 0 and tc == 0:
                        t.wait_ge(xnb_ready[0].h, xnb_ready[1])
                    if hc <= 1 and tc == 1:
                        t.wait_ge(P_.ve.h, ctx[("l1dn", 0)])
                    if ev_prev is not None:
                        t.wait_ge(P_.ve.h, ev_prev)
                    pg = pool[kb % 4 // 2 * 2 % 4]
                    pg = pool[(hc % 2) * 2]
                    pu = pool[(hc % 2) * 2 + 1]
                    for i in range(NDT):
                        t.matmul(pg[:, :], wst[kb % 4][:, 128 * i:128 * (i + 1)],
                                 xnb[i][:, sl], start=(i == 0), stop=(i == NDT - 1))
                        ins = t.matmul(pu[:, :], wst[(kb + 1) % 4][:, 128 * i:128 * (i + 1)],
                                       xnb[i][:, sl], start=(i == 0), stop=(i == NDT - 1))
                    ins.then_inc(P_.pe.h, 1)
            ctx[("l1mm", kb)] = P_.pe.inc(1)
            v_mm = P_.pe.n
            with _blk(nc) as b:
                @b.scalar
                def _(s, hc=hc, v_mm=v_mm):
                    s.wait_ge(P_.pe.h, v_mm)
                    k = ctx.get(("l1ev", 99, hc))
                    s.activation(silt[:, :], pool[(hc % 2) * 2][:, :], AF.Silu
                                 ).then_inc(P_.sc.h, 1)
            v_sil = P_.sc.inc(1)
            with _blk(nc) as b:
                @b.vector
                def _(v, hc=hc, v_sil=v_sil):
                    v.wait_ge(P_.sc.h, v_sil)
                    v.tensor_tensor(ht[hc][:, :], silt[:, :],
                                    pool[(hc % 2) * 2 + 1][:, :], AL.mult
                                    ).then_inc(P_.ve.h, 1)
            ctx[("l1ev", tc, hc)] = P_.ve.inc(1)
        v_he = P_.ve.n

        for oh in range(2):
            for hc in range(NHC1):
                bufi = hc % 2
                v_war = ctx.get(("l1dld", tc, oh, hc - 2))
                with _blk(nc) as b:
                    @b.gpsimd
                    def _(g, oh=oh, hc=hc, bufi=bufi, v_war=v_war):
                        g.wait_ge(P_.cc.h, wready[1])
                        if v_war is not None:
                            g.wait_ge(P_.pe.h, v_war)
                        for rr in range(4):
                            r = 4 * oh + rr
                            g.dma_start(out=w2st[bufi][:, 256 * rr:256 * (rr + 1)],
                                        in_=db.ap()[r * HL1 + 128 * hc:
                                                    r * HL1 + 128 * (hc + 1), :]
                                        ).then_inc(P_.gd.h, 16)
                v_ld = P_.gd.inc(64)
                with _blk(nc) as b:
                    @b.tensor
                    def _(t, oh=oh, hc=hc, bufi=bufi, v_ld=v_ld, v_he=v_he, tc=tc):
                        t.wait_ge(P_.gd.h, v_ld)
                        if hc == 0:
                            t.wait_ge(P_.ve.h, v_he)
                            if oh == 1:
                                t.wait_ge(P_.ve.h, ctx[("l1dn", tc)])
                        for oc in range(8):
                            ins = t.matmul(pool[oc][:, :],
                                           w2st[bufi][:, 128 * oc:128 * (oc + 1)],
                                           ht[hc][:, :],
                                           start=(hc == 0), stop=(hc == NHC1 - 1))
                        ins.then_inc(P_.pe.h, 1)
                ctx[("l1dld", tc, oh, hc)] = P_.pe.inc(1)
            v_dn = P_.pe.n
            with _blk(nc) as b:
                @b.vector
                def _(v, oh=oh, tc=tc, v_dn=v_dn):
                    v.wait_ge(P_.pe.h, v_dn)
                    for oc in range(8):
                        i = 8 * oh + oc
                        v.tensor_add(xr[i][:, 512 * tc:512 * (tc + 1)],
                                     xr[i][:, 512 * tc:512 * (tc + 1)], pool[oc][:, :]
                                     ).then_inc(P_.ve.h, 1)
            ctx[("l1dn", tc)] = P_.ve.inc(8)
    return [(P_.ve, P_.ve.n)]


def build_l1_ffn_prog():
    Pr = P()
    nc = Pr.nc
    xin = nc.dram_tensor("xin", [D, TL], F32, kind="ExternalInput").ap()
    lut = nc.dram_tensor("l1f_lut", [128, 16], F32, kind="ExternalInput").ap()
    gsh = nc.dram_tensor("l1f_gsh", [256, HL1], I32, kind="ExternalInput").ap()
    ush = nc.dram_tensor("l1f_ush", [256, HL1], I32, kind="ExternalInput").ap()
    dsh = nc.dram_tensor("l1f_dsh", [HL1, 256], I32, kind="ExternalInput").ap()
    xout = nc.dram_tensor("xout", [D, TL], F32, kind="ExternalOutput").ap()
    Pr.make_pool()
    xr = [Pr.sb(f"xr{i}", [128, TL], F32) for i in range(NDT)]
    ht = [Pr.sb(f"l1ht{i}", [128, 512], BF16) for i in range(NHC1)]
    xnb = [Pr.sb(f"xnb{i}", [128, TL], BF16) for i in range(NDT)]
    ctx = {
        "ht1": ht,
        "wst1": [Pr.sb(f"l1wst{i}", [128, 2048], BF16) for i in range(4)],
        "w2st1": [Pr.sb(f"l1w2st{i}", [128, 1024], BF16) for i in range(2)],
        "silt1": Pr.sb("l1silt", [128, 512], BF16),
        "idxst": Pr.sb("l1idx", [128, 1376], I32),
        "wtile": Pr.sb("l1wtile", [128, HL1], BF16),
        "ttmp": Pr.sb("l1ttmp", [128, 1376], BF16),
        "tacc": Pr.sb("l1tacc", [128, 1376], BF16),
        "lutsb": Pr.sb("l1lut", [128, 16], F32),
    }
    nctx = make_norm_ctx(Pr, sq_alias=[ht[i] for i in range(NDT)])
    Pr.ones_col_b(); Pr.ones_row_f()
    Pr.flush_consts()
    gb, ub, db, wready = emit_l1_ffn_build(Pr, ctx, gsh, ush, dsh, lut)
    v_x = load_residual(Pr, xr, xin)
    with _blk(nc) as b:
        @b.scalar
        def _(s):
            s.wait_ge(Pr.ld.h, v_x)
            for i in range(NDT):
                s.copy(xnb[i][:, :], xr[i][:, :]).then_inc(Pr.sc.h, 1)
    xnb_ready = (Pr.sc, Pr.sc.inc(NDT))
    fin = emit_l1_ffn(Pr, ctx, xr, xnb, xnb_ready, gb, ub, db, wready)
    fin2 = emit_norm(Pr, nctx, xr, xnb, None, None, subtract_mean=True,
                     pre_waits=fin)
    store_residual(Pr, xr, xout, [fin2])
    Pr.close_block()
    return Pr


def run_l1_ffn(x, lut, gate_idx, up_idx, down_idx):
    Pr = build_l1_ffn_prog()
    xs = _shard_x(x)
    gT = np.ascontiguousarray(gate_idx.T)
    uT = np.ascontiguousarray(up_idx.T)
    dT = np.ascontiguousarray(down_idx.T)
    in_maps = []
    for c in range(NCORES):
        in_maps.append({
            "xin": xs[c], "l1f_lut": np.ascontiguousarray(np.broadcast_to(lut.reshape(1, 16), (128, 16))),
            "l1f_gsh": np.ascontiguousarray(gT[256 * c:256 * (c + 1)]),
            "l1f_ush": np.ascontiguousarray(uT[256 * c:256 * (c + 1)]),
            "l1f_dsh": np.ascontiguousarray(dT[:, 256 * c:256 * (c + 1)]),
        })
    res = run_bass_kernel_spmd(Pr.nc, in_maps, core_ids=list(range(NCORES)))
    return _unshard_x([res.results[c]["xout"] for c in range(NCORES)])





def emit_l0_attn(P_, ctx, xr, xnb, xnb_ready, wqT, wkT, wvs, wgs, wpT, v_x):
    nc = P_.nc
    pool = P_.pool
    xstage = ctx["xstage"]
    wst = ctx["wst"]
    wst2 = ctx["wst2"]
    kvf = ctx["kvf"]
    kvb = ctx["kvb"]
    qst, kst, ktdst = ctx["qst"], ctx["kst"], ctx["ktdst"]
    vst, gst = ctx["vst"], ctx["gst"]
    ssb = ctx["ssb"]
    otmp = ctx["otmp"]
    obuf = ctx["obuf"]
    identf = ctx["identf"]
    dmask = ctx["dmask"]
    ones_b = P_.ones_col_b()

    ql = nc.dram_tensor("l0a_ql", [D, TL], BF16)
    kl = nc.dram_tensor("l0a_kl", [D, TL], BF16)
    ktdl = nc.dram_tensor("l0a_ktdl", [TL, D], BF16)
    vl = nc.dram_tensor("l0a_vl", [T, 256], BF16)
    gl = nc.dram_tensor("l0a_gl", [T, 256], BF16)
    ogl = nc.dram_tensor("l0a_ogl", [256, T], BF16)
    xlo = nc.dram_tensor("l0a_xl", [D, TL], BF16)
    qall = nc.dram_tensor("l0a_qall", [NCORES * D, TL], BF16, addr_space="Shared")
    kall = nc.dram_tensor("l0a_kall", [NCORES * D, TL], BF16, addr_space="Shared")
    ktall = nc.dram_tensor("l0a_ktall", [T, D], BF16, addr_space="Shared")
    ogall = nc.dram_tensor("l0a_ogall", [NCORES * 256, T], BF16, addr_space="Shared")

    with _blk(nc) as b:
        @b.gpsimd
        def _(g):
            g.wait_ge(xnb_ready[0].h, xnb_ready[1])
            for i in range(NDT):
                g.dma_start(out=xlo.ap()[128 * i:128 * (i + 1), :], in_=xnb[i][:, :]
                            ).then_inc(P_.gd.h, 16)
    v_xl = P_.gd.inc(16 * NDT)
    xall = nc.dram_tensor("l0a_xall", [NCORES * D, TL], BF16, addr_space="Shared")
    with _blk(nc) as b:
        @b.gpsimd
        def _(g):
            g.wait_ge(P_.gd.h, v_xl)
            g.collective_compute("AllGather", AL.bypass, replica_groups=RG,
                                 ins=[xlo.ap()], outs=[xall.ap()]).then_inc(P_.cc.h)
    v_ag = P_.cc.inc(1)

    for (w_ap, dst, tag) in ((wqT, ql, "q"), (wkT, kl, "k")):
        for oc in range(NDT):
            kb = ("l0p", tag, oc)
            v_war = ctx.get(("l0p_mm", tag, oc - 2))
            with _blk(nc) as b:
                @b.gpsimd
                def _(g, oc=oc, v_war=v_war, w_ap=w_ap):
                    if v_war is not None:
                        g.wait_ge(P_.pe.h, v_war)
                    g.dma_start(out=wst[oc % 2][:, :].rearrange("p (i f) -> p i f", i=NDT),
                                in_=w_ap[:, 128 * oc:128 * (oc + 1)]
                                .rearrange("(i p) f -> p i f", p=128)).then_inc(P_.gd.h, 16)
            v_ld = P_.gd.inc(16)
            for tc in range(2):
                pa = pool[tc]
                with _blk(nc) as b:
                    @b.tensor
                    def _(t, oc=oc, tc=tc, v_ld=v_ld, pa=pa, tag=tag):
                        t.wait_ge(P_.gd.h, v_ld)
                        t.wait_ge(xnb_ready[0].h, xnb_ready[1])
                        k = ctx.get(("l0p_ev", tag, oc - 1, tc))
                        if k is not None:
                            t.wait_ge(P_.sc.h, k)
                        for i in range(NDT):
                            ins = t.matmul(pa[:, :], wst[oc % 2][:, 128 * i:128 * (i + 1)],
                                           xnb[i][:, 512 * tc:512 * (tc + 1)],
                                           start=(i == 0), stop=(i == NDT - 1))
                        ins.then_inc(P_.pe.h, 1)
                ctx[("l0p_mm", tag, oc)] = P_.pe.inc(1)
                v_mm = P_.pe.n
                with _blk(nc) as b:
                    @b.scalar
                    def _(s, oc=oc, tc=tc, v_mm=v_mm, pa=pa):
                        s.wait_ge(P_.pe.h, v_mm)
                        k = ctx.get(("l0_ssb_st",))
                        if k is not None:
                            s.wait_ge(P_.gd.h, k)
                        s.activation(ssb[:, 0:512], pa[:, :], AF.Silu).then_inc(P_.sc.h, 1)
                v_sl = P_.sc.inc(1)
                ctx[("l0p_ev", tag, oc, tc)] = v_sl
                with _blk(nc) as b:
                    @b.gpsimd
                    def _(g, oc=oc, tc=tc, v_sl=v_sl, dst=dst):
                        g.wait_ge(P_.sc.h, v_sl)
                        g.dma_start(out=dst.ap()[128 * oc:128 * (oc + 1),
                                                 512 * tc:512 * (tc + 1)],
                                    in_=ssb[:, 0:512]).then_inc(P_.gd.h, 16)
                ctx[("l0p_st", tag, oc, tc)] = P_.gd.inc(16)
                ctx[("l0_ssb_st",)] = P_.gd.n
    v_qk_st = P_.gd.n

    for oc4 in range(4):
        v_war = ctx.get(("l0ktd_mm", 7, oc4 - 1))
        with _blk(nc) as b:
            @b.gpsimd
            def _(g, oc4=oc4, v_war=v_war):
                if v_war is not None:
                    g.wait_ge(P_.pe.h, v_war)
                g.dma_start(out=ctx["wstb"][:, :]
                            .rearrange("p (i f) -> p i f", i=NDT),
                            in_=wkT[:, 512 * oc4:512 * (oc4 + 1)]
                            .rearrange("(i p) f -> p i f", p=128)
                            ).then_inc(P_.gd.h, 16)
        v_ld = P_.gd.inc(16)
        for tt in range(8):
            pa = pool[2 + tt % 2]
            with _blk(nc) as b:
                @b.tensor
                def _(t, tt=tt, oc4=oc4, v_ld=v_ld, pa=pa):
                    t.wait_ge(P_.gd.h, v_ld)
                    k = ctx.get(("l0ktd_ev2", 8 * oc4 + tt - 2))
                    if k is not None:
                        t.wait_ge(P_.sc.h, k)
                    for i in range(NDT):
                        ins = t.matmul(pa[:, :],
                                       xnb[i][:, 128 * tt:128 * (tt + 1)],
                                       ctx["wstb"][:, 512 * i:512 * (i + 1)],
                                       start=(i == 0), stop=(i == NDT - 1))
                    ins.then_inc(P_.pe.h, 1)
            ctx[("l0ktd_mm", tt, oc4)] = P_.pe.inc(1)
            v_mm = P_.pe.n
            with _blk(nc) as b:
                @b.scalar
                def _(s, tt=tt, oc4=oc4, v_mm=v_mm, pa=pa):
                    s.wait_ge(P_.pe.h, v_mm)
                    k = ctx.get(("l0_ssb_st",))
                    if k is not None:
                        s.wait_ge(P_.gd.h, k)
                    s.activation(ssb[:, 0:512], pa[:, :], AF.Silu).then_inc(P_.sc.h, 1)
            v_sl = P_.sc.inc(1)
            ctx[("l0ktd_ev2", 8 * oc4 + tt)] = v_sl
            with _blk(nc) as b:
                @b.gpsimd
                def _(g, tt=tt, oc4=oc4, v_sl=v_sl):
                    g.wait_ge(P_.sc.h, v_sl)
                    g.dma_start(out=ktdl.ap()[128 * tt:128 * (tt + 1),
                                              512 * oc4:512 * (oc4 + 1)],
                                in_=ssb[:, 0:512]).then_inc(P_.gd.h, 16)
            ctx[("l0ktd_st", tt, oc4)] = P_.gd.inc(16)
            ctx[("l0_ssb_st",)] = P_.gd.n
    v_ktd_st = P_.gd.n

    v_ktd_mm_all = P_.pe.n
    with _blk(nc) as b:
        @b.gpsimd
        def _(g):
            g.wait_ge(P_.pe.h, v_ktd_mm_all)
            g.dma_start(out=ctx["wstb"][:, 0:4096].rearrange("p (i f) -> p i f", i=NDT),
                        in_=wvs.rearrange("(i p) f -> p i f", p=128)).then_inc(P_.gd.h, 16)
            g.dma_start(out=ctx["wstb"][:, 4096:8192].rearrange("p (i f) -> p i f", i=NDT),
                        in_=wgs.rearrange("(i p) f -> p i f", p=128)).then_inc(P_.gd.h, 16)
    v_wvg = P_.gd.inc(32)
    for tb in range(16):
        r, tl0 = tb // 2, 512 * (tb % 2)
        v_war = ctx.get(("l0vg_mm", tb - 1))
        with _blk(nc) as b:
            @b.sync
            def _(s, r=r, tl0=tl0, v_war=v_war):
                s.wait_ge(P_.cc.h, v_ag)
                if v_war is not None:
                    s.wait_ge(P_.pe.h, v_war)
                s.dma_start(out=xstage[:, :].rearrange("p (i f) -> p i f", i=NDT),
                            in_=xall.ap()[r * D:(r + 1) * D, tl0:tl0 + 512]
                            .rearrange("(i p) f -> p i f", p=128)).then_inc(P_.ld.h, 16)
        v_ld = P_.ld.inc(16)
        for ts4 in range(4):
            pv_ = pool[4 + ts4 % 2]
            pg_ = pool[6 + ts4 % 2]
            with _blk(nc) as b:
                @b.tensor
                def _(t, tb=tb, ts4=ts4, v_ld=v_ld, pv_=pv_, pg_=pg_):
                    t.wait_ge(P_.ld.h, v_ld)
                    t.wait_ge(P_.gd.h, v_wvg)
                    k = ctx.get(("l0vg_ev", tb, ts4 - 2))
                    if k is not None:
                        t.wait_ge(P_.sc.h, k)
                    for i in range(NDT):
                        t.matmul(pv_[:, 0:256], xstage[:, 512 * i + 128 * ts4:512 * i + 128 * (ts4 + 1)],
                                 ctx["wstb"][:, 256 * i:256 * (i + 1)],
                                 start=(i == 0), stop=(i == NDT - 1))
                        ins = t.matmul(pg_[:, 0:256], xstage[:, 512 * i + 128 * ts4:512 * i + 128 * (ts4 + 1)],
                                       ctx["wstb"][:, 4096 + 256 * i:4096 + 256 * (i + 1)],
                                       start=(i == 0), stop=(i == NDT - 1))
                    ins.then_inc(P_.pe.h, 1)
            ctx[("l0vg_mm", tb)] = P_.pe.inc(1)
            v_mm = P_.pe.n
            with _blk(nc) as b:
                @b.scalar
                def _(s, tb=tb, ts4=ts4, v_mm=v_mm, pv_=pv_, pg_=pg_):
                    s.wait_ge(P_.pe.h, v_mm)
                    k = ctx.get(("l0_ssb_st",))
                    if k is not None:
                        s.wait_ge(P_.gd.h, k)
                    s.activation(obuf[:, :], pv_[:, 0:256], AF.Silu).then_inc(P_.sc.h, 1)
                    s.activation(ssb[:, 0:256], pg_[:, 0:256], AF.Sigmoid).then_inc(P_.sc.h, 1)
            v_ev = P_.sc.inc(2)
            ctx[("l0vg_ev", tb, ts4)] = v_ev
            with _blk(nc) as b:
                @b.gpsimd
                def _(g, tb=tb, ts4=ts4, v_ev=v_ev):
                    g.wait_ge(P_.sc.h, v_ev)
                    row0 = 512 * tb + 128 * ts4
                    g.dma_start(out=vl.ap()[row0:row0 + 128, :], in_=obuf[:, :]
                                ).then_inc(P_.gd.h, 16)
                    g.dma_start(out=gl.ap()[row0:row0 + 128, :], in_=ssb[:, 0:256]
                                ).then_inc(P_.gd.h, 16)
            ctx[("l0vg_st", tb, ts4)] = P_.gd.inc(32)
            ctx[("l0_ssb_st",)] = P_.gd.n
    v_vg_st = P_.gd.n

    with _blk(nc) as b:
        @b.gpsimd
        def _(g):
            g.wait_ge(P_.gd.h, max(v_qk_st, v_ktd_st, v_vg_st))
            g.collective_compute("AllGather", AL.bypass, replica_groups=RG,
                                 ins=[ql.ap()], outs=[qall.ap()]).then_inc(P_.cc.h)
            g.collective_compute("AllGather", AL.bypass, replica_groups=RG,
                                 ins=[kl.ap()], outs=[kall.ap()]).then_inc(P_.cc.h)
            g.collective_compute("AllGather", AL.bypass, replica_groups=RG,
                                 ins=[ktdl.ap()], outs=[ktall.ap()]).then_inc(P_.cc.h)
    v_ag2 = P_.cc.inc(3)

    for n in range(32):
        r, col0 = n // 4, 256 * (n % 4)
        row0 = 256 * n
        v_war = ctx.get(("l0s_done", n - 1))
        with _blk(nc) as b:
            @b.sync
            def _(s, r=r, col0=col0, row0=row0, v_war=v_war, n=n):
                s.wait_ge(P_.cc.h, v_ag2)
                if v_war is not None:
                    s.wait_ge(P_.pe.h, v_war)
                k2 = ctx.get(("l0s_kvadd", n - 1, NDT - 1))
                if k2 is not None:
                    s.wait_ge(P_.ve.h, k2)
                s.dma_start(out=qst[:, :].rearrange("p (i f) -> p i f", i=NDT),
                            in_=qall.ap()[r * D:(r + 1) * D, col0:col0 + 256]
                            .rearrange("(i p) f -> p i f", p=128)).then_inc(P_.ld.h, 16)
                s.dma_start(out=kst[:, :].rearrange("p (i f) -> p i f", i=NDT),
                            in_=kall.ap()[r * D:(r + 1) * D, col0:col0 + 256]
                            .rearrange("(i p) f -> p i f", p=128)).then_inc(P_.ld.h, 16)
                s.dma_start(out=ktdst[:, :].rearrange("p (i f) -> p i f", i=2),
                            in_=ktall.ap()[row0:row0 + 256, :]
                            .rearrange("(i p) f -> p i f", p=128)).then_inc(P_.ld.h, 16)
                s.dma_start(out=vst[:, :].rearrange("p (i f) -> p i f", i=2),
                            in_=vl.ap()[row0:row0 + 256, :]
                            .rearrange("(i p) f -> p i f", p=128)).then_inc(P_.ld.h, 16)
                s.dma_start(out=gst[:, :].rearrange("p (i f) -> p i f", i=2),
                            in_=gl.ap()[row0:row0 + 256, :]
                            .rearrange("(i p) f -> p i f", p=128)).then_inc(P_.ld.h, 16)
        v_ld = P_.ld.inc(80)
        pairs = [(0, 0), (0, 1), (1, 1)]
        for pi_, (ks, qs) in enumerate(pairs):
            pa = pool[0]
            v_pe_prev = ctx.get(("l0s_sev", n, pi_ - 1))
            with _blk(nc) as b:
                @b.tensor
                def _(t, ks=ks, qs=qs, v_ld=v_ld, pa=pa, v_pe_prev=v_pe_prev):
                    t.wait_ge(P_.ld.h, v_ld)
                    if v_pe_prev is not None:
                        t.wait_ge(P_.sc.h, v_pe_prev)
                    for i in range(NDT):
                        ins = t.matmul(pa[:, 0:128],
                                       kst[:, 256 * i + 128 * ks:256 * i + 128 * (ks + 1)],
                                       qst[:, 256 * i + 128 * qs:256 * i + 128 * (qs + 1)],
                                       start=(i == 0), stop=(i == NDT - 1))
                    ins.then_inc(P_.pe.h, 1)
            v_smm = P_.pe.inc(1)
            with _blk(nc) as b:
                @b.scalar
                def _(s, pi_=pi_, v_smm=v_smm, pa=pa):
                    s.wait_ge(P_.pe.h, v_smm)
                    s.copy(ssb[:, 128 * pi_:128 * (pi_ + 1)], pa[:, 0:128]
                           ).then_inc(P_.sc.h, 1)
            ctx[("l0s_sev", n, pi_)] = P_.sc.inc(1)
        v_sev = P_.sc.n
        with _blk(nc) as b:
            @b.vector
            def _(v, v_sev=v_sev):
                v.wait_ge(P_.sc.h, v_sev)
                v.tensor_tensor(ssb[:, 0:128], ssb[:, 0:128], dmask[:, 0:128],
                                AL.mult).then_inc(P_.ve.h, 1)
                v.tensor_tensor(ssb[:, 256:384], ssb[:, 256:384], dmask[:, 0:128],
                                AL.mult).then_inc(P_.ve.h, 1)
        v_msk = P_.ve.inc(2)
        for qs in range(2):
            pa = pool[2 + qs]
            with _blk(nc) as b:
                @b.tensor
                def _(t, n=n, qs=qs, v_msk=v_msk, pa=pa):
                    t.wait_ge(P_.ve.h, v_msk)
                    if n > 0:
                        t.wait_ge(P_.sc.h, ctx[("l0s_kvb", n - 1)])
                    ops = []
                    for (ks2, qs2) in [(0, 0), (0, 1), (1, 1)]:
                        if qs2 != qs:
                            continue
                        pi_ = [(0, 0), (0, 1), (1, 1)].index((ks2, qs2))
                        ops.append((ssb[:, 128 * pi_:128 * (pi_ + 1)],
                                    vst[:, 256 * ks2:256 * (ks2 + 1)]))
                    if n > 0:
                        for i in range(NDT):
                            ops.append((qst[:, 256 * i + 128 * qs:256 * i + 128 * (qs + 1)],
                                        kvb[i][:, :]))
                    for j, (lhs_, rhs_) in enumerate(ops):
                        ins = t.matmul(pa[:, 0:256], lhs_, rhs_,
                                       start=(j == 0), stop=(j == len(ops) - 1))
                    ins.then_inc(P_.pe.h, 1)
            v_omm = P_.pe.inc(1)
            with _blk(nc) as b:
                @b.vector
                def _(v, n=n, qs=qs, v_omm=v_omm, pa=pa):
                    v.wait_ge(P_.pe.h, v_omm)
                    v.tensor_tensor(otmp[:, :], pa[:, 0:256],
                                    gst[:, 256 * qs:256 * (qs + 1)], AL.mult
                                    ).then_inc(P_.ve.h, 1)
            v_og = P_.ve.inc(1)
            with _blk(nc) as b:
                @b.tensor
                def _(t, v_og=v_og, qs=qs):
                    t.wait_ge(P_.ve.h, v_og)
                    k = ctx.get(("l0s_obst",))
                    if k is not None:
                        t.wait_ge(P_.sc.h, k)
                    t.transpose(pool[4][:, 0:128], otmp[:, 0:128], identf[:, :])
                    ins = t.transpose(pool[4][:, 128:256], otmp[:, 128:256], identf[:, :])
                    ins.then_inc(P_.pe.h, 1)
            v_tr = P_.pe.inc(1)
            with _blk(nc) as b:
                @b.scalar
                def _(s, v_tr=v_tr):
                    s.wait_ge(P_.pe.h, v_tr)
                    s.copy(obuf[:, 0:128], pool[4][:, 0:128]).then_inc(P_.sc.h, 1)
                    s.copy(obuf[:, 128:256], pool[4][:, 128:256]).then_inc(P_.sc.h, 1)
            v_ob = P_.sc.inc(2)
            ctx[("l0s_obst",)] = v_ob
            with _blk(nc) as b:
                @b.gpsimd
                def _(g, n=n, qs=qs, v_ob=v_ob):
                    g.wait_ge(P_.sc.h, v_ob)
                    col = 256 * n + 128 * qs
                    g.dma_start(out=ogl.ap()[0:128, col:col + 128], in_=obuf[:, 0:128]
                                ).then_inc(P_.gd.h, 16)
                    g.dma_start(out=ogl.ap()[128:256, col:col + 128], in_=obuf[:, 128:256]
                                ).then_inc(P_.gd.h, 16)
            ctx[("l0s_ogst", n, qs)] = P_.gd.inc(32)
        for i in range(NDT):
            pa = pool[5 + i % 2]
            with _blk(nc) as b:
                @b.tensor
                def _(t, n=n, i=i, pa=pa, v_ld=v_ld):
                    t.wait_ge(P_.ld.h, v_ld)
                    k = (ctx.get(("l0s_kvadd", n, i - 2)) or
                         ctx.get(("l0s_kvadd", n - 1, NDT + i - 2)))
                    if k is not None:
                        t.wait_ge(P_.ve.h, k)
                    for ts2 in range(2):
                        ins = t.matmul(pa[:, 0:256],
                                       ktdst[:, 2048 * ts2 + 128 * i:2048 * ts2 + 128 * (i + 1)],
                                       vst[:, 256 * ts2:256 * (ts2 + 1)],
                                       start=(ts2 == 0), stop=(ts2 == 1))
                    ins.then_inc(P_.pe.h, 1)
            v_kmm = P_.pe.inc(1)
            with _blk(nc) as b:
                @b.vector
                def _(v, n=n, i=i, v_kmm=v_kmm, pa=pa):
                    v.wait_ge(P_.pe.h, v_kmm)
                    v.tensor_add(kvf[i][:, :], kvf[i][:, :], pa[:, 0:256]
                                 ).then_inc(P_.ve.h, 1)
            ctx[("l0s_kvadd", n, i)] = P_.ve.inc(1)
            with _blk(nc) as b:
                @b.scalar
                def _(s, n=n, i=i):
                    s.wait_ge(P_.ve.h, ctx[("l0s_kvadd", n, i)])
                    s.copy(kvb[i][:, :], kvf[i][:, :]).then_inc(P_.sc.h, 1)
            ctx[("l0s_kvb", n)] = P_.sc.inc(1)
        ctx[("l0s_done", n)] = P_.pe.n
    v_og_all = P_.gd.n
    with _blk(nc) as b:
        @b.gpsimd
        def _(g):
            g.wait_ge(P_.gd.h, v_og_all)
            g.collective_compute("AllGather", AL.bypass, replica_groups=RG,
                                 ins=[ogl.ap()], outs=[ogall.ap()]).then_inc(P_.cc.h)
    v_ogag = P_.cc.inc(1)

    for tc in range(2):
        off = ctx["pid_off"][tc]
        with _blk(nc) as b:
            @b.sync
            def _(s, tc=tc, off=off):
                s.wait_ge(P_.cc.h, v_ogag)
                if tc == 1:
                    s.wait_ge(P_.pe.h, ctx[("l0a_wo", 0, NDT - 1)])
                s.dma_start(out=xstage[:, :].rearrange("p (i f) -> p i f", i=NDT),
                            in_=ogall.ap().rearrange("(i p) t -> p i t", p=128)
                            [:, :, bass.ts(off, 512)]).then_inc(P_.ld.h, 16)
        v_cs = P_.ld.inc(16)
        for oc in range(NDT):
            v_warw = ctx.get(("l0a_wol", tc, oc - 2))
            with _blk(nc) as b:
                @b.gpsimd
                def _(g, oc=oc, v_warw=v_warw):
                    if v_warw is not None:
                        g.wait_ge(P_.pe.h, v_warw)
                    g.dma_start(out=wst[oc % 2][:, :].rearrange("p (i f) -> p i f", i=NDT),
                                in_=wpT[:, 128 * oc:128 * (oc + 1)]
                                .rearrange("(i p) f -> p i f", p=128)).then_inc(P_.gd.h, 16)
            v_ldw = P_.gd.inc(16)
            pa = pool[oc % 2]
            with _blk(nc) as b:
                @b.tensor
                def _(t, tc=tc, oc=oc, v_ldw=v_ldw, pa=pa, v_cs=v_cs):
                    t.wait_ge(P_.gd.h, v_ldw)
                    t.wait_ge(P_.ld.h, v_cs)
                    if oc <= 1:
                        if tc == 1:
                            t.wait_ge(P_.ve.h, ctx[("l0a_woe", 0, NDT - 2 + oc)])
                    else:
                        t.wait_ge(P_.ve.h, ctx[("l0a_woe", tc, oc - 2)])
                    for i in range(NDT):
                        ins = t.matmul(pa[:, :], wst[oc % 2][:, 128 * i:128 * (i + 1)],
                                       xstage[:, 512 * i:512 * (i + 1)],
                                       start=(i == 0), stop=(i == NDT - 1))
                    ins.then_inc(P_.pe.h, 1)
            ctx[("l0a_wo", tc, oc)] = P_.pe.inc(1)
            ctx[("l0a_wol", tc, oc)] = P_.pe.n
            v_mm = P_.pe.n
            with _blk(nc) as b:
                @b.vector
                def _(v, tc=tc, oc=oc, v_mm=v_mm, pa=pa):
                    v.wait_ge(P_.pe.h, v_mm)
                    v.wait_ge(v_x[0].h, v_x[1])
                    v.tensor_add(xr[oc][:, 512 * tc:512 * (tc + 1)],
                                 xr[oc][:, 512 * tc:512 * (tc + 1)], pa[:, :]
                                 ).then_inc(P_.ve.h, 1)
            ctx[("l0a_woe", tc, oc)] = P_.ve.inc(1)
    return [(P_.ve, P_.ve.n)]




def build_l0_attn_prog():
    Pr = P()
    nc = Pr.nc
    xin = nc.dram_tensor("xin", [D, TL], F32, kind="ExternalInput").ap()
    wqT = nc.dram_tensor("l0a_wqT", [D, D], F32, kind="ExternalInput").ap()
    wkT = nc.dram_tensor("l0a_wkT", [D, D], F32, kind="ExternalInput").ap()
    wvs = nc.dram_tensor("l0a_wvs", [D, 256], F32, kind="ExternalInput").ap()
    wgs = nc.dram_tensor("l0a_wgs", [D, 256], F32, kind="ExternalInput").ap()
    wpT = nc.dram_tensor("l0a_wpT", [D, D], F32, kind="ExternalInput").ap()
    xout = nc.dram_tensor("xout", [D, TL], F32, kind="ExternalOutput").ap()
    Pr.make_pool()
    xr = [Pr.sb(f"xr{i}", [128, TL], F32) for i in range(NDT)]
    xnb = [Pr.sb(f"xnb{i}", [128, TL], BF16) for i in range(NDT)]
    ctx = {
        "xstage": Pr.sb("xstage", [128, 8192], BF16),
        "wst": [Pr.sb(f"wst{i}", [128, 2048], BF16) for i in range(2)],
        "wst2": [Pr.sb(f"wst2{i}", [128, 512], BF16) for i in range(2)],
        "wstb": Pr.sb("wstb", [128, 8192], BF16),
        "kvf": [Pr.sb(f"kvf{i}", [128, 256], F32) for i in range(NDT)],
        "kvb": [Pr.sb(f"kvb{i}", [128, 256], BF16) for i in range(NDT)],
        "qst": Pr.sb("qst", [128, 4096], BF16),
        "kst": Pr.sb("kst", [128, 4096], BF16),
        "ktdst": Pr.sb("ktdst", [128, 4096], BF16),
        "vst": Pr.sb("vst", [128, 512], BF16),
        "gst": Pr.sb("gst", [128, 512], BF16),
        "ssb": Pr.sb("ssb", [128, 512], BF16),
        "otmp": Pr.sb("otmp", [128, 256], F32),
        "obuf": Pr.sb("obuf", [128, 256], BF16),
    }
    ctx["identf"] = Pr.sconst("ident_f", np.eye(128, dtype=np.float32))
    ctx["dmask"] = Pr.sconst("l3a_dmask", _l3_dmask())
    Pr.ones_col_b(); Pr.ones_row_f()
    Pr.flush_consts()
    nctx = make_norm_ctx(Pr, sq_alias=[ctx["xstage"][:, 512 * i:512 * (i + 1)]
                                       for i in range(NDT)],
                         atmp_alias=None)
    with _blk(nc) as b:
        @b.sync
        def _(s):
            pid = nc.partition_id()
            ctx["pid_off"] = [nc.snap(pid * 2 + tc, min_val=0,
                                      max_val=2 * NCORES - 1) for tc in range(2)]
    v_x = load_residual(Pr, xr, xin)
    rfin = emit_norm(Pr, nctx, xr, xnb, None, None, subtract_mean=False,
                     eps=RMS_EPS, pre_waits=[(Pr.ld, v_x)], write_back=False)
    kv0 = Pr.ve.n
    with _blk(nc) as b:
        @b.vector
        def _(v):
            for i in range(NDT):
                v.memset(ctx["kvf"][i][:, :], 0.0)
            v.engine_nop().then_inc(Pr.ve.h, 1)
    Pr.ve.inc(1)
    fin = emit_l0_attn(Pr, ctx, xr, xnb, rfin, wqT, wkT, wvs, wgs, wpT,
                       (Pr.ld, v_x))
    fin2 = emit_norm(Pr, nctx, xr, xnb, None, None, subtract_mean=True,
                     pre_waits=fin)
    store_residual(Pr, xr, xout, [fin2])
    Pr.close_block()
    return Pr


def run_l0_attn(x, Wq, Wk, Wv, Wg, Wp):
    Pr = build_l0_attn_prog()
    xs = _shard_x(x)
    wqT = np.ascontiguousarray(Wq.T); wkT = np.ascontiguousarray(Wk.T)
    wvT = np.ascontiguousarray(Wv.T); wgT = np.ascontiguousarray(Wg.T)
    wpT = np.ascontiguousarray(Wp.T)
    in_maps = []
    for c in range(NCORES):
        in_maps.append({
            "xin": xs[c], "l0a_wqT": wqT, "l0a_wkT": wkT,
            "l0a_wvs": np.ascontiguousarray(wvT[:, 256 * c:256 * (c + 1)]),
            "l0a_wgs": np.ascontiguousarray(wgT[:, 256 * c:256 * (c + 1)]),
            "l0a_wpT": wpT,
        })
    res = run_bass_kernel_spmd(Pr.nc, in_maps, core_ids=list(range(NCORES)))
    return _unshard_x([res.results[c]["xout"] for c in range(NCORES)])





CL1 = 128
NCH1 = T // CL1


def _triu_incl():
    return np.triu(np.ones((128, 128), np.float32))


def _m_lt():
    return (-1e30 * np.tril(np.ones((128, 128), np.float32), -1)).astype(np.float32)


def _m_le():
    return (-1e30 * np.tril(np.ones((128, 128), np.float32), 0)).astype(np.float32)


def _m_ge():
    return (1e30 * np.triu(np.ones((128, 128), np.float32), 0)).astype(np.float32)


def emit_l1_attn(P_, ctx, xr, xnb, xnb_ready, wmix_ap, conv_ap, dtb_ap, alog_ap,
                 nw_ap, wout_ap, v_x):
    set_fast(False)
    nc = P_.nc
    pool = P_.pool
    xstage = ctx["xstage"]
    wst = ctx["wst"]
    msbq, msbk = ctx["msbq"], ctx["msbk"]
    convw, ctmp, carry = ctx["convw"], ctx["ctmp"], ctx["carry"]
    ssb = ctx["ssb"]
    identf, identb = ctx["identf"], ctx["identb"]
    triu, mlt, mle, mge = ctx["triu"], ctx["mlt"], ctx["mle"], ctx["mge"]
    ones_r = P_.ones_row_f()
    ones_cf = ctx["ones_col_f"]
    DKS = float(128 ** -0.5)
    W = ctx

    zl = nc.dram_tensor("l1a_zl", [T, 256], BF16)
    vl = nc.dram_tensor("l1a_vl", [256, T], BF16)
    bal = nc.dram_tensor("l1a_bal", [4, T], F32)
    ogl = nc.dram_tensor("l1a_ogl", [256, T], BF16)
    xlo = nc.dram_tensor("l1a_xl", [D, TL], BF16)
    xall = nc.dram_tensor("l1a_xall", [NCORES * D, TL], BF16, addr_space="Shared")
    ogall = nc.dram_tensor("l1a_ogall", [NCORES * 256, T], BF16, addr_space="Shared")

    def step(eng, fn, waits=()):
        sem = {"tensor": P_.pe, "vector": P_.ve, "scalar": P_.sc,
               "sync": P_.ld, "gpsimd": P_.gd}[eng]
        amt = 16 if eng in ("sync", "gpsimd") else 1
        with _blk(nc) as b:
            def body(e, fn=fn, waits=tuple(waits)):
                for (sm, w) in waits:
                    e.wait_ge(sm.h, w)
                fn(e).then_inc(sem.h, amt)
            getattr(b, eng)(body)
        return (sem, sem.inc(amt))

    with _blk(nc) as b:
        @b.gpsimd
        def _(g):
            g.wait_ge(xnb_ready[0].h, xnb_ready[1])
            for i in range(NDT):
                g.dma_start(out=xlo.ap()[128 * i:128 * (i + 1), :], in_=xnb[i][:, :]
                            ).then_inc(P_.gd.h, 16)
            g.dma_start(out=convw[:, :], in_=conv_ap).then_inc(P_.gd.h, 16)
            g.dma_start(out=W["dtbc"][0:4, :], in_=dtb_ap).then_inc(P_.gd.h, 16)
            g.dma_start(out=W["alog4"][0:1, :], in_=alog_ap).then_inc(P_.gd.h, 16)
            g.dma_start(out=W["nwrow"][0:1, :], in_=nw_ap).then_inc(P_.gd.h, 16)
            g.dma_start(out=ctx["wmx"][:, :].rearrange("p (i f) -> p i f", i=NDT),
                        in_=wmix_ap.rearrange("(i p) f -> p i f", p=128)
                        ).then_inc(P_.gd.h, 16)
    v_pre = P_.gd.inc(16 * (NDT + 5))
    with _blk(nc) as b:
        @b.gpsimd
        def _(g):
            g.wait_ge(P_.gd.h, v_pre)
            g.collective_compute("AllGather", AL.bypass, replica_groups=RG,
                                 ins=[xlo.ap()], outs=[xall.ap()]).then_inc(P_.cc.h)
    v_ag = P_.cc.inc(1)

    s_t1 = step("tensor", lambda e: e.transpose(pool[7][0:4, 0:1],
                                                W["alog4"][0:1, 0:4],
                                                identf[0:1, 0:1]),
                waits=[(P_.gd, v_pre), (P_.ld, P_.v_consts)])
    s_t2 = step("scalar", lambda e: e.activation(W["eAc"][0:4, 0:1], pool[7][0:4, 0:1],
                                                 AF.Exp), waits=[s_t1])
    s_t3 = step("vector", lambda e: e.tensor_scalar(W["eAc"][0:4, 0:1],
                                                    W["eAc"][0:4, 0:1], -1.0, None,
                                                    AL.mult), waits=[s_t2])
    s_nw1 = step("tensor", lambda e: e.matmul(pool[7][:, 0:128], ones_r[:, :],
                                              W["nwrow"][0:1, :], start=True,
                                              stop=True), waits=[s_t2])
    s_nw2 = step("scalar", lambda e: e.copy(W["nwmat"][:, :], pool[7][:, 0:128]),
                 waits=[s_nw1])

    for tg in range(16):
        r, tl0 = tg // 2, 512 * (tg % 2)
        v_war = ctx.get(("l1a_pj", tg - 1))
        ld = step("sync", lambda e, r=r, tl0=tl0: e.dma_start(
            out=xstage[:, :].rearrange("p (i f) -> p i f", i=NDT),
            in_=xall.ap()[r * D:(r + 1) * D, tl0:tl0 + 512]
            .rearrange("(i p) f -> p i f", p=128)),
            waits=[(P_.cc, v_ag)] + ([v_war] if v_war else []))

        def pj(t, ld=ld):
            for i in range(NDT):
                st, sp = (i == 0), (i == NDT - 1)
                for och in range(6):
                    t.matmul(pool[och][:, :],
                             ctx["wmx"][:, 772 * i + 128 * och:772 * i + 128 * (och + 1)],
                             xstage[:, 512 * i:512 * (i + 1)], start=st, stop=sp)
                ins = t.matmul(pool[6][0:4, :],
                               ctx["wmx"][:, 772 * i + 768:772 * i + 772],
                               xstage[:, 512 * i:512 * (i + 1)], start=st, stop=sp)
            return ins
        prev_ev = ctx.get(("l1a_pjdone", tg - 1))
        mm = step("tensor", pj, waits=[ld, s_nw2] + ([prev_ev] if prev_ev else []))
        ctx[("l1a_pj", tg)] = mm

        last = mm
        for j in range(4):
            pa = pool[j]
            s_cp = step("scalar", lambda e, pa=pa: e.copy(ctmp[:, 4:516], pa[:, :]),
                        waits=[mm, last])
            if tg == 0:
                s_ci = step("vector", lambda e: e.memset(ctmp[:, 0:4], 0.0),
                            waits=[s_cp])
            else:
                s_ci = step("vector", lambda e, j=j: e.tensor_copy(
                    ctmp[:, 0:4], carry[:, 4 * j:4 * (j + 1)]), waits=[s_cp])
            s_co = step("vector", lambda e, j=j: e.tensor_copy(
                carry[:, 4 * j:4 * (j + 1)], ctmp[:, 512:516]), waits=[s_ci])
            with _blk(nc) as b:
                @b.vector
                def _(v, j=j, s_co=s_co):
                    v.wait_ge(s_co[0].h, s_co[1])
                    b0 = P_.ve.n
                    v.tensor_scalar(W["cacc"][:, :], ctmp[:, 1:513],
                                    convw[:, 4 * j:4 * j + 1], None, AL.mult
                                    ).then_inc(P_.ve.h, 1)
                    for t_ in range(1, 4):
                        v.wait_ge(P_.ve.h, b0 + t_)
                        v.scalar_tensor_tensor(W["cacc"][:, :],
                                               ctmp[:, 1 + t_:513 + t_],
                                               convw[:, 4 * j + t_:4 * j + t_ + 1],
                                               W["cacc"][:, :], AL.mult, AL.add
                                               ).then_inc(P_.ve.h, 1)
            s_cv = (P_.ve, P_.ve.inc(4))

            def silu_out(e, j=j, tg=tg):
                if j == 0:
                    return e.activation(msbq[:, 512 * tg:512 * (tg + 1)],
                                        W["cacc"][:, :], AF.Silu)
                if j == 1:
                    return e.activation(msbk[:, 512 * tg:512 * (tg + 1)],
                                        W["cacc"][:, :], AF.Silu)
                return e.activation(ssb[:, 0:512], W["cacc"][:, :], AF.Silu)
            w_ = [s_cv]
            if j >= 2 and ctx.get(("l1a_vst",)):
                w_.append(ctx[("l1a_vst",)])
            s_so = step("scalar", silu_out, waits=w_)
            if j >= 2:
                s_vs = step("gpsimd", lambda e, j=j, tg=tg: e.dma_start(
                    out=vl.ap()[128 * (j - 2):128 * (j - 1), 512 * tg:512 * (tg + 1)],
                    in_=ssb[:, 0:512]), waits=[s_so])
                ctx[("l1a_vst",)] = s_vs
            last = s_so
        for j in range(2):
            pa = pool[4 + j]
            w_ = [mm]
            if ctx.get(("l1a_zst", j)):
                w_.append(ctx[("l1a_zst", j)])
            if ctx.get(("l1a_vst",)):
                w_.append(ctx[("l1a_vst",)])
            s_z = step("scalar", lambda e, pa=pa: e.activation(W["cacc"][:, :],
                                                               pa[:, :], AF.Silu),
                       waits=w_)

            def ztr(e):
                for u in range(4):
                    ins = e.transpose(pool[7][:, 128 * u:128 * (u + 1)],
                                      W["cacc"][:, 128 * u:128 * (u + 1)],
                                      identf[:, :])
                return ins
            s_zt = step("tensor", ztr, waits=[s_z])
            s_zc = step("scalar", lambda e: e.copy(W["ztmp"][:, :], pool[7][:, 0:512]),
                        waits=[s_zt])
            s_zs = step("gpsimd", lambda e, j=j, tg=tg: e.dma_start(
                out=zl.ap()[512 * tg:512 * (tg + 1), 128 * j:128 * (j + 1)]
                .rearrange("(u p) f -> p u f", p=128),
                in_=W["ztmp"][:, :].rearrange("p (u f) -> p u f", u=4)),
                waits=[s_zc])
            ctx[("l1a_zst", j)] = s_zs
        s_b1 = step("vector", lambda e: e.tensor_scalar(W["bae"][0:4, 0:512],
                                                        pool[6][0:4, :],
                                                        W["dtbc"][0:4, 0:1], None,
                                                        AL.add), waits=[mm, (P_.gd, v_pre)])
        s_b2 = step("scalar", lambda e: e.activation(W["baa"][0:4, 0:512],
                                                     W["bae"][0:4, 0:512], AF.Abs),
                    waits=[s_b1])
        s_b3 = step("scalar", lambda e: e.activation(W["baa"][0:4, 0:512],
                                                     W["baa"][0:4, 0:512], AF.Exp,
                                                     scale=-1.0), waits=[s_b2])
        s_b4 = step("scalar", lambda e: e.activation(W["baa"][0:4, 0:512],
                                                     W["baa"][0:4, 0:512], AF.Ln,
                                                     bias=1.0), waits=[s_b3])
        s_b5 = step("vector", lambda e: e.tensor_scalar(W["bae"][0:4, 0:512],
                                                        W["bae"][0:4, 0:512], 0.0,
                                                        None, AL.max), waits=[s_b4])
        s_b6 = step("vector", lambda e: e.tensor_add(W["baa"][0:4, 0:512],
                                                     W["baa"][0:4, 0:512],
                                                     W["bae"][0:4, 0:512]),
                    waits=[s_b5])
        s_b7 = step("vector", lambda e: e.tensor_scalar(W["baa"][0:4, 0:512],
                                                        W["baa"][0:4, 0:512],
                                                        W["eAc"][0:4, 0:1], None,
                                                        AL.mult), waits=[s_b6, s_t3])
        s_b8 = step("scalar", lambda e: e.activation(W["bae"][0:4, 0:512],
                                                     pool[6][0:4, :], AF.Sigmoid),
                    waits=[s_b7])
        s_b9 = step("gpsimd", lambda e, tg=tg: e.dma_start(
            out=bal.ap()[0:2, 512 * tg:512 * (tg + 1)], in_=W["bae"][0:2, 0:512]),
            waits=[s_b8])
        s_b10 = step("gpsimd", lambda e, tg=tg: e.dma_start(
            out=bal.ap()[2:4, 512 * tg:512 * (tg + 1)], in_=W["baa"][2:4, 0:512]),
            waits=[s_b9])
        ctx[("l1a_pjdone", tg)] = s_b10 if False else (P_.sc, P_.sc.n)
        ctx[("l1a_badone", tg)] = s_b10

    prev_q = ctx[("l1a_pjdone", 15)]
    for tg in range(16):
        sl = slice(512 * tg, 512 * (tg + 1))
        for (buf, scale) in ((msbq, DKS), (msbk, 1.0)):
            s_q1 = step("scalar", lambda e, buf=buf, sl=sl: e.activation(
                ssb[:, 0:512], buf[:, sl], AF.Square),
                waits=[prev_q])
            s_q2 = step("tensor", lambda e: e.matmul(pool[6][0:1, 0:512],
                                                     ctx["ones_col_b2"][:, :],
                                                     ssb[:, 0:512], start=True,
                                                     stop=True), waits=[s_q1])
            s_q3 = step("vector", lambda e: e.tensor_scalar(W["qrow"][0:1, 0:512],
                                                            pool[6][0:1, 0:512], 1.0,
                                                            1e-6, AL.mult, AL.add),
                        waits=[s_q2])
            s_q4 = step("scalar", lambda e: e.activation(W["qrow"][0:1, 0:512],
                                                         W["qrow"][0:1, 0:512], AF.Ln),
                        waits=[s_q3])
            s_q5 = step("scalar", lambda e, scale=scale: e.activation(
                W["qrow"][0:1, 0:512], W["qrow"][0:1, 0:512], AF.Exp, scale=-0.5),
                waits=[s_q4])
            if scale != 1.0:
                s_q5 = step("scalar", lambda e, scale=scale: e.mul(
                    W["qrow"][0:1, 0:512], W["qrow"][0:1, 0:512], scale), waits=[s_q5])
            s_q6 = step("tensor", lambda e: e.matmul(pool[6][:, 0:512], ones_r[:, :],
                                                     W["qrow"][0:1, 0:512],
                                                     start=True, stop=True),
                        waits=[s_q5])
            s_q7 = step("vector", lambda e, buf=buf, sl=sl: e.tensor_tensor(
                buf[:, sl], buf[:, sl], pool[6][:, 0:512], AL.mult), waits=[s_q6])
            prev_q = s_q7
    v_qk_done = (P_.ve, P_.ve.n)

    set_fast(2 in FAST_BLOCKS)
    Sb = ctx["Sb"]
    s_init = step("vector", lambda e: (e.memset(Sb[0][:, :], 0.0),
                                       e.memset(Sb[1][:, :], 0.0))[-1], waits=[])
    for n in range(NCH1):
        c0 = CL1 * n
        csl = slice(c0, c0 + CL1)
        wld = [(P_.cc, v_ag)]
        wld.append(ctx[("l1a_badone", n // 4)])
        k = ctx.get(("l1a_chdone", n - 1))
        if k is not None:
            wld.append(k)
        ld1 = step("sync", lambda e, c0=c0: e.dma_start(
            out=W["bcols"][:, 0:4],
            in_=bal.ap()[0:4, c0:c0 + 128].rearrange("a f -> f a"),
            allow_slow_non_contiguous=True), waits=wld)
        ld2 = ld1
        ld3 = step("sync", lambda e, c0=c0: e.dma_start(
            out=W["vch"][:, 0:256].rearrange("p (u f) -> p u f", u=2),
            in_=vl.ap()[0:256, c0:c0 + 128].rearrange("(u p) f -> p u f", p=128)),
            waits=[ld2])
        ld4 = step("sync", lambda e, c0=c0: e.dma_start(
            out=W["zch"][:, 0:256], in_=zl.ap()[c0:c0 + 128, :]), waits=[ld3])
        s_c1 = step("tensor", lambda e: e.matmul(pool[6][:, 0:2], triu[:, :],
                                                 W["bcols"][:, 2:4], start=True,
                                                 stop=True),
                    waits=[ld1, v_qk_done, (P_.ld, P_.v_consts)])
        s_c2 = step("scalar", lambda e: e.copy(W["cumc"][:, 0:2], pool[6][:, 0:2]),
                    waits=[s_c1])
        s_c3 = step("tensor", lambda e: e.matmul(pool[6][0:1, 2:4], ones_cf[:, :],
                                                 W["bcols"][:, 2:4], start=True,
                                                 stop=True), waits=[s_c2])
        s_c3b = step("scalar", lambda e: e.copy(W["sgr"][0:1, 0:2],
                                                pool[6][0:1, 2:4]), waits=[s_c3])
        s_c4 = step("scalar", lambda e: e.activation(W["eACr"][0:1, 0:2],
                                                     pool[6][0:1, 2:4], AF.Exp),
                    waits=[s_c3b])
        for h in range(2):
            hb = 2 * n + h
            s_h1 = step("tensor", lambda e, h=h: e.transpose(
                pool[6][0:1, 4:132], W["cumc"][:, h:h + 1], identf[:, :]),
                waits=[s_c4, ctx.get(("l1a_hdone", hb - 1), s_c4)])
            s_h2 = step("scalar", lambda e: e.copy(W["cumr"][0:1, 0:128],
                                                   pool[6][0:1, 4:132]), waits=[s_h1])
            s_h3 = step("tensor", lambda e: e.matmul(pool[5][:, 0:128], ones_r[:, :],
                                                     W["cumr"][0:1, 0:128],
                                                     start=True, stop=True),
                        waits=[s_h2])
            s_h4 = step("vector", lambda e, h=h: e.tensor_scalar(
                W["Dt"][:, :], pool[5][:, 0:128], W["cumc"][:, h:h + 1], None,
                AL.subtract), waits=[s_h3])
            s_h5 = step("scalar", lambda e: e.activation(W["ecumR"][:, :],
                                                         pool[5][:, 0:128], AF.Exp),
                        waits=[s_h4])
            s_h6 = step("vector", lambda e: e.tensor_add(W["tA"][:, :], W["Dt"][:, :],
                                                         mlt[:, :]), waits=[s_h5])
            s_h7 = step("scalar", lambda e: e.activation(W["EmDT"][:, :], W["tA"][:, :],
                                                         AF.Exp), waits=[s_h6])
            s_h8 = step("vector", lambda e: e.tensor_add(W["tA"][:, :], W["Dt"][:, :],
                                                         mle[:, :]), waits=[s_h7])
            s_h9 = step("scalar", lambda e: e.activation(W["EkDT"][:, :], W["tA"][:, :],
                                                         AF.Exp), waits=[s_h8])
            s_h10 = step("vector", lambda e: e.tensor_add(W["tA"][:, :], W["Dt"][:, :],
                                                          mge[:, :]), waits=[s_h9])
            s_h11 = step("scalar", lambda e: e.activation(W["EkD"][:, :], W["tA"][:, :],
                                                          AF.Exp, scale=-1.0),
                         waits=[s_h10])
            s_h12 = step("tensor", lambda e, csl=csl: e.matmul(
                pool[5][:, 0:128], msbk[:, csl], msbk[:, csl], start=True, stop=True),
                waits=[s_h11])
            s_h13 = step("scalar", lambda e: e.copy(W["kkf"][:, :], pool[5][:, 0:128]),
                         waits=[s_h12])
            s_h14 = step("tensor", lambda e, csl=csl: e.matmul(
                pool[5][:, 0:128], msbk[:, csl], msbq[:, csl], start=True, stop=True),
                waits=[s_h13])
            s_h15 = step("scalar", lambda e: e.copy(W["qkf"][:, :], pool[5][:, 0:128]),
                         waits=[s_h14])
            s_h16 = step("tensor", lambda e, h=h: e.transpose(
                pool[6][0:1, 4:132], W["bcols"][:, h:h + 1], identf[:, :]),
                waits=[s_h15])
            s_h17 = step("scalar", lambda e: e.copy(W["cumr"][0:1, 0:128],
                                                    pool[6][0:1, 4:132]),
                         waits=[s_h16])
            s_h18 = step("tensor", lambda e: e.matmul(pool[5][:, 0:128], ones_r[:, :],
                                                      W["cumr"][0:1, 0:128],
                                                      start=True, stop=True),
                        waits=[s_h17])
            s_h19 = step("scalar", lambda e: e.copy(W["bmat"][:, :], pool[5][:, 0:128]),
                         waits=[s_h18])
            s_h20 = step("vector", lambda e: e.tensor_tensor(W["MT"][:, :],
                                                             W["qkf"][:, :],
                                                             W["EmDT"][:, :],
                                                             AL.mult), waits=[s_h19])
            s_h21 = step("vector", lambda e: e.tensor_tensor(W["tA"][:, :],
                                                             W["kkf"][:, :],
                                                             W["EkDT"][:, :],
                                                             AL.mult), waits=[s_h20])
            s_h22 = step("vector", lambda e: e.scalar_tensor_tensor(
                W["NT"][:, :], W["tA"][:, :], -1.0, W["bmat"][:, :], AL.mult,
                AL.mult), waits=[s_h21])
            s_h23 = step("vector", lambda e: e.tensor_tensor(W["tA"][:, :],
                                                             W["kkf"][:, :],
                                                             W["EkD"][:, :], AL.mult),
                         waits=[s_h22])
            s_h24 = step("vector", lambda e, h=h: e.tensor_scalar(
                W["NTT"][:, :], W["tA"][:, :], W["bcols"][:, h:h + 1], -1.0,
                AL.mult, AL.mult), waits=[s_h23])
            s_h25 = step("vector", lambda e: e.tensor_add(W["PT"][:, :], identb[:, :],
                                                          W["NT"][:, :]), waits=[s_h24])
            last = s_h25
            for kk_ in range(1, 7):
                sA = step("tensor", lambda e: e.matmul(pool[5][:, 0:128],
                                                       W["NTT"][:, :], W["NT"][:, :],
                                                       start=True, stop=True),
                          waits=[last])
                sB = step("scalar", lambda e: e.copy(W["NT2"][:, :], pool[5][:, 0:128]),
                          waits=[sA])
                sC = step("tensor", lambda e: e.matmul(pool[5][:, 0:128],
                                                       W["NT"][:, :], W["NTT"][:, :],
                                                       start=True, stop=True),
                          waits=[sB])
                sD = step("scalar", lambda e: e.copy(W["NTT"][:, :],
                                                     pool[5][:, 0:128]), waits=[sC])
                def pupd(e):
                    e.matmul(pool[5][:, 0:128], identb[:, :], W["PT"][:, :],
                             start=True, stop=False)
                    return e.matmul(pool[5][:, 0:128], W["NTT"][:, :], W["PT"][:, :],
                                    start=False, stop=True)
                sE = step("tensor", pupd, waits=[sD])
                sF = step("scalar", lambda e: e.copy(W["PT"][:, :], pool[5][:, 0:128]),
                          waits=[sE])
                sG = step("vector", lambda e: e.tensor_copy(W["NT"][:, :],
                                                            W["NT2"][:, :]), waits=[sF])
                last = sG
            s_h26a = step("scalar", lambda e, csl=csl: e.copy(W["tA"][:, :],
                                                              msbk[:, csl]),
                          waits=[last])
            s_h26 = step("tensor", lambda e: e.transpose(
                pool[5][:, 0:128], W["tA"][:, :], identf[:, :]), waits=[s_h26a])
            s_h27 = step("scalar", lambda e: e.copy(W["kT"][:, :], pool[5][:, 0:128]),
                         waits=[s_h26])
            s_h28a = step("scalar", lambda e, h=h: e.copy(
                W["tA"][:, :], W["vch"][:, 128 * h:128 * (h + 1)]), waits=[s_h27])
            s_h28 = step("tensor", lambda e: e.transpose(
                pool[5][:, 0:128], W["tA"][:, :], identf[:, :]), waits=[s_h28a])
            s_h29 = step("scalar", lambda e: e.copy(W["vT"][:, :], pool[5][:, 0:128]),
                         waits=[s_h28])
            s_h30 = step("scalar", lambda e, h=h: e.activation(
                W["ecumc"][:, 0:1], W["cumc"][:, h:h + 1], AF.Exp), waits=[s_h29])
            s_h31 = step("vector", lambda e, h=h: e.tensor_tensor(
                W["beE"][:, 0:1], W["bcols"][:, h:h + 1], W["ecumc"][:, 0:1],
                AL.mult), waits=[s_h30])
            s_h34 = step("tensor", lambda e, h=h: e.matmul(
                pool[6][:, 4:5], ones_r[:, :], W["sgr"][0:1, h:h + 1], start=True,
                stop=True), waits=[s_h31])
            s_h34b = step("scalar", lambda e: e.copy(W["eACc"][:, 0:1],
                                                     pool[6][:, 4:5]), waits=[s_h34])
            s_h35a = step("vector", lambda e, h=h: e.tensor_tensor(
                W["ecr"][:, 0:1], W["eACc"][:, 0:1], W["cumc"][:, h:h + 1],
                AL.subtract), waits=[s_h34b])
            s_h35b = step("scalar", lambda e: e.activation(W["ecr"][:, 0:1],
                                                           W["ecr"][:, 0:1], AF.Exp),
                          waits=[s_h35a])
            s_h35 = step("scalar", lambda e: e.activation(W["eACc"][:, 0:1],
                                                          W["eACc"][:, 0:1], AF.Exp),
                         waits=[s_h35b])
            s_h36 = step("vector", lambda e: e.tensor_scalar(
                W["Wb"][:, :], W["kT"][:, :], W["beE"][:, 0:1], None, AL.mult),
                waits=[s_h35])
            s_h37 = step("vector", lambda e, h=h: e.tensor_scalar(
                W["Bv"][:, :], W["vT"][:, :], W["bcols"][:, h:h + 1], None, AL.mult),
                waits=[s_h36])
            s_h38 = step("vector", lambda e: e.tensor_scalar(
                W["Kp"][:, :], W["kT"][:, :], W["ecr"][:, 0:1], None, AL.mult),
                waits=[s_h37])
            s_h39 = step("tensor", lambda e: e.matmul(pool[5][:, 0:128], W["PT"][:, :],
                                                      W["Wb"][:, :], start=True,
                                                      stop=True), waits=[s_h38])
            s_h40 = step("scalar", lambda e: e.copy(W["Wp"][:, :], pool[5][:, 0:128]),
                         waits=[s_h39])
            s_h41 = step("tensor", lambda e: e.matmul(pool[5][:, 0:128], W["PT"][:, :],
                                                      W["Bv"][:, :], start=True,
                                                      stop=True), waits=[s_h40])
            s_h42 = step("scalar", lambda e: e.copy(W["Ub"][:, :], pool[5][:, 0:128]),
                         waits=[s_h41])
            s_h43 = step("vector", lambda e, csl=csl: e.tensor_tensor(
                W["QtT"][:, :], msbq[:, csl], W["ecumR"][:, :], AL.mult),
                waits=[s_h42])
            s_h44 = step("tensor", lambda e: e.matmul(pool[5][:, 0:128], W["Wp"][:, :],
                                                      W["MT"][:, :], start=True,
                                                      stop=True), waits=[s_h43])
            s_h45 = step("vector", lambda e: e.tensor_sub(W["QeT"][:, :],
                                                          W["QtT"][:, :],
                                                          pool[5][:, 0:128]),
                         waits=[s_h44])
            def omm(e, n=n, h=h):
                if n > 0:
                    e.matmul(pool[4][:, 0:128], W["QeT"][:, :], Sb[h][:, :],
                             start=True, stop=False)
                    return e.matmul(pool[4][:, 0:128], W["MT"][:, :], W["Ub"][:, :],
                                    start=False, stop=True)
                return e.matmul(pool[4][:, 0:128], W["MT"][:, :], W["Ub"][:, :],
                                start=True, stop=True)
            s_h46 = step("tensor", omm, waits=[s_h45, s_init])
            s_h47 = step("tensor", lambda e: e.matmul(pool[5][:, 0:128], W["Wp"][:, :],
                                                      W["Kp"][:, :], start=True,
                                                      stop=True), waits=[s_h46])
            s_h48 = step("scalar", lambda e: e.activation(W["nG2T"][:, :],
                                                          pool[5][:, 0:128], AF.Copy,
                                                          scale=-1.0), waits=[s_h47])
            s_h49 = step("vector", lambda e: e.tensor_scalar(
                W["IA"][:, :], identb[:, :], W["eACc"][:, 0:1], None, AL.mult),
                waits=[s_h48])
            def smm(e, n=n, h=h):
                if n > 0:
                    e.matmul(pool[5][:, 0:128], W["IA"][:, :], Sb[h][:, :],
                             start=True, stop=False)
                    e.matmul(pool[5][:, 0:128], W["nG2T"][:, :], Sb[h][:, :],
                             start=False, stop=False)
                    return e.matmul(pool[5][:, 0:128], W["Kp"][:, :], W["Ub"][:, :],
                                    start=False, stop=True)
                return e.matmul(pool[5][:, 0:128], W["Kp"][:, :], W["Ub"][:, :],
                                start=True, stop=True)
            s_h50 = step("tensor", smm, waits=[s_h49])
            s_h51 = step("scalar", lambda e, h=h: e.copy(Sb[h][:, :],
                                                         pool[5][:, 0:128]),
                         waits=[s_h50])
            s_h52 = step("scalar", lambda e: e.copy(W["och"][:, :], pool[4][:, 0:128]),
                         waits=[s_h46, s_h51])
            s_h53 = step("scalar", lambda e: e.activation(W["osq"][:, :],
                                                          W["och"][:, :], AF.Square),
                         waits=[s_h52])
            s_h54 = step("vector", lambda e: e.tensor_reduce(
                W["ovar"][:, 0:1], W["osq"][:, :], AX.X, AL.add), waits=[s_h53])
            s_h55 = step("vector", lambda e: e.tensor_scalar(
                W["ovar"][:, 0:1], W["ovar"][:, 0:1], 1.0 / 128, 1e-6, AL.mult,
                AL.add), waits=[s_h54])
            s_h56 = step("scalar", lambda e: e.activation(W["ovar"][:, 0:1],
                                                          W["ovar"][:, 0:1], AF.Ln),
                         waits=[s_h55])
            s_h57 = step("scalar", lambda e: e.activation(W["ovar"][:, 0:1],
                                                          W["ovar"][:, 0:1], AF.Exp,
                                                          scale=-0.5), waits=[s_h56])
            s_h58 = step("vector", lambda e: e.tensor_scalar(
                W["och"][:, :], W["och"][:, :], W["ovar"][:, 0:1], None, AL.mult),
                waits=[s_h57])
            s_h59 = step("vector", lambda e: e.tensor_tensor(
                W["och"][:, :], W["och"][:, :], W["nwmat"][:, :], AL.mult),
                waits=[s_h58])
            s_h60 = step("vector", lambda e, h=h: e.tensor_tensor(
                W["och"][:, :], W["och"][:, :], W["zch"][:, 128 * h:128 * (h + 1)],
                AL.mult), waits=[s_h59])
            s_h61 = step("tensor", lambda e: e.transpose(pool[5][:, 0:128],
                                                         W["och"][:, :], identf[:, :]),
                         waits=[s_h60])
            s_h62 = step("scalar", lambda e: e.copy(W["obuf"][:, 0:128],
                                                    pool[5][:, 0:128]), waits=[s_h61])
            s_h63 = step("gpsimd", lambda e, n=n, h=h: e.dma_start(
                out=ogl.ap()[128 * h:128 * (h + 1), CL1 * n:CL1 * (n + 1)],
                in_=W["obuf"][:, 0:128]), waits=[s_h62])
            ctx[("l1a_hdone", hb)] = s_h63
        ctx[("l1a_chdone", n)] = ctx[("l1a_hdone", 2 * n + 1)]

    v_last = ctx[("l1a_chdone", NCH1 - 1)]
    with _blk(nc) as b:
        @b.gpsimd
        def _(g):
            g.wait_ge(v_last[0].h, v_last[1])
            g.collective_compute("AllGather", AL.bypass, replica_groups=RG,
                                 ins=[ogl.ap()], outs=[ogall.ap()]).then_inc(P_.cc.h)
    v_ogag = P_.cc.inc(1)

    for tc in range(2):
        off = ctx["pid_off"][tc]
        with _blk(nc) as b:
            @b.sync
            def _(s, tc=tc, off=off):
                s.wait_ge(P_.cc.h, v_ogag)
                if tc == 1:
                    s.wait_ge(P_.pe.h, ctx[("l1a_wo", 0, NDT - 1)])
                s.dma_start(out=xstage[:, :].rearrange("p (i f) -> p i f", i=NDT),
                            in_=ogall.ap().rearrange("(i p) t -> p i t", p=128)
                            [:, :, bass.ts(off, 512)]).then_inc(P_.ld.h, 16)
        v_cs = P_.ld.inc(16)
        for oc in range(NDT):
            if oc >= 2:
                v_warw = ctx[("l1a_wol", tc, oc - 2)]
            elif tc == 1:
                v_warw = ctx[("l1a_wol", 0, NDT - 2 + oc)]
            else:
                v_warw = None
            with _blk(nc) as b:
                @b.gpsimd
                def _(g, oc=oc, v_warw=v_warw):
                    if v_warw is not None:
                        g.wait_ge(P_.pe.h, v_warw)
                    g.dma_start(out=wst[oc % 2][:, :].rearrange("p (i f) -> p i f", i=NDT),
                                in_=wout_ap[:, 128 * oc:128 * (oc + 1)]
                                .rearrange("(i p) f -> p i f", p=128)).then_inc(P_.gd.h, 16)
            v_ldw = P_.gd.inc(16)
            pa = pool[oc % 2]
            with _blk(nc) as b:
                @b.tensor
                def _(t, tc=tc, oc=oc, v_ldw=v_ldw, pa=pa, v_cs=v_cs):
                    t.wait_ge(P_.gd.h, v_ldw)
                    t.wait_ge(P_.ld.h, v_cs)
                    if oc <= 1:
                        if tc == 1:
                            t.wait_ge(P_.ve.h, ctx[("l1a_woe", 0, NDT - 2 + oc)])
                    else:
                        t.wait_ge(P_.ve.h, ctx[("l1a_woe", tc, oc - 2)])
                    for i in range(NDT):
                        ins = t.matmul(pa[:, :], wst[oc % 2][:, 128 * i:128 * (i + 1)],
                                       xstage[:, 512 * i:512 * (i + 1)],
                                       start=(i == 0), stop=(i == NDT - 1))
                    ins.then_inc(P_.pe.h, 1)
            ctx[("l1a_wo", tc, oc)] = P_.pe.inc(1)
            ctx[("l1a_wol", tc, oc)] = P_.pe.n
            v_mm = P_.pe.n
            with _blk(nc) as b:
                @b.vector
                def _(v, tc=tc, oc=oc, v_mm=v_mm, pa=pa):
                    v.wait_ge(P_.pe.h, v_mm)
                    v.wait_ge(v_x[0].h, v_x[1])
                    v.tensor_add(xr[oc][:, 512 * tc:512 * (tc + 1)],
                                 xr[oc][:, 512 * tc:512 * (tc + 1)], pa[:, :]
                                 ).then_inc(P_.ve.h, 1)
            ctx[("l1a_woe", tc, oc)] = P_.ve.inc(1)
    return [(P_.ve, P_.ve.n)]




def build_l1_attn_prog():
    Pr = P()
    nc = Pr.nc
    xin = nc.dram_tensor("xin", [D, TL], F32, kind="ExternalInput").ap()
    wmix = nc.dram_tensor("l1a_wmix", [D, 772], F32, kind="ExternalInput").ap()
    conv = nc.dram_tensor("l1a_conv", [128, 16], F32, kind="ExternalInput").ap()
    dtb = nc.dram_tensor("l1a_dtb", [4, 1], F32, kind="ExternalInput").ap()
    alog4 = nc.dram_tensor("l1a_alog4", [1, 4], F32, kind="ExternalInput").ap()
    nw = nc.dram_tensor("l1a_nw", [1, 128], F32, kind="ExternalInput").ap()
    wout = nc.dram_tensor("l1a_wout", [D, D], F32, kind="ExternalInput").ap()
    xout = nc.dram_tensor("xout", [D, TL], F32, kind="ExternalOutput").ap()
    Pr.make_pool()
    xr = [Pr.sb(f"xr{i}", [128, TL], F32) for i in range(NDT)]
    xnb = [Pr.sb(f"xnb{i}", [128, TL], BF16) for i in range(NDT)]
    ctx = {
        "xstage": Pr.sb("xstage", [128, 8192], BF16),
        "wst": [Pr.sb(f"wst{i}", [128, 2048], BF16) for i in range(2)],
        "wmx": Pr.sb("wmx", [128, 772 * NDT], BF16),
        "msbq": Pr.sb("msbq", [128, T], BF16),
        "msbk": Pr.sb("msbk", [128, T], BF16),
        "convw": Pr.sb("convw", [128, 16], F32),
        "ctmp": Pr.sb("ctmp", [128, 516], F32),
        "carry": Pr.sb("carry", [128, 16], F32),
        "ssb": Pr.sb("ssb", [128, 512], BF16),
        "cacc": Pr.sb("cacc", [128, 512], F32),
        "ztmp": Pr.sb("ztmp", [128, 512], BF16),
        "bae": Pr.sb("bae", [4, 512], F32),
        "baa": Pr.sb("baa", [4, 512], F32),
        "dtbc": Pr.sb("dtbc", [4, 1], F32),
        "alog4": Pr.sb("alog4s", [1, 4], F32),
        "eAc": Pr.sb("eAc", [4, 1], F32),
        "eACr": Pr.sb("eACr", [1, 2], F32),
        "eACc": Pr.sb("eACc", [128, 1], F32),
        "sgr": Pr.sb("sgr", [1, 2], F32),
        "nwrow": Pr.sb("nwrow", [1, 128], F32),
        "nwmat": Pr.sb("nwmat", [128, 128], BF16),
        "bcols": Pr.sb("bcols", [128, 4], F32),
        "vch": Pr.sb("vch", [128, 256], BF16),
        "zch": Pr.sb("zch", [128, 256], BF16),
        "cumc": Pr.sb("cumc", [128, 2], F32),
        "cumr": Pr.sb("cumr", [1, 128], F32),
        "Dt": Pr.sb("Dt", [128, 128], F32),
        "tA": Pr.sb("tA", [128, 128], F32),
        "ecumR": Pr.sb("ecumR", [128, 128], BF16),
        "EmDT": Pr.sb("EmDT", [128, 128], BF16),
        "EkDT": Pr.sb("EkDT", [128, 128], BF16),
        "EkD": Pr.sb("EkD", [128, 128], BF16),
        "kkf": Pr.sb("kkf", [128, 128], F32),
        "qkf": Pr.sb("qkf", [128, 128], F32),
        "bmat": Pr.sb("bmat", [128, 128], F32),
        "MT": Pr.sb("MT", [128, 128], BF16),
        "NT": Pr.sb("NT", [128, 128], BF16),
        "NTT": Pr.sb("NTT", [128, 128], BF16),
        "NT2": Pr.sb("NT2", [128, 128], BF16),
        "PT": Pr.sb("PT", [128, 128], BF16),
        "kT": Pr.sb("kT", [128, 128], BF16),
        "vT": Pr.sb("vT", [128, 128], BF16),
        "ecumc": Pr.sb("ecumc", [128, 1], F32),
        "beE": Pr.sb("beE", [128, 1], F32),
        "ecr": Pr.sb("ecr", [128, 1], F32),
        "Wb": Pr.sb("Wb", [128, 128], BF16),
        "Bv": Pr.sb("Bv", [128, 128], BF16),
        "Kp": Pr.sb("Kp", [128, 128], BF16),
        "Wp": Pr.sb("Wp", [128, 128], BF16),
        "Ub": Pr.sb("Ub", [128, 128], BF16),
        "QtT": Pr.sb("QtT", [128, 128], BF16),
        "QeT": Pr.sb("QeT", [128, 128], BF16),
        "nG2T": Pr.sb("nG2T", [128, 128], BF16),
        "IA": Pr.sb("IA", [128, 128], BF16),
        "Sb": [Pr.sb(f"Sb{h}", [128, 128], BF16) for h in range(2)],
        "och": Pr.sb("och", [128, 128], F32),
        "osq": Pr.sb("osq", [128, 128], BF16),
        "ovar": Pr.sb("ovar", [128, 1], F32),
        "obuf": Pr.sb("obuf", [128, 128], BF16),
        "qrow": Pr.sb("qrow", [1, 512], F32),
    }
    ctx["identf"] = Pr.sconst("ident_f", np.eye(128, dtype=np.float32))
    ctx["identb"] = Pr.sconst("ident_b", np.eye(128, dtype=bf16dt()))
    ctx["triu"] = Pr.sconst("l1a_triu", _triu_incl())
    ctx["mlt"] = Pr.sconst("l1a_mlt", _m_lt())
    ctx["mle"] = Pr.sconst("l1a_mle", _m_le())
    ctx["mge"] = Pr.sconst("l1a_mge", _m_ge())
    ctx["ones_col_f"] = Pr.sconst("ones_col_f", np.ones((128, 1), np.float32))
    ctx["ones_col_b2"] = Pr.ones_col_b()
    Pr.ones_row_f()
    Pr.flush_consts()
    nctx = make_norm_ctx(Pr, sq_alias=[ctx["xstage"][:, 512 * i:512 * (i + 1)]
                                       for i in range(NDT)],
                         atmp_alias=ctx["cacc"],
                         rows_alias=[ctx["baa"][0:1, 0:512], ctx["bae"][0:1, 0:512],
                                     ctx["ctmp"][0:1, 0:512], ctx["qrow"][0:1, 0:512],
                                     None])
    with _blk(nc) as b:
        @b.sync
        def _(s):
            pid = nc.partition_id()
            ctx["pid_off"] = [nc.snap(pid * 2 + tc, min_val=0,
                                      max_val=2 * NCORES - 1) for tc in range(2)]
    v_x = load_residual(Pr, xr, xin)
    with _blk(nc) as b:
        @b.scalar
        def _(s):
            s.wait_ge(Pr.ld.h, v_x)
            for i in range(NDT):
                s.copy(xnb[i][:, :], xr[i][:, :]).then_inc(Pr.sc.h, 1)
    xnb_ready = (Pr.sc, Pr.sc.inc(NDT))
    fin = emit_l1_attn(Pr, ctx, xr, xnb, xnb_ready, wmix, conv, dtb, alog4, nw,
                       wout, (Pr.ld, v_x))
    fin2 = emit_norm(Pr, nctx, xr, xnb, None, None, subtract_mean=True,
                     pre_waits=fin)
    store_residual(Pr, xr, xout, [fin2])
    Pr.close_block()
    return Pr


def run_l1_attn(x, W_qkv, W_z, W_b, W_a, conv_w, dt_bias, A_log, norm_weight, W_out):
    Pr = build_l1_attn_prog()
    xs = _shard_x(x)
    in_maps = []
    for c in range(NCORES):
        qcol = W_qkv[:, 128 * c:128 * (c + 1)]
        kcol = W_qkv[:, 1024 + 128 * c:1024 + 128 * (c + 1)]
        vcol = W_qkv[:, 2048 + 256 * c:2048 + 256 * (c + 1)]
        zcol = W_z[:, 256 * c:256 * (c + 1)]
        bcol = W_b[:, 2 * c:2 * (c + 1)]
        acol = W_a[:, 2 * c:2 * (c + 1)]
        wmix = np.ascontiguousarray(np.concatenate(
            [qcol, kcol, vcol, zcol, bcol, acol], axis=1))
        chans = np.concatenate([np.arange(128 * c, 128 * (c + 1)),
                                np.arange(1024 + 128 * c, 1024 + 128 * (c + 1)),
                                np.arange(2048 + 256 * c, 2048 + 256 * (c + 1))])
        cw = conv_w[chans]
        cl = np.zeros((128, 16), np.float32)
        for j in range(4):
            cl[:, 4 * j:4 * (j + 1)] = cw[128 * j:128 * (j + 1)]
        dtb4 = np.zeros((4, 1), np.float32)
        dtb4[2, 0] = dt_bias[2 * c]; dtb4[3, 0] = dt_bias[2 * c + 1]
        al4 = np.zeros((1, 4), np.float32)
        al4[0, 2] = A_log[2 * c]; al4[0, 3] = A_log[2 * c + 1]
        in_maps.append({
            "xin": xs[c], "l1a_wmix": wmix, "l1a_conv": cl,
            "l1a_dtb": dtb4, "l1a_alog4": al4,
            "l1a_nw": np.ascontiguousarray(norm_weight.reshape(1, 128)),
            "l1a_wout": W_out,
        })
    res = run_bass_kernel_spmd(Pr.nc, in_maps, core_ids=list(range(NCORES)))
    return _unshard_x([res.results[c]["xout"] for c in range(NCORES)])



def _shard_x(x):
    xs = x.reshape(T, D)
    return [np.ascontiguousarray(xs[TL * c:TL * (c + 1), :].T) for c in range(NCORES)]


def _unshard_x(parts):
    return np.concatenate([p.T for p in parts], axis=0).reshape(1, T, D)


def run_l2_ffn(x, weight, bias, skip_norm=False):
    Pr = build_l2_ffn_prog(skip_norm)
    xs = _shard_x(x)
    wT = np.ascontiguousarray(weight.T)
    bias_l = np.ascontiguousarray(bias.reshape(16, 128).T)
    in_maps = [{"xin": xs[c], "l2f_wT": wT, "l2f_w": weight, "l2f_bias": bias_l}
               for c in range(NCORES)]
    res = run_bass_kernel_spmd(Pr.nc, in_maps, core_ids=list(range(NCORES)))
    return _unshard_x([res.results[c]["xout"] for c in range(NCORES)])



class Arena:

    def __init__(self, P_, size):
        self.P_ = P_
        self.size = size
        self.t = P_.nc.alloc_sbuf_tensor("arena", [128, size], mybir.dt.uint8)
        self.base = P_.nc.lookup_mloc(self.t).addr
        self.off = 0
        self.hwm = 0

    def reset(self):
        self.off = 0

    def sb(self, name, shape, dtype):
        bpp = 1
        for s in shape[1:]:
            bpp *= s
        bpp *= mybir.dt.size(dtype)
        off = (self.off + 31) & ~31
        self.off = off + bpp
        self.hwm = max(self.hwm, self.off)
        assert self.off <= self.size, (name, self.off, self.size)
        return self.P_.nc.alloc_sbuf_tensor_at(name, shape, dtype,
                                               offset=self.base + off)


def stage_consts(P_, A, pairs):
    out, todo = {}, []
    for (name, arr) in pairs:
        arr = np.ascontiguousarray(arr)
        dtt = {np.dtype(np.float32): F32, np.dtype(bf16dt()): BF16,
               np.dtype(np.int32): I32}[arr.dtype]
        t = A.sb("bc_" + name, list(arr.shape), dtt)
        d = P_.const("cd_" + name, arr)
        out[name] = t
        todo.append((t, d))
    with _blk(P_.nc) as b:
        @b.sync
        def _(s):
            for (t, d) in todo:
                s.dma_start(out=t[:, :], in_=d[:, :]).then_inc(P_.ld.h, 16)
    P_.v_consts = P_.ld.inc(16 * len(todo))
    v = P_.v_consts
    with _blk(P_.nc) as b:
        @b.vector
        def _(e):
            e.wait_ge(P_.ld.h, v)
        @b.scalar
        def _(e):
            e.wait_ge(P_.ld.h, v)
        @b.tensor
        def _(e):
            e.wait_ge(P_.ld.h, v)
        @b.gpsimd
        def _(e):
            e.wait_ge(P_.ld.h, v)
    return out


def build_fused_prog(nblk=8):
    Pr = P()
    nc = Pr.nc
    dt = lambda n, sh: nc.dram_tensor(n, sh, F32, kind="ExternalInput").ap()
    xin = dt("xin", [D, TL])
    xout = nc.dram_tensor("xout", [D, TL], F32, kind="ExternalOutput").ap()
    l0a_wqT = dt("l0a_wqT", [D, D]); l0a_wkT = dt("l0a_wkT", [D, D])
    l0a_wvs = dt("l0a_wvs", [D, 256]); l0a_wgs = dt("l0a_wgs", [D, 256])
    l0a_wpT = dt("l0a_wpT", [D, D])
    l0f_gateT = dt("l0f_gateT", [D, 8])
    l0f_w1s = dt("l0f_w1s", [D, 1024]); l0f_w3s = dt("l0f_w3s", [D, 1024])
    l0f_w2s = dt("l0f_w2s", [1024, D])
    l0f_ws1s = dt("l0f_ws1s", [256, 1024]); l0f_ws3s = dt("l0f_ws3s", [256, 1024])
    l0f_ws2s = dt("l0f_ws2s", [128, D])
    l1a_wmix = dt("l1a_wmix", [D, 772]); l1a_conv = dt("l1a_conv", [128, 16])
    l1a_dtb = dt("l1a_dtb", [4, 1]); l1a_alog4 = dt("l1a_alog4", [1, 4])
    l1a_nw = dt("l1a_nw", [1, 128]); l1a_wout = dt("l1a_wout", [D, D])
    l1f_lut = dt("l1f_lut", [128, 16])
    l1f_gsh = nc.dram_tensor("l1f_gsh", [256, HL1], I32, kind="ExternalInput").ap()
    l1f_ush = nc.dram_tensor("l1f_ush", [256, HL1], I32, kind="ExternalInput").ap()
    l1f_dsh = nc.dram_tensor("l1f_dsh", [HL1, 256], I32, kind="ExternalInput").ap()
    l2a_wq = dt("l2a_wq", [D, 256]); l2a_wk = dt("l2a_wk", [D, 256])
    l2a_wv = dt("l2a_wv", [D, 256]); l2a_wo = dt("l2a_wo", [D, D])
    l2a_qs = dt("l2a_qs", [128, 2]); l2a_ks = dt("l2a_ks", [128, 2])
    l2f_wT = dt("l2f_wT", [D, D]); l2f_w = dt("l2f_w", [D, D])
    l2f_bias = dt("l2f_bias", [128, 16])
    l3a_wq = dt("l3a_wq", [D, 256]); l3a_wk = dt("l3a_wk", [D, 128])
    l3a_wv = dt("l3a_wv", [D, 128]); l3a_wo = dt("l3a_wo", [D, D])
    l3a_bo = dt("l3a_bo", [128, 16])
    l3f_w1s = dt("l3f_w1s", [D, 1024]); l3f_w3s = dt("l3f_w3s", [D, 1024])
    l3f_w2s = dt("l3f_w2s", [H_L3, 256])

    Pr.make_pool()
    xr = [Pr.sb(f"xr{i}", [128, TL], F32) for i in range(NDT)]
    xnb_back = Pr.sb("xnb_back", [128, NDT * TL], BF16)
    xnb_base = nc.lookup_mloc(xnb_back).addr
    xnb = [nc.alloc_sbuf_tensor_at(f"xnb{i}", [128, TL], BF16,
                                   offset=xnb_base + 2 * TL * i)
           for i in range(NDT)]
    rows4 = [Pr.sb(f"nrow{k}", [1, 512], F32) for k in range(4)]
    rows = [rows4[0], rows4[1], rows4[2], rows4[3], rows4[2]]
    atmp_p = Pr.sb("natmp", [128, 512], F32)
    identf = Pr.sconst("ident_f", np.eye(128, dtype=np.float32))
    identb = Pr.ident_b()
    Pr.ones_col_b(); Pr.ones_row_f()
    Pr.sconst("h128_bf", _hadamard128_bf())
    Pr.flush_consts()
    pid_box = {}
    with _blk(nc) as b:
        @b.sync
        def _(s):
            pid = nc.partition_id()
            pid_box["off"] = [nc.snap(pid * 2 + tc, min_val=0,
                                      max_val=2 * NCORES - 1) for tc in range(2)]
    pid_off = pid_box["off"]

    A = Arena(Pr, (nc.sbuf_bytes_remaining - 32) // 32 * 32)

    def mk_nctx(sq_alias, atmp_alias=None):
        return make_norm_ctx(Pr, sq_alias=sq_alias,
                             atmp_alias=(atmp_alias if atmp_alias is not None
                                         else atmp_p),
                             rows_alias=rows)

    set_fast(0 in FAST_BLOCKS)
    ctx = {
        "xstage": A.sb("a0_xstage", [128, 8192], BF16),
        "wst": [A.sb(f"a0_wst{i}", [128, 2048], BF16) for i in range(2)],
        "wst2": [A.sb(f"a0_wst2{i}", [128, 512], BF16) for i in range(2)],
        "wstb": A.sb("a0_wstb", [128, 8192], BF16),
        "kvf": [A.sb(f"a0_kvf{i}", [128, 256], F32) for i in range(NDT)],
        "kvb": [A.sb(f"a0_kvb{i}", [128, 256], BF16) for i in range(NDT)],
        "qst": A.sb("a0_qst", [128, 4096], BF16),
        "kst": A.sb("a0_kst", [128, 4096], BF16),
        "ktdst": A.sb("a0_ktdst", [128, 4096], BF16),
        "vst": A.sb("a0_vst", [128, 512], BF16),
        "gst": A.sb("a0_gst", [128, 512], BF16),
        "ssb": A.sb("a0_ssb", [128, 512], BF16),
        "otmp": A.sb("a0_otmp", [128, 256], F32),
        "obuf": A.sb("a0_obuf", [128, 256], BF16),
        "identf": identf, "pid_off": pid_off,
    }
    ctx["dmask"] = stage_consts(Pr, A, [("l3a_dmask", _l3_dmask())])["l3a_dmask"]
    nctx = mk_nctx([ctx["xstage"][:, 512 * i:512 * (i + 1)] for i in range(NDT)])
    v_x = load_residual(Pr, xr, xin)
    rfin = emit_norm(Pr, nctx, xr, xnb, None, None, subtract_mean=False,
                     eps=RMS_EPS, pre_waits=[(Pr.ld, v_x)], write_back=False)
    with _blk(nc) as b:
        @b.vector
        def _(v):
            for i in range(NDT):
                v.memset(ctx["kvf"][i][:, :], 0.0)
            v.engine_nop().then_inc(Pr.ve.h, 1)
    Pr.ve.inc(1)
    fin = emit_l0_attn(Pr, ctx, xr, xnb, rfin, l0a_wqT, l0a_wkT, l0a_wvs,
                       l0a_wgs, l0a_wpT, (Pr.ld, v_x))
    prev = emit_norm(Pr, nctx, xr, xnb, None, None, subtract_mean=True,
                     pre_waits=fin)
    if nblk >= 2:
        Pr.close_block()
        set_fast(1 in FAST_BLOCKS)
        A.reset()
        Pr.new_pool("f0")
        he = [A.sb(f"f0_he{i}", [128, 512], BF16) for i in range(72)]
        ctx = {
            "he": he,
            "w13st": [A.sb(f"f0_w13st{i}", [128, 2048], BF16) for i in range(2)],
            "w2st": [A.sb(f"f0_w2st{i}", [128, 1024], BF16) for i in range(2)],
            "silt": A.sb("f0_silt", [128, 512], BF16),
            "silt2": A.sb("f0_silt2", [128, 512], F32),
            "gsb": A.sb("f0_gsb", [128, 32], F32),
            "wE": A.sb("f0_wE", [8, TL], BF16),
            "wexp": [A.sb(f"f0_wexp{i}", [128, 512], BF16) for i in range(8)],
            "gstage": A.sb("f0_gstage", [128, 128], F32),
            "identf": identf,
        }
        Pr._sconsts["l0f_sel"] = stage_consts(
            Pr, A, [("l0f_sel", _sel8_const())])["l0f_sel"]
        nctx = mk_nctx([he[i] for i in range(NDT)], atmp_alias=ctx["silt2"])
        wbs, wready = emit_l0_ffn_prep(Pr, l0f_w1s, l0f_w3s, l0f_w2s,
                                       l0f_ws1s, l0f_ws3s, l0f_ws2s)
        fin = emit_l0_ffn(Pr, ctx, xr, xnb, prev, l0f_gateT, wbs, wready, prev)
        prev = emit_norm(Pr, nctx, xr, xnb, None, None, subtract_mean=True,
                         pre_waits=fin)
    if nblk >= 3:
        Pr.close_block()
        set_fast(2 in FAST_BLOCKS)
        A.reset()
        Pr.new_pool("a1")
        ctx = {
            "xstage": A.sb("a1_xstage", [128, 8192], BF16),
            "wst": [A.sb(f"a1_wst{i}", [128, 2048], BF16) for i in range(2)],
            "wmx": A.sb("a1_wmx", [128, 772 * NDT], BF16),
            "msbq": nc.alloc_sbuf_tensor_at("a1_msbq", [128, T], BF16,
                                            offset=xnb_base),
            "msbk": nc.alloc_sbuf_tensor_at("a1_msbk", [128, T], BF16,
                                            offset=xnb_base + 2 * T),
            "convw": A.sb("a1_convw", [128, 16], F32),
            "ctmp": A.sb("a1_ctmp", [128, 516], F32),
            "carry": A.sb("a1_carry", [128, 16], F32),
            "ssb": A.sb("a1_ssb", [128, 512], BF16),
            "cacc": A.sb("a1_cacc", [128, 512], F32),
            "ztmp": A.sb("a1_ztmp", [128, 512], BF16),
            "bae": A.sb("a1_bae", [4, 512], F32),
            "baa": A.sb("a1_baa", [4, 512], F32),
            "dtbc": A.sb("a1_dtbc", [4, 1], F32),
            "alog4": A.sb("a1_alog4s", [1, 4], F32),
            "eAc": A.sb("a1_eAc", [4, 1], F32),
            "eACr": A.sb("a1_eACr", [1, 2], F32),
            "eACc": A.sb("a1_eACc", [128, 1], F32),
            "sgr": A.sb("a1_sgr", [1, 2], F32),
            "nwrow": A.sb("a1_nwrow", [1, 128], F32),
            "nwmat": A.sb("a1_nwmat", [128, 128], BF16),
            "bcols": A.sb("a1_bcols", [128, 4], F32),
            "vch": A.sb("a1_vch", [128, 256], BF16),
            "zch": A.sb("a1_zch", [128, 256], BF16),
            "cumc": A.sb("a1_cumc", [128, 2], F32),
            "cumr": A.sb("a1_cumr", [1, 128], F32),
            "Dt": A.sb("a1_Dt", [128, 128], F32),
            "tA": A.sb("a1_tA", [128, 128], F32),
            "ecumR": A.sb("a1_ecumR", [128, 128], BF16),
            "EmDT": A.sb("a1_EmDT", [128, 128], BF16),
            "EkDT": A.sb("a1_EkDT", [128, 128], BF16),
            "EkD": A.sb("a1_EkD", [128, 128], BF16),
            "kkf": A.sb("a1_kkf", [128, 128], F32),
            "qkf": A.sb("a1_qkf", [128, 128], F32),
            "bmat": A.sb("a1_bmat", [128, 128], F32),
            "MT": A.sb("a1_MT", [128, 128], BF16),
            "NT": A.sb("a1_NT", [128, 128], BF16),
            "NTT": A.sb("a1_NTT", [128, 128], BF16),
            "NT2": A.sb("a1_NT2", [128, 128], BF16),
            "PT": A.sb("a1_PT", [128, 128], BF16),
            "kT": A.sb("a1_kT", [128, 128], BF16),
            "vT": A.sb("a1_vT", [128, 128], BF16),
            "ecumc": A.sb("a1_ecumc", [128, 1], F32),
            "beE": A.sb("a1_beE", [128, 1], F32),
            "ecr": A.sb("a1_ecr", [128, 1], F32),
            "Wb": A.sb("a1_Wb", [128, 128], BF16),
            "Bv": A.sb("a1_Bv", [128, 128], BF16),
            "Kp": A.sb("a1_Kp", [128, 128], BF16),
            "Wp": A.sb("a1_Wp", [128, 128], BF16),
            "Ub": A.sb("a1_Ub", [128, 128], BF16),
            "QtT": A.sb("a1_QtT", [128, 128], BF16),
            "QeT": A.sb("a1_QeT", [128, 128], BF16),
            "nG2T": A.sb("a1_nG2T", [128, 128], BF16),
            "IA": A.sb("a1_IA", [128, 128], BF16),
            "Sb": [A.sb(f"a1_Sb{h}", [128, 128], BF16) for h in range(2)],
            "och": A.sb("a1_och", [128, 128], F32),
            "osq": A.sb("a1_osq", [128, 128], BF16),
            "ovar": A.sb("a1_ovar", [128, 1], F32),
            "obuf": A.sb("a1_obuf", [128, 128], BF16),
            "qrow": A.sb("a1_qrow", [1, 512], F32),
            "identf": identf, "identb": identb,
            "ones_col_b2": Pr.ones_col_b(), "pid_off": pid_off,
        }
        bc = stage_consts(Pr, A, [("l1a_triu", _triu_incl()),
                                  ("l1a_mlt", _m_lt()), ("l1a_mle", _m_le()),
                                  ("l1a_mge", _m_ge()),
                                  ("ones_col_f", np.ones((128, 1), np.float32))])
        ctx["triu"] = bc["l1a_triu"]; ctx["mlt"] = bc["l1a_mlt"]
        ctx["mle"] = bc["l1a_mle"]; ctx["mge"] = bc["l1a_mge"]
        ctx["ones_col_f"] = bc["ones_col_f"]
        nctx = mk_nctx([ctx["xstage"][:, 512 * i:512 * (i + 1)]
                        for i in range(NDT)], atmp_alias=ctx["cacc"])
        fin = emit_l1_attn(Pr, ctx, xr, xnb, prev, l1a_wmix, l1a_conv, l1a_dtb,
                           l1a_alog4, l1a_nw, l1a_wout, prev)
        set_fast(False)
        prev = emit_norm(Pr, nctx, xr, xnb, None, None, subtract_mean=True,
                         pre_waits=fin)
    if nblk >= 4:
        Pr.close_block()
        set_fast(3 in FAST_BLOCKS)
        A.reset()
        Pr.new_pool("f1")
        ht1 = [A.sb(f"f1_ht{i}", [128, 512], BF16) for i in range(NHC1)]
        ctx = {
            "ht1": ht1,
            "wst1": [A.sb(f"f1_wst{i}", [128, 2048], BF16) for i in range(4)],
            "w2st1": [A.sb(f"f1_w2st{i}", [128, 1024], BF16) for i in range(2)],
            "silt1": A.sb("f1_silt", [128, 512], BF16),
            "idxst": A.sb("f1_idx", [128, 1376], I32),
            "wtile": A.sb("f1_wtile", [128, HL1], BF16),
            "ttmp": A.sb("f1_ttmp", [128, 1376], BF16),
            "tacc": A.sb("f1_tacc", [128, 1376], BF16),
            "lutsb": A.sb("f1_lut", [128, 16], F32),
        }
        nctx = mk_nctx([ht1[i] for i in range(NDT)])
        gb, ub, db, wready = emit_l1_ffn_build(Pr, ctx, l1f_gsh, l1f_ush,
                                               l1f_dsh, l1f_lut)
        fin = emit_l1_ffn(Pr, ctx, xr, xnb, prev, gb, ub, db, wready)
        prev = emit_norm(Pr, nctx, xr, xnb, None, None, subtract_mean=True,
                         pre_waits=fin)
    if nblk >= 5:
        Pr.close_block()
        set_fast(4 in FAST_BLOCKS)
        A.reset()
        Pr.new_pool("a2")
        ctx = {
            "qbuf": A.sb("a2_qbuf", [128, 1024], BF16),
            "kring": A.sb("a2_kring", [128, 4096], BF16),
            "vring": A.sb("a2_vring", [128, 4096], BF16),
            "xstage": A.sb("a2_xstage", [128, 8192], BF16),
            "wqkvt": A.sb("a2_wqkvt", [128, 12288], BF16),
            "wost": [A.sb(f"a2_wost{i}", [128, 2048], BF16) for i in range(2)],
            "es": [A.sb(f"a2_es{i}", [128, 512], BF16) for i in range(2)],
            "cbuf": [A.sb(f"a2_cbuf{i}", [128, 512], BF16) for i in range(2)],
            "qtmp": A.sb("a2_qtmp", [128, 1024], F32),
            "vtmp0": A.sb("a2_vtmp0", [128, 512], F32),
            "vtmp": A.sb("a2_vtmp", [128, 512], F32),
            "cst": A.sb("a2_cst", [128, 1024], BF16),
            "scl": A.sb("a2_scl", [128, 4], F32),
            "cfix": A.sb("a2_cfix", [128, 512], F32),
            "cfix2": A.sb("a2_cfix2", [128, 512], F32),
            "sqn": A.sb("a2_sqn", [128, 1024], BF16),
            "identf": identf, "pid_off": pid_off,
        }
        bc = stage_consts(Pr, A, [("l2a_pswap", _pswap()),
                                  ("l3a_dmask", _l3_dmask()),
                                  ("l2a_wmask", _l2_wmask())])
        ctx["pswap"] = bc["l2a_pswap"]; ctx["dmask"] = bc["l3a_dmask"]
        ctx["wmask"] = bc["l2a_wmask"]
        C, S = _l2_rope_tables()
        ctx["ropeC"] = Pr.const("l2a_ropeC", C)
        ctx["ropeS"] = Pr.const("l2a_ropeS", S)
        nctx = mk_nctx([ctx["xstage"][:, 512 * i:512 * (i + 1)]
                        for i in range(NDT)], atmp_alias=ctx["cfix"])
        ctx["crow"] = nctx["r_msq"]
        fin = emit_l2_attn(Pr, ctx, xr, prev, l2a_wq, l2a_wk, l2a_wv, l2a_wo,
                           l2a_qs, l2a_ks)
        prev = emit_norm(Pr, nctx, xr, xnb, None, None, subtract_mean=True,
                         pre_waits=fin)
    if nblk >= 6:
        Pr.close_block()
        set_fast(5 in FAST_BLOCKS)
        A.reset()
        Pr.new_pool("f2")
        ctx = {
            "sgnT": [A.sb(f"f2_sgnT{i}", [128, D], BF16) for i in range(NDT)],
            "wstage": A.sb("f2_wstage", [128, D], BF16),
            "alphab": A.sb("f2_alphab", [128, 33], F32),
            "psA": Pr.make_pool()[4],
            "psB": Pr.make_pool()[5],
            "v_x": prev,
        }
        sq2 = [A.sb(f"f2_sq{i}", [128, 512], BF16) for i in range(NDT)]
        nctx = mk_nctx(sq2)
        _, y_ready = emit_l2_ffn(Pr, ctx, xr, xnb, prev, l2f_wT, l2f_w,
                                 l2f_bias)
        prev = emit_norm(Pr, nctx, xr, xnb, None, None, subtract_mean=True,
                         pre_waits=y_ready)
    if nblk >= 7:
        Pr.close_block()
        set_fast(6 in FAST_BLOCKS)
        A.reset()
        Pr.new_pool("a3")
        qsb = [nc.alloc_sbuf_tensor_at(f"a3_qsb{j}", [128, 8192], BF16,
                                       offset=xnb_base + 2 * 8192 * j)
               for j in range(2)]
        ctx = {
            "qsb": qsb,
            "ksb": A.sb("a3_ksb", [128, T], BF16),
            "vTsb": A.sb("a3_vTsb", [128, T], BF16),
            "xstage": A.sb("a3_xstage", [128, 8192], BF16),
            "wqkvt": A.sb("a3_wqkvt", [128, 8192], BF16),
            "wost": [A.sb(f"a3_wost{i}", [128, 2048], BF16) for i in range(2)],
            "es": [A.sb(f"a3_es{i}", [128, 512], BF16) for i in range(2)],
            "cbuf": [A.sb(f"a3_cbuf{i}", [128, 512], BF16) for i in range(2)],
            "vtmp": A.sb("a3_vtmp", [128, 512], F32),
            "cfix": A.sb("a3_cfix", [128, 512], F32),
            "identf": identf, "pid_off": pid_off,
        }
        ctx["dmask"] = stage_consts(Pr, A,
                                    [("l3a_dmask", _l3_dmask())])["l3a_dmask"]
        nctx = mk_nctx([ctx["vTsb"][:, 512 * i:512 * (i + 1)]
                        for i in range(NDT)], atmp_alias=ctx["cfix"])
        ctx["crow"] = nctx["r_msq"]
        fin, _call = emit_l3_attn(Pr, ctx, xr, prev, l3a_wq, l3a_wk, l3a_wv,
                                  l3a_wo, l3a_bo)
        prev = emit_norm(Pr, nctx, xr, xnb, None, None, subtract_mean=True,
                         pre_waits=fin)
    if nblk >= 8:
        Pr.close_block()
        set_fast(7 in FAST_BLOCKS)
        A.reset()
        Pr.new_pool("f3")
        ht = [A.sb(f"f3_ht{i}", [128, 512], BF16) for i in range(NHT)]
        ctx = {
            "ht": ht,
            "w1st": [A.sb(f"f3_w1st{i}", [128, 2048], BF16) for i in range(2)],
            "w3st": [A.sb(f"f3_w3st{i}", [128, 2048], BF16) for i in range(2)],
            "w2st": [A.sb(f"f3_w2st{i}", [128, 1024], BF16) for i in range(2)],
            "silt": A.sb("f3_silt", [128, 2048], BF16),
            "wt_alias": ht,
        }
        nctx = mk_nctx([ht[i] for i in range(NDT)])
        Pr._l3_wt = [ht[i][:, 0:256] for i in range(NHT)]
        Pr._l3_wtmp = ctx["silt"][:, 0:256]
        w1b, w3b, w2b, wready = emit_l3_ffn_prep(Pr, l3f_w1s, l3f_w3s, l3f_w2s)
        fin = emit_l3_ffn(Pr, ctx, xr, xnb, prev, w1b, w3b, w2b, wready, prev)
        prev = emit_norm(Pr, nctx, xr, xnb, None, None, subtract_mean=True,
                         pre_waits=fin)

    Pr.close_block()
    set_fast(False)
    store_residual(Pr, xr, xout, [prev])
    Pr.close_block()
    return Pr


def fused_in_maps(I):
    x = np.asarray(I["L0_attn_x"], dtype=np.float32)
    xs = _shard_x(x)
    ca = np.ascontiguousarray
    wqT = ca(I["L0_attn_Wq"].T); wkT = ca(I["L0_attn_Wk"].T)
    wvT = ca(I["L0_attn_Wv"].T); wgT = ca(I["L0_attn_Wg"].T)
    wpT = ca(I["L0_attn_Wp"].T)
    w1T = ca(I["L0_ffn_w1"].transpose(0, 2, 1))
    w3T = ca(I["L0_ffn_w3"].transpose(0, 2, 1))
    w2T = ca(I["L0_ffn_w2"].transpose(0, 2, 1))
    ws1T = ca(I["L0_ffn_ws1"].T); ws3T = ca(I["L0_ffn_ws3"].T)
    ws2T = ca(I["L0_ffn_ws2"].T)
    gT = ca(I["L0_ffn_gate_w"].T)
    W_qkv, W_z = I["L1_attn_W_qkv"], I["L1_attn_W_z"]
    W_b, W_a = I["L1_attn_W_b"], I["L1_attn_W_a"]
    conv_w, dt_bias = I["L1_attn_conv_w"], I["L1_attn_dt_bias"]
    A_log = I["L1_attn_A_log"]
    lut = I["L1_ffn_lut"]
    gT1 = ca(I["L1_ffn_gate_idx"].T); uT1 = ca(I["L1_ffn_up_idx"].T)
    dT1 = ca(I["L1_ffn_down_idx"].T)
    qs2 = ca(I["L2_attn_q_scale"].reshape(2, 128).T)
    ks2 = ca(I["L2_attn_k_scale"].reshape(2, 128).T)
    l2wT = ca(I["L2_ffn_weight"].T)
    l2bias = ca(I["L2_ffn_bias"].reshape(16, 128).T)
    bo_l = ca(I["L3_attn_bo"].reshape(16, 128).T)
    w1T3 = ca(I["L3_ffn_W1"].T); w3T3 = ca(I["L3_ffn_W3"].T)
    w2T3 = ca(I["L3_ffn_W2"].T)
    in_maps = []
    for c in range(NCORES):
        qcol = W_qkv[:, 128 * c:128 * (c + 1)]
        kcol = W_qkv[:, 1024 + 128 * c:1024 + 128 * (c + 1)]
        vcol = W_qkv[:, 2048 + 256 * c:2048 + 256 * (c + 1)]
        zcol = W_z[:, 256 * c:256 * (c + 1)]
        bcol = W_b[:, 2 * c:2 * (c + 1)]
        acol = W_a[:, 2 * c:2 * (c + 1)]
        wmix = ca(np.concatenate([qcol, kcol, vcol, zcol, bcol, acol], axis=1))
        chans = np.concatenate([np.arange(128 * c, 128 * (c + 1)),
                                np.arange(1024 + 128 * c, 1024 + 128 * (c + 1)),
                                np.arange(2048 + 256 * c, 2048 + 256 * (c + 1))])
        cw = conv_w[chans]
        cl = np.zeros((128, 16), np.float32)
        for j in range(4):
            cl[:, 4 * j:4 * (j + 1)] = cw[128 * j:128 * (j + 1)]
        dtb4 = np.zeros((4, 1), np.float32)
        dtb4[2, 0] = dt_bias[2 * c]; dtb4[3, 0] = dt_bias[2 * c + 1]
        al4 = np.zeros((1, 4), np.float32)
        al4[0, 2] = A_log[2 * c]; al4[0, 3] = A_log[2 * c + 1]
        g2 = c // 2
        in_maps.append({
            "xin": xs[c],
            "l0a_wqT": wqT, "l0a_wkT": wkT,
            "l0a_wvs": ca(wvT[:, 256 * c:256 * (c + 1)]),
            "l0a_wgs": ca(wgT[:, 256 * c:256 * (c + 1)]),
            "l0a_wpT": wpT,
            "l0f_gateT": gT, "l0f_w1s": w1T[c], "l0f_w3s": w3T[c],
            "l0f_w2s": w2T[c],
            "l0f_ws1s": ca(ws1T[256 * c:256 * (c + 1)]),
            "l0f_ws3s": ca(ws3T[256 * c:256 * (c + 1)]),
            "l0f_ws2s": ca(ws2T[128 * c:128 * (c + 1)]),
            "l1a_wmix": wmix, "l1a_conv": cl, "l1a_dtb": dtb4,
            "l1a_alog4": al4,
            "l1a_nw": ca(I["L1_attn_norm_weight"].reshape(1, 128)),
            "l1a_wout": I["L1_attn_W_out"],
            "l1f_lut": ca(np.broadcast_to(lut.reshape(1, 16), (128, 16))),
            "l1f_gsh": ca(gT1[256 * c:256 * (c + 1)]),
            "l1f_ush": ca(uT1[256 * c:256 * (c + 1)]),
            "l1f_dsh": ca(dT1[:, 256 * c:256 * (c + 1)]),
            "l2a_wq": ca(I["L2_attn_Wq"][:, 256 * c:256 * (c + 1)]),
            "l2a_wk": ca(I["L2_attn_Wk"][:, 256 * g2:256 * (g2 + 1)]),
            "l2a_wv": ca(I["L2_attn_Wv"][:, 256 * g2:256 * (g2 + 1)]),
            "l2a_wo": I["L2_attn_Wo"], "l2a_qs": qs2, "l2a_ks": ks2,
            "l2f_wT": l2wT, "l2f_w": I["L2_ffn_weight"], "l2f_bias": l2bias,
            "l3a_wq": ca(I["L3_attn_Wq"][:, 256 * c:256 * (c + 1)]),
            "l3a_wk": ca(I["L3_attn_Wk"][:, 128 * g2:128 * (g2 + 1)]),
            "l3a_wv": ca(I["L3_attn_Wv"][:, 128 * g2:128 * (g2 + 1)]),
            "l3a_wo": I["L3_attn_Wo"], "l3a_bo": bo_l,
            "l3f_w1s": ca(w1T3[:, 1024 * c:1024 * (c + 1)]),
            "l3f_w3s": ca(w3T3[:, 1024 * c:1024 * (c + 1)]),
            "l3f_w2s": ca(w2T3[:, 256 * c:256 * (c + 1)]),
        })
    return in_maps


def kernel(**inputs):
    I = {k: np.asarray(v) for k, v in inputs.items()}
    Pr = build_fused_prog()
    in_maps = fused_in_maps(I)
    res = run_bass_kernel_spmd(Pr.nc, in_maps, core_ids=list(range(NCORES)))
    x = _unshard_x([res.results[c]["xout"] for c in range(NCORES)])
    return np.ascontiguousarray(x.astype(np.float32))


# version 13
# speedup vs baseline: 1.0168x; 1.0168x over previous
import math
import resource as _resource
import numpy as np
import concourse.bass as bass
import concourse.mybir as mybir
from concourse.bass_utils import run_bass_kernel_spmd

try:
    _resource.setrlimit(_resource.RLIMIT_STACK,
                        (1 << 30, _resource.getrlimit(_resource.RLIMIT_STACK)[1]))
except (ValueError, OSError):
    pass

F32 = mybir.dt.float32
BF16 = mybir.dt.bfloat16
F32R = mybir.dt.float32r
I32 = mybir.dt.int32
AF = mybir.ActivationFunctionType
AL = mybir.AluOpType
AX = mybir.AxisListType

NCORES = 8
D = 2048
T = 8192
TL = T // NCORES
NDT = D // 128
RG = [list(range(NCORES))]
LN_EPS = 1e-5
RMS_EPS = 1.1920929e-07


def bf16dt():
    import ml_dtypes
    return ml_dtypes.bfloat16


class Sem:
    __slots__ = ("h", "n")

    def __init__(self, nc, name):
        self.h = nc.alloc_semaphore(name)
        self.n = 0

    def inc(self, n=1):
        self.n += n
        return self.n


_P_BY_NC = {}


class _BlockProxy:
    __slots__ = ("blk",)

    def __init__(self, blk):
        self.blk = blk

    def __enter__(self):
        return self.blk

    def __exit__(self, *a):
        return False


_FAST = [False]
FAST_BLOCKS = frozenset({2})


def _blk(nc):
    P_ = _P_BY_NC[id(nc)]
    if not _FAST[0]:
        if P_._blk is not None:
            P_.close_block()
        return nc.Block()
    if P_._blk is None:
        P_.open_block()
    return _BlockProxy(P_._blk)


def set_fast(on):
    _FAST[0] = on


class P:

    def __init__(self):
        self.nc = bass.Bass("TRN2", target_bir_lowering=False)
        self._consts = {}
        self.ld = Sem(self.nc, "ld")
        self.gd = Sem(self.nc, "gd")
        self.sd = Sem(self.nc, "sdq")
        self.pe = Sem(self.nc, "pe")
        self.ve = Sem(self.nc, "ve")
        self.sc = Sem(self.nc, "sc")
        self.cc = Sem(self.nc, "cc")
        self.pool = None
        self._blk = None
        _P_BY_NC[id(self.nc)] = self

    def open_block(self):
        assert self._blk is None
        self._blk = self.nc.Block()
        self._blk.__enter__()

    def close_block(self):
        if self._blk is not None:
            self._blk.__exit__(None, None, None)
            self._blk = None

    def make_pool(self):
        if self.pool is None:
            self._psum_base0 = self.nc.psum_base
            self.pool = [self.ps(f"pool{i}", [128, 512], F32) for i in range(8)]
        return self.pool

    def new_pool(self, tag):
        self.make_pool()
        self.nc.psum_base = self._psum_base0
        self.pool = [self.ps(f"pool{i}_{tag}", [128, 512], F32) for i in range(8)]
        return self.pool

    def const(self, name, arr):
        if name not in self._consts:
            self._consts[name] = self.nc.inline_tensor(np.ascontiguousarray(arr), name=name)
        return self._consts[name].ap()

    def sb(self, name, shape, dtype):
        return self.nc.alloc_sbuf_tensor(name, shape, dtype)

    def ps(self, name, shape, dtype=F32):
        return self.nc.alloc_psum_tensor(name, shape, dtype)

    def sconst(self, name, arr):
        arr = np.ascontiguousarray(arr)
        if not hasattr(self, "_sconsts"):
            self._sconsts, self._sconst_pend, self.v_consts = {}, [], 0
        if name not in self._sconsts:
            dt = {np.dtype(np.float32): F32, np.dtype(bf16dt()): BF16,
                  np.dtype(np.int32): I32}[arr.dtype]
            t = self.sb("sc_" + name, list(arr.shape), dt)
            d = self.const("cd_" + name, arr)
            self._sconsts[name] = t
            self._sconst_pend.append((t, d))
        return self._sconsts[name]

    def flush_consts(self):
        pend = getattr(self, "_sconst_pend", [])
        if not pend:
            return self.ld.n
        with _blk(self.nc) as b:
            @b.sync
            def _(s):
                for (t, d) in pend:
                    s.dma_start(out=t[:, :], in_=d[:, :]).then_inc(self.ld.h, 16)
        self.v_consts = self.ld.inc(16 * len(pend))
        self._sconst_pend = []
        return self.v_consts

    def ones_col_b(self):
        return self.sconst("ones_col_b", np.ones((128, 1), bf16dt()))

    def ones_row_f(self):
        return self.sconst("ones_row_f", np.ones((1, 128), np.float32))

    def ident_b(self):
        return self.sconst("ident_b", np.eye(128, dtype=bf16dt()))



def load_residual(P_, xr, xin_ap):
    with _blk(P_.nc) as b:
        @b.sync
        def _(s):
            for i in range(NDT):
                s.dma_start(out=xr[i][:, :], in_=xin_ap[128 * i:128 * (i + 1), :]
                            ).then_inc(P_.ld.h, 16)
    return P_.ld.inc(16 * NDT)


def store_residual(P_, xr, xout_ap, waits):
    with _blk(P_.nc) as b:
        @b.sync
        def _(s):
            for (sm, v) in waits:
                s.wait_ge(sm.h, v)
            for i in range(NDT):
                s.dma_start(out=xout_ap[128 * i:128 * (i + 1), :], in_=xr[i][:, :]
                            ).then_inc(P_.ld.h, 16)
            s.wait_ge(P_.ld.h, P_.ld.inc(16 * NDT))



def emit_norm(P_, ctx, xr, xnb, y_tiles=None, y_ready=None, subtract_mean=True,
              eps=LN_EPS, pre_waits=(), write_back=True):
    nc = P_.nc
    sq = ctx["sq"]
    atmp = ctx["atmp"]
    r_msq, r_mean, r_tmp, r_a, r_nb = (ctx["r_msq"], ctx["r_mean"], ctx["r_tmp"],
                                       ctx["r_a"], ctx["r_nb"])
    pstat, pstat2, pbc, pbc2 = ctx["pstat"], ctx["pstat2"], ctx["pbc"], ctx["pbc2"]
    ones_b = P_.ones_col_b()[:, :]
    ones_r = P_.ones_row_f()[:, :]
    v_consts = P_.v_consts

    v_add0 = P_.ve.n
    with _blk(nc) as b:
        @b.vector
        def _(v):
            for (sm, w) in pre_waits:
                v.wait_ge(sm.h, w)
            for i in range(NDT):
                if y_tiles is not None:
                    if y_ready is not None and y_ready[i] is not None:
                        sm, w = y_ready[i]
                        v.wait_ge(sm.h, w)
                    v.tensor_add(xr[i][:, :], xr[i][:, :], y_tiles[i]).then_inc(P_.ve.h, 1)
                else:
                    v.engine_nop().then_inc(P_.ve.h, 1)
    P_.ve.inc(NDT)

    with _blk(nc) as b:
        @b.scalar
        def _(s):
            for i in range(NDT):
                s.wait_ge(P_.ve.h, v_add0 + i + 1)
                s.copy(xnb[i][:, :], xr[i][:, :]).then_inc(P_.sc.h, 1)
    P_.sc.inc(NDT)

    for tc in range(TL // 512):
        sl = slice(512 * tc, 512 * (tc + 1))
        sc0 = P_.sc.n
        with _blk(nc) as b:
            @b.scalar
            def _(s):
                if tc > 0:
                    s.wait_ge(P_.pe.h, ctx["_norm_pelast"])
                for i in range(NDT):
                    s.activation(sq[i][:, :], xr[i][:, sl], AF.Square
                                 ).then_inc(P_.sc.h, 1)
        P_.sc.inc(NDT)
        with _blk(nc) as b:
            @b.tensor
            def _(t):
                if tc > 0:
                    t.wait_ge(P_.ve.h, ctx["_norm_vlast"])
                t.wait_ge(P_.ld.h, v_consts)
                for i in range(NDT):
                    t.wait_ge(P_.sc.h, sc0 + i + 1)
                    t.matmul(pstat[0:1, :], ones_b, sq[i][:, :],
                             start=(i == 0), stop=(i == NDT - 1))
                    ins = t.matmul(pstat2[0:1, :], ones_b, xnb[i][:, sl],
                                   start=(i == 0), stop=(i == NDT - 1))
                    if i == NDT - 1:
                        ins.then_inc(P_.pe.h, 1)
        v_sums = P_.pe.inc(1)
        ctx["_norm_pelast"] = v_sums

        with _blk(nc) as b:
            @b.vector
            def _(v):
                v.wait_ge(P_.pe.h, v_sums)
                msq, mean, tmp = r_msq[0:1, :], r_mean[0:1, :], r_tmp[0:1, :]
                v.tensor_scalar(msq, pstat[0:1, :], 1.0 / D, None, AL.mult)
                if subtract_mean:
                    v.tensor_scalar(mean, pstat2[0:1, :], 1.0 / D, None, AL.mult)
                    v.tensor_tensor(tmp, mean, mean, AL.mult)
                    v.tensor_sub(msq, msq, tmp)
                v.tensor_scalar(msq, msq, float(eps), None, AL.add).then_inc(P_.ve.h, 1)
        v_var = P_.ve.inc(1)

        with _blk(nc) as b:
            @b.scalar
            def _(s):
                s.wait_ge(P_.ve.h, v_var)
                s.activation(r_a[0:1, :], r_msq[0:1, :], AF.Ln)
                s.activation(r_a[0:1, :], r_a[0:1, :], AF.Exp, scale=-0.5
                             ).then_inc(P_.sc.h, 1)
        v_a = P_.sc.inc(1)

        if subtract_mean:
            with _blk(nc) as b:
                @b.vector
                def _(v):
                    v.wait_ge(P_.sc.h, v_a)
                    v.tensor_tensor(r_nb[0:1, :], r_mean[0:1, :], r_a[0:1, :],
                                    AL.mult).then_inc(P_.ve.h, 1)
            P_.ve.inc(1)
        v_rows_v, v_rows_s = P_.ve.n, P_.sc.n

        with _blk(nc) as b:
            @b.tensor
            def _(t):
                t.wait_ge(P_.sc.h, v_rows_s)
                t.wait_ge(P_.ve.h, v_rows_v)
                t.matmul(pbc[:, :], ones_r, r_a[0:1, :], start=True, stop=True
                         ).then_inc(P_.pe.h, 1)
                if subtract_mean:
                    t.matmul(pbc2[:, :], ones_r, r_nb[0:1, :], start=True, stop=True
                             ).then_inc(P_.pe.h, 1)
        v_bc = P_.pe.inc(2 if subtract_mean else 1)

        with _blk(nc) as b:
            @b.vector
            def _(v):
                v.wait_ge(P_.pe.h, v_bc)
                for i in range(NDT):
                    if write_back:
                        v.tensor_tensor(atmp[:, :], xr[i][:, sl], pbc[:, :], AL.mult)
                        if subtract_mean:
                            v.tensor_sub(xr[i][:, sl], atmp[:, :], pbc2[:, :])
                        else:
                            v.tensor_copy(xr[i][:, sl], atmp[:, :])
                        v.tensor_copy(xnb[i][:, sl], xr[i][:, sl])
                    else:
                        v.tensor_tensor(xnb[i][:, sl], xr[i][:, sl], pbc[:, :],
                                        AL.mult)
                v.engine_nop().then_inc(P_.ve.h, 1)
        ctx["_norm_vlast"] = P_.ve.inc(1)
    return (P_.ve, P_.ve.n)


def make_norm_ctx(P_, tag="", sq_alias=None, atmp_alias=None, rows_alias=None):
    d = {
        "sq": (sq_alias if sq_alias is not None else
               [P_.sb(f"nsq{tag}_{i}", [128, 512], BF16) for i in range(NDT)]),
        "atmp": (atmp_alias if atmp_alias is not None else
                 P_.sb(f"natmp{tag}", [128, 512], F32)),
        "r_msq": (rows_alias[0] if rows_alias else P_.sb(f"nrmsq{tag}", [1, 512], F32)),
        "r_mean": (rows_alias[1] if rows_alias else P_.sb(f"nrmean{tag}", [1, 512], F32)),
        "r_tmp": (rows_alias[2] if rows_alias else P_.sb(f"nrtmp{tag}", [1, 512], F32)),
        "r_a": (rows_alias[3] if rows_alias else P_.sb(f"nra{tag}", [1, 512], F32)),
        "r_nb": ((rows_alias[4] if rows_alias[4] is not None
                  else P_.sb(f"nrnb{tag}", [1, 512], F32)) if rows_alias
                 else P_.sb(f"nrnb{tag}", [1, 512], F32)),
        "_norm_pelast": 0,
        "pstat": P_.make_pool()[0],
        "pstat2": P_.make_pool()[1],
        "pbc": P_.make_pool()[2],
        "pbc2": P_.make_pool()[3],
    }
    return d



def emit_proj_dp(P_, wT_ap, o_size, xnb, xnb_ready, evac, wbuf, psums, tag,
                 t_len=TL, tch=512):
    nc = P_.nc
    n_oc = (o_size + 127) // 128
    ntc = t_len // tch
    ld_done = []
    for i in range(NDT):
        buf = wbuf[i % 2]
        waits = []
        if i >= 2:
            waits.append(("pe", ld_done[i - 2][1]))
        with _blk(nc) as b:
            @b.gpsimd
            def _(g, i=i, buf=buf, waits=waits):
                for (kind, w) in waits:
                    g.wait_ge(P_.pe.h, w)
                g.dma_start(out=buf[:, :], in_=wT_ap[128 * i:128 * (i + 1), :]
                            ).then_inc(P_.gd.h, 16)
        v_ld = P_.gd.inc(16)
        with _blk(nc) as b:
            @b.tensor
            def _(t, i=i, buf=buf, v_ld=v_ld):
                t.wait_ge(P_.gd.h, v_ld)
                if i == 0 and xnb_ready is not None:
                    t.wait_ge(xnb_ready[0].h, xnb_ready[1])
                for oc in range(n_oc):
                    osz = min(128, o_size - 128 * oc)
                    for tc in range(ntc):
                        pa = psums[(oc * ntc + tc) % len(psums)]
                        ins = t.matmul(pa[0:osz, :],
                                       buf[:, 128 * oc:128 * oc + osz],
                                       xnb_sl(xnb, i, tc, tch),
                                       start=(i == 0), stop=(i == NDT - 1))
                        if i == NDT - 1:
                            ins.then_inc(P_.pe.h, 1)
        if i == NDT - 1:
            P_.pe.inc(n_oc * ntc)
        ld_done.append(("pe", P_.pe.n))
    base = P_.pe.n - n_oc * ntc
    for oc in range(n_oc):
        for tc in range(ntc):
            k = oc * ntc + tc
            evac(oc, psums[k % len(psums)], tc, (P_.pe, base + k + 1))
    return


def xnb_sl(xnb, i, tc, tch):
    return xnb[i][:, tch * tc:tch * (tc + 1)]



def emit_l2_ffn(P_, ctx, xr, xnb, xnb_ready, wT_ap, w_ap, bias_ap):
    nc = P_.nc
    sgnT = ctx["sgnT"]
    wstage = ctx["wstage"]
    alphab = ctx["alphab"]
    psA, psB = ctx["psA"], ctx["psB"]
    v_x = ctx["v_x"]

    ctx["_a0"] = P_.ve.n
    for j in range(NDT):
        with _blk(nc) as b:
            @b.gpsimd
            def _(g, j=j):
                if j >= 1:
                    g.wait_ge(P_.ve.h, ctx["_a0"] + j)
                g.dma_start(out=wstage[:, :], in_=w_ap[128 * j:128 * (j + 1), :]
                            ).then_inc(P_.gd.h, 16)
        v_ld = P_.gd.inc(16)
        with _blk(nc) as b:
            @b.vector
            def _(v, j=j, v_ld=v_ld):
                v.wait_ge(P_.gd.h, v_ld)
                v.tensor_reduce(alphab[:, j:j + 1], wstage[:, :], AX.X, AL.add,
                                apply_absolute_value=True).then_inc(P_.ve.h, 1)
    v_alpha0 = P_.ve.n
    with _blk(nc) as b:
        @b.vector
        def _(v):
            v.wait_ge(P_.ve.h, v_alpha0)
            v.tensor_scalar(alphab[:, 0:NDT], alphab[:, 0:NDT], 1.0 / D, None,
                            AL.mult).then_inc(P_.ve.h, 1)
    v_alpha = P_.ve.inc(1)

    v_bias = P_.ld.inc(16)
    with _blk(nc) as b:
        @b.sync
        def _(s):
            s.dma_start(out=alphab[:, 17:33], in_=bias_ap[:, :]).then_inc(P_.ld.h, 16)

    ctx["_s0"] = P_.sc.n
    for i in range(NDT):
        with _blk(nc) as b:
            @b.gpsimd
            def _(g, i=i):
                if i == 0:
                    g.wait_ge(P_.ve.h, ctx["_a0"] + NDT)
                else:
                    g.wait_ge(P_.sc.h, ctx["_s0"] + i)
                g.dma_start(out=wstage[:, :], in_=wT_ap[128 * i:128 * (i + 1), :]
                            ).then_inc(P_.gd.h, 16)
        v_ld = P_.gd.inc(16)
        with _blk(nc) as b:
            @b.scalar
            def _(s, i=i, v_ld=v_ld):
                s.wait_ge(P_.gd.h, v_ld)
                s.sign(sgnT[i][:, :], wstage[:, :]).then_inc(P_.sc.h, 1)
    v_sgn = P_.sc.n

    vb0 = P_.ve.n
    with _blk(nc) as b:
        @b.vector
        def _(v):
            v.wait_ge(P_.ld.h, v_bias)
            v.wait_ge(v_x[0].h, v_x[1])
            for oc in range(NDT):
                v.tensor_scalar(xr[oc][:, :], xr[oc][:, :],
                                alphab[:, 17 + oc:18 + oc], None, AL.add
                                ).then_inc(P_.ve.h, 1)
    P_.ve.inc(NDT)

    ev_waits = {}
    for oc in range(NDT):
        for tc in range(2):
            pa = psA if (oc * 2 + tc) % 2 == 0 else psB
            with _blk(nc) as b:
                @b.tensor
                def _(t, oc=oc, tc=tc, pa=pa):
                    t.wait_ge(P_.sc.h, v_sgn)
                    t.wait_ge(xnb_ready[0].h, xnb_ready[1])
                    k = oc * 2 + tc
                    if k >= 2:
                        t.wait_ge(P_.ve.h, ev_waits[k - 2])
                    for i in range(NDT):
                        ins = t.matmul(pa[:, :], sgnT[i][:, 128 * oc:128 * (oc + 1)],
                                       xnb[i][:, 512 * tc:512 * (tc + 1)],
                                       start=(i == 0), stop=(i == NDT - 1))
                    ins.then_inc(P_.pe.h, 1)
            v_mm = P_.pe.inc(1)
            with _blk(nc) as b:
                @b.vector
                def _(v, oc=oc, tc=tc, pa=pa, v_mm=v_mm):
                    v.wait_ge(P_.pe.h, v_mm)
                    v.wait_ge(P_.ve.h, v_alpha)
                    sl = slice(512 * tc, 512 * (tc + 1))
                    v.scalar_tensor_tensor(xr[oc][:, sl], pa[:, :],
                                           alphab[:, oc:oc + 1], xr[oc][:, sl],
                                           AL.mult, AL.add).then_inc(P_.ve.h, 1)
            ev_waits[oc * 2 + tc] = P_.ve.inc(1)

    return None, [(P_.ve, P_.ve.n)]



def build_l2_ffn_prog(skip_norm=False):
    Pr = P()
    nc = Pr.nc
    xin = nc.dram_tensor("xin", [D, TL], F32, kind="ExternalInput").ap()
    wT = nc.dram_tensor("l2f_wT", [D, D], F32, kind="ExternalInput").ap()
    w = nc.dram_tensor("l2f_w", [D, D], F32, kind="ExternalInput").ap()
    bias = nc.dram_tensor("l2f_bias", [128, 16], F32, kind="ExternalInput").ap()
    xout = nc.dram_tensor("xout", [D, TL], F32, kind="ExternalOutput").ap()

    xr = [Pr.sb(f"xr{i}", [128, TL], F32) for i in range(NDT)]
    xnb = [Pr.sb(f"xnb{i}", [128, TL], BF16) for i in range(NDT)]
    nctx = make_norm_ctx(Pr)
    ctx = {
        "sgnT": [Pr.sb(f"sgnT{i}", [128, D], BF16) for i in range(NDT)],
        "wstage": Pr.sb("wstage", [128, D], BF16),
        "alphab": Pr.sb("alphab", [128, 33], F32),
        "psA": Pr.make_pool()[4],
        "psB": Pr.make_pool()[5],
    }
    Pr.ones_col_b(); Pr.ones_row_f()
    Pr.flush_consts()
    v_x = load_residual(Pr, xr, xin)

    with _blk(nc) as b:
        @b.scalar
        def _(s):
            s.wait_ge(Pr.ld.h, v_x)
            for i in range(NDT):
                s.copy(xnb[i][:, :], xr[i][:, :]).then_inc(Pr.sc.h, 1)
    xnb_ready = (Pr.sc, Pr.sc.inc(NDT))

    ctx["v_x"] = (Pr.ld, v_x)
    _, y_ready = emit_l2_ffn(Pr, ctx, xr, xnb, xnb_ready, wT, w, bias)
    if skip_norm:
        store_residual(Pr, xr, xout, y_ready)
        Pr.close_block()
        return Pr
    fin = emit_norm(Pr, nctx, xr, xnb, None, None, subtract_mean=True,
                    pre_waits=y_ready)
    store_residual(Pr, xr, xout, [fin])
    Pr.close_block()
    return Pr


def build_l3_ffn_prog():
    Pr = P()
    nc = Pr.nc
    xin = nc.dram_tensor("xin", [D, TL], F32, kind="ExternalInput").ap()
    w1s = nc.dram_tensor("l3f_w1s", [D, 1024], F32, kind="ExternalInput").ap()
    w3s = nc.dram_tensor("l3f_w3s", [D, 1024], F32, kind="ExternalInput").ap()
    w2s = nc.dram_tensor("l3f_w2s", [H_L3, 256], F32, kind="ExternalInput").ap()
    xout = nc.dram_tensor("xout", [D, TL], F32, kind="ExternalOutput").ap()
    Pr.make_pool()
    xr = [Pr.sb(f"xr{i}", [128, TL], F32) for i in range(NDT)]
    ht = [Pr.sb(f"l3ht{i}", [128, 512], BF16) for i in range(NHT)]
    xnb = [Pr.sb(f"xnb{i}", [128, TL], BF16) for i in range(NDT)]
    ctx = {
        "ht": ht,
        "w1st": [Pr.sb(f"w1st{i}", [128, 2048], BF16) for i in range(2)],
        "w3st": [Pr.sb(f"w3st{i}", [128, 2048], BF16) for i in range(2)],
        "w2st": [Pr.sb(f"w2st{i}", [128, 1024], BF16) for i in range(2)],
        "silt": Pr.sb("silt", [128, 2048], BF16),
        "wt_alias": ht,
    }
    nctx = make_norm_ctx(Pr, sq_alias=[ht[i] for i in range(NDT)])
    Pr.ones_col_b(); Pr.ones_row_f()
    Pr.sconst("h128_bf", _hadamard128_bf())
    Pr.flush_consts()
    Pr._l3_wt = [ht[i][:, 0:256] for i in range(NHT)]
    Pr._l3_wtmp = ctx["silt"][:, 0:256]
    w1b, w3b, w2b, wready = emit_l3_ffn_prep(Pr, w1s, w3s, w2s)
    v_x = load_residual(Pr, xr, xin)
    with _blk(nc) as b:
        @b.scalar
        def _(s):
            s.wait_ge(Pr.ld.h, v_x)
            for i in range(NDT):
                s.copy(xnb[i][:, :], xr[i][:, :]).then_inc(Pr.sc.h, 1)
    xnb_ready = (Pr.sc, Pr.sc.inc(NDT))
    fin = emit_l3_ffn(Pr, ctx, xr, xnb, xnb_ready, w1b, w3b, w2b, wready,
                      (Pr.ld, v_x))
    fin2 = emit_norm(Pr, nctx, xr, xnb, None, None, subtract_mean=True,
                     pre_waits=fin)
    store_residual(Pr, xr, xout, [fin2])
    Pr.close_block()
    return Pr


def run_l3_ffn(x, W1, W2, W3):
    Pr = build_l3_ffn_prog()
    xs = _shard_x(x)
    w1T = np.ascontiguousarray(W1.T)
    w3T = np.ascontiguousarray(W3.T)
    w2T = np.ascontiguousarray(W2.T)
    in_maps = [{"xin": xs[c],
                "l3f_w1s": np.ascontiguousarray(w1T[:, 1024 * c:1024 * (c + 1)]),
                "l3f_w3s": np.ascontiguousarray(w3T[:, 1024 * c:1024 * (c + 1)]),
                "l3f_w2s": np.ascontiguousarray(w2T[:, 256 * c:256 * (c + 1)])}
               for c in range(NCORES)]
    res = run_bass_kernel_spmd(Pr.nc, in_maps, core_ids=list(range(NCORES)))
    return _unshard_x([res.results[c]["xout"] for c in range(NCORES)])


def build_norm_prog():
    Pr = P()
    nc = Pr.nc
    xin = nc.dram_tensor("xin", [D, TL], F32, kind="ExternalInput").ap()
    xout = nc.dram_tensor("xout", [D, TL], F32, kind="ExternalOutput").ap()
    nout = nc.dram_tensor("nout", [D, TL], F32, kind="ExternalOutput").ap()
    xr = [Pr.sb(f"xr{i}", [128, TL], F32) for i in range(NDT)]
    xnb = [Pr.sb(f"xnb{i}", [128, TL], BF16) for i in range(NDT)]
    nb32 = [Pr.sb(f"nb32{i}", [128, TL], F32) for i in range(NDT)]
    nctx = make_norm_ctx(Pr)
    Pr.ones_col_b(); Pr.ones_row_f()
    Pr.flush_consts()
    v_x = load_residual(Pr, xr, xin)
    fin = emit_norm(Pr, nctx, xr, xnb, None, None, subtract_mean=True,
                    pre_waits=[(Pr.ld, v_x)])
    with _blk(nc) as b:
        @b.vector
        def _(v):
            v.wait_ge(fin[0].h, fin[1])
            for i in range(NDT):
                v.tensor_copy(nb32[i][:, :], xnb[i][:, :]).then_inc(Pr.ve.h, 1)
    vc = Pr.ve.inc(NDT)
    store_residual(Pr, xr, xout, [(Pr.ve, vc)])
    with _blk(nc) as b:
        @b.sync
        def _(s):
            for i in range(NDT):
                s.dma_start(out=nout[128 * i:128 * (i + 1), :], in_=nb32[i][:, :]
                            ).then_inc(Pr.ld.h, 16)
            s.wait_ge(Pr.ld.h, Pr.ld.inc(16 * NDT))
    Pr.close_block()
    return Pr


def run_norm(x):
    Pr = build_norm_prog()
    xs = _shard_x(x)
    res = run_bass_kernel_spmd(Pr.nc, [{"xin": xs[c]} for c in range(NCORES)],
                               core_ids=list(range(NCORES)))
    return (_unshard_x([res.results[c]["xout"] for c in range(NCORES)]),
            _unshard_x([res.results[c]["nout"] for c in range(NCORES)]))





H_L3 = 8192
NHT = H_L3 // 128


def emit_l3_ffn_prep(P_, w1s_ap, w3s_ap, w2s_ap):
    nc = P_.nc
    w1l = nc.dram_tensor("l3f_w1l", [D, 1024], BF16)
    w3l = nc.dram_tensor("l3f_w3l", [D, 1024], BF16)
    w2l = nc.dram_tensor("l3f_w2l", [H_L3, 256], BF16)
    w1b = nc.dram_tensor("l3f_w1b", [NCORES * D, 1024], BF16, addr_space="Shared")
    w3b = nc.dram_tensor("l3f_w3b", [NCORES * D, 1024], BF16, addr_space="Shared")
    w2b = nc.dram_tensor("l3f_w2b", [NCORES * H_L3, 256], BF16, addr_space="Shared")
    with _blk(nc) as b:
        @b.gpsimd
        def _(g):
            for j in range(2):
                g.dma_start(out=w1l.ap()[1024 * j:1024 * (j + 1), :],
                            in_=w1s_ap[1024 * j:1024 * (j + 1), :]).then_inc(P_.gd.h, 16)
                g.dma_start(out=w3l.ap()[1024 * j:1024 * (j + 1), :],
                            in_=w3s_ap[1024 * j:1024 * (j + 1), :]).then_inc(P_.gd.h, 16)
    v_cast13 = P_.gd.inc(64)
    wt = [w[:, 0:256] if hasattr(w, "shape") else w for w in P_._l3_wt]
    wtmp = P_._l3_wtmp
    h128 = P_.sconst("h128_bf", _hadamard128_bf())
    pw = P_.pool[6][:, 0:256]
    with _blk(nc) as b:
        @b.gpsimd
        def _(g):
            for i in range(NHT):
                g.dma_start(out=wt[i], in_=w2s_ap[128 * i:128 * (i + 1), :]
                            ).then_inc(P_.gd.h, 16)
    v_w2ld = P_.gd.inc(16 * NHT)
    v0 = P_.ve.n
    with _blk(nc) as b:
        @b.vector
        def _(v):
            v.wait_ge(P_.gd.h, v_w2ld)
            for s in range(6):
                step = 1 << s
                for a in range(NHT):
                    if a & step:
                        continue
                    A, B = wt[a], wt[a ^ step]
                    v.tensor_copy(wtmp, A)
                    v.tensor_add(A, wtmp, B)
                    v.tensor_sub(B, wtmp, B)
            v.engine_nop().then_inc(P_.ve.h, 1)
    v_bfly = P_.ve.inc(1)
    for i in range(NHT):
        v_prev_ev = P_.sc.n
        with _blk(nc) as b:
            @b.tensor
            def _(t, i=i, v_prev_ev=v_prev_ev):
                t.wait_ge(P_.ve.h, v_bfly)
                t.wait_ge(P_.ld.h, P_.v_consts)
                if i >= 1:
                    t.wait_ge(P_.sc.h, v_prev_ev)
                t.matmul(pw, h128[:, :], wt[i], start=True, stop=True
                         ).then_inc(P_.pe.h, 1)
        v_mm = P_.pe.inc(1)
        with _blk(nc) as b:
            @b.scalar
            def _(s, i=i, v_mm=v_mm):
                s.wait_ge(P_.pe.h, v_mm)
                s.copy(wt[i], pw).then_inc(P_.sc.h, 1)
        v_ev = P_.sc.inc(1)
        with _blk(nc) as b:
            @b.gpsimd
            def _(g, i=i, v_ev=v_ev):
                g.wait_ge(P_.sc.h, v_ev)
                g.dma_start(out=w2l.ap()[128 * i:128 * (i + 1), :], in_=wt[i]
                            ).then_inc(P_.gd.h, 16)
    v_w2st = P_.gd.n
    with _blk(nc) as b:
        @b.gpsimd
        def _(g):
            g.wait_ge(P_.gd.h, v_cast13)
            g.collective_compute("AllGather", AL.bypass, replica_groups=RG,
                                 ins=[w1l.ap()], outs=[w1b.ap()]).then_inc(P_.cc.h)
            g.collective_compute("AllGather", AL.bypass, replica_groups=RG,
                                 ins=[w3l.ap()], outs=[w3b.ap()]).then_inc(P_.cc.h)
            g.wait_ge(P_.gd.h, v_w2st)
            g.collective_compute("AllGather", AL.bypass, replica_groups=RG,
                                 ins=[w2l.ap()], outs=[w2b.ap()]).then_inc(P_.cc.h)
    ready = (P_.cc, P_.cc.inc(3))
    return w1b, w3b, w2b, ready


def _hadamard128_bf():
    h = np.array([[1.0]], np.float32)
    while h.shape[0] < 128:
        h = np.block([[h, h], [h, -h]])
    return (h * (2.0 ** -7)).astype(bf16dt())


def _w1b_tile_ap(wb, i, hc, hw):
    r = hc // 1024
    off = hc - 1024 * r
    assert off + hw <= 1024
    return wb.ap()[r * D + 128 * i: r * D + 128 * (i + 1), off:off + hw]


def emit_l3_ffn(P_, ctx, xr, xnb, xnb_ready, w1b, w3b, w2b, wready, v_x):
    nc = P_.nc
    ht = ctx["ht"]
    w1st = ctx["w1st"]
    w3st = ctx["w3st"]
    w2st = ctx["w2st"]
    silt = ctx["silt"]
    pool = P_.pool
    SQ2 = float(np.sqrt(2.0))

    for tc in range(2):
        for g_ in range(16):
            hc = 512 * g_
            for qt in range(4):
                hb = 4 * g_ + qt
                bufi = hb % 2
                v_war = ctx.get(("l3mm", tc, hb - 2))
                with _blk(nc) as b:
                    @b.sync
                    def _(s, hc=hc, qt=qt, bufi=bufi, v_war=v_war):
                        if v_war is None:
                            s.wait_ge(P_.cc.h, wready[1])
                        else:
                            s.wait_ge(P_.pe.h, v_war)
                        r, off = hc // 1024, hc % 1024
                        d0 = 512 * qt
                        src1 = w1b.ap()[r * D + d0: r * D + d0 + 512, off:off + 512]
                        src3 = w3b.ap()[r * D + d0: r * D + d0 + 512, off:off + 512]
                        s.dma_start(out=w1st[bufi][:, :].rearrange("p (i f) -> p i f", i=4),
                                    in_=src1.rearrange("(i p) f -> p i f", p=128)
                                    ).then_inc(P_.ld.h, 16)
                        s.dma_start(out=w3st[bufi][:, :].rearrange("p (i f) -> p i f", i=4),
                                    in_=src3.rearrange("(i p) f -> p i f", p=128)
                                    ).then_inc(P_.ld.h, 16)
                v_ld = P_.ld.inc(32)
                ev_g = ctx.get(("l3ev", tc, g_ - 1))
                with _blk(nc) as b:
                    @b.tensor
                    def _(t, g_=g_, qt=qt, bufi=bufi, v_ld=v_ld, tc=tc, ev_g=ev_g):
                        t.wait_ge(P_.ld.h, v_ld)
                        if qt == 0 and g_ == 0:
                            t.wait_ge(xnb_ready[0].h, xnb_ready[1])
                            if tc == 1 and ctx.get(("l3dn_war", 0)) is not None:
                                t.wait_ge(P_.ve.h, ctx[("l3dn_war", 0)])
                        if qt == 0 and ev_g is not None:
                            t.wait_ge(P_.ve.h, ev_g)
                        for di in range(4):
                            i = 4 * qt + di
                            for j in range(4):
                                t.matmul(pool[j][:, :],
                                         w1st[bufi][:, 512 * di + 128 * j:512 * di + 128 * (j + 1)],
                                         xnb[i][:, 512 * tc:512 * (tc + 1)],
                                         start=(i == 0), stop=(i == NDT - 1))
                                ins = t.matmul(pool[4 + j][:, :],
                                               w3st[bufi][:, 512 * di + 128 * j:512 * di + 128 * (j + 1)],
                                               xnb[i][:, 512 * tc:512 * (tc + 1)],
                                               start=(i == 0), stop=(i == NDT - 1))
                        ins.then_inc(P_.pe.h, 1)
                ctx[("l3mm", tc, hb)] = P_.pe.inc(1)
            v_mm = P_.pe.n
            sc0 = P_.sc.n
            ev_prev = ctx.get(("l3ev", tc, g_ - 1))
            with _blk(nc) as b:
                @b.scalar
                def _(s, v_mm=v_mm, ev_prev=ev_prev):
                    s.wait_ge(P_.pe.h, v_mm)
                    if ev_prev is not None:
                        s.wait_ge(P_.ve.h, ev_prev)
                    for j in range(4):
                        s.activation(silt[:, 512 * j:512 * (j + 1)],
                                     pool[j][:, :], AF.Silu).then_inc(P_.sc.h, 1)
            P_.sc.inc(4)
            with _blk(nc) as b:
                @b.vector
                def _(v, g_=g_, sc0=sc0):
                    for j in range(4):
                        v.wait_ge(P_.sc.h, sc0 + j + 1)
                        v.tensor_tensor(ht[4 * g_ + j][:, :],
                                        silt[:, 512 * j:512 * (j + 1)],
                                        pool[4 + j][:, :], AL.mult).then_inc(P_.ve.h, 1)
            ctx[("l3ev", tc, g_)] = P_.ve.inc(4)
        v_h_all = P_.ve.n

        for oh in range(2):
            for hi in range(NHT):
                bufi = hi % 2
                v_war = ctx.get(("l3dld", tc, oh, hi - 2))
                with _blk(nc) as b:
                    @b.sync
                    def _(s, oh=oh, hi=hi, bufi=bufi, v_war=v_war):
                        s.wait_ge(P_.cc.h, wready[1])
                        if v_war is not None:
                            s.wait_ge(P_.pe.h, v_war)
                        for rr in range(4):
                            r = 4 * oh + rr
                            s.dma_start(out=w2st[bufi][:, 256 * rr:256 * (rr + 1)],
                                        in_=w2b.ap()[r * H_L3 + 128 * hi:
                                                     r * H_L3 + 128 * (hi + 1), :]
                                        ).then_inc(P_.ld.h, 16)
                v_ld = P_.ld.inc(64)
                with _blk(nc) as b:
                    @b.tensor
                    def _(t, oh=oh, hi=hi, bufi=bufi, v_ld=v_ld, v_h_all=v_h_all, tc=tc):
                        t.wait_ge(P_.ld.h, v_ld)
                        if hi == 0:
                            t.wait_ge(P_.ve.h, v_h_all)
                            if oh == 1:
                                t.wait_ge(P_.ve.h, ctx[("l3dn_war", tc)])
                        for oc in range(8):
                            ins = t.matmul(pool[oc][:, :],
                                           w2st[bufi][:, 128 * oc:128 * (oc + 1)],
                                           ht[hi][:, :],
                                           start=(hi == 0), stop=(hi == NHT - 1))
                        ins.then_inc(P_.pe.h, 1)
                ctx[("l3dld", tc, oh, hi)] = P_.pe.inc(1)
            v_dn = P_.pe.n
            with _blk(nc) as b:
                @b.vector
                def _(v, oh=oh, tc=tc, v_dn=v_dn):
                    v.wait_ge(P_.pe.h, v_dn)
                    if oh == 0 and tc == 0:
                        v.wait_ge(v_x[0].h, v_x[1])
                    for oc in range(8):
                        i = 8 * oh + oc
                        v.scalar_tensor_tensor(xr[i][:, 512 * tc:512 * (tc + 1)],
                                               pool[oc][:, :], SQ2,
                                               xr[i][:, 512 * tc:512 * (tc + 1)],
                                               AL.mult, AL.add).then_inc(P_.ve.h, 1)
            ctx[("l3dn_war", tc)] = P_.ve.inc(8)
    return [(P_.ve, P_.ve.n)]





def emit_l0_ffn_prep(P_, w1s, w3s, w2s, ws1s, ws3s, ws2s):
    nc = P_.nc
    mk = lambda n, sh: nc.dram_tensor(n, sh, BF16)
    w1l, w3l = mk("l0f_w1l", [D, 1024]), mk("l0f_w3l", [D, 1024])
    w2l = mk("l0f_w2l", [1024, D])
    ws1l, ws3l = mk("l0f_ws1l", [256, 1024]), mk("l0f_ws3l", [256, 1024])
    ws2l = mk("l0f_ws2l", [128, D])
    sh = lambda n, sh_: nc.dram_tensor(n, sh_, BF16, addr_space="Shared")
    w1b, w3b = sh("l0f_w1b", [NCORES * D, 1024]), sh("l0f_w3b", [NCORES * D, 1024])
    w2b = sh("l0f_w2b", [NCORES * 1024, D])
    ws1b, ws3b = sh("l0f_ws1b", [D, 1024]), sh("l0f_ws3b", [D, 1024])
    ws2b = sh("l0f_ws2b", [1024, D])
    with _blk(nc) as b:
        @b.gpsimd
        def _(g):
            for (dst, srcp, n) in ((w1l, w1s, 2), (w3l, w3s, 2), (w2l, w2s, 1),
                                   (ws1l, ws1s, 1), (ws3l, ws3s, 1), (ws2l, ws2s, 1)):
                rows = dst.shape[0] // n
                for j in range(n):
                    g.dma_start(out=dst.ap()[rows * j:rows * (j + 1), :],
                                in_=srcp[rows * j:rows * (j + 1), :]).then_inc(P_.gd.h, 16)
    v_cast = P_.gd.inc(16 * 8)
    with _blk(nc) as b:
        @b.gpsimd
        def _(g):
            g.wait_ge(P_.gd.h, v_cast)
            for (i_, o_) in ((w1l, w1b), (w3l, w3b), (w2l, w2b), (ws1l, ws1b),
                             (ws3l, ws3b), (ws2l, ws2b)):
                g.collective_compute("AllGather", AL.bypass, replica_groups=RG,
                                     ins=[i_.ap()], outs=[o_.ap()]).then_inc(P_.cc.h)
    ready = (P_.cc, P_.cc.inc(6))
    return (w1b, w3b, w2b, ws1b, ws3b, ws2b), ready


def emit_l0_ffn(P_, ctx, xr, xnb, xnb_ready, gateT_ap, wbs, wready, v_x):
    nc = P_.nc
    (w1b, w3b, w2b, ws1b, ws3b, ws2b) = wbs
    he = ctx["he"]
    w13st = ctx["w13st"]
    w2st = ctx["w2st"]
    silt = ctx["silt"]
    silt2 = ctx["silt2"]
    gsb = ctx["gsb"]
    wE = ctx["wE"]
    wexp = ctx["wexp"]
    pool = P_.pool
    ident = P_.ident_b()

    gstage = ctx["gstage"]
    with _blk(nc) as b:
        @b.sync
        def _(s):
            s.dma_start(out=gstage[:, :].rearrange("p (i f) -> p i f", i=NDT),
                        in_=gateT_ap.rearrange("(i p) f -> p i f", p=128)
                        ).then_inc(P_.ld.h, 16)
    v_gld = P_.ld.inc(16)

    for tt in range(8):
        v_prev = ctx.get(("gts", tt - 1))
        with _blk(nc) as b:
            @b.tensor
            def _(t, tt=tt, v_prev=v_prev):
                t.wait_ge(P_.ld.h, v_gld)
                t.wait_ge(v_x[0].h, v_x[1])
                if v_prev is not None:
                    t.wait_ge(P_.ve.h, v_prev)
                if tt > 0:
                    t.wait_ge(P_.sc.h, ctx[("gtev", tt - 1)])
                for i in range(NDT):
                    ins = t.matmul(pool[6][:, 0:8], xr[i][:, 128 * tt:128 * (tt + 1)],
                                   gstage[:, 8 * i:8 * (i + 1)],
                                   start=(i == 0), stop=(i == NDT - 1))
                ins.then_inc(P_.pe.h, 1)
        v_sc = P_.pe.inc(1)
        with _blk(nc) as b:
            @b.scalar
            def _(s, tt=tt, v_sc=v_sc):
                s.wait_ge(P_.pe.h, v_sc)
                s.activation(gsb[:, 0:8], pool[6][:, 0:8], AF.Sigmoid).then_inc(P_.sc.h, 1)
        v_sg = P_.sc.inc(1)
        with _blk(nc) as b:
            @b.vector
            def _(v, tt=tt, v_sg=v_sg):
                b0 = P_.ve.n
                v.wait_ge(P_.sc.h, v_sg)
                v.reduce_max(gsb[:, 8:9], gsb[:, 0:8], AX.X).then_inc(P_.ve.h, 1)
                v.wait_ge(P_.ve.h, b0 + 1)
                v.tensor_scalar(gsb[:, 16:24], gsb[:, 0:8], gsb[:, 8:9], None,
                                AL.is_equal).then_inc(P_.ve.h, 1)
                v.wait_ge(P_.ve.h, b0 + 2)
                v.scalar_tensor_tensor(gsb[:, 16:24], gsb[:, 16:24], -1e30, gsb[:, 0:8],
                                       AL.mult, AL.add).then_inc(P_.ve.h, 1)
                v.wait_ge(P_.ve.h, b0 + 3)
                v.reduce_max(gsb[:, 9:10], gsb[:, 16:24], AX.X).then_inc(P_.ve.h, 1)
                v.wait_ge(P_.ve.h, b0 + 4)
                v.tensor_scalar(gsb[:, 16:24], gsb[:, 0:8], gsb[:, 9:10], None,
                                AL.is_ge).then_inc(P_.ve.h, 1)
                v.wait_ge(P_.ve.h, b0 + 5)
                v.tensor_tensor(gsb[:, 24:32], gsb[:, 0:8], gsb[:, 16:24], AL.mult
                                ).then_inc(P_.ve.h, 1)
        v_w = P_.ve.inc(6)
        ctx[("gts", tt)] = v_w
        with _blk(nc) as b:
            @b.tensor
            def _(t, tt=tt, v_w=v_w):
                t.wait_ge(P_.ve.h, v_w)
                if tt > 0:
                    t.wait_ge(P_.sc.h, ctx[("gtev", tt - 1)])
                t.transpose(pool[6][0:8, 0:128], gsb[:, 24:32],
                            ctx["identf"][:, :]).then_inc(P_.pe.h, 1)
        v_tr = P_.pe.inc(1)
        with _blk(nc) as b:
            @b.scalar
            def _(s, tt=tt, v_tr=v_tr):
                s.wait_ge(P_.pe.h, v_tr)
                s.copy(wE[0:8, 128 * tt:128 * (tt + 1)], pool[6][0:8, 0:128]
                       ).then_inc(P_.sc.h, 1)
        ctx[("gtev", tt)] = P_.sc.inc(1)
    v_wE = P_.sc.n

    selc = P_.sconst("l0f_sel", _sel8_const())

    fin_tc = []
    for tc in range(2):
        sl = slice(512 * tc, 512 * (tc + 1))
        for e in range(8):
            with _blk(nc) as b:
                @b.tensor
                def _(t, e=e, tc=tc):
                    t.wait_ge(P_.sc.h, v_wE)
                    t.wait_ge(P_.ld.h, P_.v_consts)
                    if ctx.get(("wexpv", tc, e - 1)) is not None:
                        t.wait_ge(P_.sc.h, ctx[("wexpv", tc, e - 1)])
                    if tc == 1 and e == 0:
                        t.wait_ge(P_.ve.h, ctx[("l0dn", 0)])
                    t.matmul(pool[7][:, 0:512], selc[:, 128 * e:128 * (e + 1)],
                             wE[0:8, sl], start=True, stop=True).then_inc(P_.pe.h, 1)
            v_m = P_.pe.inc(1)
            with _blk(nc) as b:
                @b.scalar
                def _(s, e=e, v_m=v_m):
                    s.wait_ge(P_.pe.h, v_m)
                    s.copy(wexp[e][:, :], pool[7][:, 0:512]).then_inc(P_.sc.h, 1)
            ctx[("wexpv", tc, e)] = P_.sc.inc(1)
        v_wexp = P_.sc.n

        for e in range(9):
            for hs in range(8):
                k = e * 8 + hs
                pg = pool[(k % 2) * 2]
                pu = pool[(k % 2) * 2 + 1]
                v_war = ctx.get(("l0ev", tc, k - 2))
                with _blk(nc) as b:
                    @b.sync
                    def _(s, e=e, hs=hs, v_war=v_war, tc=tc):
                        s.wait_ge(P_.cc.h, wready[1])
                        if ctx.get(("l0ld", tc, e * 8 + hs - 1)) is not None:
                            s.wait_ge(P_.pe.h, ctx[("l0ld", tc, e * 8 + hs - 1)])
                        if e < 8:
                            s1 = w1b.ap()[e * D: (e + 1) * D, 128 * hs:128 * (hs + 1)]
                            s3 = w3b.ap()[e * D: (e + 1) * D, 128 * hs:128 * (hs + 1)]
                        else:
                            s1 = ws1b.ap()[:, 128 * hs:128 * (hs + 1)]
                            s3 = ws3b.ap()[:, 128 * hs:128 * (hs + 1)]
                        s.dma_start(out=w13st[0][:, :].rearrange("p (i f) -> p i f", i=NDT),
                                    in_=s1.rearrange("(i p) f -> p i f", p=128)
                                    ).then_inc(P_.ld.h, 16)
                        s.dma_start(out=w13st[1][:, :].rearrange("p (i f) -> p i f", i=NDT),
                                    in_=s3.rearrange("(i p) f -> p i f", p=128)
                                    ).then_inc(P_.ld.h, 16)
                v_ld = P_.ld.inc(32)
                with _blk(nc) as b:
                    @b.tensor
                    def _(t, e=e, hs=hs, v_ld=v_ld, v_war=v_war, tc=tc, pg=pg, pu=pu):
                        t.wait_ge(P_.ld.h, v_ld)
                        if v_war is not None:
                            t.wait_ge(P_.ve.h, v_war)
                        if e == 0 and hs == 0:
                            t.wait_ge(xnb_ready[0].h, xnb_ready[1])
                        if tc == 1 and e * 8 + hs < 2:
                            t.wait_ge(P_.ve.h, ctx[("l0dn", 0)])
                        for i in range(NDT):
                            t.matmul(pg[:, :], w13st[0][:, 128 * i:128 * (i + 1)],
                                     xnb[i][:, sl], start=(i == 0), stop=(i == NDT - 1))
                            ins = t.matmul(pu[:, :], w13st[1][:, 128 * i:128 * (i + 1)],
                                           xnb[i][:, sl], start=(i == 0), stop=(i == NDT - 1))
                        ins.then_inc(P_.pe.h, 1)
                ctx[("l0ld", tc, e * 8 + hs)] = P_.pe.inc(1)
                v_mm = P_.pe.n
                sc_a = P_.sc.n
                with _blk(nc) as b:
                    @b.vector
                    def _(v, e=e, hs=hs, v_mm=v_mm, tc=tc, pg=pg, pu=pu, sc_a=sc_a):
                        v.wait_ge(P_.pe.h, v_mm)
                        if e < 8:
                            v.wait_ge(P_.sc.h, ctx[("wexpv", tc, e)])
                            v.tensor_tensor(silt2[:, :], pg[:, :], wexp[e][:, :], AL.mult
                                            ).then_inc(P_.ve.h, 1)
                        else:
                            v.engine_nop().then_inc(P_.ve.h, 1)
                va = P_.ve.inc(1)
                with _blk(nc) as b:
                    @b.scalar
                    def _(s, e=e, hs=hs, va=va, v_mm=v_mm, pg=pg):
                        s.wait_ge(P_.ve.h, va)
                        if e < 8:
                            s.activation(silt[:, :], silt2[:, :], AF.Silu).then_inc(P_.sc.h, 1)
                        else:
                            s.activation(silt[:, :], pg[:, :], AF.Silu).then_inc(P_.sc.h, 1)
                v_sil = P_.sc.inc(1)
                with _blk(nc) as b:
                    @b.vector
                    def _(v, e=e, hs=hs, v_sil=v_sil, tc=tc, pu=pu):
                        v.wait_ge(P_.sc.h, v_sil)
                        if e < 8:
                            v.tensor_tensor(silt2[:, :], pu[:, :], wexp[e][:, :], AL.mult)
                            v.tensor_tensor(he[e * 8 + hs][:, :], silt[:, :], silt2[:, :],
                                            AL.mult).then_inc(P_.ve.h, 2)
                        else:
                            v.tensor_tensor(he[64 + hs][:, :], silt[:, :], pu[:, :],
                                            AL.mult).then_inc(P_.ve.h, 2)
                ctx[("l0ev", tc, k)] = P_.ve.inc(2)
        v_he = P_.ve.n

        for oh in range(2):
            for e in range(9):
                for hs in range(8):
                    ki = e * 8 + hs
                    bufi = ki % 2
                    v_war = ctx.get(("l0dld", tc, oh, ki - 2))
                    with _blk(nc) as b:
                        @b.sync
                        def _(s, oh=oh, e=e, hs=hs, bufi=bufi, v_war=v_war):
                            if v_war is not None:
                                s.wait_ge(P_.pe.h, v_war)
                            if e < 8:
                                src = w2b.ap()[e * 1024 + 128 * hs: e * 1024 + 128 * (hs + 1),
                                               1024 * oh:1024 * (oh + 1)]
                            else:
                                src = ws2b.ap()[128 * hs:128 * (hs + 1),
                                                1024 * oh:1024 * (oh + 1)]
                            s.dma_start(out=w2st[bufi][:, :], in_=src).then_inc(P_.ld.h, 16)
                    v_ld = P_.ld.inc(16)
                    with _blk(nc) as b:
                        @b.tensor
                        def _(t, oh=oh, e=e, hs=hs, bufi=bufi, v_ld=v_ld, v_he=v_he, tc=tc):
                            t.wait_ge(P_.ld.h, v_ld)
                            ki = e * 8 + hs
                            if ki == 0:
                                t.wait_ge(P_.ve.h, v_he)
                                if oh == 1:
                                    t.wait_ge(P_.ve.h, ctx[("l0dn", tc)])
                            for oc in range(8):
                                ins = t.matmul(pool[oc][:, :],
                                               w2st[bufi][:, 128 * oc:128 * (oc + 1)],
                                               he[ki][:, :],
                                               start=(ki == 0), stop=(ki == 71))
                            ins.then_inc(P_.pe.h, 1)
                    ctx[("l0dld", tc, oh, ki)] = P_.pe.inc(1)
            v_dn = P_.pe.n
            with _blk(nc) as b:
                @b.vector
                def _(v, oh=oh, tc=tc, v_dn=v_dn):
                    v.wait_ge(P_.pe.h, v_dn)
                    for oc in range(8):
                        i = 8 * oh + oc
                        v.tensor_add(xr[i][:, sl], xr[i][:, sl], pool[oc][:, :]
                                     ).then_inc(P_.ve.h, 1)
            ctx[("l0dn", tc)] = P_.ve.inc(8)
        fin_tc.append(P_.ve.n)
    return [(P_.ve, P_.ve.n)]


def _sel8_const():
    m = np.zeros((8, 8 * 128), np.float32)
    for e in range(8):
        m[e, 128 * e:128 * (e + 1)] = 1.0
    return m.astype(bf16dt())




def build_l0_ffn_prog(dbg=False):
    Pr = P()
    nc = Pr.nc
    xin = nc.dram_tensor("xin", [D, TL], F32, kind="ExternalInput").ap()
    gateT = nc.dram_tensor("l0f_gateT", [D, 8], F32, kind="ExternalInput").ap()
    w1s = nc.dram_tensor("l0f_w1s", [D, 1024], F32, kind="ExternalInput").ap()
    w3s = nc.dram_tensor("l0f_w3s", [D, 1024], F32, kind="ExternalInput").ap()
    w2s = nc.dram_tensor("l0f_w2s", [1024, D], F32, kind="ExternalInput").ap()
    ws1s = nc.dram_tensor("l0f_ws1s", [256, 1024], F32, kind="ExternalInput").ap()
    ws3s = nc.dram_tensor("l0f_ws3s", [256, 1024], F32, kind="ExternalInput").ap()
    ws2s = nc.dram_tensor("l0f_ws2s", [128, D], F32, kind="ExternalInput").ap()
    xout = nc.dram_tensor("xout", [D, TL], F32, kind="ExternalOutput").ap()
    Pr.make_pool()
    xr = [Pr.sb(f"xr{i}", [128, TL], F32) for i in range(NDT)]
    he = [Pr.sb(f"he{i}", [128, 512], BF16) for i in range(72)]
    xnb = [Pr.sb(f"xnb{i}", [128, TL], BF16) for i in range(NDT)]
    ctx = {
        "he": he,
        "w13st": [Pr.sb(f"w13st{i}", [128, 2048], BF16) for i in range(2)],
        "w2st": [Pr.sb(f"w2st{i}", [128, 1024], BF16) for i in range(2)],
        "silt": Pr.sb("silt", [128, 512], BF16),
        "silt2": Pr.sb("silt2", [128, 512], F32),
        "gsb": Pr.sb("gsb", [128, 32], F32),
        "wE": Pr.sb("wE", [8, TL], BF16),
        "wexp": [Pr.sb(f"wexp{i}", [128, 512], BF16) for i in range(8)],
        "gstage": Pr.sb("gstage", [128, 128], F32),
    }
    nctx = make_norm_ctx(Pr, sq_alias=[he[i] for i in range(NDT)],
                         atmp_alias=ctx["silt2"])
    ctx["identf"] = Pr.sconst("ident_f", np.eye(128, dtype=np.float32))
    Pr.sconst("l0f_sel", _sel8_const())
    Pr.ones_col_b(); Pr.ones_row_f()
    Pr.flush_consts()
    wbs, wready = emit_l0_ffn_prep(Pr, w1s, w3s, w2s, ws1s, ws3s, ws2s)
    v_x = load_residual(Pr, xr, xin)
    with _blk(nc) as b:
        @b.scalar
        def _(s):
            s.wait_ge(Pr.ld.h, v_x)
            for i in range(NDT):
                s.copy(xnb[i][:, :], xr[i][:, :]).then_inc(Pr.sc.h, 1)
    xnb_ready = (Pr.sc, Pr.sc.inc(NDT))
    fin = emit_l0_ffn(Pr, ctx, xr, xnb, xnb_ready, gateT, wbs, wready,
                      (Pr.ld, v_x))
    if dbg:
        dwe = nc.dram_tensor("dbg_wE", [8, TL], F32, kind="ExternalOutput").ap()
        dhe = nc.dram_tensor("dbg_he", [128, 512 * 4], F32, kind="ExternalOutput").ap()
        srcs = [ctx["he"][0], ctx["he"][64], ctx["wexp"][0]]
        with _blk(nc) as b:
            @b.vector
            def _(v):
                v.wait_ge(fin[0][0].h, fin[0][1])
                v.engine_nop().then_inc(Pr.ve.h, 1)
        vd = Pr.ve.inc(1)
        with _blk(nc) as b:
            @b.gpsimd
            def _(g):
                g.wait_ge(Pr.ve.h, vd)
                g.dma_start(out=dwe, in_=ctx["wE"][:, :]).then_inc(Pr.gd.h, 16)
                for i in range(3):
                    g.dma_start(out=dhe[:, 512 * i:512 * (i + 1)], in_=srcs[i][:, :]
                                ).then_inc(Pr.gd.h, 16)
                g.dma_start(out=dhe[:, 1536:2048], in_=xr[0][:, 0:512]).then_inc(Pr.gd.h, 16)
                g.wait_ge(Pr.gd.h, Pr.gd.inc(16 * 5))
    fin2 = emit_norm(Pr, nctx, xr, xnb, None, None, subtract_mean=True,
                     pre_waits=fin)
    store_residual(Pr, xr, xout, [fin2])
    Pr.close_block()
    return Pr


def run_l0_ffn(x, gate_w, w1, w2, w3, ws1, ws2, ws3, dbg=False):
    Pr = build_l0_ffn_prog(dbg)
    xs = _shard_x(x)
    w1T = np.ascontiguousarray(w1.transpose(0, 2, 1))
    w3T = np.ascontiguousarray(w3.transpose(0, 2, 1))
    w2T = np.ascontiguousarray(w2.transpose(0, 2, 1))
    ws1T = np.ascontiguousarray(ws1.T); ws3T = np.ascontiguousarray(ws3.T)
    ws2T = np.ascontiguousarray(ws2.T)
    gT = np.ascontiguousarray(gate_w.T)
    in_maps = [{"xin": xs[c], "l0f_gateT": gT,
                "l0f_w1s": w1T[c], "l0f_w3s": w3T[c], "l0f_w2s": w2T[c],
                "l0f_ws1s": np.ascontiguousarray(ws1T[256 * c:256 * (c + 1)]),
                "l0f_ws3s": np.ascontiguousarray(ws3T[256 * c:256 * (c + 1)]),
                "l0f_ws2s": np.ascontiguousarray(ws2T[128 * c:128 * (c + 1)])}
               for c in range(NCORES)]
    res = run_bass_kernel_spmd(Pr.nc, in_maps, core_ids=list(range(NCORES)))
    out = _unshard_x([res.results[c]["xout"] for c in range(NCORES)])
    if dbg:
        return out, [res.results[c] for c in range(NCORES)]
    return out





def emit_l3_attn(P_, ctx, xr, v_x, wq_ap, wk_ap, wv_ap, wo_ap, bo_ap):
    nc = P_.nc
    pool = P_.pool
    qsb, ksb, vTsb = ctx["qsb"], ctx["ksb"], ctx["vTsb"]
    xstage = ctx["xstage"]
    wqkvt = ctx["wqkvt"]
    wost = ctx["wost"]
    es = ctx["es"]
    cbuf = ctx["cbuf"]
    vtmp = ctx["vtmp"]
    crow = ctx["crow"]
    cfix = ctx["cfix"]
    identf = ctx["identf"]
    ones_b = P_.ones_col_b()
    ones_r = P_.ones_row_f()
    QKSCALE = float(1.0 / np.sqrt(128.0))

    xl = nc.dram_tensor("l3a_xl", [D, TL], BF16)
    xall = nc.dram_tensor("l3a_xall", [NCORES * D, TL], BF16, addr_space="Shared")
    cl = nc.dram_tensor("l3a_cl", [256, T], BF16)
    call = nc.dram_tensor("l3a_call", [NCORES * 256, T], BF16, addr_space="Shared")

    with _blk(nc) as b:
        @b.gpsimd
        def _(g):
            g.dma_start(out=wqkvt[:, 0:4096].rearrange("p (i f) -> p i f", i=NDT),
                        in_=wq_ap.rearrange("(i p) f -> p i f", p=128)).then_inc(P_.gd.h, 16)
            g.dma_start(out=wqkvt[:, 4096:6144].rearrange("p (i f) -> p i f", i=NDT),
                        in_=wk_ap.rearrange("(i p) f -> p i f", p=128)).then_inc(P_.gd.h, 16)
            g.dma_start(out=wqkvt[:, 6144:8192].rearrange("p (i f) -> p i f", i=NDT),
                        in_=wv_ap.rearrange("(i p) f -> p i f", p=128)).then_inc(P_.gd.h, 16)
            g.wait_ge(v_x[0].h, v_x[1])
            for i in range(NDT):
                g.dma_start(out=xl.ap()[128 * i:128 * (i + 1), :], in_=xr[i][:, :]
                            ).then_inc(P_.gd.h, 16)
    v_w = P_.gd.inc(48 + 16 * NDT)
    with _blk(nc) as b:
        @b.gpsimd
        def _(g):
            g.wait_ge(P_.gd.h, v_w)
            g.collective_compute("AllGather", AL.bypass, replica_groups=RG,
                                 ins=[xl.ap()], outs=[xall.ap()]).then_inc(P_.cc.h)
    v_ag = P_.cc.inc(1)

    for tg in range(16):
        r, tl0 = tg // 2, 512 * (tg % 2)
        v_war = ctx.get(("l3a_pj", tg - 1))
        with _blk(nc) as b:
            @b.sync
            def _(s, r=r, tl0=tl0, v_war=v_war):
                s.wait_ge(P_.cc.h, v_ag)
                if v_war is not None:
                    s.wait_ge(P_.pe.h, v_war)
                s.dma_start(out=xstage[:, :].rearrange("p (i f) -> p i f", i=NDT),
                            in_=xall.ap()[r * D:(r + 1) * D, tl0:tl0 + 512]
                            .rearrange("(i p) f -> p i f", p=128)).then_inc(P_.ld.h, 16)
        v_ld = P_.ld.inc(16)
        with _blk(nc) as b:
            @b.tensor
            def _(t, tg=tg, v_ld=v_ld):
                t.wait_ge(P_.ld.h, v_ld)
                t.wait_ge(P_.gd.h, v_w)
                if tg >= 1:
                    t.wait_ge(P_.sc.h, ctx[("l3a_vT", tg - 1)])
                for i in range(NDT):
                    st, sp = (i == 0), (i == NDT - 1)
                    t.matmul(pool[0][:, :], wqkvt[:, 256 * i:256 * i + 128],
                             xstage[:, 512 * i:512 * (i + 1)], start=st, stop=sp)
                    t.matmul(pool[1][:, :], wqkvt[:, 256 * i + 128:256 * i + 256],
                             xstage[:, 512 * i:512 * (i + 1)], start=st, stop=sp)
                    t.matmul(pool[2][:, :], wqkvt[:, 4096 + 128 * i:4096 + 128 * (i + 1)],
                             xstage[:, 512 * i:512 * (i + 1)], start=st, stop=sp)
                    ins = t.matmul(pool[3][:, :], wqkvt[:, 6144 + 128 * i:6144 + 128 * (i + 1)],
                                   xstage[:, 512 * i:512 * (i + 1)], start=st, stop=sp)
                ins.then_inc(P_.pe.h, 1)
        ctx[("l3a_pj", tg)] = P_.pe.inc(1)
        v_mm = P_.pe.n
        with _blk(nc) as b:
            @b.scalar
            def _(s, tg=tg, v_mm=v_mm):
                s.wait_ge(P_.pe.h, v_mm)
                sl = slice(512 * tg, 512 * (tg + 1))
                s.copy(qsb[0][:, sl], pool[0][:, :]).then_inc(P_.sc.h, 1)
                s.copy(qsb[1][:, sl], pool[1][:, :]).then_inc(P_.sc.h, 1)
                s.copy(ksb[:, sl], pool[2][:, :]).then_inc(P_.sc.h, 1)
                s.copy(vtmp[:, :], pool[3][:, :]).then_inc(P_.sc.h, 1)
        v_ev = P_.sc.inc(4)
        with _blk(nc) as b:
            @b.tensor
            def _(t, tg=tg, v_ev=v_ev):
                t.wait_ge(P_.sc.h, v_ev)
                t.wait_ge(P_.ld.h, P_.v_consts)
                if tg >= 1:
                    t.wait_ge(P_.sc.h, ctx[("l3a_vT", tg - 1)])
                for j in range(4):
                    ins = t.transpose(pool[4][:, 128 * j:128 * (j + 1)],
                                      vtmp[:, 128 * j:128 * (j + 1)], identf[:, :])
                ins.then_inc(P_.pe.h, 1)
        v_tr = P_.pe.inc(1)
        with _blk(nc) as b:
            @b.scalar
            def _(s, tg=tg, v_tr=v_tr):
                s.wait_ge(P_.pe.h, v_tr)
                for j in range(4):
                    kt = 4 * tg + j
                    s.copy(vTsb[:, 128 * kt:128 * (kt + 1)],
                           pool[4][:, 128 * j:128 * (j + 1)]).then_inc(P_.sc.h, 1)
        ctx[("l3a_vT", tg)] = P_.sc.inc(4)
    v_proj_all = P_.sc.n

    for h in range(2):
        for qc in range(16):
            nkt = 4 * qc + 4
            prev_norm = ctx.get("_l3a_prev_norm")
            for kt in range(nkt):
                diag = kt - 4 * qc
                ps_s = pool[6 + (kt % 2)]
                esb = es[kt % 2]
                v_esprev = ctx.get(("l3a_es", h, qc, kt - 2))
                with _blk(nc) as b:
                    @b.tensor
                    def _(t, h=h, qc=qc, kt=kt, ps_s=ps_s, v_esprev=v_esprev,
                          prev_norm=prev_norm):
                        t.wait_ge(P_.sc.h, v_proj_all)
                        if v_esprev is not None:
                            t.wait_ge(P_.sc.h, v_esprev[1] if isinstance(v_esprev, tuple) else v_esprev)
                        if kt <= 1 and prev_norm is not None:
                            t.wait_ge(P_.sc.h, prev_norm[0])
                            t.wait_ge(P_.ve.h, prev_norm[1])
                        t.matmul(ps_s[:, :], ksb[:, 128 * kt:128 * (kt + 1)],
                                 qsb[h][:, 512 * qc:512 * (qc + 1)],
                                 start=True, stop=True).then_inc(P_.pe.h, 1)
                v_s = P_.pe.inc(1)
                with _blk(nc) as b:
                    @b.scalar
                    def _(s, v_s=v_s, ps_s=ps_s, esb=esb):
                        s.wait_ge(P_.pe.h, v_s)
                        s.activation(esb[:, :], ps_s[:, :], AF.Exp, scale=QKSCALE
                                     ).then_inc(P_.sc.h, 1)
                v_e = P_.sc.inc(1)
                ctx[("l3a_es", h, qc, kt)] = v_e
                if diag >= 0:
                    with _blk(nc) as b:
                        @b.vector
                        def _(v, diag=diag, v_e=v_e, esb=esb):
                            v.wait_ge(P_.sc.h, v_e)
                            v.tensor_tensor(esb[:, :], esb[:, :],
                                            ctx["dmask"][:, 512 * diag:512 * (diag + 1)],
                                            AL.mult).then_inc(P_.ve.h, 1)
                    gate = ("ve", P_.ve.inc(1))
                else:
                    gate = ("sc", v_e)
                with _blk(nc) as b:
                    @b.tensor
                    def _(t, kt=kt, gate=gate, esb=esb, nkt=nkt):
                        t.wait_ge((P_.ve if gate[0] == "ve" else P_.sc).h, gate[1])
                        t.matmul(pool[4][:, :], vTsb[:, 128 * kt:128 * (kt + 1)],
                                 esb[:, :], start=(kt == 0), stop=(kt == nkt - 1))
                        ins = t.matmul(pool[5][0:1, :], ones_b[:, :], esb[:, :],
                                       start=(kt == 0), stop=(kt == nkt - 1))
                        if kt == nkt - 1:
                            ins.then_inc(P_.pe.h, 1)
                if kt == nkt - 1:
                    P_.pe.inc(1)
            v_pv = P_.pe.n
            with _blk(nc) as b:
                @b.vector
                def _(v, v_pv=v_pv):
                    v.wait_ge(P_.pe.h, v_pv)
                    v.reciprocal(crow[0:1, :], pool[5][0:1, :]).then_inc(P_.ve.h, 1)
            v_r = P_.ve.inc(1)
            with _blk(nc) as b:
                @b.tensor
                def _(t, v_r=v_r):
                    t.wait_ge(P_.ve.h, v_r)
                    t.matmul(pool[6][:, :], ones_r[:, :], crow[0:1, :],
                             start=True, stop=True).then_inc(P_.pe.h, 1)
            v_bc = P_.pe.inc(1)
            with _blk(nc) as b:
                @b.scalar
                def _(s, v_bc=v_bc):
                    s.wait_ge(P_.pe.h, v_bc)
                    s.copy(cfix[:, :], pool[4][:, :]).then_inc(P_.sc.h, 1)
            v_cp = P_.sc.inc(1)
            cb = cbuf[(h * 16 + qc) % 2]
            v_cbw = ctx.get(("l3a_cb", h * 16 + qc - 2))
            with _blk(nc) as b:
                @b.vector
                def _(v, v_cp=v_cp, cb=cb, v_cbw=v_cbw):
                    v.wait_ge(P_.sc.h, v_cp)
                    if v_cbw is not None:
                        v.wait_ge(P_.gd.h, v_cbw)
                    v.tensor_tensor(cb[:, :], cfix[:, :], pool[6][:, :], AL.mult
                                    ).then_inc(P_.ve.h, 1)
            v_ct = P_.ve.inc(1)
            with _blk(nc) as b:
                @b.gpsimd
                def _(g, h=h, qc=qc, v_ct=v_ct, cb=cb):
                    g.wait_ge(P_.ve.h, v_ct)
                    g.dma_start(out=cl.ap()[128 * h:128 * (h + 1),
                                            512 * qc:512 * (qc + 1)],
                                in_=cb[:, :]).then_inc(P_.gd.h, 16)
            ctx[("l3a_cb", h * 16 + qc)] = P_.gd.inc(16)
            ctx["_l3a_prev_norm"] = (v_cp, v_ct)
    v_attn = P_.ve.n
    v_cl = P_.gd.n
    with _blk(nc) as b:
        @b.gpsimd
        def _(g):
            g.wait_ge(P_.gd.h, v_cl)
            g.collective_compute("AllGather", AL.bypass, replica_groups=RG,
                                 ins=[cl.ap()], outs=[call.ap()]).then_inc(P_.cc.h)
    v_cag = P_.cc.inc(1)

    v_bo = P_.ld.inc(16)
    with _blk(nc) as b:
        @b.sync
        def _(s):
            s.dma_start(out=cfix[:, 0:16], in_=bo_ap[:, :]).then_inc(P_.ld.h, 16)
    v_preadd0 = P_.ve.n
    with _blk(nc) as b:
        @b.vector
        def _(v):
            v.wait_ge(P_.ld.h, v_bo)
            v.wait_ge(v_x[0].h, v_x[1])
            v.wait_ge(P_.ve.h, v_attn)
            for i in range(NDT):
                v.tensor_scalar(xr[i][:, :], xr[i][:, :], cfix[:, i:i + 1], None,
                                AL.add).then_inc(P_.ve.h, 1)
    v_preadd = P_.ve.inc(NDT)

    for tc in range(2):
        off = ctx["pid_off"][tc]
        with _blk(nc) as b:
            @b.sync
            def _(s, tc=tc, off=off):
                s.wait_ge(P_.cc.h, v_cag)
                if tc == 1:
                    s.wait_ge(P_.pe.h, ctx[("l3a_wo", 0, NDT - 1)])
                s.dma_start(out=xstage[:, :].rearrange("p (i f) -> p i f", i=NDT),
                            in_=call.ap().rearrange("(i p) t -> p i t", p=128)
                            [:, :, bass.ts(off, 512)]).then_inc(P_.ld.h, 16)
        v_cs = P_.ld.inc(16)
        for oc in range(NDT):
            v_warw = ctx.get(("l3a_wol", tc, oc - 2))
            with _blk(nc) as b:
                @b.gpsimd
                def _(g, tc=tc, oc=oc, v_warw=v_warw):
                    if v_warw is not None:
                        g.wait_ge(P_.pe.h, v_warw)
                    g.dma_start(out=wost[oc % 2][:, :].rearrange("p (i f) -> p i f", i=NDT),
                                in_=wo_ap[:, 128 * oc:128 * (oc + 1)]
                                .rearrange("(i p) f -> p i f", p=128)).then_inc(P_.gd.h, 16)
            v_ld = P_.gd.inc(16)
            pa = pool[oc % 2]
            with _blk(nc) as b:
                @b.tensor
                def _(t, tc=tc, oc=oc, v_ld=v_ld, pa=pa, v_cs=v_cs):
                    t.wait_ge(P_.gd.h, v_ld)
                    t.wait_ge(P_.ld.h, v_cs)
                    if oc <= 1:
                        t.wait_ge(P_.ve.h, v_preadd)
                        if tc == 1:
                            t.wait_ge(P_.ve.h, ctx[("l3a_woe", 0, NDT - 1 - (1 - oc))])
                    else:
                        t.wait_ge(P_.ve.h, ctx[("l3a_woe", tc, oc - 2)])
                    for i in range(NDT):
                        ins = t.matmul(pa[:, :], wost[oc % 2][:, 128 * i:128 * (i + 1)],
                                       xstage[:, 512 * i:512 * (i + 1)],
                                       start=(i == 0), stop=(i == NDT - 1))
                    ins.then_inc(P_.pe.h, 1)
            ctx[("l3a_wo", tc, oc)] = P_.pe.inc(1)
            ctx[("l3a_wol", tc, oc)] = P_.pe.n
            v_mm = P_.pe.n
            with _blk(nc) as b:
                @b.vector
                def _(v, tc=tc, oc=oc, v_mm=v_mm, pa=pa):
                    v.wait_ge(P_.pe.h, v_mm)
                    v.tensor_add(xr[oc][:, 512 * tc:512 * (tc + 1)],
                                 xr[oc][:, 512 * tc:512 * (tc + 1)], pa[:, :]
                                 ).then_inc(P_.ve.h, 1)
            ctx[("l3a_woe", tc, oc)] = P_.ve.inc(1)
    return [(P_.ve, P_.ve.n)], call




def build_l3_attn_prog():
    Pr = P()
    nc = Pr.nc
    xin = nc.dram_tensor("xin", [D, TL], F32, kind="ExternalInput").ap()
    wq = nc.dram_tensor("l3a_wq", [D, 256], F32, kind="ExternalInput").ap()
    wk = nc.dram_tensor("l3a_wk", [D, 128], F32, kind="ExternalInput").ap()
    wv = nc.dram_tensor("l3a_wv", [D, 128], F32, kind="ExternalInput").ap()
    wo = nc.dram_tensor("l3a_wo", [D, D], F32, kind="ExternalInput").ap()
    bo = nc.dram_tensor("l3a_bo", [128, 16], F32, kind="ExternalInput").ap()
    xout = nc.dram_tensor("xout", [D, TL], F32, kind="ExternalOutput").ap()
    Pr.make_pool()
    xr = [Pr.sb(f"xr{i}", [128, TL], F32) for i in range(NDT)]
    qsb = [Pr.sb(f"qsb{i}", [128, T], BF16) for i in range(2)]
    ksb = Pr.sb("ksb", [128, T], BF16)
    vTsb = Pr.sb("vTsb", [128, T], BF16)
    ctx = {
        "qsb": qsb, "ksb": ksb, "vTsb": vTsb,
        "xstage": Pr.sb("xstage", [128, 8192], BF16),
        "wqkvt": Pr.sb("wqkvt", [128, 8192], BF16),
        "wost": [Pr.sb(f"wost{i}", [128, 2048], BF16) for i in range(2)],
        "es": [Pr.sb(f"es{i}", [128, 512], BF16) for i in range(2)],
        "cbuf": [Pr.sb(f"cbuf{i}", [128, 512], BF16) for i in range(2)],
        "vtmp": Pr.sb("vtmp", [128, 512], F32),
        "cfix": Pr.sb("cfix", [128, 512], F32),
    }
    ctx["dmask"] = Pr.sconst("l3a_dmask", _l3_dmask())
    ctx["identf"] = Pr.sconst("ident_f", np.eye(128, dtype=np.float32))
    Pr.ones_col_b(); Pr.ones_row_f()
    Pr.flush_consts()
    sq_alias = [vTsb[:, 512 * i:512 * (i + 1)] for i in range(NDT)]
    xnb_alias = [qsb[i // 8][:, 1024 * (i % 8):1024 * (i % 8 + 1)] for i in range(NDT)]
    nctx = make_norm_ctx(Pr, sq_alias=sq_alias, atmp_alias=ctx["cfix"])
    ctx["crow"] = nctx["r_msq"]
    with _blk(nc) as b:
        @b.sync
        def _(s):
            pid = nc.partition_id()
            ctx["pid_off"] = [nc.snap(pid * 2 + tc, min_val=0,
                                      max_val=2 * NCORES - 1) for tc in range(2)]
    v_x = load_residual(Pr, xr, xin)
    fin, _call = emit_l3_attn(Pr, ctx, xr, (Pr.ld, v_x), wq, wk, wv, wo, bo)
    fin2 = emit_norm(Pr, nctx, xr, xnb_alias, None, None, subtract_mean=True,
                     pre_waits=fin)
    store_residual(Pr, xr, xout, [fin2])
    Pr.close_block()
    return Pr


def _l3_dmask():
    m = np.zeros((128, 4 * 512), np.float32)
    for r in range(4):
        for kk in range(128):
            for qq in range(512):
                if 128 * r + kk <= qq:
                    m[kk, 512 * r + qq] = 1.0
    return m.astype(bf16dt())


def run_l3_attn(x, Wq, Wk, Wv, Wo, bo):
    Pr = build_l3_attn_prog()
    xs = _shard_x(x)
    bo_l = np.ascontiguousarray(bo.reshape(16, 128).T)
    in_maps = []
    for c in range(NCORES):
        g = c // 2
        in_maps.append({
            "xin": xs[c],
            "l3a_wq": np.ascontiguousarray(Wq[:, 256 * c:256 * (c + 1)]),
            "l3a_wk": np.ascontiguousarray(Wk[:, 128 * g:128 * (g + 1)]),
            "l3a_wv": np.ascontiguousarray(Wv[:, 128 * g:128 * (g + 1)]),
            "l3a_wo": Wo, "l3a_bo": bo_l,
        })
    res = run_bass_kernel_spmd(Pr.nc, in_maps, core_ids=list(range(NCORES)))
    return _unshard_x([res.results[c]["xout"] for c in range(NCORES)])





L2CAP = 50.0
L2SCALE = float(256 ** -0.5)


def _l2_rope_tables():
    ROT, BASE = 128, 10000.0
    theta = 1.0 / BASE ** (np.arange(0, ROT, 2, dtype=np.float64) / ROT)
    pos = np.arange(T, dtype=np.float64)
    ang = pos[:, None] * theta[None, :]
    cos = np.cos(ang); sin = np.sin(ang)
    C = np.empty((ROT, T), np.float32); S = np.empty((ROT, T), np.float32)
    C[0::2] = cos.T; C[1::2] = cos.T
    S[0::2] = -sin.T; S[1::2] = sin.T
    return C.astype(bf16dt()), S.astype(bf16dt())


def _pswap():
    m = np.zeros((128, 128), np.float32)
    for i in range(64):
        m[2 * i, 2 * i + 1] = 1.0
        m[2 * i + 1, 2 * i] = 1.0
    return m


def _l2_wmask():
    m = np.zeros((128, 4 * 512), np.float32)
    for d_ in range(4):
        delta = 5 + d_
        for kk in range(128):
            for qq in range(512):
                if 128 * delta + qq - kk < 1024:
                    m[kk, 512 * d_ + qq] = 1.0
    return m.astype(bf16dt())


def emit_l2_attn(P_, ctx, xr, v_x, wq_ap, wk_ap, wv_ap, wo_ap, qs_ap, ks_ap):
    nc = P_.nc
    pool = P_.pool
    qbuf = ctx["qbuf"]
    kring = ctx["kring"]
    vring = ctx["vring"]
    xstage = ctx["xstage"]
    wqkvt = ctx["wqkvt"]
    wost = ctx["wost"]
    es = ctx["es"]
    cbuf = ctx["cbuf"]
    qtmp = ctx["qtmp"]
    vtmp = ctx["vtmp"]
    cst = ctx["cst"]
    scl = ctx["scl"]
    crow = ctx["crow"]
    cfix = ctx["cfix"]
    identf = ctx["identf"]
    pswap = ctx["pswap"]
    dmask, wmask = ctx["dmask"], ctx["wmask"]
    ropeC, ropeS = ctx["ropeC"], ctx["ropeS"]
    ones_b = P_.ones_col_b()
    ones_r = P_.ones_row_f()

    xl = nc.dram_tensor("l2a_xl", [D, TL], BF16)
    xall = nc.dram_tensor("l2a_xall", [NCORES * D, TL], BF16, addr_space="Shared")
    cl = nc.dram_tensor("l2a_cl", [256, T], BF16)
    call = nc.dram_tensor("l2a_call", [NCORES * 256, T], BF16, addr_space="Shared")

    with _blk(nc) as b:
        @b.gpsimd
        def _(g):
            for (dst0, src_, w_) in ((0, wq_ap, 256), (4096, wk_ap, 256), (8192, wv_ap, 256)):
                g.dma_start(out=wqkvt[:, dst0:dst0 + 4096]
                            .rearrange("p (i f) -> p i f", i=NDT),
                            in_=src_.rearrange("(i p) f -> p i f", p=128)
                            ).then_inc(P_.gd.h, 16)
            g.dma_start(out=scl[:, 0:2], in_=qs_ap).then_inc(P_.gd.h, 16)
            g.dma_start(out=scl[:, 2:4], in_=ks_ap).then_inc(P_.gd.h, 16)
            g.wait_ge(v_x[0].h, v_x[1])
            for i in range(NDT):
                g.dma_start(out=xl.ap()[128 * i:128 * (i + 1), :], in_=xr[i][:, :]
                            ).then_inc(P_.gd.h, 16)
    v_w = P_.gd.inc(16 * (5 + NDT))
    with _blk(nc) as b:
        @b.gpsimd
        def _(g):
            g.wait_ge(P_.gd.h, v_w)
            g.collective_compute("AllGather", AL.bypass, replica_groups=RG,
                                 ins=[xl.ap()], outs=[xall.ap()]).then_inc(P_.cc.h)
    v_ag = P_.cc.inc(1)

    for tg in range(16):
        r, tl0 = tg // 2, 512 * (tg % 2)
        sl_glob = slice(512 * tg, 512 * (tg + 1))
        v_warx = ctx.get(("l2a_pj", tg - 1))
        with _blk(nc) as b:
            @b.sync
            def _(s, r=r, tl0=tl0, tg=tg, v_warx=v_warx):
                s.wait_ge(P_.cc.h, v_ag)
                if v_warx is not None:
                    s.wait_ge(P_.pe.h, v_warx)
                kk = ctx.get(("l2a_kv", tg - 1))
                if kk is not None:
                    s.wait_ge(P_.ve.h, kk)
                s.dma_start(out=xstage[:, :].rearrange("p (i f) -> p i f", i=NDT),
                            in_=xall.ap()[r * D:(r + 1) * D, tl0:tl0 + 512]
                            .rearrange("(i p) f -> p i f", p=128)).then_inc(P_.ld.h, 16)
                s.dma_start(out=cst[:, 0:512], in_=ropeC[:, sl_glob]).then_inc(P_.ld.h, 16)
                s.dma_start(out=cst[:, 512:1024], in_=ropeS[:, sl_glob]).then_inc(P_.ld.h, 16)
        v_ld = P_.ld.inc(48)
        with _blk(nc) as b:
            @b.tensor
            def _(t, tg=tg, v_ld=v_ld):
                t.wait_ge(P_.ld.h, v_ld)
                t.wait_ge(P_.gd.h, v_w)
                k = ctx.get(("l2a_pje", tg - 1))
                if k is not None:
                    t.wait_ge(P_.sc.h, k[0])
                    t.wait_ge(P_.ve.h, k[1])
                for i in range(NDT):
                    st, sp = (i == 0), (i == NDT - 1)
                    for oc in range(2):
                        t.matmul(pool[oc][:, :], wqkvt[:, 256 * i + 128 * oc:256 * i + 128 * (oc + 1)],
                                 xstage[:, 512 * i:512 * (i + 1)], start=st, stop=sp)
                        t.matmul(pool[2 + oc][:, :], wqkvt[:, 4096 + 256 * i + 128 * oc:4096 + 256 * i + 128 * (oc + 1)],
                                 xstage[:, 512 * i:512 * (i + 1)], start=st, stop=sp)
                        ins = t.matmul(pool[4 + oc][:, :], wqkvt[:, 8192 + 256 * i + 128 * oc:8192 + 256 * i + 128 * (oc + 1)],
                                       xstage[:, 512 * i:512 * (i + 1)], start=st, stop=sp)
                ins.then_inc(P_.pe.h, 1)
        ctx[("l2a_pj", tg)] = P_.pe.inc(1)
        v_pj = P_.pe.n

        with _blk(nc) as b:
            @b.scalar
            def _(s, v_pj=v_pj, tg=tg):
                s.wait_ge(P_.pe.h, v_pj)
                for oc in range(2):
                    s.copy(qtmp[:, 512 * oc:512 * (oc + 1)], pool[oc][:, :]).then_inc(P_.sc.h, 1)
                s.copy(ctx["vtmp0"][:, :], pool[4][:, :]).then_inc(P_.sc.h, 1)
        v_qev = P_.sc.inc(3)
        sq = ctx["sqn"]
        with _blk(nc) as b:
            @b.scalar
            def _(s, v_qev=v_qev):
                s.wait_ge(P_.sc.h, v_qev)
                s.activation(sq[:, 0:512], qtmp[:, 0:512], AF.Square).then_inc(P_.sc.h, 1)
                s.activation(sq[:, 512:1024], qtmp[:, 512:1024], AF.Square).then_inc(P_.sc.h, 1)
        v_sq = P_.sc.inc(2)
        with _blk(nc) as b:
            @b.tensor
            def _(t, v_sq=v_sq, tg=tg):
                t.wait_ge(P_.sc.h, v_sq)
                t.wait_ge(P_.ld.h, P_.v_consts)
                k = ctx.get(("l2a_aev", tg - 1))
                if k is not None:
                    t.wait_ge(P_.ve.h, k)
                t.matmul(pool[6][0:1, :], ones_b[:, :], sq[:, 0:512], start=True, stop=False)
                t.matmul(pool[6][0:1, :], ones_b[:, :], sq[:, 512:1024], start=False,
                         stop=True).then_inc(P_.pe.h, 1)
        v_ss = P_.pe.inc(1)
        with _blk(nc) as b:
            @b.vector
            def _(v, v_ss=v_ss):
                v.wait_ge(P_.pe.h, v_ss)
                v.tensor_scalar(crow[0:1, :], pool[6][0:1, :], 1.0 / 256, 1e-6,
                                AL.mult, AL.add).then_inc(P_.ve.h, 1)
        v_cr = P_.ve.inc(1)
        with _blk(nc) as b:
            @b.scalar
            def _(s, v_cr=v_cr):
                s.wait_ge(P_.ve.h, v_cr)
                s.activation(crow[0:1, :], crow[0:1, :], AF.Ln)
                s.activation(crow[0:1, :], crow[0:1, :], AF.Exp, scale=-0.5
                             ).then_inc(P_.sc.h, 1)
        v_a = P_.sc.inc(1)
        with _blk(nc) as b:
            @b.tensor
            def _(t, v_a=v_a):
                t.wait_ge(P_.sc.h, v_a)
                t.matmul(pool[7][:, :], ones_r[:, :], crow[0:1, :], start=True,
                         stop=True).then_inc(P_.pe.h, 1)
        v_bc = P_.pe.inc(1)
        with _blk(nc) as b:
            @b.vector
            def _(v, v_bc=v_bc):
                b0 = P_.ve.n
                v.wait_ge(P_.pe.h, v_bc)
                for oc in range(2):
                    v.tensor_tensor(qtmp[:, 512 * oc:512 * (oc + 1)],
                                    qtmp[:, 512 * oc:512 * (oc + 1)], pool[7][:, :],
                                    AL.mult).then_inc(P_.ve.h, 1)
                v.wait_ge(P_.ve.h, b0 + 2)
                for oc in range(2):
                    v.tensor_scalar(qtmp[:, 512 * oc:512 * (oc + 1)],
                                    qtmp[:, 512 * oc:512 * (oc + 1)], scl[:, oc:oc + 1],
                                    None, AL.mult).then_inc(P_.ve.h, 1)
        v_qn = P_.ve.inc(4)
        with _blk(nc) as b:
            @b.tensor
            def _(t, v_qn=v_qn):
                t.wait_ge(P_.ve.h, v_qn)
                t.matmul(pool[6][:, :], pswap[:, :], qtmp[:, 0:512], start=True,
                         stop=True).then_inc(P_.pe.h, 1)
        v_sw = P_.pe.inc(1)
        with _blk(nc) as b:
            @b.vector
            def _(v, v_sw=v_sw, tg=tg, v_qn=v_qn):
                b0 = P_.ve.n
                v.wait_ge(P_.pe.h, v_sw)
                v.tensor_tensor(sq[:, 0:512], qtmp[:, 0:512], cst[:, 0:512], AL.mult
                                ).then_inc(P_.ve.h, 1)
                v.tensor_tensor(sq[:, 512:1024], pool[6][:, :], cst[:, 512:1024],
                                AL.mult).then_inc(P_.ve.h, 1)
                v.wait_ge(P_.ve.h, b0 + 2)
                v.tensor_add(qbuf[:, 0:512], sq[:, 0:512], sq[:, 512:1024]
                             ).then_inc(P_.ve.h, 1)
                v.tensor_copy(qbuf[:, 512:1024], qtmp[:, 512:1024]).then_inc(P_.ve.h, 1)
        v_q = P_.ve.inc(4)

        with _blk(nc) as b:
            @b.scalar
            def _(s, v_q=v_q):
                s.wait_ge(P_.ve.h, v_q)
                for oc in range(2):
                    s.copy(qtmp[:, 512 * oc:512 * (oc + 1)], pool[2 + oc][:, :]
                           ).then_inc(P_.sc.h, 1)
        v_kev = P_.sc.inc(2)
        with _blk(nc) as b:
            @b.scalar
            def _(s, v_kev=v_kev):
                s.wait_ge(P_.sc.h, v_kev)
                s.activation(sq[:, 0:512], qtmp[:, 0:512], AF.Square).then_inc(P_.sc.h, 1)
                s.activation(sq[:, 512:1024], qtmp[:, 512:1024], AF.Square).then_inc(P_.sc.h, 1)
        v_sqk = P_.sc.inc(2)
        with _blk(nc) as b:
            @b.tensor
            def _(t, v_sqk=v_sqk, v_q=v_q):
                t.wait_ge(P_.sc.h, v_sqk)
                t.wait_ge(P_.ve.h, v_q)
                t.matmul(pool[6][0:1, :], ones_b[:, :], sq[:, 0:512], start=True, stop=False)
                t.matmul(pool[6][0:1, :], ones_b[:, :], sq[:, 512:1024], start=False,
                         stop=True).then_inc(P_.pe.h, 1)
        v_ssk = P_.pe.inc(1)
        with _blk(nc) as b:
            @b.vector
            def _(v, v_ssk=v_ssk):
                v.wait_ge(P_.pe.h, v_ssk)
                v.tensor_scalar(crow[0:1, :], pool[6][0:1, :], 1.0 / 256, 1e-6,
                                AL.mult, AL.add).then_inc(P_.ve.h, 1)
        v_crk = P_.ve.inc(1)
        with _blk(nc) as b:
            @b.scalar
            def _(s, v_crk=v_crk):
                s.wait_ge(P_.ve.h, v_crk)
                s.activation(crow[0:1, :], crow[0:1, :], AF.Ln)
                s.activation(crow[0:1, :], crow[0:1, :], AF.Exp, scale=-0.5
                             ).then_inc(P_.sc.h, 1)
        v_ak = P_.sc.inc(1)
        with _blk(nc) as b:
            @b.tensor
            def _(t, v_ak=v_ak):
                t.wait_ge(P_.sc.h, v_ak)
                t.matmul(pool[7][:, :], ones_r[:, :], crow[0:1, :], start=True,
                         stop=True).then_inc(P_.pe.h, 1)
        v_bck = P_.pe.inc(1)
        with _blk(nc) as b:
            @b.vector
            def _(v, v_bck=v_bck):
                b0 = P_.ve.n
                v.wait_ge(P_.pe.h, v_bck)
                for oc in range(2):
                    v.tensor_tensor(qtmp[:, 512 * oc:512 * (oc + 1)],
                                    qtmp[:, 512 * oc:512 * (oc + 1)], pool[7][:, :],
                                    AL.mult).then_inc(P_.ve.h, 1)
                v.wait_ge(P_.ve.h, b0 + 2)
                for oc in range(2):
                    v.tensor_scalar(qtmp[:, 512 * oc:512 * (oc + 1)],
                                    qtmp[:, 512 * oc:512 * (oc + 1)], scl[:, 2 + oc:3 + oc],
                                    None, AL.mult).then_inc(P_.ve.h, 1)
        v_kn = P_.ve.inc(4)
        with _blk(nc) as b:
            @b.tensor
            def _(t, v_kn=v_kn):
                t.wait_ge(P_.ve.h, v_kn)
                t.matmul(pool[6][:, :], pswap[:, :], qtmp[:, 0:512], start=True,
                         stop=True).then_inc(P_.pe.h, 1)
        v_swk = P_.pe.inc(1)
        slot = tg % 4
        with _blk(nc) as b:
            @b.vector
            def _(v, v_swk=v_swk, slot=slot):
                b0 = P_.ve.n
                v.wait_ge(P_.pe.h, v_swk)
                v.tensor_tensor(sq[:, 0:512], qtmp[:, 0:512], cst[:, 0:512], AL.mult
                                ).then_inc(P_.ve.h, 1)
                v.tensor_tensor(sq[:, 512:1024], pool[6][:, :], cst[:, 512:1024],
                                AL.mult).then_inc(P_.ve.h, 1)
                v.wait_ge(P_.ve.h, b0 + 2)
                v.tensor_add(kring[:, 4096 * 0 + 512 * slot:4096 * 0 + 512 * (slot + 1)],
                             sq[:, 0:512], sq[:, 512:1024]).then_inc(P_.ve.h, 1)
                v.tensor_copy(kring[:, 2048 + 512 * slot:2048 + 512 * (slot + 1)],
                              qtmp[:, 512:1024]).then_inc(P_.ve.h, 1)
        v_k = P_.ve.inc(4)

        with _blk(nc) as b:
            @b.scalar
            def _(s, v_pj=v_pj, v_k=v_k):
                s.wait_ge(P_.ve.h, v_k)
                s.copy(vtmp[:, :], pool[5][:, :]).then_inc(P_.sc.h, 1)
        v_vev = P_.sc.inc(1)
        with _blk(nc) as b:
            @b.tensor
            def _(t, v_vev=v_vev, slot=slot):
                t.wait_ge(P_.sc.h, v_vev)
                k = ctx.get(("l2a_vtr", tg - 1))
                for j in range(4):
                    t.transpose(pool[6][:, 128 * j:128 * (j + 1)],
                                ctx["vtmp0"][:, 128 * j:128 * (j + 1)], identf[:, :])
                ins = None
                for j in range(4):
                    ins = t.transpose(pool[7][:, 128 * j:128 * (j + 1)],
                                      vtmp[:, 128 * j:128 * (j + 1)], identf[:, :])
                ins.then_inc(P_.pe.h, 1)
        v_vt = P_.pe.inc(1)
        ctx[("l2a_vtr", tg)] = v_vt
        with _blk(nc) as b:
            @b.scalar
            def _(s, v_vt=v_vt, slot=slot):
                s.wait_ge(P_.pe.h, v_vt)
                for j in range(4):
                    s.copy(vring[:, 1024 * slot + 256 * j:1024 * slot + 256 * j + 128],
                           pool[6][:, 128 * j:128 * (j + 1)]).then_inc(P_.sc.h, 1)
                    s.copy(vring[:, 1024 * slot + 256 * j + 128:1024 * slot + 256 * (j + 1)],
                           pool[7][:, 128 * j:128 * (j + 1)]).then_inc(P_.sc.h, 1)
        v_vr = P_.sc.inc(8)

        qc = tg
        kts = [kt for kt in range(max(0, 4 * qc - 8), 4 * qc + 4)]
        nkt = len(kts)
        for idx, kt in enumerate(kts):
            diag = kt - 4 * qc
            delta = 4 * qc - kt
            ps_s = pool[0 if idx % 2 == 0 else 1]
            esb = es[idx % 2]
            v_esprev = ctx.get(("l2a_es", qc, idx - 2))
            kslot = (kt // 4) % 4
            koff = 512 * kslot + 128 * (kt % 4)
            with _blk(nc) as b:
                @b.tensor
                def _(t, qc=qc, kt=kt, idx=idx, ps_s=ps_s, v_esprev=v_esprev,
                      koff=koff, v_q=v_q, v_k=v_k):
                    t.wait_ge(P_.ve.h, v_q)
                    t.wait_ge(P_.ve.h, v_k)
                    if v_esprev is not None:
                        t.wait_ge(P_.sc.h, v_esprev)
                    t.matmul(ps_s[:, :], kring[:, koff:koff + 128],
                             qbuf[:, 0:512], start=True, stop=False)
                    t.matmul(ps_s[:, :], kring[:, 2048 + koff:2048 + koff + 128],
                             qbuf[:, 512:1024], start=False, stop=True).then_inc(P_.pe.h, 1)
            v_s = P_.pe.inc(1)
            with _blk(nc) as b:
                @b.scalar
                def _(s, v_s=v_s, ps_s=ps_s, esb=esb):
                    s.wait_ge(P_.pe.h, v_s)
                    s.activation(esb[:, :], ps_s[:, :], AF.Tanh,
                                 scale=L2SCALE / L2CAP).then_inc(P_.sc.h, 1)
                    s.activation(esb[:, :], esb[:, :], AF.Exp, scale=L2CAP
                                 ).then_inc(P_.sc.h, 1)
            v_e = P_.sc.inc(2)
            ctx[("l2a_es", qc, idx)] = v_e
            mask = None
            if diag >= 0:
                mask = dmask[:, 512 * diag:512 * (diag + 1)]
            elif delta >= 5:
                mask = wmask[:, 512 * (delta - 5):512 * (delta - 4)]
            if mask is not None:
                with _blk(nc) as b:
                    @b.vector
                    def _(v, v_e=v_e, esb=esb, mask=mask):
                        v.wait_ge(P_.sc.h, v_e)
                        v.tensor_tensor(esb[:, :], esb[:, :], mask, AL.mult
                                        ).then_inc(P_.ve.h, 1)
                gate = ("ve", P_.ve.inc(1))
            else:
                gate = ("sc", v_e)
            vs0 = 1024 * ((kt // 4) % 4) + 256 * (kt % 4)
            with _blk(nc) as b:
                @b.tensor
                def _(t, idx=idx, gate=gate, esb=esb, nkt=nkt, vs0=vs0):
                    t.wait_ge((P_.ve if gate[0] == "ve" else P_.sc).h, gate[1])
                    t.matmul(pool[2][:, :], vring[:, vs0:vs0 + 128], esb[:, :],
                             start=(idx == 0), stop=(idx == nkt - 1))
                    t.matmul(pool[3][:, :], vring[:, vs0 + 128:vs0 + 256], esb[:, :],
                             start=(idx == 0), stop=(idx == nkt - 1))
                    ins = t.matmul(pool[4][0:1, :], ones_b[:, :], esb[:, :],
                                   start=(idx == 0), stop=(idx == nkt - 1))
                    if idx == nkt - 1:
                        ins.then_inc(P_.pe.h, 1)
            if idx == nkt - 1:
                P_.pe.inc(1)
        v_pv = P_.pe.n
        with _blk(nc) as b:
            @b.vector
            def _(v, v_pv=v_pv):
                v.wait_ge(P_.pe.h, v_pv)
                v.reciprocal(crow[0:1, :], pool[4][0:1, :]).then_inc(P_.ve.h, 1)
        v_r = P_.ve.inc(1)
        with _blk(nc) as b:
            @b.tensor
            def _(t, v_r=v_r):
                t.wait_ge(P_.ve.h, v_r)
                t.matmul(pool[5][:, :], ones_r[:, :], crow[0:1, :], start=True,
                         stop=True).then_inc(P_.pe.h, 1)
        v_bc2 = P_.pe.inc(1)
        with _blk(nc) as b:
            @b.scalar
            def _(s, v_bc2=v_bc2):
                s.wait_ge(P_.pe.h, v_bc2)
                s.copy(cfix[:, :], pool[2][:, :]).then_inc(P_.sc.h, 1)
                s.copy(ctx["cfix2"][:, :], pool[3][:, :]).then_inc(P_.sc.h, 1)
        v_cp = P_.sc.inc(2)
        v_cbw = ctx.get(("l2a_cb", qc - 1))
        with _blk(nc) as b:
            @b.vector
            def _(v, v_cp=v_cp, v_cbw=v_cbw):
                v.wait_ge(P_.sc.h, v_cp)
                if v_cbw is not None:
                    v.wait_ge(P_.gd.h, v_cbw)
                v.tensor_tensor(cbuf[0][:, :], cfix[:, :], pool[5][:, :], AL.mult
                                ).then_inc(P_.ve.h, 1)
                v.tensor_tensor(cbuf[1][:, :], ctx["cfix2"][:, :], pool[5][:, :],
                                AL.mult).then_inc(P_.ve.h, 1)
        v_ct = P_.ve.inc(2)
        ctx[("l2a_aev", tg)] = v_ct
        ctx[("l2a_pje", tg)] = (P_.sc.n, v_ct)
        ctx[("l2a_kv", tg)] = v_k
        with _blk(nc) as b:
            @b.gpsimd
            def _(g, qc=qc, v_ct=v_ct):
                g.wait_ge(P_.ve.h, v_ct)
                g.dma_start(out=cl.ap()[0:128, 512 * qc:512 * (qc + 1)],
                            in_=cbuf[0][:, :]).then_inc(P_.gd.h, 16)
                g.dma_start(out=cl.ap()[128:256, 512 * qc:512 * (qc + 1)],
                            in_=cbuf[1][:, :]).then_inc(P_.gd.h, 16)
        ctx[("l2a_cb", qc)] = P_.gd.inc(32)
    v_cl = P_.gd.n

    with _blk(nc) as b:
        @b.gpsimd
        def _(g):
            g.wait_ge(P_.gd.h, v_cl)
            g.collective_compute("AllGather", AL.bypass, replica_groups=RG,
                                 ins=[cl.ap()], outs=[call.ap()]).then_inc(P_.cc.h)
    v_cag = P_.cc.inc(1)

    for tc in range(2):
        off = ctx["pid_off"][tc]
        with _blk(nc) as b:
            @b.sync
            def _(s, tc=tc, off=off):
                s.wait_ge(P_.cc.h, v_cag)
                if tc == 1:
                    s.wait_ge(P_.pe.h, ctx[("l2a_wo", 0, NDT - 1)])
                s.dma_start(out=xstage[:, :].rearrange("p (i f) -> p i f", i=NDT),
                            in_=call.ap().rearrange("(i p) t -> p i t", p=128)
                            [:, :, bass.ts(off, 512)]).then_inc(P_.ld.h, 16)
        v_cs = P_.ld.inc(16)
        for oc in range(NDT):
            v_warw = ctx.get(("l2a_wol", tc, oc - 2))
            with _blk(nc) as b:
                @b.gpsimd
                def _(g, oc=oc, v_warw=v_warw):
                    if v_warw is not None:
                        g.wait_ge(P_.pe.h, v_warw)
                    g.dma_start(out=wost[oc % 2][:, :].rearrange("p (i f) -> p i f", i=NDT),
                                in_=wo_ap[:, 128 * oc:128 * (oc + 1)]
                                .rearrange("(i p) f -> p i f", p=128)).then_inc(P_.gd.h, 16)
            v_ldw = P_.gd.inc(16)
            pa = pool[oc % 2]
            with _blk(nc) as b:
                @b.tensor
                def _(t, tc=tc, oc=oc, v_ldw=v_ldw, pa=pa, v_cs=v_cs):
                    t.wait_ge(P_.gd.h, v_ldw)
                    t.wait_ge(P_.ld.h, v_cs)
                    if oc <= 1:
                        if tc == 1:
                            t.wait_ge(P_.ve.h, ctx[("l2a_woe", 0, NDT - 2 + oc)])
                    else:
                        t.wait_ge(P_.ve.h, ctx[("l2a_woe", tc, oc - 2)])
                    for i in range(NDT):
                        ins = t.matmul(pa[:, :], wost[oc % 2][:, 128 * i:128 * (i + 1)],
                                       xstage[:, 512 * i:512 * (i + 1)],
                                       start=(i == 0), stop=(i == NDT - 1))
                    ins.then_inc(P_.pe.h, 1)
            ctx[("l2a_wo", tc, oc)] = P_.pe.inc(1)
            ctx[("l2a_wol", tc, oc)] = P_.pe.n
            v_mm = P_.pe.n
            with _blk(nc) as b:
                @b.vector
                def _(v, tc=tc, oc=oc, v_mm=v_mm, pa=pa):
                    v.wait_ge(P_.pe.h, v_mm)
                    v.wait_ge(v_x[0].h, v_x[1])
                    v.tensor_add(xr[oc][:, 512 * tc:512 * (tc + 1)],
                                 xr[oc][:, 512 * tc:512 * (tc + 1)], pa[:, :]
                                 ).then_inc(P_.ve.h, 1)
            ctx[("l2a_woe", tc, oc)] = P_.ve.inc(1)
    return [(P_.ve, P_.ve.n)]


def build_l2_attn_prog():
    Pr = P()
    nc = Pr.nc
    xin = nc.dram_tensor("xin", [D, TL], F32, kind="ExternalInput").ap()
    wq = nc.dram_tensor("l2a_wq", [D, 256], F32, kind="ExternalInput").ap()
    wk = nc.dram_tensor("l2a_wk", [D, 256], F32, kind="ExternalInput").ap()
    wv = nc.dram_tensor("l2a_wv", [D, 256], F32, kind="ExternalInput").ap()
    wo = nc.dram_tensor("l2a_wo", [D, D], F32, kind="ExternalInput").ap()
    qs = nc.dram_tensor("l2a_qs", [128, 2], F32, kind="ExternalInput").ap()
    ks = nc.dram_tensor("l2a_ks", [128, 2], F32, kind="ExternalInput").ap()
    xout = nc.dram_tensor("xout", [D, TL], F32, kind="ExternalOutput").ap()
    Pr.make_pool()
    xr = [Pr.sb(f"xr{i}", [128, TL], F32) for i in range(NDT)]
    xnb = [Pr.sb(f"xnb{i}", [128, TL], BF16) for i in range(NDT)]
    C, S = _l2_rope_tables()
    ctx = {
        "qbuf": Pr.sb("qbuf", [128, 1024], BF16),
        "kring": Pr.sb("kring", [128, 4096], BF16),
        "vring": Pr.sb("vring", [128, 4096], BF16),
        "xstage": Pr.sb("xstage", [128, 8192], BF16),
        "wqkvt": Pr.sb("wqkvt", [128, 12288], BF16),
        "wost": [Pr.sb(f"wost{i}", [128, 2048], BF16) for i in range(2)],
        "es": [Pr.sb(f"es{i}", [128, 512], BF16) for i in range(2)],
        "cbuf": [Pr.sb(f"cbuf{i}", [128, 512], BF16) for i in range(2)],
        "qtmp": Pr.sb("qtmp", [128, 1024], F32),
        "vtmp0": Pr.sb("vtmp0", [128, 512], F32),
        "vtmp": Pr.sb("vtmp", [128, 512], F32),
        "cst": Pr.sb("cst", [128, 1024], BF16),
        "scl": Pr.sb("scl", [128, 4], F32),
        "cfix": Pr.sb("cfix", [128, 512], F32),
        "cfix2": Pr.sb("cfix2", [128, 512], F32),
        "sqn": Pr.sb("sqn", [128, 1024], BF16),
    }
    ctx["ropeC"] = Pr.const("l2a_ropeC", C)
    ctx["ropeS"] = Pr.const("l2a_ropeS", S)
    ctx["identf"] = Pr.sconst("ident_f", np.eye(128, dtype=np.float32))
    ctx["pswap"] = Pr.sconst("l2a_pswap", _pswap())
    ctx["dmask"] = Pr.sconst("l3a_dmask", _l3_dmask())
    ctx["wmask"] = Pr.sconst("l2a_wmask", _l2_wmask())
    Pr.ones_col_b(); Pr.ones_row_f()
    Pr.flush_consts()
    nctx = make_norm_ctx(Pr, atmp_alias=ctx["cfix"],
                         sq_alias=[ctx["xstage"][:, 512 * i:512 * (i + 1)]
                                   for i in range(NDT)])
    ctx["crow"] = nctx["r_msq"]
    with _blk(nc) as b:
        @b.sync
        def _(s):
            pid = nc.partition_id()
            ctx["pid_off"] = [nc.snap(pid * 2 + tc, min_val=0,
                                      max_val=2 * NCORES - 1) for tc in range(2)]
    v_x = load_residual(Pr, xr, xin)
    fin = emit_l2_attn(Pr, ctx, xr, (Pr.ld, v_x), wq, wk, wv, wo, qs, ks)
    fin2 = emit_norm(Pr, nctx, xr, xnb, None, None, subtract_mean=True,
                     pre_waits=fin)
    store_residual(Pr, xr, xout, [fin2])
    Pr.close_block()
    return Pr


def run_l2_attn(x, Wq, Wk, Wv, Wo, q_scale, k_scale):
    Pr = build_l2_attn_prog()
    xs = _shard_x(x)
    qs = np.ascontiguousarray(q_scale.reshape(2, 128).T)
    ks = np.ascontiguousarray(k_scale.reshape(2, 128).T)
    in_maps = []
    for c in range(NCORES):
        g = c // 2
        in_maps.append({
            "xin": xs[c],
            "l2a_wq": np.ascontiguousarray(Wq[:, 256 * c:256 * (c + 1)]),
            "l2a_wk": np.ascontiguousarray(Wk[:, 256 * g:256 * (g + 1)]),
            "l2a_wv": np.ascontiguousarray(Wv[:, 256 * g:256 * (g + 1)]),
            "l2a_wo": Wo, "l2a_qs": qs, "l2a_ks": ks,
        })
    res = run_bass_kernel_spmd(Pr.nc, in_maps, core_ids=list(range(NCORES)))
    return _unshard_x([res.results[c]["xout"] for c in range(NCORES)])





HL1 = 5504
NHC1 = HL1 // 128


def emit_l1_ffn_build(P_, ctx, gsh, ush, dsh, lut_ap):
    nc = P_.nc
    pool = P_.pool
    idxst = ctx["idxst"]
    wtile = ctx["wtile"]
    ttmp = ctx["ttmp"]
    tacc = ctx["tacc"]
    lutsb = ctx["lutsb"]
    gb_l = nc.dram_tensor("l1f_gb_l", [256, HL1], BF16)
    ub_l = nc.dram_tensor("l1f_ub_l", [256, HL1], BF16)
    db_l = nc.dram_tensor("l1f_db_l", [HL1, 256], BF16)
    gb = nc.dram_tensor("l1f_gb", [NCORES * 256, HL1], BF16, addr_space="Shared")
    ub = nc.dram_tensor("l1f_ub", [NCORES * 256, HL1], BF16, addr_space="Shared")
    db = nc.dram_tensor("l1f_db", [NCORES * HL1, 256], BF16, addr_space="Shared")
    v_lut = P_.ld.inc(16)
    with _blk(nc) as b:
        @b.sync
        def _(s):
            s.dma_start(out=lutsb[:, :], in_=lut_ap).then_inc(P_.ld.h, 16)

    def build(src_ap, dst_dram, n_rows, n_cols, tag):
        nrt = (n_rows + 127) // 128
        for rt in range(nrt):
            rows = min(128, n_rows - 128 * rt)
            ncc = (n_cols + 1375) // 1376
            for cchunk in range(ncc):
                c0 = 1376 * cchunk
                cw = min(1376, n_cols - c0)
                v_war = ctx.get(("l1b", tag, rt, cchunk - 1)) or ctx.get(("l1b_last",))
                with _blk(nc) as b:
                    @b.sync
                    def _(s, rt=rt, c0=c0, cw=cw, rows=rows, v_war=v_war):
                        if v_war is not None:
                            s.wait_ge(P_.ve.h, v_war)
                        s.dma_start(out=idxst[0:rows, 0:cw],
                                    in_=src_ap[128 * rt:128 * rt + rows, c0:c0 + cw]
                                    ).then_inc(P_.ld.h, 16)
                v_ld = P_.ld.inc(16)
                with _blk(nc) as b:
                    @b.vector
                    def _(v, rt=rt, c0=c0, cw=cw, rows=rows, v_ld=v_ld):
                        b0 = P_.ve.n
                        v.wait_ge(P_.ld.h, v_ld)
                        v.wait_ge(P_.ld.h, v_lut)
                        v.tensor_copy(ttmp[0:rows, 0:cw], idxst[0:rows, 0:cw]
                                      ).then_inc(P_.ve.h, 1)
                        v.wait_ge(P_.ve.h, b0 + 1)
                        v.tensor_scalar(wtile[0:rows, c0:c0 + cw], ttmp[0:rows, 0:cw],
                                        0.0, lutsb[:, 0:1], AL.is_equal, AL.mult
                                        ).then_inc(P_.ve.h, 1)
                        for i in range(1, 16):
                            v.wait_ge(P_.ve.h, b0 + 2 * i)
                            v.tensor_scalar(tacc[0:rows, 0:cw], ttmp[0:rows, 0:cw],
                                            float(i), lutsb[:, i:i + 1],
                                            AL.is_equal, AL.mult).then_inc(P_.ve.h, 1)
                            v.wait_ge(P_.ve.h, b0 + 2 * i + 1)
                            v.tensor_add(wtile[0:rows, c0:c0 + cw],
                                         wtile[0:rows, c0:c0 + cw],
                                         tacc[0:rows, 0:cw]).then_inc(P_.ve.h, 1)
                ctx[("l1b", tag, rt, cchunk)] = P_.ve.inc(32)
            v_bt = P_.ve.n
            with _blk(nc) as b:
                @b.gpsimd
                def _(g, rt=rt, rows=rows, v_bt=v_bt):
                    g.wait_ge(P_.ve.h, v_bt)
                    g.dma_start(out=dst_dram.ap()[128 * rt:128 * rt + rows, :],
                                in_=wtile[0:rows, 0:n_cols]).then_inc(P_.gd.h, 16)
            ctx[("l1b_last",)] = None
            ctx[("l1bst", tag, rt)] = P_.gd.inc(16)

    build(gsh, gb_l, 256, HL1, "g")
    build(ush, ub_l, 256, HL1, "u")
    build(dsh, db_l, HL1, 256, "d")
    v_st = P_.gd.n
    with _blk(nc) as b:
        @b.gpsimd
        def _(g):
            g.wait_ge(P_.gd.h, v_st)
            for (i_, o_) in ((gb_l, gb), (ub_l, ub), (db_l, db)):
                g.collective_compute("AllGather", AL.bypass, replica_groups=RG,
                                     ins=[i_.ap()], outs=[o_.ap()]).then_inc(P_.cc.h)
    ready = (P_.cc, P_.cc.inc(3))
    return gb, ub, db, ready


def emit_l1_ffn(P_, ctx, xr, xnb, xnb_ready, gb, ub, db, wready):
    nc = P_.nc
    pool = P_.pool
    ht = ctx["ht1"]
    wst = ctx["wst1"]
    w2st = ctx["w2st1"]
    silt = ctx["silt1"]

    for tc in range(2):
        sl = slice(512 * tc, 512 * (tc + 1))
        for hc in range(NHC1):
            kb = 2 * NHC1 * tc + 2 * hc
            v_war = ctx.get(("l1mm", kb - 4))
            with _blk(nc) as b:
                @b.sync
                def _(s, hc=hc, kb=kb, v_war=v_war):
                    s.wait_ge(P_.cc.h, wready[1])
                    if v_war is not None:
                        s.wait_ge(P_.pe.h, v_war)
                    for (mat, col) in ((gb, 0), (ub, 1)):
                        s.dma_start(out=wst[(kb + col) % 4][:, :]
                                    .rearrange("p (i f) -> p i f", i=NDT),
                                    in_=mat.ap()[:, 128 * hc:128 * (hc + 1)]
                                    .rearrange("(i p) f -> p i f", p=128)
                                    ).then_inc(P_.ld.h, 16)
            v_ld = P_.ld.inc(32)
            ev_prev = ctx.get(("l1ev", tc, hc - 2))
            with _blk(nc) as b:
                @b.tensor
                def _(t, hc=hc, kb=kb, tc=tc, v_ld=v_ld, ev_prev=ev_prev):
                    t.wait_ge(P_.ld.h, v_ld)
                    if hc == 0 and tc == 0:
                        t.wait_ge(xnb_ready[0].h, xnb_ready[1])
                    if hc <= 1 and tc == 1:
                        t.wait_ge(P_.ve.h, ctx[("l1dn", 0)])
                    if ev_prev is not None:
                        t.wait_ge(P_.ve.h, ev_prev)
                    pg = pool[kb % 4 // 2 * 2 % 4]
                    pg = pool[(hc % 2) * 2]
                    pu = pool[(hc % 2) * 2 + 1]
                    for i in range(NDT):
                        t.matmul(pg[:, :], wst[kb % 4][:, 128 * i:128 * (i + 1)],
                                 xnb[i][:, sl], start=(i == 0), stop=(i == NDT - 1))
                        ins = t.matmul(pu[:, :], wst[(kb + 1) % 4][:, 128 * i:128 * (i + 1)],
                                       xnb[i][:, sl], start=(i == 0), stop=(i == NDT - 1))
                    ins.then_inc(P_.pe.h, 1)
            ctx[("l1mm", kb)] = P_.pe.inc(1)
            v_mm = P_.pe.n
            with _blk(nc) as b:
                @b.scalar
                def _(s, hc=hc, v_mm=v_mm):
                    s.wait_ge(P_.pe.h, v_mm)
                    k = ctx.get(("l1ev", 99, hc))
                    s.activation(silt[:, :], pool[(hc % 2) * 2][:, :], AF.Silu
                                 ).then_inc(P_.sc.h, 1)
            v_sil = P_.sc.inc(1)
            with _blk(nc) as b:
                @b.vector
                def _(v, hc=hc, v_sil=v_sil):
                    v.wait_ge(P_.sc.h, v_sil)
                    v.tensor_tensor(ht[hc][:, :], silt[:, :],
                                    pool[(hc % 2) * 2 + 1][:, :], AL.mult
                                    ).then_inc(P_.ve.h, 1)
            ctx[("l1ev", tc, hc)] = P_.ve.inc(1)
        v_he = P_.ve.n

        for oh in range(2):
            for hc in range(NHC1):
                bufi = hc % 2
                v_war = ctx.get(("l1dld", tc, oh, hc - 2))
                with _blk(nc) as b:
                    @b.gpsimd
                    def _(g, oh=oh, hc=hc, bufi=bufi, v_war=v_war):
                        g.wait_ge(P_.cc.h, wready[1])
                        if v_war is not None:
                            g.wait_ge(P_.pe.h, v_war)
                        for rr in range(4):
                            r = 4 * oh + rr
                            g.dma_start(out=w2st[bufi][:, 256 * rr:256 * (rr + 1)],
                                        in_=db.ap()[r * HL1 + 128 * hc:
                                                    r * HL1 + 128 * (hc + 1), :]
                                        ).then_inc(P_.gd.h, 16)
                v_ld = P_.gd.inc(64)
                with _blk(nc) as b:
                    @b.tensor
                    def _(t, oh=oh, hc=hc, bufi=bufi, v_ld=v_ld, v_he=v_he, tc=tc):
                        t.wait_ge(P_.gd.h, v_ld)
                        if hc == 0:
                            t.wait_ge(P_.ve.h, v_he)
                            if oh == 1:
                                t.wait_ge(P_.ve.h, ctx[("l1dn", tc)])
                        for oc in range(8):
                            ins = t.matmul(pool[oc][:, :],
                                           w2st[bufi][:, 128 * oc:128 * (oc + 1)],
                                           ht[hc][:, :],
                                           start=(hc == 0), stop=(hc == NHC1 - 1))
                        ins.then_inc(P_.pe.h, 1)
                ctx[("l1dld", tc, oh, hc)] = P_.pe.inc(1)
            v_dn = P_.pe.n
            with _blk(nc) as b:
                @b.vector
                def _(v, oh=oh, tc=tc, v_dn=v_dn):
                    v.wait_ge(P_.pe.h, v_dn)
                    for oc in range(8):
                        i = 8 * oh + oc
                        v.tensor_add(xr[i][:, 512 * tc:512 * (tc + 1)],
                                     xr[i][:, 512 * tc:512 * (tc + 1)], pool[oc][:, :]
                                     ).then_inc(P_.ve.h, 1)
            ctx[("l1dn", tc)] = P_.ve.inc(8)
    return [(P_.ve, P_.ve.n)]


def build_l1_ffn_prog():
    Pr = P()
    nc = Pr.nc
    xin = nc.dram_tensor("xin", [D, TL], F32, kind="ExternalInput").ap()
    lut = nc.dram_tensor("l1f_lut", [128, 16], F32, kind="ExternalInput").ap()
    gsh = nc.dram_tensor("l1f_gsh", [256, HL1], I32, kind="ExternalInput").ap()
    ush = nc.dram_tensor("l1f_ush", [256, HL1], I32, kind="ExternalInput").ap()
    dsh = nc.dram_tensor("l1f_dsh", [HL1, 256], I32, kind="ExternalInput").ap()
    xout = nc.dram_tensor("xout", [D, TL], F32, kind="ExternalOutput").ap()
    Pr.make_pool()
    xr = [Pr.sb(f"xr{i}", [128, TL], F32) for i in range(NDT)]
    ht = [Pr.sb(f"l1ht{i}", [128, 512], BF16) for i in range(NHC1)]
    xnb = [Pr.sb(f"xnb{i}", [128, TL], BF16) for i in range(NDT)]
    ctx = {
        "ht1": ht,
        "wst1": [Pr.sb(f"l1wst{i}", [128, 2048], BF16) for i in range(4)],
        "w2st1": [Pr.sb(f"l1w2st{i}", [128, 1024], BF16) for i in range(2)],
        "silt1": Pr.sb("l1silt", [128, 512], BF16),
        "idxst": Pr.sb("l1idx", [128, 1376], I32),
        "wtile": Pr.sb("l1wtile", [128, HL1], BF16),
        "ttmp": Pr.sb("l1ttmp", [128, 1376], BF16),
        "tacc": Pr.sb("l1tacc", [128, 1376], BF16),
        "lutsb": Pr.sb("l1lut", [128, 16], F32),
    }
    nctx = make_norm_ctx(Pr, sq_alias=[ht[i] for i in range(NDT)])
    Pr.ones_col_b(); Pr.ones_row_f()
    Pr.flush_consts()
    gb, ub, db, wready = emit_l1_ffn_build(Pr, ctx, gsh, ush, dsh, lut)
    v_x = load_residual(Pr, xr, xin)
    with _blk(nc) as b:
        @b.scalar
        def _(s):
            s.wait_ge(Pr.ld.h, v_x)
            for i in range(NDT):
                s.copy(xnb[i][:, :], xr[i][:, :]).then_inc(Pr.sc.h, 1)
    xnb_ready = (Pr.sc, Pr.sc.inc(NDT))
    fin = emit_l1_ffn(Pr, ctx, xr, xnb, xnb_ready, gb, ub, db, wready)
    fin2 = emit_norm(Pr, nctx, xr, xnb, None, None, subtract_mean=True,
                     pre_waits=fin)
    store_residual(Pr, xr, xout, [fin2])
    Pr.close_block()
    return Pr


def run_l1_ffn(x, lut, gate_idx, up_idx, down_idx):
    Pr = build_l1_ffn_prog()
    xs = _shard_x(x)
    gT = np.ascontiguousarray(gate_idx.T)
    uT = np.ascontiguousarray(up_idx.T)
    dT = np.ascontiguousarray(down_idx.T)
    in_maps = []
    for c in range(NCORES):
        in_maps.append({
            "xin": xs[c], "l1f_lut": np.ascontiguousarray(np.broadcast_to(lut.reshape(1, 16), (128, 16))),
            "l1f_gsh": np.ascontiguousarray(gT[256 * c:256 * (c + 1)]),
            "l1f_ush": np.ascontiguousarray(uT[256 * c:256 * (c + 1)]),
            "l1f_dsh": np.ascontiguousarray(dT[:, 256 * c:256 * (c + 1)]),
        })
    res = run_bass_kernel_spmd(Pr.nc, in_maps, core_ids=list(range(NCORES)))
    return _unshard_x([res.results[c]["xout"] for c in range(NCORES)])





def emit_l0_attn(P_, ctx, xr, xnb, xnb_ready, wqT, wkT, wvs, wgs, wpT, v_x):
    nc = P_.nc
    pool = P_.pool
    xstage = ctx["xstage"]
    wst = ctx["wst"]
    wst2 = ctx["wst2"]
    kvf = ctx["kvf"]
    kvb = ctx["kvb"]
    qst, kst, ktdst = ctx["qst"], ctx["kst"], ctx["ktdst"]
    vst, gst = ctx["vst"], ctx["gst"]
    ssb = ctx["ssb"]
    otmp = ctx["otmp"]
    obuf = ctx["obuf"]
    identf = ctx["identf"]
    dmask = ctx["dmask"]
    ones_b = P_.ones_col_b()

    ql = nc.dram_tensor("l0a_ql", [D, TL], BF16)
    kl = nc.dram_tensor("l0a_kl", [D, TL], BF16)
    ktdl = nc.dram_tensor("l0a_ktdl", [TL, D], BF16)
    vl = nc.dram_tensor("l0a_vl", [T, 256], BF16)
    gl = nc.dram_tensor("l0a_gl", [T, 256], BF16)
    ogl = nc.dram_tensor("l0a_ogl", [256, T], BF16)
    xlo = nc.dram_tensor("l0a_xl", [D, TL], BF16)
    qall = nc.dram_tensor("l0a_qall", [NCORES * D, TL], BF16, addr_space="Shared")
    kall = nc.dram_tensor("l0a_kall", [NCORES * D, TL], BF16, addr_space="Shared")
    ktall = nc.dram_tensor("l0a_ktall", [T, D], BF16, addr_space="Shared")
    ogall = nc.dram_tensor("l0a_ogall", [NCORES * 256, T], BF16, addr_space="Shared")

    with _blk(nc) as b:
        @b.gpsimd
        def _(g):
            g.wait_ge(xnb_ready[0].h, xnb_ready[1])
            for i in range(NDT):
                g.dma_start(out=xlo.ap()[128 * i:128 * (i + 1), :], in_=xnb[i][:, :]
                            ).then_inc(P_.gd.h, 16)
    v_xl = P_.gd.inc(16 * NDT)
    xall = nc.dram_tensor("l0a_xall", [NCORES * D, TL], BF16, addr_space="Shared")
    with _blk(nc) as b:
        @b.gpsimd
        def _(g):
            g.wait_ge(P_.gd.h, v_xl)
            g.collective_compute("AllGather", AL.bypass, replica_groups=RG,
                                 ins=[xlo.ap()], outs=[xall.ap()]).then_inc(P_.cc.h)
    v_ag = P_.cc.inc(1)

    for (w_ap, dst, tag) in ((wqT, ql, "q"), (wkT, kl, "k")):
        for oc in range(NDT):
            kb = ("l0p", tag, oc)
            v_war = ctx.get(("l0p_mm", tag, oc - 2))
            with _blk(nc) as b:
                @b.gpsimd
                def _(g, oc=oc, v_war=v_war, w_ap=w_ap):
                    if v_war is not None:
                        g.wait_ge(P_.pe.h, v_war)
                    g.dma_start(out=wst[oc % 2][:, :].rearrange("p (i f) -> p i f", i=NDT),
                                in_=w_ap[:, 128 * oc:128 * (oc + 1)]
                                .rearrange("(i p) f -> p i f", p=128)).then_inc(P_.gd.h, 16)
            v_ld = P_.gd.inc(16)
            for tc in range(2):
                pa = pool[tc]
                with _blk(nc) as b:
                    @b.tensor
                    def _(t, oc=oc, tc=tc, v_ld=v_ld, pa=pa, tag=tag):
                        t.wait_ge(P_.gd.h, v_ld)
                        t.wait_ge(xnb_ready[0].h, xnb_ready[1])
                        k = ctx.get(("l0p_ev", tag, oc - 1, tc))
                        if k is not None:
                            t.wait_ge(P_.sc.h, k)
                        for i in range(NDT):
                            ins = t.matmul(pa[:, :], wst[oc % 2][:, 128 * i:128 * (i + 1)],
                                           xnb[i][:, 512 * tc:512 * (tc + 1)],
                                           start=(i == 0), stop=(i == NDT - 1))
                        ins.then_inc(P_.pe.h, 1)
                ctx[("l0p_mm", tag, oc)] = P_.pe.inc(1)
                v_mm = P_.pe.n
                with _blk(nc) as b:
                    @b.scalar
                    def _(s, oc=oc, tc=tc, v_mm=v_mm, pa=pa):
                        s.wait_ge(P_.pe.h, v_mm)
                        k = ctx.get(("l0_ssb_st",))
                        if k is not None:
                            s.wait_ge(P_.gd.h, k)
                        s.activation(ssb[:, 0:512], pa[:, :], AF.Silu).then_inc(P_.sc.h, 1)
                v_sl = P_.sc.inc(1)
                ctx[("l0p_ev", tag, oc, tc)] = v_sl
                with _blk(nc) as b:
                    @b.gpsimd
                    def _(g, oc=oc, tc=tc, v_sl=v_sl, dst=dst):
                        g.wait_ge(P_.sc.h, v_sl)
                        g.dma_start(out=dst.ap()[128 * oc:128 * (oc + 1),
                                                 512 * tc:512 * (tc + 1)],
                                    in_=ssb[:, 0:512]).then_inc(P_.gd.h, 16)
                ctx[("l0p_st", tag, oc, tc)] = P_.gd.inc(16)
                ctx[("l0_ssb_st",)] = P_.gd.n
    v_qk_st = P_.gd.n

    for oc4 in range(4):
        v_war = ctx.get(("l0ktd_mm", 7, oc4 - 1))
        with _blk(nc) as b:
            @b.gpsimd
            def _(g, oc4=oc4, v_war=v_war):
                if v_war is not None:
                    g.wait_ge(P_.pe.h, v_war)
                g.dma_start(out=ctx["wstb"][:, :]
                            .rearrange("p (i f) -> p i f", i=NDT),
                            in_=wkT[:, 512 * oc4:512 * (oc4 + 1)]
                            .rearrange("(i p) f -> p i f", p=128)
                            ).then_inc(P_.gd.h, 16)
        v_ld = P_.gd.inc(16)
        for tt in range(8):
            pa = pool[2 + tt % 2]
            with _blk(nc) as b:
                @b.tensor
                def _(t, tt=tt, oc4=oc4, v_ld=v_ld, pa=pa):
                    t.wait_ge(P_.gd.h, v_ld)
                    k = ctx.get(("l0ktd_ev2", 8 * oc4 + tt - 2))
                    if k is not None:
                        t.wait_ge(P_.sc.h, k)
                    for i in range(NDT):
                        ins = t.matmul(pa[:, :],
                                       xnb[i][:, 128 * tt:128 * (tt + 1)],
                                       ctx["wstb"][:, 512 * i:512 * (i + 1)],
                                       start=(i == 0), stop=(i == NDT - 1))
                    ins.then_inc(P_.pe.h, 1)
            ctx[("l0ktd_mm", tt, oc4)] = P_.pe.inc(1)
            v_mm = P_.pe.n
            with _blk(nc) as b:
                @b.scalar
                def _(s, tt=tt, oc4=oc4, v_mm=v_mm, pa=pa):
                    s.wait_ge(P_.pe.h, v_mm)
                    k = ctx.get(("l0_ssb_st",))
                    if k is not None:
                        s.wait_ge(P_.gd.h, k)
                    s.activation(ssb[:, 0:512], pa[:, :], AF.Silu).then_inc(P_.sc.h, 1)
            v_sl = P_.sc.inc(1)
            ctx[("l0ktd_ev2", 8 * oc4 + tt)] = v_sl
            with _blk(nc) as b:
                @b.gpsimd
                def _(g, tt=tt, oc4=oc4, v_sl=v_sl):
                    g.wait_ge(P_.sc.h, v_sl)
                    g.dma_start(out=ktdl.ap()[128 * tt:128 * (tt + 1),
                                              512 * oc4:512 * (oc4 + 1)],
                                in_=ssb[:, 0:512]).then_inc(P_.gd.h, 16)
            ctx[("l0ktd_st", tt, oc4)] = P_.gd.inc(16)
            ctx[("l0_ssb_st",)] = P_.gd.n
    v_ktd_st = P_.gd.n

    v_ktd_mm_all = P_.pe.n
    with _blk(nc) as b:
        @b.gpsimd
        def _(g):
            g.wait_ge(P_.pe.h, v_ktd_mm_all)
            g.dma_start(out=ctx["wstb"][:, 0:4096].rearrange("p (i f) -> p i f", i=NDT),
                        in_=wvs.rearrange("(i p) f -> p i f", p=128)).then_inc(P_.gd.h, 16)
            g.dma_start(out=ctx["wstb"][:, 4096:8192].rearrange("p (i f) -> p i f", i=NDT),
                        in_=wgs.rearrange("(i p) f -> p i f", p=128)).then_inc(P_.gd.h, 16)
    v_wvg = P_.gd.inc(32)
    for tb in range(16):
        r, tl0 = tb // 2, 512 * (tb % 2)
        v_war = ctx.get(("l0vg_mm", tb - 1))
        with _blk(nc) as b:
            @b.sync
            def _(s, r=r, tl0=tl0, v_war=v_war):
                s.wait_ge(P_.cc.h, v_ag)
                if v_war is not None:
                    s.wait_ge(P_.pe.h, v_war)
                s.dma_start(out=xstage[:, :].rearrange("p (i f) -> p i f", i=NDT),
                            in_=xall.ap()[r * D:(r + 1) * D, tl0:tl0 + 512]
                            .rearrange("(i p) f -> p i f", p=128)).then_inc(P_.ld.h, 16)
        v_ld = P_.ld.inc(16)
        for ts4 in range(4):
            pv_ = pool[4 + ts4 % 2]
            pg_ = pool[6 + ts4 % 2]
            with _blk(nc) as b:
                @b.tensor
                def _(t, tb=tb, ts4=ts4, v_ld=v_ld, pv_=pv_, pg_=pg_):
                    t.wait_ge(P_.ld.h, v_ld)
                    t.wait_ge(P_.gd.h, v_wvg)
                    k = ctx.get(("l0vg_ev", tb, ts4 - 2))
                    if k is not None:
                        t.wait_ge(P_.sc.h, k)
                    for i in range(NDT):
                        t.matmul(pv_[:, 0:256], xstage[:, 512 * i + 128 * ts4:512 * i + 128 * (ts4 + 1)],
                                 ctx["wstb"][:, 256 * i:256 * (i + 1)],
                                 start=(i == 0), stop=(i == NDT - 1))
                        ins = t.matmul(pg_[:, 0:256], xstage[:, 512 * i + 128 * ts4:512 * i + 128 * (ts4 + 1)],
                                       ctx["wstb"][:, 4096 + 256 * i:4096 + 256 * (i + 1)],
                                       start=(i == 0), stop=(i == NDT - 1))
                    ins.then_inc(P_.pe.h, 1)
            ctx[("l0vg_mm", tb)] = P_.pe.inc(1)
            v_mm = P_.pe.n
            with _blk(nc) as b:
                @b.scalar
                def _(s, tb=tb, ts4=ts4, v_mm=v_mm, pv_=pv_, pg_=pg_):
                    s.wait_ge(P_.pe.h, v_mm)
                    k = ctx.get(("l0_ssb_st",))
                    if k is not None:
                        s.wait_ge(P_.gd.h, k)
                    s.activation(obuf[:, :], pv_[:, 0:256], AF.Silu).then_inc(P_.sc.h, 1)
                    s.activation(ssb[:, 0:256], pg_[:, 0:256], AF.Sigmoid).then_inc(P_.sc.h, 1)
            v_ev = P_.sc.inc(2)
            ctx[("l0vg_ev", tb, ts4)] = v_ev
            with _blk(nc) as b:
                @b.gpsimd
                def _(g, tb=tb, ts4=ts4, v_ev=v_ev):
                    g.wait_ge(P_.sc.h, v_ev)
                    row0 = 512 * tb + 128 * ts4
                    g.dma_start(out=vl.ap()[row0:row0 + 128, :], in_=obuf[:, :]
                                ).then_inc(P_.gd.h, 16)
                    g.dma_start(out=gl.ap()[row0:row0 + 128, :], in_=ssb[:, 0:256]
                                ).then_inc(P_.gd.h, 16)
            ctx[("l0vg_st", tb, ts4)] = P_.gd.inc(32)
            ctx[("l0_ssb_st",)] = P_.gd.n
    v_vg_st = P_.gd.n

    with _blk(nc) as b:
        @b.gpsimd
        def _(g):
            g.wait_ge(P_.gd.h, max(v_qk_st, v_ktd_st, v_vg_st))
            g.collective_compute("AllGather", AL.bypass, replica_groups=RG,
                                 ins=[ql.ap()], outs=[qall.ap()]).then_inc(P_.cc.h)
            g.collective_compute("AllGather", AL.bypass, replica_groups=RG,
                                 ins=[kl.ap()], outs=[kall.ap()]).then_inc(P_.cc.h)
            g.collective_compute("AllGather", AL.bypass, replica_groups=RG,
                                 ins=[ktdl.ap()], outs=[ktall.ap()]).then_inc(P_.cc.h)
    v_ag2 = P_.cc.inc(3)

    for n in range(32):
        r, col0 = n // 4, 256 * (n % 4)
        row0 = 256 * n
        v_war = ctx.get(("l0s_done", n - 1))
        with _blk(nc) as b:
            @b.sync
            def _(s, r=r, col0=col0, row0=row0, v_war=v_war, n=n):
                s.wait_ge(P_.cc.h, v_ag2)
                if v_war is not None:
                    s.wait_ge(P_.pe.h, v_war)
                k2 = ctx.get(("l0s_kvadd", n - 1, NDT - 1))
                if k2 is not None:
                    s.wait_ge(P_.ve.h, k2)
                s.dma_start(out=qst[:, :].rearrange("p (i f) -> p i f", i=NDT),
                            in_=qall.ap()[r * D:(r + 1) * D, col0:col0 + 256]
                            .rearrange("(i p) f -> p i f", p=128)).then_inc(P_.ld.h, 16)
                s.dma_start(out=kst[:, :].rearrange("p (i f) -> p i f", i=NDT),
                            in_=kall.ap()[r * D:(r + 1) * D, col0:col0 + 256]
                            .rearrange("(i p) f -> p i f", p=128)).then_inc(P_.ld.h, 16)
                s.dma_start(out=ktdst[:, :].rearrange("p (i f) -> p i f", i=2),
                            in_=ktall.ap()[row0:row0 + 256, :]
                            .rearrange("(i p) f -> p i f", p=128)).then_inc(P_.ld.h, 16)
                s.dma_start(out=vst[:, :].rearrange("p (i f) -> p i f", i=2),
                            in_=vl.ap()[row0:row0 + 256, :]
                            .rearrange("(i p) f -> p i f", p=128)).then_inc(P_.ld.h, 16)
                s.dma_start(out=gst[:, :].rearrange("p (i f) -> p i f", i=2),
                            in_=gl.ap()[row0:row0 + 256, :]
                            .rearrange("(i p) f -> p i f", p=128)).then_inc(P_.ld.h, 16)
        v_ld = P_.ld.inc(80)
        pairs = [(0, 0), (0, 1), (1, 1)]
        for pi_, (ks, qs) in enumerate(pairs):
            pa = pool[0]
            v_pe_prev = ctx.get(("l0s_sev", n, pi_ - 1))
            with _blk(nc) as b:
                @b.tensor
                def _(t, ks=ks, qs=qs, v_ld=v_ld, pa=pa, v_pe_prev=v_pe_prev):
                    t.wait_ge(P_.ld.h, v_ld)
                    if v_pe_prev is not None:
                        t.wait_ge(P_.sc.h, v_pe_prev)
                    for i in range(NDT):
                        ins = t.matmul(pa[:, 0:128],
                                       kst[:, 256 * i + 128 * ks:256 * i + 128 * (ks + 1)],
                                       qst[:, 256 * i + 128 * qs:256 * i + 128 * (qs + 1)],
                                       start=(i == 0), stop=(i == NDT - 1))
                    ins.then_inc(P_.pe.h, 1)
            v_smm = P_.pe.inc(1)
            with _blk(nc) as b:
                @b.scalar
                def _(s, pi_=pi_, v_smm=v_smm, pa=pa):
                    s.wait_ge(P_.pe.h, v_smm)
                    s.copy(ssb[:, 128 * pi_:128 * (pi_ + 1)], pa[:, 0:128]
                           ).then_inc(P_.sc.h, 1)
            ctx[("l0s_sev", n, pi_)] = P_.sc.inc(1)
        v_sev = P_.sc.n
        with _blk(nc) as b:
            @b.vector
            def _(v, v_sev=v_sev):
                v.wait_ge(P_.sc.h, v_sev)
                v.tensor_tensor(ssb[:, 0:128], ssb[:, 0:128], dmask[:, 0:128],
                                AL.mult).then_inc(P_.ve.h, 1)
                v.tensor_tensor(ssb[:, 256:384], ssb[:, 256:384], dmask[:, 0:128],
                                AL.mult).then_inc(P_.ve.h, 1)
        v_msk = P_.ve.inc(2)
        for qs in range(2):
            pa = pool[2 + qs]
            with _blk(nc) as b:
                @b.tensor
                def _(t, n=n, qs=qs, v_msk=v_msk, pa=pa):
                    t.wait_ge(P_.ve.h, v_msk)
                    if n > 0:
                        t.wait_ge(P_.sc.h, ctx[("l0s_kvb", n - 1)])
                    ops = []
                    for (ks2, qs2) in [(0, 0), (0, 1), (1, 1)]:
                        if qs2 != qs:
                            continue
                        pi_ = [(0, 0), (0, 1), (1, 1)].index((ks2, qs2))
                        ops.append((ssb[:, 128 * pi_:128 * (pi_ + 1)],
                                    vst[:, 256 * ks2:256 * (ks2 + 1)]))
                    if n > 0:
                        for i in range(NDT):
                            ops.append((qst[:, 256 * i + 128 * qs:256 * i + 128 * (qs + 1)],
                                        kvb[i][:, :]))
                    for j, (lhs_, rhs_) in enumerate(ops):
                        ins = t.matmul(pa[:, 0:256], lhs_, rhs_,
                                       start=(j == 0), stop=(j == len(ops) - 1))
                    ins.then_inc(P_.pe.h, 1)
            v_omm = P_.pe.inc(1)
            with _blk(nc) as b:
                @b.vector
                def _(v, n=n, qs=qs, v_omm=v_omm, pa=pa):
                    v.wait_ge(P_.pe.h, v_omm)
                    v.tensor_tensor(otmp[:, :], pa[:, 0:256],
                                    gst[:, 256 * qs:256 * (qs + 1)], AL.mult
                                    ).then_inc(P_.ve.h, 1)
            v_og = P_.ve.inc(1)
            with _blk(nc) as b:
                @b.tensor
                def _(t, v_og=v_og, qs=qs):
                    t.wait_ge(P_.ve.h, v_og)
                    k = ctx.get(("l0s_obst",))
                    if k is not None:
                        t.wait_ge(P_.sc.h, k)
                    t.transpose(pool[4][:, 0:128], otmp[:, 0:128], identf[:, :])
                    ins = t.transpose(pool[4][:, 128:256], otmp[:, 128:256], identf[:, :])
                    ins.then_inc(P_.pe.h, 1)
            v_tr = P_.pe.inc(1)
            with _blk(nc) as b:
                @b.scalar
                def _(s, v_tr=v_tr):
                    s.wait_ge(P_.pe.h, v_tr)
                    s.copy(obuf[:, 0:128], pool[4][:, 0:128]).then_inc(P_.sc.h, 1)
                    s.copy(obuf[:, 128:256], pool[4][:, 128:256]).then_inc(P_.sc.h, 1)
            v_ob = P_.sc.inc(2)
            ctx[("l0s_obst",)] = v_ob
            with _blk(nc) as b:
                @b.gpsimd
                def _(g, n=n, qs=qs, v_ob=v_ob):
                    g.wait_ge(P_.sc.h, v_ob)
                    col = 256 * n + 128 * qs
                    g.dma_start(out=ogl.ap()[0:128, col:col + 128], in_=obuf[:, 0:128]
                                ).then_inc(P_.gd.h, 16)
                    g.dma_start(out=ogl.ap()[128:256, col:col + 128], in_=obuf[:, 128:256]
                                ).then_inc(P_.gd.h, 16)
            ctx[("l0s_ogst", n, qs)] = P_.gd.inc(32)
        for i in range(NDT):
            pa = pool[5 + i % 2]
            with _blk(nc) as b:
                @b.tensor
                def _(t, n=n, i=i, pa=pa, v_ld=v_ld):
                    t.wait_ge(P_.ld.h, v_ld)
                    k = (ctx.get(("l0s_kvadd", n, i - 2)) or
                         ctx.get(("l0s_kvadd", n - 1, NDT + i - 2)))
                    if k is not None:
                        t.wait_ge(P_.ve.h, k)
                    for ts2 in range(2):
                        ins = t.matmul(pa[:, 0:256],
                                       ktdst[:, 2048 * ts2 + 128 * i:2048 * ts2 + 128 * (i + 1)],
                                       vst[:, 256 * ts2:256 * (ts2 + 1)],
                                       start=(ts2 == 0), stop=(ts2 == 1))
                    ins.then_inc(P_.pe.h, 1)
            v_kmm = P_.pe.inc(1)
            with _blk(nc) as b:
                @b.vector
                def _(v, n=n, i=i, v_kmm=v_kmm, pa=pa):
                    v.wait_ge(P_.pe.h, v_kmm)
                    v.tensor_add(kvf[i][:, :], kvf[i][:, :], pa[:, 0:256]
                                 ).then_inc(P_.ve.h, 1)
            ctx[("l0s_kvadd", n, i)] = P_.ve.inc(1)
            with _blk(nc) as b:
                @b.scalar
                def _(s, n=n, i=i):
                    s.wait_ge(P_.ve.h, ctx[("l0s_kvadd", n, i)])
                    s.copy(kvb[i][:, :], kvf[i][:, :]).then_inc(P_.sc.h, 1)
            ctx[("l0s_kvb", n)] = P_.sc.inc(1)
        ctx[("l0s_done", n)] = P_.pe.n
    v_og_all = P_.gd.n
    with _blk(nc) as b:
        @b.gpsimd
        def _(g):
            g.wait_ge(P_.gd.h, v_og_all)
            g.collective_compute("AllGather", AL.bypass, replica_groups=RG,
                                 ins=[ogl.ap()], outs=[ogall.ap()]).then_inc(P_.cc.h)
    v_ogag = P_.cc.inc(1)

    for tc in range(2):
        off = ctx["pid_off"][tc]
        with _blk(nc) as b:
            @b.sync
            def _(s, tc=tc, off=off):
                s.wait_ge(P_.cc.h, v_ogag)
                if tc == 1:
                    s.wait_ge(P_.pe.h, ctx[("l0a_wo", 0, NDT - 1)])
                s.dma_start(out=xstage[:, :].rearrange("p (i f) -> p i f", i=NDT),
                            in_=ogall.ap().rearrange("(i p) t -> p i t", p=128)
                            [:, :, bass.ts(off, 512)]).then_inc(P_.ld.h, 16)
        v_cs = P_.ld.inc(16)
        for oc in range(NDT):
            v_warw = ctx.get(("l0a_wol", tc, oc - 2))
            with _blk(nc) as b:
                @b.gpsimd
                def _(g, oc=oc, v_warw=v_warw):
                    if v_warw is not None:
                        g.wait_ge(P_.pe.h, v_warw)
                    g.dma_start(out=wst[oc % 2][:, :].rearrange("p (i f) -> p i f", i=NDT),
                                in_=wpT[:, 128 * oc:128 * (oc + 1)]
                                .rearrange("(i p) f -> p i f", p=128)).then_inc(P_.gd.h, 16)
            v_ldw = P_.gd.inc(16)
            pa = pool[oc % 2]
            with _blk(nc) as b:
                @b.tensor
                def _(t, tc=tc, oc=oc, v_ldw=v_ldw, pa=pa, v_cs=v_cs):
                    t.wait_ge(P_.gd.h, v_ldw)
                    t.wait_ge(P_.ld.h, v_cs)
                    if oc <= 1:
                        if tc == 1:
                            t.wait_ge(P_.ve.h, ctx[("l0a_woe", 0, NDT - 2 + oc)])
                    else:
                        t.wait_ge(P_.ve.h, ctx[("l0a_woe", tc, oc - 2)])
                    for i in range(NDT):
                        ins = t.matmul(pa[:, :], wst[oc % 2][:, 128 * i:128 * (i + 1)],
                                       xstage[:, 512 * i:512 * (i + 1)],
                                       start=(i == 0), stop=(i == NDT - 1))
                    ins.then_inc(P_.pe.h, 1)
            ctx[("l0a_wo", tc, oc)] = P_.pe.inc(1)
            ctx[("l0a_wol", tc, oc)] = P_.pe.n
            v_mm = P_.pe.n
            with _blk(nc) as b:
                @b.vector
                def _(v, tc=tc, oc=oc, v_mm=v_mm, pa=pa):
                    v.wait_ge(P_.pe.h, v_mm)
                    v.wait_ge(v_x[0].h, v_x[1])
                    v.tensor_add(xr[oc][:, 512 * tc:512 * (tc + 1)],
                                 xr[oc][:, 512 * tc:512 * (tc + 1)], pa[:, :]
                                 ).then_inc(P_.ve.h, 1)
            ctx[("l0a_woe", tc, oc)] = P_.ve.inc(1)
    return [(P_.ve, P_.ve.n)]




def build_l0_attn_prog():
    Pr = P()
    nc = Pr.nc
    xin = nc.dram_tensor("xin", [D, TL], F32, kind="ExternalInput").ap()
    wqT = nc.dram_tensor("l0a_wqT", [D, D], F32, kind="ExternalInput").ap()
    wkT = nc.dram_tensor("l0a_wkT", [D, D], F32, kind="ExternalInput").ap()
    wvs = nc.dram_tensor("l0a_wvs", [D, 256], F32, kind="ExternalInput").ap()
    wgs = nc.dram_tensor("l0a_wgs", [D, 256], F32, kind="ExternalInput").ap()
    wpT = nc.dram_tensor("l0a_wpT", [D, D], F32, kind="ExternalInput").ap()
    xout = nc.dram_tensor("xout", [D, TL], F32, kind="ExternalOutput").ap()
    Pr.make_pool()
    xr = [Pr.sb(f"xr{i}", [128, TL], F32) for i in range(NDT)]
    xnb = [Pr.sb(f"xnb{i}", [128, TL], BF16) for i in range(NDT)]
    ctx = {
        "xstage": Pr.sb("xstage", [128, 8192], BF16),
        "wst": [Pr.sb(f"wst{i}", [128, 2048], BF16) for i in range(2)],
        "wst2": [Pr.sb(f"wst2{i}", [128, 512], BF16) for i in range(2)],
        "wstb": Pr.sb("wstb", [128, 8192], BF16),
        "kvf": [Pr.sb(f"kvf{i}", [128, 256], F32) for i in range(NDT)],
        "kvb": [Pr.sb(f"kvb{i}", [128, 256], BF16) for i in range(NDT)],
        "qst": Pr.sb("qst", [128, 4096], BF16),
        "kst": Pr.sb("kst", [128, 4096], BF16),
        "ktdst": Pr.sb("ktdst", [128, 4096], BF16),
        "vst": Pr.sb("vst", [128, 512], BF16),
        "gst": Pr.sb("gst", [128, 512], BF16),
        "ssb": Pr.sb("ssb", [128, 512], BF16),
        "otmp": Pr.sb("otmp", [128, 256], F32),
        "obuf": Pr.sb("obuf", [128, 256], BF16),
    }
    ctx["identf"] = Pr.sconst("ident_f", np.eye(128, dtype=np.float32))
    ctx["dmask"] = Pr.sconst("l3a_dmask", _l3_dmask())
    Pr.ones_col_b(); Pr.ones_row_f()
    Pr.flush_consts()
    nctx = make_norm_ctx(Pr, sq_alias=[ctx["xstage"][:, 512 * i:512 * (i + 1)]
                                       for i in range(NDT)],
                         atmp_alias=None)
    with _blk(nc) as b:
        @b.sync
        def _(s):
            pid = nc.partition_id()
            ctx["pid_off"] = [nc.snap(pid * 2 + tc, min_val=0,
                                      max_val=2 * NCORES - 1) for tc in range(2)]
    v_x = load_residual(Pr, xr, xin)
    rfin = emit_norm(Pr, nctx, xr, xnb, None, None, subtract_mean=False,
                     eps=RMS_EPS, pre_waits=[(Pr.ld, v_x)], write_back=False)
    kv0 = Pr.ve.n
    with _blk(nc) as b:
        @b.vector
        def _(v):
            for i in range(NDT):
                v.memset(ctx["kvf"][i][:, :], 0.0)
            v.engine_nop().then_inc(Pr.ve.h, 1)
    Pr.ve.inc(1)
    fin = emit_l0_attn(Pr, ctx, xr, xnb, rfin, wqT, wkT, wvs, wgs, wpT,
                       (Pr.ld, v_x))
    fin2 = emit_norm(Pr, nctx, xr, xnb, None, None, subtract_mean=True,
                     pre_waits=fin)
    store_residual(Pr, xr, xout, [fin2])
    Pr.close_block()
    return Pr


def run_l0_attn(x, Wq, Wk, Wv, Wg, Wp):
    Pr = build_l0_attn_prog()
    xs = _shard_x(x)
    wqT = np.ascontiguousarray(Wq.T); wkT = np.ascontiguousarray(Wk.T)
    wvT = np.ascontiguousarray(Wv.T); wgT = np.ascontiguousarray(Wg.T)
    wpT = np.ascontiguousarray(Wp.T)
    in_maps = []
    for c in range(NCORES):
        in_maps.append({
            "xin": xs[c], "l0a_wqT": wqT, "l0a_wkT": wkT,
            "l0a_wvs": np.ascontiguousarray(wvT[:, 256 * c:256 * (c + 1)]),
            "l0a_wgs": np.ascontiguousarray(wgT[:, 256 * c:256 * (c + 1)]),
            "l0a_wpT": wpT,
        })
    res = run_bass_kernel_spmd(Pr.nc, in_maps, core_ids=list(range(NCORES)))
    return _unshard_x([res.results[c]["xout"] for c in range(NCORES)])





CL1 = 128
NCH1 = T // CL1


def _triu_incl():
    return np.triu(np.ones((128, 128), np.float32))


def _m_lt():
    return (-1e30 * np.tril(np.ones((128, 128), np.float32), -1)).astype(np.float32)


def _m_le():
    return (-1e30 * np.tril(np.ones((128, 128), np.float32), 0)).astype(np.float32)


def _m_ge():
    return (1e30 * np.triu(np.ones((128, 128), np.float32), 0)).astype(np.float32)


def emit_l1_attn(P_, ctx, xr, xnb, xnb_ready, wmix_ap, conv_ap, dtb_ap, alog_ap,
                 nw_ap, wout_ap, v_x):
    nc = P_.nc
    pool = P_.pool
    xstage = ctx["xstage"]
    wst = ctx["wst"]
    msbq, msbk = ctx["msbq"], ctx["msbk"]
    convw, ctmp, carry = ctx["convw"], ctx["ctmp"], ctx["carry"]
    ssb = ctx["ssb"]
    identf, identb = ctx["identf"], ctx["identb"]
    triu, mlt, mle, mge = ctx["triu"], ctx["mlt"], ctx["mle"], ctx["mge"]
    ones_r = P_.ones_row_f()
    ones_cf = ctx["ones_col_f"]
    DKS = float(128 ** -0.5)
    W = ctx

    zl = nc.dram_tensor("l1a_zl", [T, 256], BF16)
    vl = nc.dram_tensor("l1a_vl", [256, T], BF16)
    bal = nc.dram_tensor("l1a_bal", [4, T], F32)
    ogl = nc.dram_tensor("l1a_ogl", [256, T], BF16)
    xlo = nc.dram_tensor("l1a_xl", [D, TL], BF16)
    xall = nc.dram_tensor("l1a_xall", [NCORES * D, TL], BF16, addr_space="Shared")
    ogall = nc.dram_tensor("l1a_ogall", [NCORES * 256, T], BF16, addr_space="Shared")

    def step(eng, fn, waits=()):
        sem = {"tensor": P_.pe, "vector": P_.ve, "scalar": P_.sc,
               "sync": P_.ld, "gpsimd": P_.gd}[eng]
        amt = 16 if eng in ("sync", "gpsimd") else 1
        with _blk(nc) as b:
            def body(e, fn=fn, waits=tuple(waits)):
                for (sm, w) in waits:
                    e.wait_ge(sm.h, w)
                fn(e).then_inc(sem.h, amt)
            getattr(b, eng)(body)
        return (sem, sem.inc(amt))

    with _blk(nc) as b:
        @b.gpsimd
        def _(g):
            g.wait_ge(xnb_ready[0].h, xnb_ready[1])
            for i in range(NDT):
                g.dma_start(out=xlo.ap()[128 * i:128 * (i + 1), :], in_=xnb[i][:, :]
                            ).then_inc(P_.gd.h, 16)
            g.dma_start(out=convw[:, :], in_=conv_ap).then_inc(P_.gd.h, 16)
            g.dma_start(out=W["dtbc"][0:4, :], in_=dtb_ap).then_inc(P_.gd.h, 16)
            g.dma_start(out=W["alog4"][0:1, :], in_=alog_ap).then_inc(P_.gd.h, 16)
            g.dma_start(out=W["nwrow"][0:1, :], in_=nw_ap).then_inc(P_.gd.h, 16)
            g.dma_start(out=ctx["wmx"][:, :].rearrange("p (i f) -> p i f", i=NDT),
                        in_=wmix_ap.rearrange("(i p) f -> p i f", p=128)
                        ).then_inc(P_.gd.h, 16)
    v_pre = P_.gd.inc(16 * (NDT + 5))
    with _blk(nc) as b:
        @b.gpsimd
        def _(g):
            g.wait_ge(P_.gd.h, v_pre)
            g.collective_compute("AllGather", AL.bypass, replica_groups=RG,
                                 ins=[xlo.ap()], outs=[xall.ap()]).then_inc(P_.cc.h)
    v_ag = P_.cc.inc(1)

    s_t1 = step("tensor", lambda e: e.transpose(pool[7][0:4, 0:1],
                                                W["alog4"][0:1, 0:4],
                                                identf[0:1, 0:1]),
                waits=[(P_.gd, v_pre), (P_.ld, P_.v_consts)])
    s_t2 = step("scalar", lambda e: e.activation(W["eAc"][0:4, 0:1], pool[7][0:4, 0:1],
                                                 AF.Exp), waits=[s_t1])
    s_t3 = step("vector", lambda e: e.tensor_scalar(W["eAc"][0:4, 0:1],
                                                    W["eAc"][0:4, 0:1], -1.0, None,
                                                    AL.mult), waits=[s_t2])
    s_nw1 = step("tensor", lambda e: e.matmul(pool[7][:, 0:128], ones_r[:, :],
                                              W["nwrow"][0:1, :], start=True,
                                              stop=True), waits=[s_t2])
    s_nw2 = step("scalar", lambda e: e.copy(W["nwmat"][:, :], pool[7][:, 0:128]),
                 waits=[s_nw1])

    for tg in range(16):
        r, tl0 = tg // 2, 512 * (tg % 2)
        v_war = ctx.get(("l1a_pj", tg - 1))
        ld = step("sync", lambda e, r=r, tl0=tl0: e.dma_start(
            out=xstage[:, :].rearrange("p (i f) -> p i f", i=NDT),
            in_=xall.ap()[r * D:(r + 1) * D, tl0:tl0 + 512]
            .rearrange("(i p) f -> p i f", p=128)),
            waits=[(P_.cc, v_ag)] + ([v_war] if v_war else []))

        def pj(t, ld=ld):
            for i in range(NDT):
                st, sp = (i == 0), (i == NDT - 1)
                for och in range(6):
                    t.matmul(pool[och][:, :],
                             ctx["wmx"][:, 772 * i + 128 * och:772 * i + 128 * (och + 1)],
                             xstage[:, 512 * i:512 * (i + 1)], start=st, stop=sp)
                ins = t.matmul(pool[6][0:4, :],
                               ctx["wmx"][:, 772 * i + 768:772 * i + 772],
                               xstage[:, 512 * i:512 * (i + 1)], start=st, stop=sp)
            return ins
        prev_ev = ctx.get(("l1a_pjdone", tg - 1))
        mm = step("tensor", pj, waits=[ld, s_nw2] + ([prev_ev] if prev_ev else []))
        ctx[("l1a_pj", tg)] = mm

        last = mm
        for j in range(4):
            pa = pool[j]
            s_cp = step("scalar", lambda e, pa=pa: e.copy(ctmp[:, 4:516], pa[:, :]),
                        waits=[mm, last])
            if tg == 0:
                s_ci = step("vector", lambda e: e.memset(ctmp[:, 0:4], 0.0),
                            waits=[s_cp])
            else:
                s_ci = step("vector", lambda e, j=j: e.tensor_copy(
                    ctmp[:, 0:4], carry[:, 4 * j:4 * (j + 1)]), waits=[s_cp])
            s_co = step("vector", lambda e, j=j: e.tensor_copy(
                carry[:, 4 * j:4 * (j + 1)], ctmp[:, 512:516]), waits=[s_ci])
            with _blk(nc) as b:
                @b.vector
                def _(v, j=j, s_co=s_co):
                    v.wait_ge(s_co[0].h, s_co[1])
                    b0 = P_.ve.n
                    v.tensor_scalar(W["cacc"][:, :], ctmp[:, 1:513],
                                    convw[:, 4 * j:4 * j + 1], None, AL.mult
                                    ).then_inc(P_.ve.h, 1)
                    for t_ in range(1, 4):
                        v.wait_ge(P_.ve.h, b0 + t_)
                        v.scalar_tensor_tensor(W["cacc"][:, :],
                                               ctmp[:, 1 + t_:513 + t_],
                                               convw[:, 4 * j + t_:4 * j + t_ + 1],
                                               W["cacc"][:, :], AL.mult, AL.add
                                               ).then_inc(P_.ve.h, 1)
            s_cv = (P_.ve, P_.ve.inc(4))

            def silu_out(e, j=j, tg=tg):
                if j == 0:
                    return e.activation(msbq[:, 512 * tg:512 * (tg + 1)],
                                        W["cacc"][:, :], AF.Silu)
                if j == 1:
                    return e.activation(msbk[:, 512 * tg:512 * (tg + 1)],
                                        W["cacc"][:, :], AF.Silu)
                return e.activation(ssb[:, 0:512], W["cacc"][:, :], AF.Silu)
            w_ = [s_cv]
            if j >= 2 and ctx.get(("l1a_vst",)):
                w_.append(ctx[("l1a_vst",)])
            s_so = step("scalar", silu_out, waits=w_)
            if j >= 2:
                s_vs = step("gpsimd", lambda e, j=j, tg=tg: e.dma_start(
                    out=vl.ap()[128 * (j - 2):128 * (j - 1), 512 * tg:512 * (tg + 1)],
                    in_=ssb[:, 0:512]), waits=[s_so])
                ctx[("l1a_vst",)] = s_vs
            last = s_so
        for j in range(2):
            pa = pool[4 + j]
            w_ = [mm]
            if ctx.get(("l1a_zst", j)):
                w_.append(ctx[("l1a_zst", j)])
            if ctx.get(("l1a_vst",)):
                w_.append(ctx[("l1a_vst",)])
            s_z = step("scalar", lambda e, pa=pa: e.activation(W["cacc"][:, :],
                                                               pa[:, :], AF.Silu),
                       waits=w_)

            def ztr(e):
                for u in range(4):
                    ins = e.transpose(pool[7][:, 128 * u:128 * (u + 1)],
                                      W["cacc"][:, 128 * u:128 * (u + 1)],
                                      identf[:, :])
                return ins
            s_zt = step("tensor", ztr, waits=[s_z])
            s_zc = step("scalar", lambda e: e.copy(W["ztmp"][:, :], pool[7][:, 0:512]),
                        waits=[s_zt])
            s_zs = step("gpsimd", lambda e, j=j, tg=tg: e.dma_start(
                out=zl.ap()[512 * tg:512 * (tg + 1), 128 * j:128 * (j + 1)]
                .rearrange("(u p) f -> p u f", p=128),
                in_=W["ztmp"][:, :].rearrange("p (u f) -> p u f", u=4)),
                waits=[s_zc])
            ctx[("l1a_zst", j)] = s_zs
        s_b1 = step("vector", lambda e: e.tensor_scalar(W["bae"][0:4, 0:512],
                                                        pool[6][0:4, :],
                                                        W["dtbc"][0:4, 0:1], None,
                                                        AL.add), waits=[mm, (P_.gd, v_pre)])
        s_b2 = step("scalar", lambda e: e.activation(W["baa"][0:4, 0:512],
                                                     W["bae"][0:4, 0:512], AF.Abs),
                    waits=[s_b1])
        s_b3 = step("scalar", lambda e: e.activation(W["baa"][0:4, 0:512],
                                                     W["baa"][0:4, 0:512], AF.Exp,
                                                     scale=-1.0), waits=[s_b2])
        s_b4 = step("scalar", lambda e: e.activation(W["baa"][0:4, 0:512],
                                                     W["baa"][0:4, 0:512], AF.Ln,
                                                     bias=1.0), waits=[s_b3])
        s_b5 = step("vector", lambda e: e.tensor_scalar(W["bae"][0:4, 0:512],
                                                        W["bae"][0:4, 0:512], 0.0,
                                                        None, AL.max), waits=[s_b4])
        s_b6 = step("vector", lambda e: e.tensor_add(W["baa"][0:4, 0:512],
                                                     W["baa"][0:4, 0:512],
                                                     W["bae"][0:4, 0:512]),
                    waits=[s_b5])
        s_b7 = step("vector", lambda e: e.tensor_scalar(W["baa"][0:4, 0:512],
                                                        W["baa"][0:4, 0:512],
                                                        W["eAc"][0:4, 0:1], None,
                                                        AL.mult), waits=[s_b6, s_t3])
        s_b8 = step("scalar", lambda e: e.activation(W["bae"][0:4, 0:512],
                                                     pool[6][0:4, :], AF.Sigmoid),
                    waits=[s_b7])
        s_b9 = step("gpsimd", lambda e, tg=tg: e.dma_start(
            out=bal.ap()[0:2, 512 * tg:512 * (tg + 1)], in_=W["bae"][0:2, 0:512]),
            waits=[s_b8])
        s_b10 = step("gpsimd", lambda e, tg=tg: e.dma_start(
            out=bal.ap()[2:4, 512 * tg:512 * (tg + 1)], in_=W["baa"][2:4, 0:512]),
            waits=[s_b9])
        ctx[("l1a_pjdone", tg)] = s_b10 if False else (P_.sc, P_.sc.n)
        ctx[("l1a_badone", tg)] = s_b10

    prev_q = ctx[("l1a_pjdone", 15)]
    for tg in range(16):
        sl = slice(512 * tg, 512 * (tg + 1))
        for (buf, scale) in ((msbq, DKS), (msbk, 1.0)):
            s_q1 = step("scalar", lambda e, buf=buf, sl=sl: e.activation(
                ssb[:, 0:512], buf[:, sl], AF.Square),
                waits=[prev_q])
            s_q2 = step("tensor", lambda e: e.matmul(pool[6][0:1, 0:512],
                                                     ctx["ones_col_b2"][:, :],
                                                     ssb[:, 0:512], start=True,
                                                     stop=True), waits=[s_q1])
            s_q3 = step("vector", lambda e: e.tensor_scalar(W["qrow"][0:1, 0:512],
                                                            pool[6][0:1, 0:512], 1.0,
                                                            1e-6, AL.mult, AL.add),
                        waits=[s_q2])
            s_q4 = step("scalar", lambda e: e.activation(W["qrow"][0:1, 0:512],
                                                         W["qrow"][0:1, 0:512], AF.Ln),
                        waits=[s_q3])
            s_q5 = step("scalar", lambda e, scale=scale: e.activation(
                W["qrow"][0:1, 0:512], W["qrow"][0:1, 0:512], AF.Exp, scale=-0.5),
                waits=[s_q4])
            if scale != 1.0:
                s_q5 = step("scalar", lambda e, scale=scale: e.mul(
                    W["qrow"][0:1, 0:512], W["qrow"][0:1, 0:512], scale), waits=[s_q5])
            s_q6 = step("tensor", lambda e: e.matmul(pool[6][:, 0:512], ones_r[:, :],
                                                     W["qrow"][0:1, 0:512],
                                                     start=True, stop=True),
                        waits=[s_q5])
            s_q7 = step("vector", lambda e, buf=buf, sl=sl: e.tensor_tensor(
                buf[:, sl], buf[:, sl], pool[6][:, 0:512], AL.mult), waits=[s_q6])
            prev_q = s_q7
    v_qk_done = (P_.ve, P_.ve.n)

    Sb = ctx["Sb"]
    s_init = step("vector", lambda e: (e.memset(Sb[0][:, :], 0.0),
                                       e.memset(Sb[1][:, :], 0.0))[-1], waits=[])
    for n in range(NCH1):
        c0 = CL1 * n
        csl = slice(c0, c0 + CL1)
        wld = [(P_.cc, v_ag)]
        wld.append(ctx[("l1a_badone", n // 4)])
        k = ctx.get(("l1a_chdone", n - 1))
        if k is not None:
            wld.append(k)
        ld1 = step("sync", lambda e, c0=c0: e.dma_start(
            out=W["bcols"][:, 0:4],
            in_=bal.ap()[0:4, c0:c0 + 128].rearrange("a f -> f a"),
            allow_slow_non_contiguous=True), waits=wld)
        ld2 = ld1
        ld3 = step("sync", lambda e, c0=c0: e.dma_start(
            out=W["vch"][:, 0:256].rearrange("p (u f) -> p u f", u=2),
            in_=vl.ap()[0:256, c0:c0 + 128].rearrange("(u p) f -> p u f", p=128)),
            waits=[ld2])
        ld4 = step("sync", lambda e, c0=c0: e.dma_start(
            out=W["zch"][:, 0:256], in_=zl.ap()[c0:c0 + 128, :]), waits=[ld3])
        s_c1 = step("tensor", lambda e: e.matmul(pool[6][:, 0:2], triu[:, :],
                                                 W["bcols"][:, 2:4], start=True,
                                                 stop=True),
                    waits=[ld1, v_qk_done, (P_.ld, P_.v_consts)])
        s_c2 = step("scalar", lambda e: e.copy(W["cumc"][:, 0:2], pool[6][:, 0:2]),
                    waits=[s_c1])
        s_c3 = step("tensor", lambda e: e.matmul(pool[6][0:1, 2:4], ones_cf[:, :],
                                                 W["bcols"][:, 2:4], start=True,
                                                 stop=True), waits=[s_c2])
        s_c3b = step("scalar", lambda e: e.copy(W["sgr"][0:1, 0:2],
                                                pool[6][0:1, 2:4]), waits=[s_c3])
        s_c4 = step("scalar", lambda e: e.activation(W["eACr"][0:1, 0:2],
                                                     pool[6][0:1, 2:4], AF.Exp),
                    waits=[s_c3b])
        for h in range(2):
            hb = 2 * n + h
            s_h1 = step("tensor", lambda e, h=h: e.transpose(
                pool[6][0:1, 4:132], W["cumc"][:, h:h + 1], identf[:, :]),
                waits=[s_c4, ctx.get(("l1a_hdone", hb - 1), s_c4)])
            s_h2 = step("scalar", lambda e: e.copy(W["cumr"][0:1, 0:128],
                                                   pool[6][0:1, 4:132]), waits=[s_h1])
            s_h3 = step("tensor", lambda e: e.matmul(pool[5][:, 0:128], ones_r[:, :],
                                                     W["cumr"][0:1, 0:128],
                                                     start=True, stop=True),
                        waits=[s_h2])
            s_h4 = step("vector", lambda e, h=h: e.tensor_scalar(
                W["Dt"][:, :], pool[5][:, 0:128], W["cumc"][:, h:h + 1], None,
                AL.subtract), waits=[s_h3])
            s_h5 = step("scalar", lambda e: e.activation(W["ecumR"][:, :],
                                                         pool[5][:, 0:128], AF.Exp),
                        waits=[s_h4])
            s_h6 = step("vector", lambda e: e.tensor_add(W["tA"][:, :], W["Dt"][:, :],
                                                         mlt[:, :]), waits=[s_h5])
            s_h7 = step("scalar", lambda e: e.activation(W["EmDT"][:, :], W["tA"][:, :],
                                                         AF.Exp), waits=[s_h6])
            s_h8 = step("vector", lambda e: e.tensor_add(W["tA"][:, :], W["Dt"][:, :],
                                                         mle[:, :]), waits=[s_h7])
            s_h9 = step("scalar", lambda e: e.activation(W["EkDT"][:, :], W["tA"][:, :],
                                                         AF.Exp), waits=[s_h8])
            s_h10 = step("vector", lambda e: e.tensor_add(W["tA"][:, :], W["Dt"][:, :],
                                                          mge[:, :]), waits=[s_h9])
            s_h11 = step("scalar", lambda e: e.activation(W["EkD"][:, :], W["tA"][:, :],
                                                          AF.Exp, scale=-1.0),
                         waits=[s_h10])
            s_h12 = step("tensor", lambda e, csl=csl: e.matmul(
                pool[5][:, 0:128], msbk[:, csl], msbk[:, csl], start=True, stop=True),
                waits=[s_h11])
            s_h13 = step("scalar", lambda e: e.copy(W["kkf"][:, :], pool[5][:, 0:128]),
                         waits=[s_h12])
            s_h14 = step("tensor", lambda e, csl=csl: e.matmul(
                pool[5][:, 0:128], msbk[:, csl], msbq[:, csl], start=True, stop=True),
                waits=[s_h13])
            s_h15 = step("scalar", lambda e: e.copy(W["qkf"][:, :], pool[5][:, 0:128]),
                         waits=[s_h14])
            s_h16 = step("tensor", lambda e, h=h: e.transpose(
                pool[6][0:1, 4:132], W["bcols"][:, h:h + 1], identf[:, :]),
                waits=[s_h15])
            s_h17 = step("scalar", lambda e: e.copy(W["cumr"][0:1, 0:128],
                                                    pool[6][0:1, 4:132]),
                         waits=[s_h16])
            s_h18 = step("tensor", lambda e: e.matmul(pool[5][:, 0:128], ones_r[:, :],
                                                      W["cumr"][0:1, 0:128],
                                                      start=True, stop=True),
                        waits=[s_h17])
            s_h19 = step("scalar", lambda e: e.copy(W["bmat"][:, :], pool[5][:, 0:128]),
                         waits=[s_h18])
            s_h20 = step("vector", lambda e: e.tensor_tensor(W["MT"][:, :],
                                                             W["qkf"][:, :],
                                                             W["EmDT"][:, :],
                                                             AL.mult), waits=[s_h19])
            s_h21 = step("vector", lambda e: e.tensor_tensor(W["tA"][:, :],
                                                             W["kkf"][:, :],
                                                             W["EkDT"][:, :],
                                                             AL.mult), waits=[s_h20])
            s_h22 = step("vector", lambda e: e.scalar_tensor_tensor(
                W["NT"][:, :], W["tA"][:, :], -1.0, W["bmat"][:, :], AL.mult,
                AL.mult), waits=[s_h21])
            s_h23 = step("vector", lambda e: e.tensor_tensor(W["tA"][:, :],
                                                             W["kkf"][:, :],
                                                             W["EkD"][:, :], AL.mult),
                         waits=[s_h22])
            s_h24 = step("vector", lambda e, h=h: e.tensor_scalar(
                W["NTT"][:, :], W["tA"][:, :], W["bcols"][:, h:h + 1], -1.0,
                AL.mult, AL.mult), waits=[s_h23])
            s_h25 = step("vector", lambda e: e.tensor_add(W["PT"][:, :], identb[:, :],
                                                          W["NT"][:, :]), waits=[s_h24])
            last = s_h25
            for kk_ in range(1, 7):
                sA = step("tensor", lambda e: e.matmul(pool[5][:, 0:128],
                                                       W["NTT"][:, :], W["NT"][:, :],
                                                       start=True, stop=True),
                          waits=[last])
                sB = step("scalar", lambda e: e.copy(W["NT2"][:, :], pool[5][:, 0:128]),
                          waits=[sA])
                sC = step("tensor", lambda e: e.matmul(pool[5][:, 0:128],
                                                       W["NT"][:, :], W["NTT"][:, :],
                                                       start=True, stop=True),
                          waits=[sB])
                sD = step("scalar", lambda e: e.copy(W["NTT"][:, :],
                                                     pool[5][:, 0:128]), waits=[sC])
                def pupd(e):
                    e.matmul(pool[5][:, 0:128], identb[:, :], W["PT"][:, :],
                             start=True, stop=False)
                    return e.matmul(pool[5][:, 0:128], W["NTT"][:, :], W["PT"][:, :],
                                    start=False, stop=True)
                sE = step("tensor", pupd, waits=[sD])
                sF = step("scalar", lambda e: e.copy(W["PT"][:, :], pool[5][:, 0:128]),
                          waits=[sE])
                sG = step("vector", lambda e: e.tensor_copy(W["NT"][:, :],
                                                            W["NT2"][:, :]), waits=[sF])
                last = sG
            s_h26a = step("scalar", lambda e, csl=csl: e.copy(W["tA"][:, :],
                                                              msbk[:, csl]),
                          waits=[last])
            s_h26 = step("tensor", lambda e: e.transpose(
                pool[5][:, 0:128], W["tA"][:, :], identf[:, :]), waits=[s_h26a])
            s_h27 = step("scalar", lambda e: e.copy(W["kT"][:, :], pool[5][:, 0:128]),
                         waits=[s_h26])
            s_h28a = step("scalar", lambda e, h=h: e.copy(
                W["tA"][:, :], W["vch"][:, 128 * h:128 * (h + 1)]), waits=[s_h27])
            s_h28 = step("tensor", lambda e: e.transpose(
                pool[5][:, 0:128], W["tA"][:, :], identf[:, :]), waits=[s_h28a])
            s_h29 = step("scalar", lambda e: e.copy(W["vT"][:, :], pool[5][:, 0:128]),
                         waits=[s_h28])
            s_h30 = step("scalar", lambda e, h=h: e.activation(
                W["ecumc"][:, 0:1], W["cumc"][:, h:h + 1], AF.Exp), waits=[s_h29])
            s_h31 = step("vector", lambda e, h=h: e.tensor_tensor(
                W["beE"][:, 0:1], W["bcols"][:, h:h + 1], W["ecumc"][:, 0:1],
                AL.mult), waits=[s_h30])
            s_h34 = step("tensor", lambda e, h=h: e.matmul(
                pool[6][:, 4:5], ones_r[:, :], W["sgr"][0:1, h:h + 1], start=True,
                stop=True), waits=[s_h31])
            s_h34b = step("scalar", lambda e: e.copy(W["eACc"][:, 0:1],
                                                     pool[6][:, 4:5]), waits=[s_h34])
            s_h35a = step("vector", lambda e, h=h: e.tensor_tensor(
                W["ecr"][:, 0:1], W["eACc"][:, 0:1], W["cumc"][:, h:h + 1],
                AL.subtract), waits=[s_h34b])
            s_h35b = step("scalar", lambda e: e.activation(W["ecr"][:, 0:1],
                                                           W["ecr"][:, 0:1], AF.Exp),
                          waits=[s_h35a])
            s_h35 = step("scalar", lambda e: e.activation(W["eACc"][:, 0:1],
                                                          W["eACc"][:, 0:1], AF.Exp),
                         waits=[s_h35b])
            s_h36 = step("vector", lambda e: e.tensor_scalar(
                W["Wb"][:, :], W["kT"][:, :], W["beE"][:, 0:1], None, AL.mult),
                waits=[s_h35])
            s_h37 = step("vector", lambda e, h=h: e.tensor_scalar(
                W["Bv"][:, :], W["vT"][:, :], W["bcols"][:, h:h + 1], None, AL.mult),
                waits=[s_h36])
            s_h38 = step("vector", lambda e: e.tensor_scalar(
                W["Kp"][:, :], W["kT"][:, :], W["ecr"][:, 0:1], None, AL.mult),
                waits=[s_h37])
            s_h39 = step("tensor", lambda e: e.matmul(pool[5][:, 0:128], W["PT"][:, :],
                                                      W["Wb"][:, :], start=True,
                                                      stop=True), waits=[s_h38])
            s_h40 = step("scalar", lambda e: e.copy(W["Wp"][:, :], pool[5][:, 0:128]),
                         waits=[s_h39])
            s_h41 = step("tensor", lambda e: e.matmul(pool[5][:, 0:128], W["PT"][:, :],
                                                      W["Bv"][:, :], start=True,
                                                      stop=True), waits=[s_h40])
            s_h42 = step("scalar", lambda e: e.copy(W["Ub"][:, :], pool[5][:, 0:128]),
                         waits=[s_h41])
            s_h43 = step("vector", lambda e, csl=csl: e.tensor_tensor(
                W["QtT"][:, :], msbq[:, csl], W["ecumR"][:, :], AL.mult),
                waits=[s_h42])
            s_h44 = step("tensor", lambda e: e.matmul(pool[5][:, 0:128], W["Wp"][:, :],
                                                      W["MT"][:, :], start=True,
                                                      stop=True), waits=[s_h43])
            s_h45 = step("vector", lambda e: e.tensor_sub(W["QeT"][:, :],
                                                          W["QtT"][:, :],
                                                          pool[5][:, 0:128]),
                         waits=[s_h44])
            def omm(e, n=n, h=h):
                if n > 0:
                    e.matmul(pool[4][:, 0:128], W["QeT"][:, :], Sb[h][:, :],
                             start=True, stop=False)
                    return e.matmul(pool[4][:, 0:128], W["MT"][:, :], W["Ub"][:, :],
                                    start=False, stop=True)
                return e.matmul(pool[4][:, 0:128], W["MT"][:, :], W["Ub"][:, :],
                                start=True, stop=True)
            s_h46 = step("tensor", omm, waits=[s_h45, s_init])
            s_h47 = step("tensor", lambda e: e.matmul(pool[5][:, 0:128], W["Wp"][:, :],
                                                      W["Kp"][:, :], start=True,
                                                      stop=True), waits=[s_h46])
            s_h48 = step("scalar", lambda e: e.activation(W["nG2T"][:, :],
                                                          pool[5][:, 0:128], AF.Copy,
                                                          scale=-1.0), waits=[s_h47])
            s_h49 = step("vector", lambda e: e.tensor_scalar(
                W["IA"][:, :], identb[:, :], W["eACc"][:, 0:1], None, AL.mult),
                waits=[s_h48])
            def smm(e, n=n, h=h):
                if n > 0:
                    e.matmul(pool[5][:, 0:128], W["IA"][:, :], Sb[h][:, :],
                             start=True, stop=False)
                    e.matmul(pool[5][:, 0:128], W["nG2T"][:, :], Sb[h][:, :],
                             start=False, stop=False)
                    return e.matmul(pool[5][:, 0:128], W["Kp"][:, :], W["Ub"][:, :],
                                    start=False, stop=True)
                return e.matmul(pool[5][:, 0:128], W["Kp"][:, :], W["Ub"][:, :],
                                start=True, stop=True)
            s_h50 = step("tensor", smm, waits=[s_h49])
            s_h51 = step("scalar", lambda e, h=h: e.copy(Sb[h][:, :],
                                                         pool[5][:, 0:128]),
                         waits=[s_h50])
            s_h52 = step("scalar", lambda e: e.copy(W["och"][:, :], pool[4][:, 0:128]),
                         waits=[s_h46, s_h51])
            s_h53 = step("scalar", lambda e: e.activation(W["osq"][:, :],
                                                          W["och"][:, :], AF.Square),
                         waits=[s_h52])
            s_h54 = step("vector", lambda e: e.tensor_reduce(
                W["ovar"][:, 0:1], W["osq"][:, :], AX.X, AL.add), waits=[s_h53])
            s_h55 = step("vector", lambda e: e.tensor_scalar(
                W["ovar"][:, 0:1], W["ovar"][:, 0:1], 1.0 / 128, 1e-6, AL.mult,
                AL.add), waits=[s_h54])
            s_h56 = step("scalar", lambda e: e.activation(W["ovar"][:, 0:1],
                                                          W["ovar"][:, 0:1], AF.Ln),
                         waits=[s_h55])
            s_h57 = step("scalar", lambda e: e.activation(W["ovar"][:, 0:1],
                                                          W["ovar"][:, 0:1], AF.Exp,
                                                          scale=-0.5), waits=[s_h56])
            s_h58 = step("vector", lambda e: e.tensor_scalar(
                W["och"][:, :], W["och"][:, :], W["ovar"][:, 0:1], None, AL.mult),
                waits=[s_h57])
            s_h59 = step("vector", lambda e: e.tensor_tensor(
                W["och"][:, :], W["och"][:, :], W["nwmat"][:, :], AL.mult),
                waits=[s_h58])
            s_h60 = step("vector", lambda e, h=h: e.tensor_tensor(
                W["och"][:, :], W["och"][:, :], W["zch"][:, 128 * h:128 * (h + 1)],
                AL.mult), waits=[s_h59])
            s_h61 = step("tensor", lambda e: e.transpose(pool[5][:, 0:128],
                                                         W["och"][:, :], identf[:, :]),
                         waits=[s_h60])
            s_h62 = step("scalar", lambda e: e.copy(W["obuf"][:, 0:128],
                                                    pool[5][:, 0:128]), waits=[s_h61])
            s_h63 = step("gpsimd", lambda e, n=n, h=h: e.dma_start(
                out=ogl.ap()[128 * h:128 * (h + 1), CL1 * n:CL1 * (n + 1)],
                in_=W["obuf"][:, 0:128]), waits=[s_h62])
            ctx[("l1a_hdone", hb)] = s_h63
        ctx[("l1a_chdone", n)] = ctx[("l1a_hdone", 2 * n + 1)]

    v_last = ctx[("l1a_chdone", NCH1 - 1)]
    with _blk(nc) as b:
        @b.gpsimd
        def _(g):
            g.wait_ge(v_last[0].h, v_last[1])
            g.collective_compute("AllGather", AL.bypass, replica_groups=RG,
                                 ins=[ogl.ap()], outs=[ogall.ap()]).then_inc(P_.cc.h)
    v_ogag = P_.cc.inc(1)

    for tc in range(2):
        off = ctx["pid_off"][tc]
        with _blk(nc) as b:
            @b.sync
            def _(s, tc=tc, off=off):
                s.wait_ge(P_.cc.h, v_ogag)
                if tc == 1:
                    s.wait_ge(P_.pe.h, ctx[("l1a_wo", 0, NDT - 1)])
                s.dma_start(out=xstage[:, :].rearrange("p (i f) -> p i f", i=NDT),
                            in_=ogall.ap().rearrange("(i p) t -> p i t", p=128)
                            [:, :, bass.ts(off, 512)]).then_inc(P_.ld.h, 16)
        v_cs = P_.ld.inc(16)
        for oc in range(NDT):
            if oc >= 2:
                v_warw = ctx[("l1a_wol", tc, oc - 2)]
            elif tc == 1:
                v_warw = ctx[("l1a_wol", 0, NDT - 2 + oc)]
            else:
                v_warw = None
            with _blk(nc) as b:
                @b.gpsimd
                def _(g, oc=oc, v_warw=v_warw):
                    if v_warw is not None:
                        g.wait_ge(P_.pe.h, v_warw)
                    g.dma_start(out=wst[oc % 2][:, :].rearrange("p (i f) -> p i f", i=NDT),
                                in_=wout_ap[:, 128 * oc:128 * (oc + 1)]
                                .rearrange("(i p) f -> p i f", p=128)).then_inc(P_.gd.h, 16)
            v_ldw = P_.gd.inc(16)
            pa = pool[oc % 2]
            with _blk(nc) as b:
                @b.tensor
                def _(t, tc=tc, oc=oc, v_ldw=v_ldw, pa=pa, v_cs=v_cs):
                    t.wait_ge(P_.gd.h, v_ldw)
                    t.wait_ge(P_.ld.h, v_cs)
                    if oc <= 1:
                        if tc == 1:
                            t.wait_ge(P_.ve.h, ctx[("l1a_woe", 0, NDT - 2 + oc)])
                    else:
                        t.wait_ge(P_.ve.h, ctx[("l1a_woe", tc, oc - 2)])
                    for i in range(NDT):
                        ins = t.matmul(pa[:, :], wst[oc % 2][:, 128 * i:128 * (i + 1)],
                                       xstage[:, 512 * i:512 * (i + 1)],
                                       start=(i == 0), stop=(i == NDT - 1))
                    ins.then_inc(P_.pe.h, 1)
            ctx[("l1a_wo", tc, oc)] = P_.pe.inc(1)
            ctx[("l1a_wol", tc, oc)] = P_.pe.n
            v_mm = P_.pe.n
            with _blk(nc) as b:
                @b.vector
                def _(v, tc=tc, oc=oc, v_mm=v_mm, pa=pa):
                    v.wait_ge(P_.pe.h, v_mm)
                    v.wait_ge(v_x[0].h, v_x[1])
                    v.tensor_add(xr[oc][:, 512 * tc:512 * (tc + 1)],
                                 xr[oc][:, 512 * tc:512 * (tc + 1)], pa[:, :]
                                 ).then_inc(P_.ve.h, 1)
            ctx[("l1a_woe", tc, oc)] = P_.ve.inc(1)
    return [(P_.ve, P_.ve.n)]




def build_l1_attn_prog():
    Pr = P()
    nc = Pr.nc
    xin = nc.dram_tensor("xin", [D, TL], F32, kind="ExternalInput").ap()
    wmix = nc.dram_tensor("l1a_wmix", [D, 772], F32, kind="ExternalInput").ap()
    conv = nc.dram_tensor("l1a_conv", [128, 16], F32, kind="ExternalInput").ap()
    dtb = nc.dram_tensor("l1a_dtb", [4, 1], F32, kind="ExternalInput").ap()
    alog4 = nc.dram_tensor("l1a_alog4", [1, 4], F32, kind="ExternalInput").ap()
    nw = nc.dram_tensor("l1a_nw", [1, 128], F32, kind="ExternalInput").ap()
    wout = nc.dram_tensor("l1a_wout", [D, D], F32, kind="ExternalInput").ap()
    xout = nc.dram_tensor("xout", [D, TL], F32, kind="ExternalOutput").ap()
    Pr.make_pool()
    xr = [Pr.sb(f"xr{i}", [128, TL], F32) for i in range(NDT)]
    xnb = [Pr.sb(f"xnb{i}", [128, TL], BF16) for i in range(NDT)]
    ctx = {
        "xstage": Pr.sb("xstage", [128, 8192], BF16),
        "wst": [Pr.sb(f"wst{i}", [128, 2048], BF16) for i in range(2)],
        "wmx": Pr.sb("wmx", [128, 772 * NDT], BF16),
        "msbq": Pr.sb("msbq", [128, T], BF16),
        "msbk": Pr.sb("msbk", [128, T], BF16),
        "convw": Pr.sb("convw", [128, 16], F32),
        "ctmp": Pr.sb("ctmp", [128, 516], F32),
        "carry": Pr.sb("carry", [128, 16], F32),
        "ssb": Pr.sb("ssb", [128, 512], BF16),
        "cacc": Pr.sb("cacc", [128, 512], F32),
        "ztmp": Pr.sb("ztmp", [128, 512], BF16),
        "bae": Pr.sb("bae", [4, 512], F32),
        "baa": Pr.sb("baa", [4, 512], F32),
        "dtbc": Pr.sb("dtbc", [4, 1], F32),
        "alog4": Pr.sb("alog4s", [1, 4], F32),
        "eAc": Pr.sb("eAc", [4, 1], F32),
        "eACr": Pr.sb("eACr", [1, 2], F32),
        "eACc": Pr.sb("eACc", [128, 1], F32),
        "sgr": Pr.sb("sgr", [1, 2], F32),
        "nwrow": Pr.sb("nwrow", [1, 128], F32),
        "nwmat": Pr.sb("nwmat", [128, 128], BF16),
        "bcols": Pr.sb("bcols", [128, 4], F32),
        "vch": Pr.sb("vch", [128, 256], BF16),
        "zch": Pr.sb("zch", [128, 256], BF16),
        "cumc": Pr.sb("cumc", [128, 2], F32),
        "cumr": Pr.sb("cumr", [1, 128], F32),
        "Dt": Pr.sb("Dt", [128, 128], F32),
        "tA": Pr.sb("tA", [128, 128], F32),
        "ecumR": Pr.sb("ecumR", [128, 128], BF16),
        "EmDT": Pr.sb("EmDT", [128, 128], BF16),
        "EkDT": Pr.sb("EkDT", [128, 128], BF16),
        "EkD": Pr.sb("EkD", [128, 128], BF16),
        "kkf": Pr.sb("kkf", [128, 128], F32),
        "qkf": Pr.sb("qkf", [128, 128], F32),
        "bmat": Pr.sb("bmat", [128, 128], F32),
        "MT": Pr.sb("MT", [128, 128], BF16),
        "NT": Pr.sb("NT", [128, 128], BF16),
        "NTT": Pr.sb("NTT", [128, 128], BF16),
        "NT2": Pr.sb("NT2", [128, 128], BF16),
        "PT": Pr.sb("PT", [128, 128], BF16),
        "kT": Pr.sb("kT", [128, 128], BF16),
        "vT": Pr.sb("vT", [128, 128], BF16),
        "ecumc": Pr.sb("ecumc", [128, 1], F32),
        "beE": Pr.sb("beE", [128, 1], F32),
        "ecr": Pr.sb("ecr", [128, 1], F32),
        "Wb": Pr.sb("Wb", [128, 128], BF16),
        "Bv": Pr.sb("Bv", [128, 128], BF16),
        "Kp": Pr.sb("Kp", [128, 128], BF16),
        "Wp": Pr.sb("Wp", [128, 128], BF16),
        "Ub": Pr.sb("Ub", [128, 128], BF16),
        "QtT": Pr.sb("QtT", [128, 128], BF16),
        "QeT": Pr.sb("QeT", [128, 128], BF16),
        "nG2T": Pr.sb("nG2T", [128, 128], BF16),
        "IA": Pr.sb("IA", [128, 128], BF16),
        "Sb": [Pr.sb(f"Sb{h}", [128, 128], BF16) for h in range(2)],
        "och": Pr.sb("och", [128, 128], F32),
        "osq": Pr.sb("osq", [128, 128], BF16),
        "ovar": Pr.sb("ovar", [128, 1], F32),
        "obuf": Pr.sb("obuf", [128, 128], BF16),
        "qrow": Pr.sb("qrow", [1, 512], F32),
    }
    ctx["identf"] = Pr.sconst("ident_f", np.eye(128, dtype=np.float32))
    ctx["identb"] = Pr.sconst("ident_b", np.eye(128, dtype=bf16dt()))
    ctx["triu"] = Pr.sconst("l1a_triu", _triu_incl())
    ctx["mlt"] = Pr.sconst("l1a_mlt", _m_lt())
    ctx["mle"] = Pr.sconst("l1a_mle", _m_le())
    ctx["mge"] = Pr.sconst("l1a_mge", _m_ge())
    ctx["ones_col_f"] = Pr.sconst("ones_col_f", np.ones((128, 1), np.float32))
    ctx["ones_col_b2"] = Pr.ones_col_b()
    Pr.ones_row_f()
    Pr.flush_consts()
    nctx = make_norm_ctx(Pr, sq_alias=[ctx["xstage"][:, 512 * i:512 * (i + 1)]
                                       for i in range(NDT)],
                         atmp_alias=ctx["cacc"],
                         rows_alias=[ctx["baa"][0:1, 0:512], ctx["bae"][0:1, 0:512],
                                     ctx["ctmp"][0:1, 0:512], ctx["qrow"][0:1, 0:512],
                                     None])
    with _blk(nc) as b:
        @b.sync
        def _(s):
            pid = nc.partition_id()
            ctx["pid_off"] = [nc.snap(pid * 2 + tc, min_val=0,
                                      max_val=2 * NCORES - 1) for tc in range(2)]
    v_x = load_residual(Pr, xr, xin)
    with _blk(nc) as b:
        @b.scalar
        def _(s):
            s.wait_ge(Pr.ld.h, v_x)
            for i in range(NDT):
                s.copy(xnb[i][:, :], xr[i][:, :]).then_inc(Pr.sc.h, 1)
    xnb_ready = (Pr.sc, Pr.sc.inc(NDT))
    fin = emit_l1_attn(Pr, ctx, xr, xnb, xnb_ready, wmix, conv, dtb, alog4, nw,
                       wout, (Pr.ld, v_x))
    fin2 = emit_norm(Pr, nctx, xr, xnb, None, None, subtract_mean=True,
                     pre_waits=fin)
    store_residual(Pr, xr, xout, [fin2])
    Pr.close_block()
    return Pr


def run_l1_attn(x, W_qkv, W_z, W_b, W_a, conv_w, dt_bias, A_log, norm_weight, W_out):
    Pr = build_l1_attn_prog()
    xs = _shard_x(x)
    in_maps = []
    for c in range(NCORES):
        qcol = W_qkv[:, 128 * c:128 * (c + 1)]
        kcol = W_qkv[:, 1024 + 128 * c:1024 + 128 * (c + 1)]
        vcol = W_qkv[:, 2048 + 256 * c:2048 + 256 * (c + 1)]
        zcol = W_z[:, 256 * c:256 * (c + 1)]
        bcol = W_b[:, 2 * c:2 * (c + 1)]
        acol = W_a[:, 2 * c:2 * (c + 1)]
        wmix = np.ascontiguousarray(np.concatenate(
            [qcol, kcol, vcol, zcol, bcol, acol], axis=1))
        chans = np.concatenate([np.arange(128 * c, 128 * (c + 1)),
                                np.arange(1024 + 128 * c, 1024 + 128 * (c + 1)),
                                np.arange(2048 + 256 * c, 2048 + 256 * (c + 1))])
        cw = conv_w[chans]
        cl = np.zeros((128, 16), np.float32)
        for j in range(4):
            cl[:, 4 * j:4 * (j + 1)] = cw[128 * j:128 * (j + 1)]
        dtb4 = np.zeros((4, 1), np.float32)
        dtb4[2, 0] = dt_bias[2 * c]; dtb4[3, 0] = dt_bias[2 * c + 1]
        al4 = np.zeros((1, 4), np.float32)
        al4[0, 2] = A_log[2 * c]; al4[0, 3] = A_log[2 * c + 1]
        in_maps.append({
            "xin": xs[c], "l1a_wmix": wmix, "l1a_conv": cl,
            "l1a_dtb": dtb4, "l1a_alog4": al4,
            "l1a_nw": np.ascontiguousarray(norm_weight.reshape(1, 128)),
            "l1a_wout": W_out,
        })
    res = run_bass_kernel_spmd(Pr.nc, in_maps, core_ids=list(range(NCORES)))
    return _unshard_x([res.results[c]["xout"] for c in range(NCORES)])



def _shard_x(x):
    xs = x.reshape(T, D)
    return [np.ascontiguousarray(xs[TL * c:TL * (c + 1), :].T) for c in range(NCORES)]


def _unshard_x(parts):
    return np.concatenate([p.T for p in parts], axis=0).reshape(1, T, D)


def run_l2_ffn(x, weight, bias, skip_norm=False):
    Pr = build_l2_ffn_prog(skip_norm)
    xs = _shard_x(x)
    wT = np.ascontiguousarray(weight.T)
    bias_l = np.ascontiguousarray(bias.reshape(16, 128).T)
    in_maps = [{"xin": xs[c], "l2f_wT": wT, "l2f_w": weight, "l2f_bias": bias_l}
               for c in range(NCORES)]
    res = run_bass_kernel_spmd(Pr.nc, in_maps, core_ids=list(range(NCORES)))
    return _unshard_x([res.results[c]["xout"] for c in range(NCORES)])



class Arena:

    def __init__(self, P_, size):
        self.P_ = P_
        self.size = size
        self.t = P_.nc.alloc_sbuf_tensor("arena", [128, size], mybir.dt.uint8)
        self.base = P_.nc.lookup_mloc(self.t).addr
        self.off = 0
        self.hwm = 0

    def reset(self):
        self.off = 0

    def sb(self, name, shape, dtype):
        bpp = 1
        for s in shape[1:]:
            bpp *= s
        bpp *= mybir.dt.size(dtype)
        off = (self.off + 31) & ~31
        self.off = off + bpp
        self.hwm = max(self.hwm, self.off)
        assert self.off <= self.size, (name, self.off, self.size)
        return self.P_.nc.alloc_sbuf_tensor_at(name, shape, dtype,
                                               offset=self.base + off)


def stage_consts(P_, A, pairs):
    out, todo = {}, []
    for (name, arr) in pairs:
        arr = np.ascontiguousarray(arr)
        dtt = {np.dtype(np.float32): F32, np.dtype(bf16dt()): BF16,
               np.dtype(np.int32): I32}[arr.dtype]
        t = A.sb("bc_" + name, list(arr.shape), dtt)
        d = P_.const("cd_" + name, arr)
        out[name] = t
        todo.append((t, d))
    with _blk(P_.nc) as b:
        @b.sync
        def _(s):
            for (t, d) in todo:
                s.dma_start(out=t[:, :], in_=d[:, :]).then_inc(P_.ld.h, 16)
    P_.v_consts = P_.ld.inc(16 * len(todo))
    v = P_.v_consts
    with _blk(P_.nc) as b:
        @b.vector
        def _(e):
            e.wait_ge(P_.ld.h, v)
        @b.scalar
        def _(e):
            e.wait_ge(P_.ld.h, v)
        @b.tensor
        def _(e):
            e.wait_ge(P_.ld.h, v)
        @b.gpsimd
        def _(e):
            e.wait_ge(P_.ld.h, v)
    return out


def build_fused_prog(nblk=8):
    Pr = P()
    nc = Pr.nc
    dt = lambda n, sh: nc.dram_tensor(n, sh, F32, kind="ExternalInput").ap()
    xin = dt("xin", [D, TL])
    xout = nc.dram_tensor("xout", [D, TL], F32, kind="ExternalOutput").ap()
    l0a_wqT = dt("l0a_wqT", [D, D]); l0a_wkT = dt("l0a_wkT", [D, D])
    l0a_wvs = dt("l0a_wvs", [D, 256]); l0a_wgs = dt("l0a_wgs", [D, 256])
    l0a_wpT = dt("l0a_wpT", [D, D])
    l0f_gateT = dt("l0f_gateT", [D, 8])
    l0f_w1s = dt("l0f_w1s", [D, 1024]); l0f_w3s = dt("l0f_w3s", [D, 1024])
    l0f_w2s = dt("l0f_w2s", [1024, D])
    l0f_ws1s = dt("l0f_ws1s", [256, 1024]); l0f_ws3s = dt("l0f_ws3s", [256, 1024])
    l0f_ws2s = dt("l0f_ws2s", [128, D])
    l1a_wmix = dt("l1a_wmix", [D, 772]); l1a_conv = dt("l1a_conv", [128, 16])
    l1a_dtb = dt("l1a_dtb", [4, 1]); l1a_alog4 = dt("l1a_alog4", [1, 4])
    l1a_nw = dt("l1a_nw", [1, 128]); l1a_wout = dt("l1a_wout", [D, D])
    l1f_lut = dt("l1f_lut", [128, 16])
    l1f_gsh = nc.dram_tensor("l1f_gsh", [256, HL1], I32, kind="ExternalInput").ap()
    l1f_ush = nc.dram_tensor("l1f_ush", [256, HL1], I32, kind="ExternalInput").ap()
    l1f_dsh = nc.dram_tensor("l1f_dsh", [HL1, 256], I32, kind="ExternalInput").ap()
    l2a_wq = dt("l2a_wq", [D, 256]); l2a_wk = dt("l2a_wk", [D, 256])
    l2a_wv = dt("l2a_wv", [D, 256]); l2a_wo = dt("l2a_wo", [D, D])
    l2a_qs = dt("l2a_qs", [128, 2]); l2a_ks = dt("l2a_ks", [128, 2])
    l2f_wT = dt("l2f_wT", [D, D]); l2f_w = dt("l2f_w", [D, D])
    l2f_bias = dt("l2f_bias", [128, 16])
    l3a_wq = dt("l3a_wq", [D, 256]); l3a_wk = dt("l3a_wk", [D, 128])
    l3a_wv = dt("l3a_wv", [D, 128]); l3a_wo = dt("l3a_wo", [D, D])
    l3a_bo = dt("l3a_bo", [128, 16])
    l3f_w1s = dt("l3f_w1s", [D, 1024]); l3f_w3s = dt("l3f_w3s", [D, 1024])
    l3f_w2s = dt("l3f_w2s", [H_L3, 256])

    Pr.make_pool()
    xr = [Pr.sb(f"xr{i}", [128, TL], F32) for i in range(NDT)]
    xnb_back = Pr.sb("xnb_back", [128, NDT * TL], BF16)
    xnb_base = nc.lookup_mloc(xnb_back).addr
    xnb = [nc.alloc_sbuf_tensor_at(f"xnb{i}", [128, TL], BF16,
                                   offset=xnb_base + 2 * TL * i)
           for i in range(NDT)]
    rows4 = [Pr.sb(f"nrow{k}", [1, 512], F32) for k in range(4)]
    rows = [rows4[0], rows4[1], rows4[2], rows4[3], rows4[2]]
    atmp_p = Pr.sb("natmp", [128, 512], F32)
    identf = Pr.sconst("ident_f", np.eye(128, dtype=np.float32))
    identb = Pr.ident_b()
    Pr.ones_col_b(); Pr.ones_row_f()
    Pr.sconst("h128_bf", _hadamard128_bf())
    Pr.flush_consts()
    pid_box = {}
    with _blk(nc) as b:
        @b.sync
        def _(s):
            pid = nc.partition_id()
            pid_box["off"] = [nc.snap(pid * 2 + tc, min_val=0,
                                      max_val=2 * NCORES - 1) for tc in range(2)]
    pid_off = pid_box["off"]

    A = Arena(Pr, (nc.sbuf_bytes_remaining - 32) // 32 * 32)

    def mk_nctx(sq_alias, atmp_alias=None):
        return make_norm_ctx(Pr, sq_alias=sq_alias,
                             atmp_alias=(atmp_alias if atmp_alias is not None
                                         else atmp_p),
                             rows_alias=rows)

    set_fast(0 in FAST_BLOCKS)
    ctx = {
        "xstage": A.sb("a0_xstage", [128, 8192], BF16),
        "wst": [A.sb(f"a0_wst{i}", [128, 2048], BF16) for i in range(2)],
        "wst2": [A.sb(f"a0_wst2{i}", [128, 512], BF16) for i in range(2)],
        "wstb": A.sb("a0_wstb", [128, 8192], BF16),
        "kvf": [A.sb(f"a0_kvf{i}", [128, 256], F32) for i in range(NDT)],
        "kvb": [A.sb(f"a0_kvb{i}", [128, 256], BF16) for i in range(NDT)],
        "qst": A.sb("a0_qst", [128, 4096], BF16),
        "kst": A.sb("a0_kst", [128, 4096], BF16),
        "ktdst": A.sb("a0_ktdst", [128, 4096], BF16),
        "vst": A.sb("a0_vst", [128, 512], BF16),
        "gst": A.sb("a0_gst", [128, 512], BF16),
        "ssb": A.sb("a0_ssb", [128, 512], BF16),
        "otmp": A.sb("a0_otmp", [128, 256], F32),
        "obuf": A.sb("a0_obuf", [128, 256], BF16),
        "identf": identf, "pid_off": pid_off,
    }
    ctx["dmask"] = stage_consts(Pr, A, [("l3a_dmask", _l3_dmask())])["l3a_dmask"]
    nctx = mk_nctx([ctx["xstage"][:, 512 * i:512 * (i + 1)] for i in range(NDT)])
    v_x = load_residual(Pr, xr, xin)
    rfin = emit_norm(Pr, nctx, xr, xnb, None, None, subtract_mean=False,
                     eps=RMS_EPS, pre_waits=[(Pr.ld, v_x)], write_back=False)
    with _blk(nc) as b:
        @b.vector
        def _(v):
            for i in range(NDT):
                v.memset(ctx["kvf"][i][:, :], 0.0)
            v.engine_nop().then_inc(Pr.ve.h, 1)
    Pr.ve.inc(1)
    fin = emit_l0_attn(Pr, ctx, xr, xnb, rfin, l0a_wqT, l0a_wkT, l0a_wvs,
                       l0a_wgs, l0a_wpT, (Pr.ld, v_x))
    prev = emit_norm(Pr, nctx, xr, xnb, None, None, subtract_mean=True,
                     pre_waits=fin)
    if nblk >= 2:
        Pr.close_block()
        set_fast(1 in FAST_BLOCKS)
        A.reset()
        Pr.new_pool("f0")
        he = [A.sb(f"f0_he{i}", [128, 512], BF16) for i in range(72)]
        ctx = {
            "he": he,
            "w13st": [A.sb(f"f0_w13st{i}", [128, 2048], BF16) for i in range(2)],
            "w2st": [A.sb(f"f0_w2st{i}", [128, 1024], BF16) for i in range(2)],
            "silt": A.sb("f0_silt", [128, 512], BF16),
            "silt2": A.sb("f0_silt2", [128, 512], F32),
            "gsb": A.sb("f0_gsb", [128, 32], F32),
            "wE": A.sb("f0_wE", [8, TL], BF16),
            "wexp": [A.sb(f"f0_wexp{i}", [128, 512], BF16) for i in range(8)],
            "gstage": A.sb("f0_gstage", [128, 128], F32),
            "identf": identf,
        }
        Pr._sconsts["l0f_sel"] = stage_consts(
            Pr, A, [("l0f_sel", _sel8_const())])["l0f_sel"]
        nctx = mk_nctx([he[i] for i in range(NDT)], atmp_alias=ctx["silt2"])
        wbs, wready = emit_l0_ffn_prep(Pr, l0f_w1s, l0f_w3s, l0f_w2s,
                                       l0f_ws1s, l0f_ws3s, l0f_ws2s)
        fin = emit_l0_ffn(Pr, ctx, xr, xnb, prev, l0f_gateT, wbs, wready, prev)
        prev = emit_norm(Pr, nctx, xr, xnb, None, None, subtract_mean=True,
                         pre_waits=fin)
    if nblk >= 3:
        Pr.close_block()
        set_fast(2 in FAST_BLOCKS)
        A.reset()
        Pr.new_pool("a1")
        ctx = {
            "xstage": A.sb("a1_xstage", [128, 8192], BF16),
            "wst": [A.sb(f"a1_wst{i}", [128, 2048], BF16) for i in range(2)],
            "wmx": A.sb("a1_wmx", [128, 772 * NDT], BF16),
            "msbq": nc.alloc_sbuf_tensor_at("a1_msbq", [128, T], BF16,
                                            offset=xnb_base),
            "msbk": nc.alloc_sbuf_tensor_at("a1_msbk", [128, T], BF16,
                                            offset=xnb_base + 2 * T),
            "convw": A.sb("a1_convw", [128, 16], F32),
            "ctmp": A.sb("a1_ctmp", [128, 516], F32),
            "carry": A.sb("a1_carry", [128, 16], F32),
            "ssb": A.sb("a1_ssb", [128, 512], BF16),
            "cacc": A.sb("a1_cacc", [128, 512], F32),
            "ztmp": A.sb("a1_ztmp", [128, 512], BF16),
            "bae": A.sb("a1_bae", [4, 512], F32),
            "baa": A.sb("a1_baa", [4, 512], F32),
            "dtbc": A.sb("a1_dtbc", [4, 1], F32),
            "alog4": A.sb("a1_alog4s", [1, 4], F32),
            "eAc": A.sb("a1_eAc", [4, 1], F32),
            "eACr": A.sb("a1_eACr", [1, 2], F32),
            "eACc": A.sb("a1_eACc", [128, 1], F32),
            "sgr": A.sb("a1_sgr", [1, 2], F32),
            "nwrow": A.sb("a1_nwrow", [1, 128], F32),
            "nwmat": A.sb("a1_nwmat", [128, 128], BF16),
            "bcols": A.sb("a1_bcols", [128, 4], F32),
            "vch": A.sb("a1_vch", [128, 256], BF16),
            "zch": A.sb("a1_zch", [128, 256], BF16),
            "cumc": A.sb("a1_cumc", [128, 2], F32),
            "cumr": A.sb("a1_cumr", [1, 128], F32),
            "Dt": A.sb("a1_Dt", [128, 128], F32),
            "tA": A.sb("a1_tA", [128, 128], F32),
            "ecumR": A.sb("a1_ecumR", [128, 128], BF16),
            "EmDT": A.sb("a1_EmDT", [128, 128], BF16),
            "EkDT": A.sb("a1_EkDT", [128, 128], BF16),
            "EkD": A.sb("a1_EkD", [128, 128], BF16),
            "kkf": A.sb("a1_kkf", [128, 128], F32),
            "qkf": A.sb("a1_qkf", [128, 128], F32),
            "bmat": A.sb("a1_bmat", [128, 128], F32),
            "MT": A.sb("a1_MT", [128, 128], BF16),
            "NT": A.sb("a1_NT", [128, 128], BF16),
            "NTT": A.sb("a1_NTT", [128, 128], BF16),
            "NT2": A.sb("a1_NT2", [128, 128], BF16),
            "PT": A.sb("a1_PT", [128, 128], BF16),
            "kT": A.sb("a1_kT", [128, 128], BF16),
            "vT": A.sb("a1_vT", [128, 128], BF16),
            "ecumc": A.sb("a1_ecumc", [128, 1], F32),
            "beE": A.sb("a1_beE", [128, 1], F32),
            "ecr": A.sb("a1_ecr", [128, 1], F32),
            "Wb": A.sb("a1_Wb", [128, 128], BF16),
            "Bv": A.sb("a1_Bv", [128, 128], BF16),
            "Kp": A.sb("a1_Kp", [128, 128], BF16),
            "Wp": A.sb("a1_Wp", [128, 128], BF16),
            "Ub": A.sb("a1_Ub", [128, 128], BF16),
            "QtT": A.sb("a1_QtT", [128, 128], BF16),
            "QeT": A.sb("a1_QeT", [128, 128], BF16),
            "nG2T": A.sb("a1_nG2T", [128, 128], BF16),
            "IA": A.sb("a1_IA", [128, 128], BF16),
            "Sb": [A.sb(f"a1_Sb{h}", [128, 128], BF16) for h in range(2)],
            "och": A.sb("a1_och", [128, 128], F32),
            "osq": A.sb("a1_osq", [128, 128], BF16),
            "ovar": A.sb("a1_ovar", [128, 1], F32),
            "obuf": A.sb("a1_obuf", [128, 128], BF16),
            "qrow": A.sb("a1_qrow", [1, 512], F32),
            "identf": identf, "identb": identb,
            "ones_col_b2": Pr.ones_col_b(), "pid_off": pid_off,
        }
        bc = stage_consts(Pr, A, [("l1a_triu", _triu_incl()),
                                  ("l1a_mlt", _m_lt()), ("l1a_mle", _m_le()),
                                  ("l1a_mge", _m_ge()),
                                  ("ones_col_f", np.ones((128, 1), np.float32))])
        ctx["triu"] = bc["l1a_triu"]; ctx["mlt"] = bc["l1a_mlt"]
        ctx["mle"] = bc["l1a_mle"]; ctx["mge"] = bc["l1a_mge"]
        ctx["ones_col_f"] = bc["ones_col_f"]
        nctx = mk_nctx([ctx["xstage"][:, 512 * i:512 * (i + 1)]
                        for i in range(NDT)], atmp_alias=ctx["cacc"])
        fin = emit_l1_attn(Pr, ctx, xr, xnb, prev, l1a_wmix, l1a_conv, l1a_dtb,
                           l1a_alog4, l1a_nw, l1a_wout, prev)
        prev = emit_norm(Pr, nctx, xr, xnb, None, None, subtract_mean=True,
                         pre_waits=fin)
    if nblk >= 4:
        Pr.close_block()
        set_fast(3 in FAST_BLOCKS)
        A.reset()
        Pr.new_pool("f1")
        ht1 = [A.sb(f"f1_ht{i}", [128, 512], BF16) for i in range(NHC1)]
        ctx = {
            "ht1": ht1,
            "wst1": [A.sb(f"f1_wst{i}", [128, 2048], BF16) for i in range(4)],
            "w2st1": [A.sb(f"f1_w2st{i}", [128, 1024], BF16) for i in range(2)],
            "silt1": A.sb("f1_silt", [128, 512], BF16),
            "idxst": A.sb("f1_idx", [128, 1376], I32),
            "wtile": A.sb("f1_wtile", [128, HL1], BF16),
            "ttmp": A.sb("f1_ttmp", [128, 1376], BF16),
            "tacc": A.sb("f1_tacc", [128, 1376], BF16),
            "lutsb": A.sb("f1_lut", [128, 16], F32),
        }
        nctx = mk_nctx([ht1[i] for i in range(NDT)])
        gb, ub, db, wready = emit_l1_ffn_build(Pr, ctx, l1f_gsh, l1f_ush,
                                               l1f_dsh, l1f_lut)
        fin = emit_l1_ffn(Pr, ctx, xr, xnb, prev, gb, ub, db, wready)
        prev = emit_norm(Pr, nctx, xr, xnb, None, None, subtract_mean=True,
                         pre_waits=fin)
    if nblk >= 5:
        Pr.close_block()
        set_fast(4 in FAST_BLOCKS)
        A.reset()
        Pr.new_pool("a2")
        ctx = {
            "qbuf": A.sb("a2_qbuf", [128, 1024], BF16),
            "kring": A.sb("a2_kring", [128, 4096], BF16),
            "vring": A.sb("a2_vring", [128, 4096], BF16),
            "xstage": A.sb("a2_xstage", [128, 8192], BF16),
            "wqkvt": A.sb("a2_wqkvt", [128, 12288], BF16),
            "wost": [A.sb(f"a2_wost{i}", [128, 2048], BF16) for i in range(2)],
            "es": [A.sb(f"a2_es{i}", [128, 512], BF16) for i in range(2)],
            "cbuf": [A.sb(f"a2_cbuf{i}", [128, 512], BF16) for i in range(2)],
            "qtmp": A.sb("a2_qtmp", [128, 1024], F32),
            "vtmp0": A.sb("a2_vtmp0", [128, 512], F32),
            "vtmp": A.sb("a2_vtmp", [128, 512], F32),
            "cst": A.sb("a2_cst", [128, 1024], BF16),
            "scl": A.sb("a2_scl", [128, 4], F32),
            "cfix": A.sb("a2_cfix", [128, 512], F32),
            "cfix2": A.sb("a2_cfix2", [128, 512], F32),
            "sqn": A.sb("a2_sqn", [128, 1024], BF16),
            "identf": identf, "pid_off": pid_off,
        }
        bc = stage_consts(Pr, A, [("l2a_pswap", _pswap()),
                                  ("l3a_dmask", _l3_dmask()),
                                  ("l2a_wmask", _l2_wmask())])
        ctx["pswap"] = bc["l2a_pswap"]; ctx["dmask"] = bc["l3a_dmask"]
        ctx["wmask"] = bc["l2a_wmask"]
        C, S = _l2_rope_tables()
        ctx["ropeC"] = Pr.const("l2a_ropeC", C)
        ctx["ropeS"] = Pr.const("l2a_ropeS", S)
        nctx = mk_nctx([ctx["xstage"][:, 512 * i:512 * (i + 1)]
                        for i in range(NDT)], atmp_alias=ctx["cfix"])
        ctx["crow"] = nctx["r_msq"]
        fin = emit_l2_attn(Pr, ctx, xr, prev, l2a_wq, l2a_wk, l2a_wv, l2a_wo,
                           l2a_qs, l2a_ks)
        prev = emit_norm(Pr, nctx, xr, xnb, None, None, subtract_mean=True,
                         pre_waits=fin)
    if nblk >= 6:
        Pr.close_block()
        set_fast(5 in FAST_BLOCKS)
        A.reset()
        Pr.new_pool("f2")
        ctx = {
            "sgnT": [A.sb(f"f2_sgnT{i}", [128, D], BF16) for i in range(NDT)],
            "wstage": A.sb("f2_wstage", [128, D], BF16),
            "alphab": A.sb("f2_alphab", [128, 33], F32),
            "psA": Pr.make_pool()[4],
            "psB": Pr.make_pool()[5],
            "v_x": prev,
        }
        sq2 = [A.sb(f"f2_sq{i}", [128, 512], BF16) for i in range(NDT)]
        nctx = mk_nctx(sq2)
        _, y_ready = emit_l2_ffn(Pr, ctx, xr, xnb, prev, l2f_wT, l2f_w,
                                 l2f_bias)
        prev = emit_norm(Pr, nctx, xr, xnb, None, None, subtract_mean=True,
                         pre_waits=y_ready)
    if nblk >= 7:
        Pr.close_block()
        set_fast(6 in FAST_BLOCKS)
        A.reset()
        Pr.new_pool("a3")
        qsb = [nc.alloc_sbuf_tensor_at(f"a3_qsb{j}", [128, 8192], BF16,
                                       offset=xnb_base + 2 * 8192 * j)
               for j in range(2)]
        ctx = {
            "qsb": qsb,
            "ksb": A.sb("a3_ksb", [128, T], BF16),
            "vTsb": A.sb("a3_vTsb", [128, T], BF16),
            "xstage": A.sb("a3_xstage", [128, 8192], BF16),
            "wqkvt": A.sb("a3_wqkvt", [128, 8192], BF16),
            "wost": [A.sb(f"a3_wost{i}", [128, 2048], BF16) for i in range(2)],
            "es": [A.sb(f"a3_es{i}", [128, 512], BF16) for i in range(2)],
            "cbuf": [A.sb(f"a3_cbuf{i}", [128, 512], BF16) for i in range(2)],
            "vtmp": A.sb("a3_vtmp", [128, 512], F32),
            "cfix": A.sb("a3_cfix", [128, 512], F32),
            "identf": identf, "pid_off": pid_off,
        }
        ctx["dmask"] = stage_consts(Pr, A,
                                    [("l3a_dmask", _l3_dmask())])["l3a_dmask"]
        nctx = mk_nctx([ctx["vTsb"][:, 512 * i:512 * (i + 1)]
                        for i in range(NDT)], atmp_alias=ctx["cfix"])
        ctx["crow"] = nctx["r_msq"]
        fin, _call = emit_l3_attn(Pr, ctx, xr, prev, l3a_wq, l3a_wk, l3a_wv,
                                  l3a_wo, l3a_bo)
        prev = emit_norm(Pr, nctx, xr, xnb, None, None, subtract_mean=True,
                         pre_waits=fin)
    if nblk >= 8:
        Pr.close_block()
        set_fast(7 in FAST_BLOCKS)
        A.reset()
        Pr.new_pool("f3")
        ht = [A.sb(f"f3_ht{i}", [128, 512], BF16) for i in range(NHT)]
        ctx = {
            "ht": ht,
            "w1st": [A.sb(f"f3_w1st{i}", [128, 2048], BF16) for i in range(2)],
            "w3st": [A.sb(f"f3_w3st{i}", [128, 2048], BF16) for i in range(2)],
            "w2st": [A.sb(f"f3_w2st{i}", [128, 1024], BF16) for i in range(2)],
            "silt": A.sb("f3_silt", [128, 2048], BF16),
            "wt_alias": ht,
        }
        nctx = mk_nctx([ht[i] for i in range(NDT)])
        Pr._l3_wt = [ht[i][:, 0:256] for i in range(NHT)]
        Pr._l3_wtmp = ctx["silt"][:, 0:256]
        w1b, w3b, w2b, wready = emit_l3_ffn_prep(Pr, l3f_w1s, l3f_w3s, l3f_w2s)
        fin = emit_l3_ffn(Pr, ctx, xr, xnb, prev, w1b, w3b, w2b, wready, prev)
        prev = emit_norm(Pr, nctx, xr, xnb, None, None, subtract_mean=True,
                         pre_waits=fin)

    Pr.close_block()
    set_fast(False)
    store_residual(Pr, xr, xout, [prev])
    Pr.close_block()
    return Pr


def fused_in_maps(I):
    x = np.asarray(I["L0_attn_x"], dtype=np.float32)
    xs = _shard_x(x)
    ca = np.ascontiguousarray
    wqT = ca(I["L0_attn_Wq"].T); wkT = ca(I["L0_attn_Wk"].T)
    wvT = ca(I["L0_attn_Wv"].T); wgT = ca(I["L0_attn_Wg"].T)
    wpT = ca(I["L0_attn_Wp"].T)
    w1T = ca(I["L0_ffn_w1"].transpose(0, 2, 1))
    w3T = ca(I["L0_ffn_w3"].transpose(0, 2, 1))
    w2T = ca(I["L0_ffn_w2"].transpose(0, 2, 1))
    ws1T = ca(I["L0_ffn_ws1"].T); ws3T = ca(I["L0_ffn_ws3"].T)
    ws2T = ca(I["L0_ffn_ws2"].T)
    gT = ca(I["L0_ffn_gate_w"].T)
    W_qkv, W_z = I["L1_attn_W_qkv"], I["L1_attn_W_z"]
    W_b, W_a = I["L1_attn_W_b"], I["L1_attn_W_a"]
    conv_w, dt_bias = I["L1_attn_conv_w"], I["L1_attn_dt_bias"]
    A_log = I["L1_attn_A_log"]
    lut = I["L1_ffn_lut"]
    gT1 = ca(I["L1_ffn_gate_idx"].T); uT1 = ca(I["L1_ffn_up_idx"].T)
    dT1 = ca(I["L1_ffn_down_idx"].T)
    qs2 = ca(I["L2_attn_q_scale"].reshape(2, 128).T)
    ks2 = ca(I["L2_attn_k_scale"].reshape(2, 128).T)
    l2wT = ca(I["L2_ffn_weight"].T)
    l2bias = ca(I["L2_ffn_bias"].reshape(16, 128).T)
    bo_l = ca(I["L3_attn_bo"].reshape(16, 128).T)
    w1T3 = ca(I["L3_ffn_W1"].T); w3T3 = ca(I["L3_ffn_W3"].T)
    w2T3 = ca(I["L3_ffn_W2"].T)
    in_maps = []
    for c in range(NCORES):
        qcol = W_qkv[:, 128 * c:128 * (c + 1)]
        kcol = W_qkv[:, 1024 + 128 * c:1024 + 128 * (c + 1)]
        vcol = W_qkv[:, 2048 + 256 * c:2048 + 256 * (c + 1)]
        zcol = W_z[:, 256 * c:256 * (c + 1)]
        bcol = W_b[:, 2 * c:2 * (c + 1)]
        acol = W_a[:, 2 * c:2 * (c + 1)]
        wmix = ca(np.concatenate([qcol, kcol, vcol, zcol, bcol, acol], axis=1))
        chans = np.concatenate([np.arange(128 * c, 128 * (c + 1)),
                                np.arange(1024 + 128 * c, 1024 + 128 * (c + 1)),
                                np.arange(2048 + 256 * c, 2048 + 256 * (c + 1))])
        cw = conv_w[chans]
        cl = np.zeros((128, 16), np.float32)
        for j in range(4):
            cl[:, 4 * j:4 * (j + 1)] = cw[128 * j:128 * (j + 1)]
        dtb4 = np.zeros((4, 1), np.float32)
        dtb4[2, 0] = dt_bias[2 * c]; dtb4[3, 0] = dt_bias[2 * c + 1]
        al4 = np.zeros((1, 4), np.float32)
        al4[0, 2] = A_log[2 * c]; al4[0, 3] = A_log[2 * c + 1]
        g2 = c // 2
        in_maps.append({
            "xin": xs[c],
            "l0a_wqT": wqT, "l0a_wkT": wkT,
            "l0a_wvs": ca(wvT[:, 256 * c:256 * (c + 1)]),
            "l0a_wgs": ca(wgT[:, 256 * c:256 * (c + 1)]),
            "l0a_wpT": wpT,
            "l0f_gateT": gT, "l0f_w1s": w1T[c], "l0f_w3s": w3T[c],
            "l0f_w2s": w2T[c],
            "l0f_ws1s": ca(ws1T[256 * c:256 * (c + 1)]),
            "l0f_ws3s": ca(ws3T[256 * c:256 * (c + 1)]),
            "l0f_ws2s": ca(ws2T[128 * c:128 * (c + 1)]),
            "l1a_wmix": wmix, "l1a_conv": cl, "l1a_dtb": dtb4,
            "l1a_alog4": al4,
            "l1a_nw": ca(I["L1_attn_norm_weight"].reshape(1, 128)),
            "l1a_wout": I["L1_attn_W_out"],
            "l1f_lut": ca(np.broadcast_to(lut.reshape(1, 16), (128, 16))),
            "l1f_gsh": ca(gT1[256 * c:256 * (c + 1)]),
            "l1f_ush": ca(uT1[256 * c:256 * (c + 1)]),
            "l1f_dsh": ca(dT1[:, 256 * c:256 * (c + 1)]),
            "l2a_wq": ca(I["L2_attn_Wq"][:, 256 * c:256 * (c + 1)]),
            "l2a_wk": ca(I["L2_attn_Wk"][:, 256 * g2:256 * (g2 + 1)]),
            "l2a_wv": ca(I["L2_attn_Wv"][:, 256 * g2:256 * (g2 + 1)]),
            "l2a_wo": I["L2_attn_Wo"], "l2a_qs": qs2, "l2a_ks": ks2,
            "l2f_wT": l2wT, "l2f_w": I["L2_ffn_weight"], "l2f_bias": l2bias,
            "l3a_wq": ca(I["L3_attn_Wq"][:, 256 * c:256 * (c + 1)]),
            "l3a_wk": ca(I["L3_attn_Wk"][:, 128 * g2:128 * (g2 + 1)]),
            "l3a_wv": ca(I["L3_attn_Wv"][:, 128 * g2:128 * (g2 + 1)]),
            "l3a_wo": I["L3_attn_Wo"], "l3a_bo": bo_l,
            "l3f_w1s": ca(w1T3[:, 1024 * c:1024 * (c + 1)]),
            "l3f_w3s": ca(w3T3[:, 1024 * c:1024 * (c + 1)]),
            "l3f_w2s": ca(w2T3[:, 256 * c:256 * (c + 1)]),
        })
    return in_maps


def kernel(**inputs):
    I = {k: np.asarray(v) for k, v in inputs.items()}
    Pr = build_fused_prog()
    in_maps = fused_in_maps(I)
    res = run_bass_kernel_spmd(Pr.nc, in_maps, core_ids=list(range(NCORES)))
    x = _unshard_x([res.results[c]["xout"] for c in range(NCORES)])
    return np.ascontiguousarray(x.astype(np.float32))
